# version 06
# baseline (speedup 1.0000x reference)
import numpy as np
import concourse.bass as bass
import concourse.mybir as mybir
from concourse.bass_utils import run_bass_kernel_spmd

F32 = mybir.dt.float32
BF16 = mybir.dt.bfloat16
AF = mybir.ActivationFunctionType
ALU = mybir.AluOpType
AX = mybir.AxisListType

NC = 8
T_FULL = 16384
D = 1024
DI = 2048
DS = 64
NEG = -1e38



class Sem:

    def __init__(self, name):
        self.name = name
        self.n = 0
        self.h = None

    def inc(self, k=1):
        self.n += k
        return self.n


class P:

    def __init__(self):
        self.ops = []
        self.sems = []
        self.sbufs = []
        self.psums = []
        self.drams_in = []
        self.drams_out = []
        self.drams_int = []
        self.memsets = []

    def sem(self, name):
        s = Sem(name)
        self.sems.append(s)
        return s

    def sbuf(self, name, shape, dtype):
        self.sbufs.append((name, shape, dtype))
        return name

    def psum(self, name, shape, dtype=F32):
        self.psums.append((name, shape, dtype))
        return name

    def din(self, name, shape, dtype=F32):
        self.drams_in.append((name, shape, dtype))
        return name

    def dout(self, name, shape, dtype=F32):
        self.drams_out.append((name, shape, dtype))
        return name

    def dint(self, name, shape, dtype=F32):
        self.drams_int.append((name, shape, dtype))
        return name

    def add(self, tag, fn):
        self.ops.append((tag, fn))

    def g(self, fn): self.add("g", fn)
    def v(self, fn): self.add("v", fn)
    def s(self, fn): self.add("s", fn)
    def t(self, fn): self.add("t", fn)
    def y(self, fn): self.add("y", fn)


class Env(dict):
    __getattr__ = dict.__getitem__


def build_bass(p: P):
    nc = bass.Bass("TRN2", target_bir_lowering=False, debug=False)
    e = Env()
    for name, shape, dt_ in p.drams_in:
        e[name] = nc.declare_dram_parameter(name, list(shape), dt_, isOutput=False).ap()
    for name, shape, dt_ in p.drams_out:
        e[name] = nc.declare_dram_parameter(name, list(shape), dt_, isOutput=True).ap()
    for name, shape, dt_ in p.drams_int:
        e[name] = nc.dram_tensor(name, list(shape), dt_).ap()
    import contextlib
    with contextlib.ExitStack() as stack:
        for name, shape, dt_ in p.sbufs:
            e[name] = stack.enter_context(nc.sbuf_tensor(name, list(shape), dt_))
        for name, shape, dt_ in p.psums:
            e[name] = stack.enter_context(nc.psum_tensor(name, list(shape), dt_))
        for s in p.sems:
            s.h = stack.enter_context(nc.semaphore(s.name))
        block = stack.enter_context(nc.Block())

        ops = p.ops

        @block.gpsimd
        def _(g):
            for tag, fn in ops:
                if tag == "g":
                    fn(g, e)

        @block.vector
        def _(v):
            for tag, fn in ops:
                if tag == "v":
                    fn(v, e)

        @block.scalar
        def _(s):
            for tag, fn in ops:
                if tag == "s":
                    fn(s, e)

        @block.tensor
        def _(t):
            for tag, fn in ops:
                if tag == "t":
                    fn(t, e)

        @block.sync
        def _(sy):
            for tag, fn in ops:
                if tag == "y":
                    fn(sy, e)

    return nc



def make_p0(TSL):
    p = P()
    NCH = TSL // 512
    NW = TSL // 128

    p.din("xfm", (D, TSL), F32)
    p.din("gwT", (D, 9), F32)
    p.dout("xn0b", (D, TSL), BF16)
    p.dout("lgo", (TSL, 9), F32)
    p.dout("rwo", (TSL, 8), F32)
    p.dout("masko", (TSL, 8), F32)
    p.dout("cnto", (1, 8), F32)

    p.sbuf("xr", (128, 8 * TSL), F32)
    p.sbuf("xn", (128, 8 * TSL), BF16)
    p.sbuf("gw9", (128, 8 * 9), BF16)
    p.sbuf("w1024", (128, 1), F32)
    p.sbuf("onesc", (128, 1), F32)
    p.sbuf("ones1", (1, 128), F32)
    p.sbuf("sq", (128, 8 * 512), F32)
    p.sbuf("mean", (1, 512), F32)
    p.sbuf("m2", (1, 512), F32)
    p.sbuf("var", (1, 512), F32)
    p.sbuf("lnv", (1, 512), F32)
    p.sbuf("rstd", (1, 512), F32)
    p.sbuf("ceps", (1, 1), F32)
    p.sbuf("tmp", (128, 512), F32)
    p.sbuf("lgsb", (128, NW * 9), F32)
    p.sbuf("rwsb", (128, NW * 8), F32)
    p.sbuf("msksb", (128, NW * 8), F32)
    p.sbuf("cntsb", (1, 8), F32)
    for nm, w in [("srt", 8), ("esr", 8), ("es2", 8), ("psl", 8), ("cums", 4),
                  ("asrt", 8),
                  ("vld", 4), ("adyn", 8), ("gtm", 8), ("fac", 8), ("nf", 8),
                  ("dmt", 8), ("jm", 8), ("ex", 8), ("kp", 8), ("ee", 8),
                  ("ek", 8), ("rww", 8), ("oh", 8), ("ctr", 8)]:
        p.sbuf(nm, (128, w), F32)
    for nm in ["esum", "rec", "athr", "esm2", "mr", "mrv", "rs", "rr"]:
        p.sbuf(nm, (128, 1), F32)

    p.psum("psA", (1, 512))
    p.psum("psB", (1, 512))
    p.psum("pm", (128, 512))
    p.psum("pr", (128, 512))
    p.psum("lgps", (128, 9))
    p.psum("cntps", (1, 8))

    ld = p.sem("ld")
    ini = p.sem("ini")
    sqs = p.sem("sqs")
    mmst = p.sem("mmst")
    vstat = p.sem("vstat")
    sstat = p.sem("sstat")
    bcmm = p.sem("bcmm")
    xnd = p.sem("xnd")
    rtmm = p.sem("rtmm")
    rtev = p.sem("rtev")
    vsub = p.sem("vsub")
    ssub = p.sem("ssub")
    vrt = p.sem("vrt")
    tcnt = p.sem("tcnt")
    st = p.sem("st")

    nld = ld.inc(32)

    @p.g
    def _(g, e):
        g.dma_start(out=e.xr[:, :].rearrange("p (k t) -> p k t", k=8),
                    in_=e.xfm.rearrange("(k p) t -> p k t", p=128)).then_inc(ld.h, 16)
        g.dma_start(out=e.gw9[:, :].rearrange("p (k e) -> p k e", k=8),
                    in_=e.gwT.rearrange("(k p) e -> p k e", p=128)).then_inc(ld.h, 16)
        g.memset(e.w1024[:, :], 1.0 / 1024.0)
        g.memset(e.onesc[:, :], 1.0)
        g.memset(e.ones1[:, :], 1.0)
        g.memset(e.ceps[:, :], 1e-5)
        g.engine_nop().then_inc(ini.h, 1)
    nini = ini.inc(1)

    for c in range(NCH):
        cs = slice(c * 512, (c + 1) * 512)

        n_sq = [None] * 8
        for k in range(8):
            def sqf(s, e, k=k, c=c, nw=nld, nini=nini, nmm=mmst.n):
                if k == 0:
                    s.wait_ge(ld.h, nw)
                    s.wait_ge(ini.h, nini)
                    if c > 0:
                        s.wait_ge(mmst.h, nmm)
                s.activation(e.sq[:, k * 512:(k + 1) * 512],
                             e.xr[:, k * TSL + c * 512: k * TSL + (c + 1) * 512],
                             AF.Square).then_inc(sqs.h, 1)
            p.s(sqf)
            n_sq[k] = sqs.inc()

        def mmf(t, e, c=c, n_sq=tuple(n_sq), nw=nld, nini=nini, nvs=vstat.n):
            t.wait_ge(ld.h, nw)
            t.wait_ge(ini.h, nini)
            if c > 0:
                t.wait_ge(vstat.h, nvs)
            for k in range(8):
                t.matmul(e.psA[:, :], e.w1024[:, :],
                         e.xr[:, k * TSL + c * 512: k * TSL + (c + 1) * 512],
                         start=(k == 0), stop=(k == 7))
            for k in range(8):
                t.wait_ge(sqs.h, n_sq[k])
                mm = t.matmul(e.psB[:, :], e.w1024[:, :],
                              e.sq[:, k * 512:(k + 1) * 512],
                              start=(k == 0), stop=(k == 7))
            mm.then_inc(mmst.h, 1)
        p.t(mmf)
        nmm = mmst.inc()

        def stv(v, e, nmm=nmm):
            v.wait_ge(mmst.h, nmm)
            v.tensor_copy(e.mean[:, :], e.psA[:, :])
            v.drain()
            v.tensor_mul(e.m2[:, :], e.mean[:, :], e.mean[:, :])
            v.drain()
            v.tensor_sub(e.var[:, :], e.psB[:, :], e.m2[:, :]).then_inc(vstat.h, 1)
        p.v(stv)
        nvs = vstat.inc()

        def sts(s, e, nvs=nvs):
            s.wait_ge(vstat.h, nvs)
            s.activation(e.lnv[:, :], e.var[:, :], AF.Ln, bias=e.ceps[:, :])
            s.drain()
            s.activation(e.rstd[:, :], e.lnv[:, :], AF.Exp, scale=-0.5).then_inc(sstat.h, 1)
        p.s(sts)
        nss = sstat.inc()

        def bcm(t, e, nss=nss, nxd=xnd.n):
            t.wait_ge(sstat.h, nss)
            if c > 0:
                t.wait_ge(xnd.h, nxd)
            t.matmul(e.pm[:, :], e.ones1[:, :], e.mean[:, :], start=True, stop=True)
            t.matmul(e.pr[:, :], e.ones1[:, :], e.rstd[:, :], start=True, stop=True).then_inc(bcmm.h, 1)
        p.t(bcm)
        nbc = bcmm.inc()

        def xnf(v, e, c=c, nbc=nbc):
            v.wait_ge(bcmm.h, nbc)
            for k in range(8):
                v.tensor_sub(e.tmp[:, :], e.xr[:, k * TSL + c * 512: k * TSL + (c + 1) * 512], e.pm[:, :])
                v.drain()
                v.tensor_mul(e.xn[:, k * TSL + c * 512: k * TSL + (c + 1) * 512], e.tmp[:, :], e.pr[:, :])
                v.drain()
            v.engine_nop().then_inc(xnd.h, 1)
        p.v(xnf)
        xnd.inc()

    for w in range(NW):
        c = (w * 128) // 512
        ws = slice(w * 128, (w + 1) * 128)

        def rmm(t, e, w=w, nxd=c + 1, nrtev=rtev.n):
            t.wait_ge(xnd.h, nxd)
            if w > 0:
                t.wait_ge(rtev.h, nrtev)
            for k in range(8):
                mm = t.matmul(e.lgps[:, :], e.xn[:, k * TSL + w * 128: k * TSL + (w + 1) * 128],
                              e.gw9[:, k * 9:(k + 1) * 9], start=(k == 0), stop=(k == 7))
            mm.then_inc(rtmm.h, 1)
        p.t(rmm)
        nrm = rtmm.inc()

        def rev(s, e, w=w, nrm=nrm, nvr=vrt.n):
            s.wait_ge(rtmm.h, nrm)
            if w > 0:
                s.wait_ge(vrt.h, nvr)
            s.copy(e.lgsb[:, w * 9:(w + 1) * 9], e.lgps[:, :]).then_inc(rtev.h, 1)
        p.s(rev)
        nre = rtev.inc()

        def rv1(v, e, w=w, nre=nre):
            dyn = e.lgsb[:, w * 9: w * 9 + 8]
            v.wait_ge(rtev.h, nre)
            v.max(e.srt[:, 0:8], dyn)
            v.drain()
            v.tensor_scalar(e.esr[:, :], e.srt[:, :], e.srt[:, 0:1], None, ALU.subtract).then_inc(vsub.h, 1)
        p.v(rv1)
        nv1 = vsub.inc()

        def rs1(s, e, w=w, nv1=nv1):
            s.wait_ge(vsub.h, nv1)
            s.activation(e.asrt[:, :], e.srt[:, :], AF.Abs)
            s.activation(e.adyn[:, :], e.lgsb[:, w * 9: w * 9 + 8], AF.Abs)
            s.drain()
            s.activation(e.ee[:, :], e.esr[:, :], AF.Exp).then_inc(ssub.h, 1)
        p.s(rs1)
        ns1 = ssub.inc()

        def rv2(v, e, w=w, ns1=ns1):
            dyn = e.lgsb[:, w * 9: w * 9 + 8]
            v.wait_ge(ssub.h, ns1)
            v.reduce_sum(e.esum[:, :], e.ee[:, :], AX.X)
            v.drain()
            v.reciprocal(e.rec[:, :], e.esum[:, :])
            v.drain()
            v.tensor_scalar(e.psl[:, :], e.ee[:, :], e.rec[:, 0:1], None, ALU.mult)
            v.drain()
            v.tensor_add(e.cums[:, 0:1], e.psl[:, 0:1], e.psl[:, 1:2])
            v.drain()
            v.tensor_add(e.cums[:, 1:2], e.cums[:, 0:1], e.psl[:, 2:3])
            v.drain()
            v.tensor_single_scalar(e.vld[:, 1:2], e.psl[:, 0:1], 0.75, ALU.is_lt)
            v.tensor_single_scalar(e.vld[:, 2:3], e.cums[:, 0:1], 0.75, ALU.is_lt)
            v.tensor_single_scalar(e.vld[:, 3:4], e.cums[:, 1:2], 0.75, ALU.is_lt)
            v.memset(e.rww[:, :], 0.0)
            v.engine_nop().then_inc(vsub.h, 1)
        p.v(rv2)
        nv2 = vsub.inc()

        for r in range(4):
            def rva(v, e, w=w, r=r, nprev=(nv2 if r == 0 else ssub.n)):
                dyn = e.lgsb[:, w * 9: w * 9 + 8]
                thr = e.srt[:, r:r + 1]
                if r == 0:
                    v.wait_ge(vsub.h, nprev)
                v.tensor_scalar(e.gtm[:, :], dyn, thr, None, ALU.is_gt)
                v.tensor_scalar(e.fac[:, :], e.adyn[:, :], e.asrt[:, r:r + 1], None, ALU.max)
                v.drain()
                v.tensor_scalar(e.nf[:, :], e.fac[:, :], -0.2, None, ALU.mult)
                v.tensor_scalar(e.dmt[:, :], dyn, thr, None, ALU.subtract)
                v.drain()
                v.tensor_tensor(e.jm[:, :], e.dmt[:, :], e.nf[:, :], ALU.is_lt)
                v.drain()
                v.tensor_tensor(e.ex[:, :], e.gtm[:, :], e.jm[:, :], ALU.max)
                v.drain()
                v.tensor_scalar(e.kp[:, :], e.ex[:, :], -1.0, 1.0, ALU.mult, ALU.add)
                v.engine_nop().then_inc(vsub.h, 1)
            p.v(rva)
            nva = vsub.inc()

            def rsa(s, e, nva=nva):
                s.wait_ge(vsub.h, nva)
                s.activation(e.ee[:, :], e.dmt[:, :], AF.Exp).then_inc(ssub.h, 1)
            p.s(rsa)
            nsa = ssub.inc()

            def rvb(v, e, w=w, r=r, nsa=nsa):
                dyn = e.lgsb[:, w * 9: w * 9 + 8]
                thr = e.srt[:, r:r + 1]
                v.wait_ge(ssub.h, nsa)
                v.tensor_tensor(e.ek[:, :], e.ee[:, :], e.kp[:, :], ALU.mult)
                v.drain()
                v.reduce_sum(e.esm2[:, :], e.ek[:, :], AX.X)
                v.drain()
                v.reciprocal(e.mr[:, :], e.esm2[:, :])
                v.drain()
                if r > 0:
                    v.tensor_mul(e.mrv[:, :], e.mr[:, :], e.vld[:, r:r + 1])
                else:
                    v.tensor_copy(e.mrv[:, :], e.mr[:, :])
                v.tensor_scalar(e.oh[:, :], dyn, thr, None, ALU.is_equal)
                v.drain()
                v.tensor_scalar(e.ctr[:, :], e.oh[:, :], e.mrv[:, 0:1], None, ALU.mult)
                v.drain()
                v.tensor_add(e.rww[:, :], e.rww[:, :], e.ctr[:, :])
                v.drain()
                v.engine_nop().then_inc(vsub.h, 1)
            p.v(rvb)
            vsub.inc()

        def rvf(v, e, w=w):
            v.reduce_sum(e.rs[:, :], e.rww[:, :], AX.X)
            v.drain()
            v.reciprocal(e.rr[:, :], e.rs[:, :])
            v.drain()
            v.tensor_scalar(e.rwsb[:, w * 8:(w + 1) * 8], e.rww[:, :], e.rr[:, 0:1], None, ALU.mult)
            v.drain()
            v.tensor_single_scalar(e.msksb[:, w * 8:(w + 1) * 8],
                                   e.rwsb[:, w * 8:(w + 1) * 8], 0.0, ALU.is_gt)
            v.engine_nop().then_inc(vrt.h, 1)
        p.v(rvf)
        nvrt = vrt.inc()

        def cmm(t, e, w=w, nvrt=nvrt, NW=NW):
            t.wait_ge(vrt.h, nvrt)
            mm = t.matmul(e.cntps[:, :], e.onesc[:, :], e.msksb[:, w * 8:(w + 1) * 8],
                          start=(w == 0), stop=(w == NW - 1))
            if w == NW - 1:
                mm.then_inc(tcnt.h, 1)
        p.t(cmm)
    tcnt.inc()

    def cev(s, e, n=tcnt.n):
        s.wait_ge(tcnt.h, n)
        s.copy(e.cntsb[:, :], e.cntps[:, :]).then_inc(sstat.h, 1)
    p.s(cev)
    ncev = sstat.inc()

    def stf(sy, e, NW=NW, NCH=NCH, ncev=ncev, nvrt=vrt.n, nxd=xnd.n):
        sy.wait_ge(xnd.h, nxd)
        sy.dma_start(out=e.xn0b.rearrange("(k p) t -> p k t", p=128),
                     in_=e.xn[:, :].rearrange("p (k t) -> p k t", k=8)).then_inc(st.h, 16)
        sy.wait_ge(vrt.h, nvrt)
        sy.dma_start(out=e.lgo.rearrange("(w p) e -> p w e", p=128),
                     in_=e.lgsb[:, :].rearrange("p (w e) -> p w e", e=9)).then_inc(st.h, 16)
        sy.dma_start(out=e.rwo.rearrange("(w p) e -> p w e", p=128),
                     in_=e.rwsb[:, :].rearrange("p (w e) -> p w e", e=8)).then_inc(st.h, 16)
        sy.dma_start(out=e.masko.rearrange("(w p) e -> p w e", p=128),
                     in_=e.msksb[:, :].rearrange("p (w e) -> p w e", e=8)).then_inc(st.h, 16)
        sy.wait_ge(sstat.h, ncev)
        sy.dma_start(out=e.cnto, in_=e.cntsb[:, :]).then_inc(st.h, 16)
        sy.wait_ge(st.h, st.inc(80))
    p.y(stf)

    return p



def make_p0b(TSL, CAP):
    p = P()
    NW = TSL // 128
    p.din("lgi", (TSL, 9), F32)
    p.din("rwi", (TSL, 8), F32)
    p.din("maski", (TSL, 8), F32)
    p.din("cntall", (NC, 8), F32)
    p.din("pmask", (NC, 1), F32)
    p.din("tril", (128, 128), F32)
    p.dout("gwo", (TSL, 9), F32)

    p.sbuf("lgs", (128, NW * 9), F32)
    p.sbuf("rws", (128, NW * 8), F32)
    p.sbuf("msks", (128, NW * 8), F32)
    p.sbuf("cnts", (NC, 8), F32)
    p.sbuf("pms", (NC, 1), F32)
    p.sbuf("trils", (128, 128), F32)
    p.sbuf("ones1", (1, 128), F32)
    p.sbuf("offs", (1, 8), F32)
    p.sbuf("wcnt", (1, 8), F32)
    p.sbuf("capm", (128, 8), F32)
    p.sbuf("pen", (128, 8), F32)
    p.sbuf("lgm", (128, 9), F32)
    p.sbuf("mx", (128, 1), F32)
    p.sbuf("lsh", (128, 9), F32)
    p.sbuf("e9", (128, 9), F32)
    p.sbuf("s9", (128, 1), F32)
    p.sbuf("r9", (128, 1), F32)
    p.sbuf("gw_", (128, 9), F32)
    p.sbuf("sdyn", (128, 1), F32)
    p.sbuf("gd", (128, 8), F32)
    p.sbuf("gws", (128, NW * 9), F32)

    p.psum("offps", (1, 8))
    p.psum("cumps", (128, 8))
    p.psum("wcps", (1, 8))

    ld = p.sem("ld")
    ini = p.sem("ini")
    mm1 = p.sem("mm1")
    vv = p.sem("vv")
    ss = p.sem("ss")
    tt = p.sem("tt")
    st = p.sem("st")

    @p.g
    def _(g, e):
        g.dma_start(out=e.lgs[:, :].rearrange("p (w e) -> p w e", e=9),
                    in_=e.lgi.rearrange("(w p) e -> p w e", p=128)).then_inc(ld.h, 16)
        g.dma_start(out=e.rws[:, :].rearrange("p (w e) -> p w e", e=8),
                    in_=e.rwi.rearrange("(w p) e -> p w e", p=128)).then_inc(ld.h, 16)
        g.dma_start(out=e.msks[:, :].rearrange("p (w e) -> p w e", e=8),
                    in_=e.maski.rearrange("(w p) e -> p w e", p=128)).then_inc(ld.h, 16)
        g.dma_start(out=e.cnts[:, :], in_=e.cntall).then_inc(ld.h, 16)
        g.dma_start(out=e.pms[:, :], in_=e.pmask).then_inc(ld.h, 16)
        g.dma_start(out=e.trils[:, :], in_=e.tril).then_inc(ld.h, 16)
        g.memset(e.ones1[:, :], 1.0)
        g.engine_nop().then_inc(ini.h, 1)
    nld = ld.inc(96)
    nini = ini.inc(1)

    def omm(t, e, nld=nld, nini=nini):
        t.wait_ge(ld.h, nld)
        t.wait_ge(ini.h, nini)
        t.matmul(e.offps[:, :], e.pms[:, :], e.cnts[:, :], start=True, stop=True).then_inc(mm1.h, 1)
    p.t(omm)
    nomm = mm1.inc()

    def oev(v, e, nomm=nomm):
        v.wait_ge(mm1.h, nomm)
        v.tensor_copy(e.offs[:, :], e.offps[:, :]).then_inc(vv.h, 1)
    p.v(oev)
    vv.inc()

    for w in range(NW):
        def wmm(t, e, w=w, nvv=vv.n):
            t.wait_ge(vv.h, nvv)
            t.matmul(e.cumps[:, :], e.trils[:, :], e.msks[:, w * 8:(w + 1) * 8],
                     start=True, stop=False)
            t.matmul(e.cumps[:, :], e.ones1[:, :], e.offs[:, :], start=False, stop=True)
            t.matmul(e.wcps[:, :], e.trils[:, 127:128],
                     e.msks[:, w * 8:(w + 1) * 8], start=True, stop=True).then_inc(tt.h, 1)
        p.t(wmm)
        ntt = tt.inc()

        def wv1(v, e, w=w, ntt=ntt, CAP=CAP):
            v.wait_ge(tt.h, ntt)
            v.tensor_single_scalar(e.capm[:, :], e.cumps[:, :], float(CAP), ALU.is_le)
            v.tensor_copy(e.wcnt[:, :], e.wcps[:, :])
            v.drain()
            v.tensor_mul(e.capm[:, :], e.capm[:, :], e.msks[:, w * 8:(w + 1) * 8])
            v.tensor_add(e.offs[:, :], e.offs[:, :], e.wcnt[:, :])
            v.drain()
            v.tensor_scalar(e.pen[:, :], e.capm[:, :], 1e38, -1e38, ALU.mult, ALU.add)
            v.tensor_copy(e.lgm[:, :], e.lgs[:, w * 9:(w + 1) * 9])
            v.drain()
            v.tensor_add(e.lgm[:, 0:8], e.lgm[:, 0:8], e.pen[:, :])
            v.drain()
            v.reduce_max(e.mx[:, :], e.lgm[:, :], AX.X)
            v.drain()
            v.tensor_scalar(e.lsh[:, :], e.lgm[:, :], e.mx[:, 0:1], None, ALU.subtract)
            v.engine_nop().then_inc(vv.h, 1)
        p.v(wv1)
        nv1 = vv.inc()

        def ws1(s, e, nv1=nv1):
            s.wait_ge(vv.h, nv1)
            s.activation(e.e9[:, :], e.lsh[:, :], AF.Exp).then_inc(ss.h, 1)
        p.s(ws1)
        ns1 = ss.inc()

        def wv2(v, e, w=w, ns1=ns1):
            v.wait_ge(ss.h, ns1)
            v.reduce_sum(e.s9[:, :], e.e9[:, :], AX.X)
            v.drain()
            v.reciprocal(e.r9[:, :], e.s9[:, :])
            v.drain()
            v.tensor_scalar(e.gw_[:, :], e.e9[:, :], e.r9[:, 0:1], None, ALU.mult)
            v.drain()
            v.reduce_sum(e.sdyn[:, :], e.gw_[:, 0:8], AX.X)
            v.drain()
            v.tensor_scalar(e.gd[:, :], e.rws[:, w * 8:(w + 1) * 8], e.sdyn[:, 0:1], None, ALU.mult)
            v.drain()
            v.tensor_mul(e.gws[:, w * 9: w * 9 + 8], e.gd[:, :], e.capm[:, :])
            v.tensor_copy(e.gws[:, w * 9 + 8: w * 9 + 9], e.gw_[:, 8:9])
            v.engine_nop().then_inc(vv.h, 1)
        p.v(wv2)
        vv.inc()

    def stf(sy, e, nvv=vv.n):
        sy.wait_ge(vv.h, nvv)
        sy.dma_start(out=e.gwo.rearrange("(w p) e -> p w e", p=128),
                     in_=e.gws[:, :].rearrange("p (w e) -> p w e", e=9)).then_inc(st.h, 16)
        sy.wait_ge(st.h, st.inc(16))
    p.y(stf)

    return p




def make_p1(T):
    p = P()
    NCH = T // 512

    p.din("xn0f", (D, T), BF16)
    p.din("wxs", (D, 256), F32)
    p.din("wz", (D, 256), F32)
    p.din("cwc", (128, 2 * 4), F32)
    p.din("cbc", (128, 2), F32)
    p.din("xpw", (256, 129), F32)
    p.dout("u_o", (256, T), BF16)
    p.dout("zs_o", (256, T), BF16)
    p.dout("xpp_o", (129, T), F32)

    p.sbuf("wxs_s", (128, 8 * 256), BF16)
    p.sbuf("cw_s", (128, 8), F32)
    p.sbuf("cb_s", (128, 2), F32)
    p.sbuf("wz_s", (128, 8 * 256), BF16)
    p.sbuf("xpj_s", (128, 2 * 129), BF16)
    p.sbuf("xnc", (128, 8 * 512), BF16)
    p.sbuf("xs0", (128, 515), F32)
    p.sbuf("xs1", (128, 515), F32)
    p.sbuf("xc", (128, 512), F32)
    p.sbuf("u0", (128, 512), BF16)
    p.sbuf("u1", (128, 512), BF16)
    p.sbuf("z0", (128, 512), BF16)
    p.sbuf("z1", (128, 512), BF16)
    p.sbuf("xpa", (128, 512), F32)
    p.sbuf("xpb", (1, 512), F32)

    p.psum("pxs0", (128, 512))
    p.psum("pxs1", (128, 512))
    p.psum("pz0", (128, 512))
    p.psum("pz1", (128, 512))
    p.psum("pxp0", (128, 512))
    p.psum("pxp1", (1, 512))

    ld = p.sem("ld")
    ini = p.sem("ini")
    mmxz = p.sem("mmxz")
    sev = p.sem("sev")
    cvv = p.sem("cvv")
    mmxp = p.sem("mmxp")
    xpe = p.sem("xpe")
    st = p.sem("st")

    @p.g
    def _(g, e):
        g.dma_start(out=e.wxs_s[:, :].rearrange("p (k m) -> p k m", k=8),
                    in_=e.wxs.rearrange("(k p) m -> p k m", p=128)).then_inc(ld.h, 16)
        g.dma_start(out=e.wz_s[:, :].rearrange("p (k m) -> p k m", k=8),
                    in_=e.wz.rearrange("(k p) m -> p k m", p=128)).then_inc(ld.h, 16)
        g.dma_start(out=e.xpj_s[:, :].rearrange("p (k j) -> p k j", k=2),
                    in_=e.xpw.rearrange("(k p) j -> p k j", p=128)).then_inc(ld.h, 16)
        g.dma_start(out=e.cw_s[:, :], in_=e.cwc).then_inc(ld.h, 16)
        g.dma_start(out=e.cb_s[:, :], in_=e.cbc).then_inc(ld.h, 16)
        g.memset(e.xs0[:, 0:3], 0.0)
        g.memset(e.xs1[:, 0:3], 0.0)
        g.engine_nop().then_inc(ini.h, 1)
    nwld = ld.inc(80)
    nini = ini.inc()

    for c in range(NCH):
        t0 = c * 512

        def gld(g, e, c=c, t0=t0, nmm=mmxz.n):
            if c > 0:
                g.wait_ge(mmxz.h, nmm)
            g.dma_start(out=e.xnc[:, :].rearrange("p (k t) -> p k t", k=8),
                        in_=e.xn0f[:, t0:t0 + 512].rearrange("(k p) t -> p k t", p=128)
                        ).then_inc(ld.h, 16)
        p.g(gld)
        nld = ld.inc(16)

        def mf(t, e, c=c, nld=nld, nwld=nwld, nsev=sev.n, nini=nini):
            t.wait_ge(ld.h, nld)
            if c == 0:
                t.wait_ge(ini.h, nini)
            if c > 0:
                t.wait_ge(sev.h, nsev)
            for m in range(2):
                ps = e.pxs0 if m == 0 else e.pxs1
                for k in range(8):
                    mm = t.matmul(ps[:, :], e.wxs_s[:, k * 256 + m * 128: k * 256 + (m + 1) * 128],
                                  e.xnc[:, k * 512:(k + 1) * 512], start=(k == 0), stop=(k == 7))
            for m in range(2):
                ps = e.pz0 if m == 0 else e.pz1
                for k in range(8):
                    mm = t.matmul(ps[:, :], e.wz_s[:, k * 256 + m * 128: k * 256 + (m + 1) * 128],
                                  e.xnc[:, k * 512:(k + 1) * 512], start=(k == 0), stop=(k == 7))
            mm.then_inc(mmxz.h, 1)
        p.t(mf)
        nmm = mmxz.inc()

        def sevac(s, e, c=c, nmm=nmm, ncv=cvv.n, nxp=mmxp.n, nst=st.n):
            s.wait_ge(mmxz.h, nmm)
            if c > 0:
                s.wait_ge(cvv.h, ncv)
                s.wait_ge(mmxp.h, nxp)
                s.wait_ge(st.h, nst)
            s.copy(e.xs0[:, 3:515], e.pxs0[:, :])
            s.copy(e.xs1[:, 3:515], e.pxs1[:, :])
            s.copy(e.z0[:, :], e.pz0[:, :])
            s.drain()
            s.activation(e.z0[:, :], e.pz0[:, :], AF.Silu)
            s.activation(e.z1[:, :], e.pz1[:, :], AF.Silu).then_inc(sev.h, 1)
        p.s(sevac)
        nse = sev.inc()

        for j in range(2):
            def cvf(v, e, j=j, nse=nse, nse2=sev.n):
                xs = e.xs0 if j == 0 else e.xs1
                if j == 0:
                    v.wait_ge(sev.h, nse)
                else:
                    v.wait_ge(sev.h, nse2)
                v.tensor_scalar(e.xc[:, :], xs[:, 0:512], e.cw_s[:, j * 4: j * 4 + 1], None, ALU.mult)
                v.drain()
                for k in range(1, 4):
                    v.scalar_tensor_tensor(e.xc[:, :], xs[:, k:k + 512],
                                           e.cw_s[:, j * 4 + k: j * 4 + k + 1], e.xc[:, :],
                                           ALU.mult, ALU.add)
                    v.drain()
                v.engine_nop().then_inc(cvv.h, 1)
            p.v(cvf)
            ncv = cvv.inc()

            def suf(s, e, j=j, ncv=ncv):
                s.wait_ge(cvv.h, ncv)
                u = e.u0 if j == 0 else e.u1
                s.activation(u[:, :], e.xc[:, :], AF.Silu, bias=e.cb_s[:, j:j + 1]).then_inc(sev.h, 1)
            p.s(suf)
            sev.inc()

            def chal(v, e, j=j):
                xs = e.xs0 if j == 0 else e.xs1
                v.tensor_copy(xs[:, 0:3], xs[:, 512:515])
                v.engine_nop().then_inc(cvv.h, 1)
            p.v(chal)
            cvv.inc()

        def xpf(t, e, c=c, nse=sev.n, nxpe=xpe.n):
            t.wait_ge(sev.h, nse)
            if c > 0:
                t.wait_ge(xpe.h, nxpe)
            for kt in range(2):
                u = e.u0 if kt == 0 else e.u1
                t.matmul(e.pxp0[:, :], e.xpj_s[:, kt * 129: kt * 129 + 128], u[:, :],
                         start=(kt == 0), stop=(kt == 1))
            for kt in range(2):
                u = e.u0 if kt == 0 else e.u1
                mm = t.matmul(e.pxp1[:, :], e.xpj_s[:, kt * 129 + 128: kt * 129 + 129], u[:, :],
                              start=(kt == 0), stop=(kt == 1))
            mm.then_inc(mmxp.h, 1)
        p.t(xpf)
        nxp = mmxp.inc()

        def xpev(v, e, nxp=nxp, nst=st.n):
            v.wait_ge(mmxp.h, nxp)
            if nst > 0:
                v.wait_ge(st.h, nst)
            v.tensor_copy(e.xpa[:, :], e.pxp0[:, :])
            v.tensor_copy(e.xpb[:, :], e.pxp1[:, :])
            v.engine_nop().then_inc(xpe.h, 1)
        p.v(xpev)
        nxpe = xpe.inc()

        def stf(sy, e, c=c, t0=t0, nxpe=nxpe, nse=sev.n, nst0=st.n):
            sy.wait_ge(sev.h, nse)
            sy.dma_start(out=e.u_o[0:128, t0:t0 + 512], in_=e.u0[:, :]).then_inc(st.h, 16)
            sy.dma_start(out=e.u_o[128:256, t0:t0 + 512], in_=e.u1[:, :]).then_inc(st.h, 16)
            sy.dma_start(out=e.zs_o[0:128, t0:t0 + 512], in_=e.z0[:, :]).then_inc(st.h, 16)
            sy.dma_start(out=e.zs_o[128:256, t0:t0 + 512], in_=e.z1[:, :]).then_inc(st.h, 16)
            sy.wait_ge(xpe.h, nxpe)
            sy.dma_start(out=e.xpp_o[0:128, t0:t0 + 512], in_=e.xpa[:, :]).then_inc(st.h, 16)
            sy.dma_start(out=e.xpp_o[128:129, t0:t0 + 512], in_=e.xpb[:, :]).then_inc(st.h, 16)
        p.y(stf)
        st.inc(96)

    def fin(sy, e, n=st.n):
        sy.wait_ge(st.h, n)
    p.y(fin)
    return p



def make_p2a(T):
    p = P()
    CH = min(2048, T)
    NK = T // CH
    NW5 = CH // 512
    H = CH // 2

    p.din("u_i", (256, T), BF16)
    p.din("zs_i", (256, T), BF16)
    p.din("xpp8", (NC, 129, T), F32)
    p.din("A_lg", (128, 128), F32)
    p.din("dtw_c", (128, 2), F32)
    p.din("dtb_c", (128, 2), F32)
    p.din("D_c", (128, 2), F32)
    p.din("selgB", (128, 16 * 128), F32)
    p.din("selR", (128, 16 * 128), F32)
    p.din("sel64q", (64, 4 * 128), F32)
    p.dout("ym_o", (256, T), BF16)

    p.sbuf("selgB_s", (128, 16 * 128), BF16)
    p.sbuf("selR_s", (128, 16 * 128), BF16)
    p.sbuf("sel64q_s", (64, 4 * 128), BF16)
    p.sbuf("A_s", (128, 128), F32)
    p.sbuf("Aex", (128, 128), F32)
    p.sbuf("dtw_s", (128, 2), F32)
    p.sbuf("dtb_s", (128, 2), F32)
    p.sbuf("D_s", (128, 2), F32)
    p.sbuf("ones1", (1, 128), F32)
    p.sbuf("cone", (128, 1), F32)
    p.sbuf("carr", (128, 128), F32)
    p.sbuf("Bc", (64, CH), BF16)
    p.sbuf("Cc", (64, CH), BF16)
    p.sbuf("xq", (128, CH), F32)
    p.sbuf("xq2", (128, CH), F32)
    p.sbuf("dr2", (8, CH), F32)
    p.sbuf("ones8", (8, 128), F32)
    for q in range(4):
        p.sbuf("Bb%d" % q, (128, CH), BF16)
        p.sbuf("Cb%d" % q, (128, CH), BF16)
    p.sbuf("pre", (128, CH), F32)
    p.sbuf("sab", (128, CH), F32)
    p.sbuf("sex", (128, CH), F32)
    p.sbuf("sln", (128, CH), F32)
    p.sbuf("srl", (128, CH), F32)
    for j in range(2):
        p.sbuf("dt%d" % j, (128, CH), BF16)
        p.sbuf("dtu%d" % j, (128, CH), BF16)
        p.sbuf("uc%d" % j, (128, CH), BF16)
        p.sbuf("zc%d" % j, (128, CH), BF16)
    p.sbuf("dub", (128, CH), BF16)
    p.sbuf("dA", (128, CH), F32)
    p.sbuf("bq", (128, CH), BF16)
    p.sbuf("state", (128, CH), F32)
    p.sbuf("cs", (128, CH), BF16)
    p.sbuf("du", (128, CH), F32)
    p.sbuf("yf", (128, CH), F32)
    p.sbuf("ym", (128, CH), BF16)

    p.psum("pp1", (128, CH))
    p.psum("pp2", (128, CH))

    ld = p.sem("ld")
    ini = p.sem("ini")
    ac = p.sem("ac")
    tP = p.sem("tP")
    sS = p.sem("sS")
    vV = p.sem("vV")
    st = p.sem("st")

    @p.g
    def _(g, e):
        g.dma_start(out=e.selgB_s[:, :], in_=e.selgB).then_inc(ld.h, 16)
        g.dma_start(out=e.selR_s[:, :], in_=e.selR).then_inc(ld.h, 16)
        g.dma_start(out=e.sel64q_s[:, :], in_=e.sel64q).then_inc(ld.h, 16)
        g.dma_start(out=e.A_s[:, :], in_=e.A_lg).then_inc(ld.h, 16)
        g.dma_start(out=e.dtw_s[:, :], in_=e.dtw_c).then_inc(ld.h, 16)
        g.dma_start(out=e.dtb_s[:, :], in_=e.dtb_c).then_inc(ld.h, 16)
        g.dma_start(out=e.D_s[:, :], in_=e.D_c).then_inc(ld.h, 16)
        g.memset(e.ones1[:, :], 1.0)
        g.memset(e.ones8[:, :], 1.0)
        g.memset(e.cone[:, :], 1.0)
        g.memset(e.carr[:, :], 0.0)
        g.engine_nop().then_inc(ini.h, 1)
    nwld = ld.inc(112)
    nini = ini.inc()

    def aexp(s, e, nwld=nwld):
        s.wait_ge(ld.h, nwld)
        s.activation(e.Aex[:, :], e.A_s[:, :], AF.Exp).then_inc(sS.h, 1)
    p.s(aexp)
    naex = sS.inc()

    def aneg(v, e, naex=naex):
        v.wait_ge(sS.h, naex)
        v.tensor_scalar(e.A_s[:, :], e.Aex[:, :], -1.0, None, ALU.mult)
        v.engine_nop().then_inc(vV.h, 1)
    p.v(aneg)
    nang = vV.inc()

    for ck in range(NK):
        t0 = ck * CH

        def cld(g, e, t0=t0, ck=ck, nwld=nwld, nep=vV.n, nsum=vV.n):
            if ck == 0:
                g.wait_ge(ld.h, nwld)
            if ck > 0:
                g.wait_ge(vV.h, nep)
            g.dma_start(out=e.xq[:, :], in_=e.xpp8[0, 0:128, t0:t0 + CH]).then_inc(ld.h, 16)
            g.dma_start(out=e.dr2[:, :], in_=e.xpp8[:, 128, t0:t0 + CH]).then_inc(ld.h, 16)
            for j in range(2):
                g.dma_start(out=e["uc%d" % j][:, :],
                            in_=e.u_i[j * 128:(j + 1) * 128, t0:t0 + CH]).then_inc(ld.h, 16)
                g.dma_start(out=e["zc%d" % j][:, :],
                            in_=e.zs_i[j * 128:(j + 1) * 128, t0:t0 + CH]).then_inc(ld.h, 16)
            g.dma_start(out=e.xq2[:, :], in_=e.xpp8[1, 0:128, t0:t0 + CH]).then_inc(ld.h, 16)
            for r in range(2, NC):
                g.wait_ge(vV.h, nsum + (r - 1))
                g.dma_start(out=e.xq2[:, :], in_=e.xpp8[r, 0:128, t0:t0 + CH]).then_inc(ld.h, 16)
        p.g(cld)
        nld0 = ld.inc(112)
        nld = nld0
        nlds = [ld.inc(16) for _ in range(NC - 2)]

        for r in range(1, NC):
            def xsum(v, e, r=r, nl=(nld0 if r == 1 else nlds[r - 2])):
                v.wait_ge(ld.h, nl)
                v.tensor_add(e.xq[:, :], e.xq[:, :], e.xq2[:, :])
                v.drain()
                v.engine_nop().then_inc(vV.h, 1)
            p.v(xsum)
            vV.inc()

        def xfin(v, e):
            v.tensor_copy(e.Bc[:, :], e.xq[0:64, :])
            v.tensor_copy(e.Cc[:, :], e.xq[64:128, :])
            v.engine_nop().then_inc(vV.h, 1)
        p.v(xfin)
        vV.inc()

        nprev_ep = vV.n
        nprev_ex = sS.n
        for r8 in range(8):
            q, isC = r8 % 4, r8 >= 4
            src, dst = ("Cc", "Cb%d" % q) if isC else ("Bc", "Bb%d" % q)

            def bmm(t, e, q=q, src=src, r8=r8, ck=ck, nld=nld, nini=nini,
                    nprev_ep=nprev_ep, nse=sS.n):
                if r8 == 0:
                    t.wait_ge(ld.h, nld)
                    t.wait_ge(ini.h, nini)
                    t.wait_ge(vV.h, nprev_ep)
                else:
                    t.wait_ge(sS.h, nse)
                for w in range(NW5):
                    mm = t.matmul(e.pp1[:, w * 512:(w + 1) * 512],
                                  e.sel64q_s[:, q * 128:(q + 1) * 128],
                                  e[src][:, w * 512:(w + 1) * 512], start=True, stop=True)
                mm.then_inc(tP.h, 1)
            p.t(bmm)
            nbm = tP.inc()

            def bev(s, e, dst=dst, nbm=nbm):
                s.wait_ge(tP.h, nbm)
                s.copy(e[dst][:, :], e.pp1[:, :]).then_inc(sS.h, 1)
            p.s(bev)
            sS.inc()

        def dmm(t, e, ck=ck, nld=nld, nprev_ex=nprev_ex, nxf=nprev_ep):
            t.wait_ge(vV.h, nxf)
            if ck > 0:
                t.wait_ge(sS.h, nprev_ex)
            for w in range(NW5):
                mm = t.matmul(e.pp2[:, w * 512:(w + 1) * 512], e.ones8[:, :],
                              e.dr2[:, w * 512:(w + 1) * 512], start=True, stop=True)
            mm.then_inc(tP.h, 1)
        p.t(dmm)
        ndmm = tP.inc()

        for j in range(2):
            def dtp(v, e, j=j, ndmm=ndmm, nsdt=sS.n):
                v.wait_ge(tP.h, ndmm)
                if j == 1:
                    v.wait_ge(sS.h, nsdt)
                v.tensor_scalar(e.pre[:, :], e.pp2[:, :], e.dtw_s[:, j:j + 1],
                                e.dtb_s[:, j:j + 1], ALU.mult, ALU.add)
                v.engine_nop().then_inc(vV.h, 1)
            p.v(dtp)
            ndtp = vV.inc()

            def dts(s, e, j=j, ndtp=ndtp):
                s.wait_ge(vV.h, ndtp)
                s.activation(e.srl[:, :], e.pre[:, :], AF.Relu)
                s.activation(e.sab[:, :], e.pre[:, :], AF.Abs)
                s.drain()
                s.activation(e.sex[:, :], e.sab[:, :], AF.Exp, scale=-1.0)
                s.drain()
                s.activation(e.sln[:, :], e.sex[:, :], AF.Ln, bias=e.cone[:, 0:1]).then_inc(sS.h, 1)
            p.s(dts)
            ndts = sS.inc()

            def dtv(v, e, j=j, ndts=ndts, nld=nld):
                v.wait_ge(sS.h, ndts)
                v.wait_ge(ld.h, nld)
                v.tensor_add(e["dt%d" % j][:, :], e.srl[:, :], e.sln[:, :])
                v.drain()
                v.tensor_mul(e["dtu%d" % j][:, :], e["dt%d" % j][:, :], e["uc%d" % j][:, :])
                v.engine_nop().then_inc(vV.h, 1)
            p.v(dtv)
            vV.inc()

        for j in range(2):
            for gi in range(16):
                g_ = j * 16 + gi
                sb = gi * 128
                dtu_name = "dtu%d" % j
                dt_name = "dt%d" % j

                def gmm1(t, e, sb=sb, dtu_name=dtu_name, nfree=sS.n, nvv=vV.n):
                    t.wait_ge(sS.h, nfree)
                    t.wait_ge(vV.h, nvv)
                    for w in range(H // 512):
                        mm = t.matmul(e.pp2[:, w * 512:(w + 1) * 512], e.selgB_s[:, sb:sb + 128],
                                      e[dtu_name][:, w * 512:(w + 1) * 512], start=True, stop=True)
                    mm.then_inc(tP.h, 1)
                p.t(gmm1)
                ng1 = tP.inc()

                def gev1(s, e, ng1=ng1):
                    s.wait_ge(tP.h, ng1)
                    s.copy(e.dub[:, 0:H], e.pp2[:, 0:H]).then_inc(sS.h, 1)
                p.s(gev1)
                ne1 = sS.inc()

                def gmm2(t, e, sb=sb, dtu_name=dtu_name):
                    for w in range(H // 512, NW5):
                        mm = t.matmul(e.pp2[:, w * 512:(w + 1) * 512], e.selgB_s[:, sb:sb + 128],
                                      e[dtu_name][:, w * 512:(w + 1) * 512], start=True, stop=True)
                    mm.then_inc(tP.h, 1)
                p.t(gmm2)
                ng2 = tP.inc()

                def gev2(s, e, ng2=ng2):
                    s.wait_ge(tP.h, ng2)
                    s.copy(e.dub[:, H:CH], e.pp2[:, H:CH]).then_inc(sS.h, 1)
                p.s(gev2)
                ne2 = sS.inc()

                def gmm3(t, e, sb=sb, dt_name=dt_name, ne2=ne2):
                    t.wait_ge(sS.h, ne2)
                    for w in range(NW5):
                        mm = t.matmul(e.pp2[:, w * 512:(w + 1) * 512],
                                      e.selgB_s[:, sb:sb + 128],
                                      e[dt_name][:, w * 512:(w + 1) * 512],
                                      start=True, stop=True)
                    mm.then_inc(tP.h, 1)
                p.t(gmm3)
                ng3 = tP.inc()

                for q in range(4):
                    col = 4 * g_ + q

                    def qex(s, e, col=col, q=q, ng3=ng3, nvsc=max(vV.n, nang)):
                        if q == 0:
                            s.wait_ge(tP.h, ng3)
                        s.wait_ge(vV.h, nvsc)
                        s.activation(e.dA[:, :], e.pp2[:, :], AF.Exp,
                                     scale=e.A_s[:, col:col + 1]).then_inc(sS.h, 1)
                    p.s(qex)
                    nex = sS.inc()

                    def qv(v, e, col=col, q=q, nex=nex, nymm=tP.n):
                        v.wait_ge(sS.h, nex)
                        if not (ck == 0 and g_ == 0 and q == 0):
                            v.wait_ge(tP.h, nymm)
                        v.tensor_mul(e.bq[:, :], e.dub[:, :], e["Bb%d" % q][:, :])
                        v.drain()
                        v.tensor_tensor_scan(e.state[:, :], e.dA[:, :], e.bq[:, :],
                                             e.carr[:, col:col + 1], ALU.mult, ALU.add)
                        v.drain()
                        v.tensor_copy(e.carr[:, col:col + 1], e.state[:, CH - 1:CH])
                        v.drain()
                        v.tensor_mul(e.cs[:, :], e.state[:, :], e["Cb%d" % q][:, :])
                        v.engine_nop().then_inc(vV.h, 1)
                    p.v(qv)
                    nqv = vV.inc()

                    def qy(t, e, sb=sb, gi=gi, q=q, j=j, nqv=nqv, ck=ck):
                        t.wait_ge(vV.h, nqv)
                        for w in range(NW5):
                            mm = t.matmul(e.pp1[:, w * 512:(w + 1) * 512],
                                          e.selR_s[:, sb:sb + 128],
                                          e.cs[:, w * 512:(w + 1) * 512],
                                          start=(gi == 0 and q == 0),
                                          stop=(gi == 15 and q == 3))
                        mm.then_inc(tP.h, 1)
                    p.t(qy)
                    tP.inc()

            def epi(v, e, j=j, nylast=tP.n, t0=t0):
                v.wait_ge(tP.h, nylast)
                v.tensor_scalar(e.du[:, :], e["uc%d" % j][:, :], e.D_s[:, j:j + 1], None, ALU.mult)
                v.drain()
                v.tensor_add(e.yf[:, :], e.pp1[:, :], e.du[:, :])
                v.drain()
                v.tensor_mul(e.ym[:, :], e.yf[:, :], e["zc%d" % j][:, :])
                v.engine_nop().then_inc(vV.h, 1)
            p.v(epi)
            nepi = vV.inc()

            def sto(sy, e, j=j, t0=t0, nepi=nepi, nst=st.n):
                sy.wait_ge(vV.h, nepi)
                sy.dma_start(out=e.ym_o[j * 128:(j + 1) * 128, t0:t0 + CH],
                             in_=e.ym[:, :]).then_inc(st.h, 16)
            p.y(sto)
            nst = st.inc(16)

            def ymfree(v, e, nst=nst):
                v.wait_ge(st.h, nst)
                v.engine_nop().then_inc(vV.h, 1)
            p.v(ymfree)
            vV.inc()

    def fin(sy, e, n=st.n):
        sy.wait_ge(st.h, n)
    p.y(fin)
    return p




def make_p2b(T):
    p = P()
    NCH = T // 512

    p.din("ym_i", (256, T), BF16)
    p.din("xn0f", (D, T), BF16)
    p.din("gw_fm", (9, T), F32)
    p.din("wop", (128, 2 * D), F32)
    p.din("wgu", (128, 8 * 8 * 128), F32)
    p.din("wdn", (64, 8 * D), F32)
    p.din("wsgu", (128, 8 * 512), F32)
    p.din("wsdn", (128, 2 * D), F32)
    p.dout("part0", (D, T), F32)

    p.sbuf("wop_s", (128, 2 * D), BF16)
    p.sbuf("wgu_s", (128, 8 * 8 * 128), BF16)
    p.sbuf("wdn_s", (64, 8 * D), BF16)
    p.sbuf("wsgu_s", (128, 8 * 512), BF16)
    p.sbuf("wsdn_s", (128, 2 * D), BF16)
    p.sbuf("xnc", (128, 8 * 512), BF16)
    p.sbuf("ymc", (128, 2 * 512), BF16)
    p.sbuf("gA", (128, 512), F32)
    p.sbuf("gB", (128, 512), F32)
    p.sbuf("gC", (128, 512), F32)
    p.sbuf("gs64", (64, 512), F32)
    p.sbuf("hwu", (64, 512), F32)
    p.sbuf("gs128", (128, 512), F32)
    p.sbuf("hwu2", (128, 512), F32)
    p.sbuf("hsb", (64, 8 * 512), BF16)
    p.sbuf("hsh", (128, 2 * 512), BF16)
    p.sbuf("po", (128, 8 * 512), F32)
    p.sbuf("onesw", (128, 128), F32)

    p.psum("ph", (128, 512))
    p.psum("ph2", (128, 512))
    p.psum("pwb", (128, 512))
    p.psum("pout", (128, 512))

    ld = p.sem("ld")
    tP = p.sem("tP")
    sS = p.sem("sS")
    vV = p.sem("vV")
    st = p.sem("st")

    @p.g
    def _(g, e):
        g.dma_start(out=e.wop_s[:, :], in_=e.wop).then_inc(ld.h, 16)
        g.dma_start(out=e.wgu_s[:, :], in_=e.wgu).then_inc(ld.h, 16)
        g.dma_start(out=e.wdn_s[:, :], in_=e.wdn).then_inc(ld.h, 16)
        g.dma_start(out=e.wsgu_s[:, :], in_=e.wsgu).then_inc(ld.h, 16)
        g.dma_start(out=e.wsdn_s[:, :], in_=e.wsdn).then_inc(ld.h, 16)
        g.memset(e.onesw[:, :], 1.0)
    nwld = ld.inc(80)

    for c in range(NCH):
        t0 = c * 512

        def cld(g, e, t0=t0, c=c, nwld=nwld, ntp=tP.n, nvv=vV.n):
            if c > 0:
                g.wait_ge(tP.h, ntp)
                g.wait_ge(vV.h, nvv)
            g.dma_start(out=e.xnc[:, :].rearrange("p (k t) -> p k t", k=8),
                        in_=e.xn0f[:, t0:t0 + 512].rearrange("(k p) t -> p k t", p=128)
                        ).then_inc(ld.h, 16)
            g.dma_start(out=e.ymc[:, :].rearrange("p (k t) -> p k t", k=2),
                        in_=e.ym_i[:, t0:t0 + 512].rearrange("(k p) t -> p k t", p=128)
                        ).then_inc(ld.h, 16)
            g.dma_start(out=e.gA[0:96:32, :], in_=e.gw_fm[0:3, t0:t0 + 512]).then_inc(ld.h, 16)
            g.dma_start(out=e.gB[0:96:32, :], in_=e.gw_fm[3:6, t0:t0 + 512]).then_inc(ld.h, 16)
            g.dma_start(out=e.gC[0:96:32, :], in_=e.gw_fm[6:9, t0:t0 + 512]).then_inc(ld.h, 16)
        p.g(cld)
        nld = ld.inc(80)

        for ei in range(8):
            phx = "ph" if ei % 2 == 0 else "ph2"

            def hmm(t, e, ei=ei, phx=phx, nld=nld, nve=vV.n):
                if ei == 0:
                    t.wait_ge(ld.h, nld)
                if ei >= 2:
                    t.wait_ge(vV.h, nve)
                for k in range(8):
                    mm = t.matmul(e[phx][:, :],
                                  e.wgu_s[:, (k * 8 + ei) * 128:(k * 8 + ei) * 128 + 128],
                                  e.xnc[:, k * 512:(k + 1) * 512], start=(k == 0), stop=(k == 7))
                mm.then_inc(tP.h, 1)
            p.t(hmm)
            nhm = tP.inc()

            def wbm2(t, e, ei=ei, nvv=vV.n):
                ga = [e.gA, e.gB, e.gC][ei // 3]
                r = (ei % 3) * 32
                t.wait_ge(vV.h, nvv)
                t.matmul(e.pwb[:, :], e.onesw[r:r + 1, :], ga[r:r + 1, :],
                         start=True, stop=True).then_inc(tP.h, 1)
            p.t(wbm2)
            nwb = tP.inc()

            def hep_s(s, e, ei=ei, phx=phx, nhm=nhm, nve=vV.n):
                s.wait_ge(tP.h, nhm)
                if ei > 0:
                    s.wait_ge(vV.h, nve)
                s.activation(e.gs64[:, :], e[phx][0:64, :], AF.Silu).then_inc(sS.h, 1)
            p.s(hep_s)
            nsi = sS.inc()

            def hep_v(v, e, ei=ei, phx=phx, nsi=nsi, nwb=nwb):
                v.wait_ge(sS.h, nsi)
                v.wait_ge(tP.h, nwb)
                v.tensor_mul(e.hwu[:, :], e.gs64[:, :], e[phx][64:128, :])
                v.drain()
                v.tensor_mul(e.hsb[:, ei * 512:(ei + 1) * 512],
                             e.hwu[:, :], e.pwb[0:64, :])
                v.engine_nop().then_inc(vV.h, 1)
            p.v(hep_v)
            vV.inc()

        def wbsh(t, e, nvv=vV.n):
            t.wait_ge(vV.h, nvv)
            t.matmul(e.pwb[:, :], e.onesw[64:65, :], e.gC[64:65, :], start=True, stop=True).then_inc(tP.h, 1)
        p.t(wbsh)
        nwbs = tP.inc()

        for m in range(2):
            def shm(t, e, m=m, nld=nld, nve=vV.n):
                t.wait_ge(vV.h, nve)
                for k in range(8):
                    t.matmul(e.ph[:, :], e.wsgu_s[:, k * 512 + m * 128: k * 512 + (m + 1) * 128],
                             e.xnc[:, k * 512:(k + 1) * 512], start=(k == 0), stop=(k == 7))
                for k in range(8):
                    mm = t.matmul(e.ph2[:, :], e.wsgu_s[:, k * 512 + 256 + m * 128: k * 512 + 256 + (m + 1) * 128],
                                  e.xnc[:, k * 512:(k + 1) * 512], start=(k == 0), stop=(k == 7))
                mm.then_inc(tP.h, 1)
            p.t(shm)
            nshm = tP.inc()

            def shs(s, e, m=m, nshm=nshm, nve=vV.n):
                s.wait_ge(tP.h, nshm)
                if m > 0:
                    s.wait_ge(vV.h, nve)
                s.activation(e.gs128[:, :], e.ph[:, :], AF.Silu).then_inc(sS.h, 1)
            p.s(shs)
            nshs = sS.inc()

            def shv(v, e, m=m, nshs=nshs, nwbs=nwbs):
                v.wait_ge(sS.h, nshs)
                v.wait_ge(tP.h, nwbs)
                v.tensor_mul(e.hwu2[:, :], e.gs128[:, :], e.ph2[:, :])
                v.drain()
                v.tensor_mul(e.hsh[:, m * 512:(m + 1) * 512], e.hwu2[:, :], e.pwb[:, :])
                v.engine_nop().then_inc(vV.h, 1)
            p.v(shv)
            vV.inc()

        for m in range(8):
            def omm(t, e, m=m, nhid=vV.n, nsev=sS.n):
                t.wait_ge(vV.h, nhid)
                if m > 0:
                    t.wait_ge(sS.h, nsev)
                for k in range(2):
                    t.matmul(e.pout[:, :], e.wop_s[:, k * D + m * 128: k * D + (m + 1) * 128],
                             e.ymc[:, k * 512:(k + 1) * 512], start=(k == 0), stop=False)
                for ei in range(8):
                    t.matmul(e.pout[:, :], e.wdn_s[:, ei * D + m * 128: ei * D + (m + 1) * 128],
                             e.hsb[:, ei * 512:(ei + 1) * 512],
                             start=False, stop=False)
                for k in range(2):
                    mm = t.matmul(e.pout[:, :], e.wsdn_s[:, k * D + m * 128: k * D + (m + 1) * 128],
                                  e.hsh[:, k * 512:(k + 1) * 512], start=False, stop=(k == 1))
                mm.then_inc(tP.h, 1)
            p.t(omm)
            nom = tP.inc()

            def oev(s, e, m=m, nom=nom, nst0=st.n):
                s.wait_ge(tP.h, nom)
                if m == 0 and nst0 > 0:
                    s.wait_ge(st.h, nst0)
                s.copy(e.po[:, m * 512:(m + 1) * 512], e.pout[:, :]).then_inc(sS.h, 1)
            p.s(oev)
            sS.inc()

        def sto(sy, e, t0=t0, nse=sS.n, nst=st.n):
            sy.wait_ge(sS.h, nse)
            sy.dma_start(out=e.part0[:, t0:t0 + 512].rearrange("(m p) t -> p m t", p=128),
                         in_=e.po[:, :].rearrange("p (m t) -> p m t", m=8)).then_inc(st.h, 16)
        p.y(sto)
        st.inc(16)

    def fin(sy, e, n=st.n):
        sy.wait_ge(st.h, n)
    p.y(fin)
    return p


def tile_kpm(w):
    Kd, M = w.shape
    return np.ascontiguousarray(w.reshape(Kd // 128, 128, M).transpose(1, 0, 2).reshape(128, -1))


def p2b_weights(out_proj_w, dyn_gate_w, dyn_up_w, dyn_down_w, sh_gate_w, sh_up_w, sh_down_w, core):
    cs = slice(core * 256, (core + 1) * 256)
    isl = slice(core * 64, (core + 1) * 64)
    ish = slice(core * 256, (core + 1) * 256)
    wop = tile_kpm(out_proj_w[cs, :])
    wgu = np.stack([np.concatenate([dyn_gate_w[e][isl].T, dyn_up_w[e][isl].T], axis=1)
                    for e in range(8)])
    wgu = np.ascontiguousarray(wgu.reshape(8, 8, 128, 128).transpose(2, 1, 0, 3).reshape(128, -1))
    wdn = np.ascontiguousarray(np.stack([dyn_down_w[e].T[isl] for e in range(8)])
                               .transpose(1, 0, 2).reshape(64, -1))
    wsgu = np.concatenate([sh_gate_w[0][ish].T, sh_up_w[0][ish].T], axis=1)
    wsgu = tile_kpm(wsgu)
    wsdn = tile_kpm(sh_down_w[0].T[ish])
    return wop, wgu, wdn, wsgu, wsdn


def p2a_consts():
    selgB = np.zeros((128, 16 * 128), np.float32)
    selR = np.zeros((128, 16 * 128), np.float32)
    sel64q = np.zeros((64, 4 * 128), np.float32)
    for gi in range(16):
        for m in range(128):
            ch = gi * 8 + (m >> 5) * 2 + ((m >> 4) & 1)
            selgB[ch, gi * 128 + m] = 1.0
    for gi in range(16):
        for k in range(128):
            m = gi * 8 + (k >> 5) * 2 + ((k >> 4) & 1)
            selR[k, gi * 128 + m] = 1.0
    for q in range(4):
        for m in range(128):
            sel64q[q * 16 + (m & 15), q * 128 + m] = 1.0
    return selgB, selR, sel64q


def p2a_part_maps():
    p_idx = np.arange(128)
    choff = (p_idx >> 5) * 2 + ((p_idx >> 4) & 1)
    nq = p_idx & 15
    return choff, nq


def p2a_A_layout(A_slice):
    choff, nq = p2a_part_maps()
    A_sc = np.zeros((128, 128), np.float32)
    for g in range(32):
        base_ch = (g // 16) * 128 + (g % 16) * 8
        for q in range(4):
            A_sc[:, 4 * g + q] = A_slice[base_ch + choff, q * 16 + nq]
    return A_sc



def make_p3(TSL):
    p = P()
    NCH = TSL // 512
    NW = TSL // 128

    p.din("x0fm", (D, TSL), F32)
    p.din("part0a", (NC, D, TSL), F32)
    p.din("wk", (128, 8 * 256), F32)
    p.din("bkc", (128, 2), F32)
    p.din("wv", (128, 8 * 256), F32)
    p.din("bvc", (128, 2), F32)
    p.din("wc2", (128, 2 * 64), F32)
    p.din("wg1", (128, 8 * 8), F32)
    p.din("eye", (128, 128), F32)
    p.dout("x1_o", (D, TSL), F32)
    p.dout("xn1b_o", (D, TSL), BF16)
    p.dout("xn1f_o", (D, TSL), F32)
    p.dout("kc_o", (64, TSL), BF16)
    p.dout("vc_o", (64, TSL), BF16)
    p.dout("sc_o", (TSL, 8), F32)
    p.dout("scfm_o", (8, TSL), BF16)

    p.sbuf("x0r", (128, 8 * 512), F32)
    p.sbuf("s0r", (128, 8 * 512), F32)
    p.sbuf("trr", (128, 8 * 512), F32)
    p.sbuf("x1r", (128, 8 * 512), F32)
    p.sbuf("xn1", (128, 8 * TSL), BF16)
    p.sbuf("xnf", (128, 8 * 512), F32)
    p.sbuf("wk_s", (128, 8 * 256), BF16)
    p.sbuf("wv_s", (128, 8 * 256), BF16)
    p.sbuf("wc_s", (128, 2 * 64), BF16)
    p.sbuf("wg_s", (128, 8 * 8), BF16)
    p.sbuf("eye_s", (128, 128), BF16)
    p.sbuf("bk_s", (128, 2), F32)
    p.sbuf("bv_s", (128, 2), F32)
    p.sbuf("w1024", (128, 1), F32)
    p.sbuf("ones1", (1, 128), F32)
    p.sbuf("ceps", (1, 1), F32)
    p.sbuf("sq", (128, 8 * 512), F32)
    p.sbuf("mean", (1, 512), F32)
    p.sbuf("m2", (1, 512), F32)
    p.sbuf("var", (1, 512), F32)
    p.sbuf("lnv", (1, 512), F32)
    p.sbuf("rstd", (1, 512), F32)
    p.sbuf("tmp", (128, 512), F32)
    p.sbuf("k256", (128, 2 * 512), BF16)
    p.sbuf("v256", (128, 2 * 512), BF16)
    p.sbuf("kcs", (64, 512), BF16)
    p.sbuf("vcs", (64, 512), BF16)
    p.sbuf("lgs", (128, 8), F32)
    p.sbuf("srt", (128, 8), F32)
    p.sbuf("dd", (128, 1), F32)
    p.sbuf("dp1", (128, 1), F32)
    p.sbuf("rec", (128, 1), F32)
    p.sbuf("n1", (128, 1), F32)
    p.sbuf("oh0", (128, 8), F32)
    p.sbuf("oh1", (128, 8), F32)
    p.sbuf("sc0", (128, 8), F32)
    p.sbuf("sc1", (128, 8), F32)
    p.sbuf("scw", (128, NW * 8), F32)
    p.sbuf("scb", (128, 8), BF16)
    p.sbuf("scf", (8, NW * 128), BF16)

    p.psum("psA", (1, 512))
    p.psum("psB", (1, 512))
    p.psum("pm", (128, 512))
    p.psum("pr", (128, 512))
    p.psum("pk0", (128, 512))
    p.psum("pkc", (64, 512))
    p.psum("plg", (128, 8))
    p.psum("ptr", (8, 128), BF16)

    ld = p.sem("ld")
    ini = p.sem("ini")
    ac = p.sem("ac")
    tP = p.sem("tP")
    sS = p.sem("sS")
    vV = p.sem("vV")
    st = p.sem("st")

    @p.g
    def _(g, e):
        g.dma_start(out=e.wk_s[:, :], in_=e.wk).then_inc(ld.h, 16)
        g.dma_start(out=e.wv_s[:, :], in_=e.wv).then_inc(ld.h, 16)
        g.dma_start(out=e.wc_s[:, :], in_=e.wc2).then_inc(ld.h, 16)
        g.dma_start(out=e.wg_s[:, :], in_=e.wg1).then_inc(ld.h, 16)
        g.dma_start(out=e.eye_s[:, :], in_=e.eye).then_inc(ld.h, 16)
        g.dma_start(out=e.bk_s[:, :], in_=e.bkc).then_inc(ld.h, 16)
        g.dma_start(out=e.bv_s[:, :], in_=e.bvc).then_inc(ld.h, 16)
        g.memset(e.w1024[:, :], 1.0 / 1024.0)
        g.memset(e.ones1[:, :], 1.0)
        g.memset(e.ceps[:, :], 1e-5)
        g.engine_nop().then_inc(ini.h, 1)
    nwld = ld.inc(112)
    nini = ini.inc()

    for c in range(NCH):
        t0 = c * 512

        def cld(g, e, t0=t0, c=c, nwld=nwld, nvv=vV.n, nss=sS.n, nst=st.n, nsum=vV.n):
            if c == 0:
                g.wait_ge(ld.h, nwld)
            if c > 0:
                g.wait_ge(vV.h, nvv)
                g.wait_ge(sS.h, nss)
                g.wait_ge(st.h, nst)
            g.dma_start(out=e.x0r[:, :].rearrange("p (k t) -> p k t", k=8),
                        in_=e.x0fm[:, t0:t0 + 512].rearrange("(k p) t -> p k t", p=128)
                        ).then_inc(ld.h, 16)
            g.dma_start(out=e.s0r[:, :].rearrange("p (k t) -> p k t", k=8),
                        in_=e.part0a[0][:, t0:t0 + 512].rearrange("(k p) t -> p k t", p=128)
                        ).then_inc(ld.h, 16)
            g.dma_start(out=e.trr[:, :].rearrange("p (k t) -> p k t", k=8),
                        in_=e.part0a[1][:, t0:t0 + 512].rearrange("(k p) t -> p k t", p=128)
                        ).then_inc(ld.h, 16)
            for r in range(2, NC):
                g.wait_ge(vV.h, nsum + (r - 1))
                g.dma_start(out=e.trr[:, :].rearrange("p (k t) -> p k t", k=8),
                            in_=e.part0a[r][:, t0:t0 + 512].rearrange("(k p) t -> p k t", p=128)
                            ).then_inc(ld.h, 16)
        p.g(cld)
        nld0 = ld.inc(48)
        nld = nld0
        nlds = [ld.inc(16) for _ in range(NC - 2)]

        for r in range(1, NC):
            def psum_add(v, e, r=r, nl=(nld0 if r == 1 else nlds[r - 2])):
                v.wait_ge(ld.h, nl)
                v.tensor_add(e.s0r[:, :], e.s0r[:, :], e.trr[:, :])
                v.drain()
                v.engine_nop().then_inc(vV.h, 1)
            p.v(psum_add)
            vV.inc()

        def addf(v, e):
            for k in range(8):
                v.tensor_add(e.x1r[:, k * 512:(k + 1) * 512], e.x0r[:, k * 512:(k + 1) * 512],
                             e.s0r[:, k * 512:(k + 1) * 512])
            v.engine_nop().then_inc(vV.h, 1)
        p.v(addf)
        nx1 = vV.inc()

        def sqf(s, e, nx1=nx1, nini=nini):
            s.wait_ge(vV.h, nx1)
            s.wait_ge(ini.h, nini)
            for k in range(8):
                a = s.activation(e.sq[:, k * 512:(k + 1) * 512], e.x1r[:, k * 512:(k + 1) * 512],
                                 AF.Square)
            a.then_inc(sS.h, 1)
        p.s(sqf)
        nsq = sS.inc()

        def mmf(t, e, c=c, nsq=nsq, nvs=vV.n):
            t.wait_ge(sS.h, nsq)
            if c > 0:
                t.wait_ge(vV.h, nvs)
            for k in range(8):
                t.matmul(e.psA[:, :], e.w1024[:, :], e.x1r[:, k * 512:(k + 1) * 512],
                         start=(k == 0), stop=(k == 7))
            for k in range(8):
                mm = t.matmul(e.psB[:, :], e.w1024[:, :], e.sq[:, k * 512:(k + 1) * 512],
                              start=(k == 0), stop=(k == 7))
            mm.then_inc(tP.h, 1)
        p.t(mmf)
        nmm = tP.inc()

        def stv(v, e, nmm=nmm):
            v.wait_ge(tP.h, nmm)
            v.tensor_copy(e.mean[:, :], e.psA[:, :])
            v.drain()
            v.tensor_mul(e.m2[:, :], e.mean[:, :], e.mean[:, :])
            v.drain()
            v.tensor_sub(e.var[:, :], e.psB[:, :], e.m2[:, :])
            v.engine_nop().then_inc(vV.h, 1)
        p.v(stv)
        nvs = vV.inc()

        def sts(s, e, nvs=nvs):
            s.wait_ge(vV.h, nvs)
            s.activation(e.lnv[:, :], e.var[:, :], AF.Ln, bias=e.ceps[:, :])
            s.drain()
            s.activation(e.rstd[:, :], e.lnv[:, :], AF.Exp, scale=-0.5).then_inc(sS.h, 1)
        p.s(sts)
        nss = sS.inc()

        def bcm(t, e, c=c, nss=nss, nxd=vV.n):
            t.wait_ge(sS.h, nss)
            if c > 0:
                t.wait_ge(vV.h, nxd)
            t.matmul(e.pm[:, :], e.ones1[:, :], e.mean[:, :], start=True, stop=True)
            t.matmul(e.pr[:, :], e.ones1[:, :], e.rstd[:, :], start=True, stop=True).then_inc(tP.h, 1)
        p.t(bcm)
        nbc = tP.inc()

        def xnf(v, e, c=c, t0=t0, nbc=nbc):
            v.wait_ge(tP.h, nbc)
            for k in range(8):
                v.tensor_sub(e.tmp[:, :], e.x1r[:, k * 512:(k + 1) * 512], e.pm[:, :])
                v.drain()
                v.tensor_mul(e.xn1[:, k * TSL + t0: k * TSL + t0 + 512], e.tmp[:, :], e.pr[:, :])
                v.drain()
                v.tensor_mul(e.xnf[:, k * 512:(k + 1) * 512], e.tmp[:, :], e.pr[:, :])
                v.drain()
            v.engine_nop().then_inc(vV.h, 1)
        p.v(xnf)
        nxn = vV.inc()

        def xsto(sy, e, t0=t0, nxn=nxn):
            sy.wait_ge(vV.h, nxn)
            sy.dma_start(out=e.x1_o[:, t0:t0 + 512].rearrange("(k p) t -> p k t", p=128),
                         in_=e.x1r[:, :].rearrange("p (k t) -> p k t", k=8)).then_inc(st.h, 16)
            sy.dma_start(out=e.xn1f_o[:, t0:t0 + 512].rearrange("(k p) t -> p k t", p=128),
                         in_=e.xnf[:, :].rearrange("p (k t) -> p k t", k=8)).then_inc(st.h, 16)
        p.y(xsto)
        st.inc(32)

        for which in ("k", "v"):
            wnm = "wk_s" if which == "k" else "wv_s"
            bnm = "bk_s" if which == "k" else "bv_s"
            s1nm = "k256" if which == "k" else "v256"
            onm = "kcs" if which == "k" else "vcs"
            dnm = "kc_o" if which == "k" else "vc_o"

            nke = None
            for m in range(2):
                def kmm(t, e, wnm=wnm, m=m, t0=t0, nxn=nxn, nse=sS.n):
                    t.wait_ge(vV.h, nxn)
                    t.wait_ge(sS.h, nse)
                    for k in range(8):
                        mm = t.matmul(e.pk0[:, :], e[wnm][:, k * 256 + m * 128: k * 256 + (m + 1) * 128],
                                      e.xn1[:, k * TSL + t0: k * TSL + t0 + 512],
                                      start=(k == 0), stop=(k == 7))
                    mm.then_inc(tP.h, 1)
                p.t(kmm)
                nkm = tP.inc()

                def kev(v, e, bnm=bnm, s1nm=s1nm, m=m, nkm=nkm):
                    v.wait_ge(tP.h, nkm)
                    v.tensor_scalar(e[s1nm][:, m * 512:(m + 1) * 512], e.pk0[:, :],
                                    e[bnm][:, m:m + 1], None, ALU.add)
                    v.engine_nop().then_inc(sS.h, 1)
                p.v(kev)
                nke = sS.inc()

            def cmm(t, e, s1nm=s1nm, nke=nke, nse2=sS.n):
                t.wait_ge(sS.h, nke)
                for k in range(2):
                    mm = t.matmul(e.pkc[:, :], e.wc_s[:, k * 64:(k + 1) * 64],
                                  e[s1nm][:, k * 512:(k + 1) * 512], start=(k == 0), stop=(k == 1))
                mm.then_inc(tP.h, 1)
            p.t(cmm)
            ncm = tP.inc()

            def cev(s, e, onm=onm, ncm=ncm, nst=st.n):
                s.wait_ge(tP.h, ncm)
                if nst > 0:
                    s.wait_ge(st.h, nst)
                s.copy(e[onm][:, :], e.pkc[:, :]).then_inc(sS.h, 1)
            p.s(cev)
            nce = sS.inc()

            def ksto(sy, e, onm=onm, dnm=dnm, t0=t0, nce=nce):
                sy.wait_ge(sS.h, nce)
                sy.dma_start(out=e[dnm][:, t0:t0 + 512], in_=e[onm][:, :]).then_inc(st.h, 16)
            p.y(ksto)
            st.inc(16)

        for wi in range(4):
            w = c * 4 + wi

            def smm(t, e, w=w, t0=t0, wi=wi, nxn=nxn, nsev=sS.n):
                t.wait_ge(vV.h, nxn)
                if w > 0:
                    t.wait_ge(sS.h, nsev)
                for k in range(8):
                    mm = t.matmul(e.plg[:, :], e.xn1[:, k * TSL + t0 + wi * 128: k * TSL + t0 + (wi + 1) * 128],
                                  e.wg_s[:, k * 8:(k + 1) * 8], start=(k == 0), stop=(k == 7))
                mm.then_inc(tP.h, 1)
            p.t(smm)
            nsm = tP.inc()

            def sev(s, e, nsm=nsm, nvr=vV.n):
                s.wait_ge(tP.h, nsm)
                if nvr > 0:
                    s.wait_ge(vV.h, nvr)
                s.copy(e.lgs[:, :], e.plg[:, :]).then_inc(sS.h, 1)
            p.s(sev)
            nsev = sS.inc()

            def rt1(v, e, nsev=nsev):
                v.wait_ge(sS.h, nsev)
                v.max(e.srt[:, :], e.lgs[:, :])
                v.drain()
                v.tensor_scalar(e.dd[:, :], e.srt[:, 1:2], e.srt[:, 0:1], None, ALU.subtract)
                v.engine_nop().then_inc(vV.h, 1)
            p.v(rt1)
            nr1 = vV.inc()

            def rt2(s, e, nr1=nr1):
                s.wait_ge(vV.h, nr1)
                s.activation(e.dd[:, :], e.dd[:, :], AF.Exp).then_inc(sS.h, 1)
            p.s(rt2)
            nr2 = sS.inc()

            def rt3(v, e, w=w, nr2=nr2, ntp=tP.n):
                v.wait_ge(sS.h, nr2)
                if w > 0:
                    v.wait_ge(tP.h, ntp)
                v.tensor_scalar(e.dp1[:, :], e.dd[:, :], 1.0, None, ALU.add)
                v.drain()
                v.reciprocal(e.rec[:, :], e.dp1[:, :])
                v.drain()
                v.tensor_mul(e.n1[:, :], e.dd[:, :], e.rec[:, :])
                v.tensor_scalar(e.oh0[:, :], e.lgs[:, :], e.srt[:, 0:1], None, ALU.is_equal)
                v.tensor_scalar(e.oh1[:, :], e.lgs[:, :], e.srt[:, 1:2], None, ALU.is_equal)
                v.drain()
                v.tensor_scalar(e.sc0[:, :], e.oh0[:, :], e.rec[:, 0:1], None, ALU.mult)
                v.tensor_scalar(e.sc1[:, :], e.oh1[:, :], e.n1[:, 0:1], None, ALU.mult)
                v.drain()
                v.tensor_add(e.scw[:, w * 8:(w + 1) * 8], e.sc0[:, :], e.sc1[:, :])
                v.drain()
                v.tensor_copy(e.scb[:, :], e.scw[:, w * 8:(w + 1) * 8])
                v.engine_nop().then_inc(vV.h, 1)
            p.v(rt3)
            nr3 = vV.inc()

            def trx(t, e, w=w, nr3=nr3, nprev=sS.n):
                t.wait_ge(vV.h, nr3)
                if w > 0:
                    t.wait_ge(sS.h, nprev)
                t.transpose(e.ptr[:, :], e.scb[:, :], e.eye_s[:, :]).then_inc(tP.h, 1)
            p.t(trx)
            ntr = tP.inc()

            def trev(s, e, w=w, ntr=ntr):
                s.wait_ge(tP.h, ntr)
                s.copy(e.scf[:, w * 128:(w + 1) * 128], e.ptr[:, :]).then_inc(sS.h, 1)
            p.s(trev)
            sS.inc()

    def stf(sy, e, nvv=vV.n, nss=sS.n):
        sy.wait_ge(vV.h, nvv)
        sy.wait_ge(sS.h, nss)
        sy.dma_start(out=e.xn1b_o.rearrange("(k p) t -> p k t", p=128),
                     in_=e.xn1[:, :].rearrange("p (k t) -> p k t", k=8)).then_inc(st.h, 16)
        sy.dma_start(out=e.sc_o.rearrange("(w p) e -> p w e", p=128),
                     in_=e.scw[:, :].rearrange("p (w e) -> p w e", e=8)).then_inc(st.h, 16)
        sy.dma_start(out=e.scfm_o.rearrange("a (w t) -> a w t", t=128),
                     in_=e.scf[:, :].rearrange("a (w t) -> a w t", t=128)).then_inc(st.h, 16)
        sy.wait_ge(st.h, st.inc(48))
    p.y(stf)
    return p



def make_p4(T):
    p = P()
    NCH = T // 512
    NAC = T // 128
    KH = T // 2 if T >= 1024 else T

    p.din("xn1f", (D, T), BF16)
    p.din("kcf", (64, T), BF16)
    p.din("vcf", (64, T), BF16)
    p.din("scfm", (8, T), BF16)
    p.din("wq", (128, 8 * 128), F32)
    p.din("bq_c", (128, 1), F32)
    p.din("we_h", (64, 64), F32)
    p.din("wo_sl", (128, 8 * 128), F32)
    p.din("w1_t", (128, 8 * 8 * 256), F32)
    p.din("b1_c", (128, 2 * 8), F32)
    p.din("w2_t", (128, 8 * 2 * 1024), F32)
    p.din("b2_f", (8, D), F32)
    p.din("c0v", (8, 1), F32)
    p.din("sel8", (8, 8 * 128), F32)
    p.din("utri", (128, 128), F32)
    p.din("eye", (128, 128), F32)
    p.dout("parto", (D, T), F32)
    p.dout("partf", (D, T), F32)
    p.dint("odram", (128, T), BF16)

    p.sbuf("wq_s", (128, 8 * 128), BF16)
    p.sbuf("we_s", (64, 64), BF16)
    p.sbuf("wo_s", (128, 8 * 128), BF16)
    p.sbuf("w1_s", (128, 8 * 8 * 256), BF16)
    p.sbuf("w2_s", (128, 8 * 2 * 1024), BF16)
    p.sbuf("b1_s", (128, 2 * 8), F32)
    p.sbuf("b2_s", (8, D), BF16)
    p.sbuf("c0_s", (8, 1), F32)
    p.sbuf("sel8_s", (8, 8 * 128), BF16)
    p.sbuf("bq_s", (128, 1), F32)
    p.sbuf("bqn", (128, 1), F32)
    p.sbuf("utri_s", (128, 128), F32)
    p.sbuf("eye_s", (128, 64), BF16)
    p.sbuf("onesc", (128, 1), BF16)
    p.sbuf("onesf", (1, 64), F32)
    p.sbuf("qf", (128, T), BF16)
    p.sbuf("kf", (128, KH), BF16)
    p.sbuf("ve", (128, KH), BF16)
    p.sbuf("ob", (128, 128), BF16)
    p.sbuf("oc", (128, 512), BF16)
    p.sbuf("xnc", (128, 8 * 512), BF16)
    p.sbuf("kcc", (64, 512), BF16)
    p.sbuf("vcc", (64, 512), BF16)
    p.sbuf("a1", (128, 512), F32)
    p.sbuf("r2", (128, 512), F32)
    p.sbuf("e2", (128, 512), F32)
    p.sbuf("qh", (64, 128), BF16)
    p.sbuf("kfc0", (64, 128), BF16)
    p.sbuf("vec0", (64, 128), BF16)
    p.sbuf("kfT", (128, 64), BF16)
    p.sbuf("vT", (128, 64), BF16)
    p.sbuf("Sm", (128, 128), BF16)
    p.sbuf("dnv", (1, 128), F32)
    p.sbuf("rcv", (1, 128), F32)
    p.sbuf("rb", (64, 128), F32)
    p.sbuf("KV", (64, 64), BF16)
    p.sbuf("ks", (64, 1), BF16)
    p.sbuf("KVf", (64, 64), F32)
    p.sbuf("ksf", (64, 1), F32)
    p.sbuf("hr", (128, 512), F32)
    p.sbuf("hstore", (128, 16 * 512), BF16)
    p.sbuf("sc8", (8, 512), BF16)
    p.sbuf("sc0", (8, 512), BF16)
    p.sbuf("po2", (128, 2 * 512), F32)
    p.sbuf("pf2", (128, 2 * 512), F32)

    p.psum("pa", (128, 512))
    p.psum("pb2", (128, 512))
    p.psum("pc64", (64, 512))
    p.psum("ptA", (128, 64), BF16)
    p.psum("ptB", (128, 64), BF16)
    p.psum("pD", (1, 128))
    p.psum("pKV", (64, 64))
    p.psum("pks", (64, 1))

    ld = p.sem("ld")
    ini = p.sem("ini")
    tP = p.sem("tP")
    sS = p.sem("sS")
    vV = p.sem("vV")
    st = p.sem("st")

    @p.g
    def _(g, e):
        g.dma_start(out=e.wq_s[:, :], in_=e.wq).then_inc(ld.h, 16)
        g.dma_start(out=e.we_s[:, :], in_=e.we_h).then_inc(ld.h, 16)
        g.dma_start(out=e.wo_s[:, :], in_=e.wo_sl).then_inc(ld.h, 16)
        g.dma_start(out=e.w1_s[:, :], in_=e.w1_t).then_inc(ld.h, 16)
        g.dma_start(out=e.w2_s[:, :], in_=e.w2_t).then_inc(ld.h, 16)
        g.dma_start(out=e.b1_s[:, :], in_=e.b1_c).then_inc(ld.h, 16)
        g.dma_start(out=e.b2_s[:, :], in_=e.b2_f).then_inc(ld.h, 16)
        g.dma_start(out=e.c0_s[:, :], in_=e.c0v).then_inc(ld.h, 16)
        g.dma_start(out=e.sel8_s[:, :], in_=e.sel8).then_inc(ld.h, 16)
        g.dma_start(out=e.bq_s[:, :], in_=e.bq_c).then_inc(ld.h, 16)
        g.dma_start(out=e.utri_s[:, :], in_=e.utri).then_inc(ld.h, 16)
        g.dma_start(out=e.eye_s[0:64, :], in_=e.eye[0:64, 0:64]).then_inc(ld.h, 16)
        g.dma_start(out=e.eye_s[64:128, :], in_=e.eye[0:64, 0:64]).then_inc(ld.h, 16)
        g.memset(e.onesc[:, :], 1.0)
        g.memset(e.onesf[:, :], 1.0)
        g.memset(e.KV[:, :], 0.0)
        g.memset(e.ks[:, :], 0.0)
        g.memset(e.KVf[:, :], 0.0)
        g.memset(e.ksf[:, :], 0.0)
        g.engine_nop().then_inc(ini.h, 1)
    nwld = ld.inc(208)
    nini = ini.inc()

    def bqneg(v, e, nwld=nwld, nini=nini):
        v.wait_ge(ld.h, nwld)
        v.wait_ge(ini.h, nini)
        v.tensor_scalar(e.bqn[:, :], e.bq_s[:, :], -1.0, None, ALU.mult)
        v.engine_nop().then_inc(vV.h, 1)
    p.v(bqneg)
    vV.inc()

    for c in range(NCH):
        t0 = c * 512

        def ald(g, e, t0=t0, c=c, ntp=tP.n, nwld=nwld):
            if c == 0:
                g.wait_ge(ld.h, nwld)
            if c > 0:
                g.wait_ge(tP.h, ntp)
            g.dma_start(out=e.xnc[:, :].rearrange("p (k t) -> p k t", k=8),
                        in_=e.xn1f[:, t0:t0 + 512].rearrange("(k p) t -> p k t", p=128)
                        ).then_inc(ld.h, 16)
            g.dma_start(out=e.kcc[:, :], in_=e.kcf[:, t0:t0 + 512]).then_inc(ld.h, 16)
            g.dma_start(out=e.vcc[:, :], in_=e.vcf[:, t0:t0 + 512]).then_inc(ld.h, 16)
        p.g(ald)
        nld = ld.inc(48)

        def qmm(t, e, nld=nld, nse=sS.n, c=c):
            t.wait_ge(ld.h, nld)
            if c > 0:
                t.wait_ge(sS.h, nse)
            for k in range(8):
                mm = t.matmul(e.pa[:, :], e.wq_s[:, k * 128:(k + 1) * 128],
                              e.xnc[:, k * 512:(k + 1) * 512], start=(k == 0), stop=(k == 7))
            mm.then_inc(tP.h, 1)
        p.t(qmm)
        nqm = tP.inc()

        def qep_s(s, e, nqm=nqm, nvp=vV.n):
            s.wait_ge(tP.h, nqm)
            s.wait_ge(vV.h, nvp)
            s.activation(e.a1[:, :], e.pa[:, :], AF.Relu, bias=e.bq_s[:, 0:1])
            s.activation(e.r2[:, :], e.pa[:, :], AF.Relu, scale=-1.0, bias=e.bqn[:, 0:1])
            s.drain()
            s.activation(e.e2[:, :], e.r2[:, :], AF.Exp, scale=-1.0).then_inc(sS.h, 1)
        p.s(qep_s)
        nqe = sS.inc()

        def qep_v(v, e, t0=t0, nqe=nqe):
            v.wait_ge(sS.h, nqe)
            v.tensor_add(e.qf[:, t0:t0 + 512], e.a1[:, :], e.e2[:, :])
            v.engine_nop().then_inc(vV.h, 1)
        p.v(qep_v)
        vV.inc()

        def kvmm(t, e, nld=nld, nse=sS.n, c=c):
            if c > 0:
                t.wait_ge(sS.h, nse)
            t.matmul(e.pc64[:, :], e.we_s[:, :], e.kcc[:, :], start=True, stop=True).then_inc(tP.h, 1)
        p.t(kvmm)
        nkm = tP.inc()

        def kep_s(s, e, nkm=nkm, nvp=vV.n):
            s.wait_ge(tP.h, nkm)
            s.wait_ge(vV.h, nvp)
            s.activation(e.a1[0:64, :], e.pc64[:, :], AF.Relu)
            s.activation(e.r2[0:64, :], e.pc64[:, :], AF.Relu, scale=-1.0)
            s.drain()
            s.activation(e.e2[0:64, :], e.r2[0:64, :], AF.Exp, scale=-1.0).then_inc(sS.h, 1)
        p.s(kep_s)
        nke = sS.inc()

        def kep_v(v, e, t0=t0, nke=nke, KH=KH):
            v.wait_ge(sS.h, nke)
            hb = 64 * (t0 // KH)
            tt = t0 % KH
            v.tensor_add(e.kf[hb:hb + 64, tt:tt + 512], e.a1[0:64, :], e.e2[0:64, :])
            v.engine_nop().then_inc(vV.h, 1)
        p.v(kep_v)
        nkv = vV.inc()

        def vmm(t, e, nkv=nkv):
            t.wait_ge(vV.h, nkv)
            t.matmul(e.pc64[:, :], e.we_s[:, :], e.vcc[:, :], start=True, stop=True).then_inc(tP.h, 1)
        p.t(vmm)
        nvm = tP.inc()

        def vev(s, e, t0=t0, nvm=nvm, KH=KH):
            s.wait_ge(tP.h, nvm)
            hb = 64 * (t0 // KH)
            tt = t0 % KH
            s.copy(e.ve[hb:hb + 64, tt:tt + 512], e.pc64[:, :]).then_inc(sS.h, 1)
        p.s(vev)
        sS.inc()

    nA_v = vV.n
    nA_s = sS.n

    for c in range(NAC):
        u0 = c * 128

        def bcp(v, e, c=c, u0=u0, nA_v=nA_v, nA_s=nA_s, KH=KH, ntpp=tP.n):
            if c == 0:
                v.wait_ge(vV.h, nA_v)
                v.wait_ge(sS.h, nA_s)
            else:
                v.wait_ge(tP.h, ntpp)
            hb = 64 * (u0 // KH)
            uu = u0 % KH
            v.tensor_copy(e.kfc0[:, :], e.kf[hb:hb + 64, uu:uu + 128])
            v.tensor_copy(e.vec0[:, :], e.ve[hb:hb + 64, uu:uu + 128])
            v.engine_nop().then_inc(vV.h, 1)
        p.v(bcp)
        nbcp = vV.inc()

        def btr(t, e, c=c, nbcp=nbcp, nsev=sS.n):
            t.wait_ge(vV.h, nbcp)
            if c > 0:
                t.wait_ge(sS.h, nsev)
            t.transpose(e.ptA[:, :], e.kfc0[:, :], e.eye_s[0:64, :])
            t.transpose(e.ptB[:, :], e.vec0[:, :], e.eye_s[0:64, :]).then_inc(tP.h, 1)
        p.t(btr)
        ntr = tP.inc()

        def bev(s, e, ntr=ntr, nvv=vV.n):
            s.wait_ge(tP.h, ntr)
            if nvv > 0:
                s.wait_ge(vV.h, nvv)
            s.copy(e.kfT[:, :], e.ptA[:, :])
            s.copy(e.vT[:, :], e.ptB[:, :]).then_inc(sS.h, 1)
        p.s(bev)
        nbe = sS.inc()

        for hh in range(2):
            def qcp(v, e, hh=hh, u0=u0, nbe=nbe, ntp=tP.n):
                v.wait_ge(sS.h, nbe)
                if not (c == 0 and hh == 0):
                    v.wait_ge(tP.h, ntp)
                v.tensor_copy(e.qh[:, :], e.qf[hh * 64:(hh + 1) * 64, u0:u0 + 128])
                v.engine_nop().then_inc(vV.h, 1)
            p.v(qcp)
            nqc = vV.inc()

            def bs(t, e, hh=hh, nqc=nqc):
                t.wait_ge(vV.h, nqc)
                t.matmul(e.pa[:, 0:128], e.kfc0[:, :], e.qh[:, :],
                         start=True, stop=True).then_inc(tP.h, 1)
            p.t(bs)
            nbs = tP.inc()

            def bsm(v, e, nbs=nbs):
                v.wait_ge(tP.h, nbs)
                v.tensor_mul(e.Sm[:, :], e.pa[:, 0:128], e.utri_s[:, :])
                v.engine_nop().then_inc(vV.h, 1)
            p.v(bsm)
            nsm = vV.inc()

            def bnd(t, e, hh=hh, c=c, nsm=nsm):
                t.wait_ge(vV.h, nsm)
                t.matmul(e.pc64[:, 0:128], e.vT[:, :], e.Sm[:, :], start=True, stop=False)
                t.matmul(e.pc64[:, 0:128], e.KV[:, :], e.qh[:, :], start=False, stop=True)
                t.matmul(e.pD[:, :], e.onesc[:, :], e.Sm[:, :], start=True, stop=False)
                t.matmul(e.pD[:, :], e.ks[:, :], e.qh[:, :], start=False, stop=True).then_inc(tP.h, 1)
            p.t(bnd)
            nnd = tP.inc()

            def brc(v, e, nnd=nnd):
                v.wait_ge(tP.h, nnd)
                v.tensor_scalar(e.dnv[:, :], e.pD[:, :], 1e-6, None, ALU.add)
                v.drain()
                v.reciprocal(e.rcv[:, :], e.dnv[:, :])
                v.engine_nop().then_inc(vV.h, 1)
            p.v(brc)
            nrc = vV.inc()

            def brb(t, e, nrc=nrc, nsrb=sS.n):
                t.wait_ge(vV.h, nrc)
                if nsrb > 0:
                    t.wait_ge(sS.h, nsrb)
                t.matmul(e.pb2[0:64, 0:128], e.onesf[:, :], e.rcv[:, :],
                         start=True, stop=True).then_inc(tP.h, 1)
            p.t(brb)
            nrb = tP.inc()

            def brbe(s, e, nrb=nrb):
                s.wait_ge(tP.h, nrb)
                s.copy(e.rb[:, :], e.pb2[0:64, 0:128]).then_inc(sS.h, 1)
            p.s(brbe)
            nrbe = sS.inc()

            def bo(v, e, hh=hh, u0=u0, nrbe=nrbe, nstb=st.n):
                v.wait_ge(sS.h, nrbe)
                if hh == 0 and nstb > 0:
                    v.wait_ge(st.h, nstb)
                v.tensor_mul(e.ob[hh * 64:(hh + 1) * 64, :], e.pc64[:, 0:128],
                             e.rb[:, :])
                v.engine_nop().then_inc(vV.h, 1)
            p.v(bo)
            vV.inc()

        def osto(sy, e, c=c, u0=u0, nvo=vV.n):
            sy.wait_ge(vV.h, nvo)
            sy.dma_start(out=e.odram[:, u0:u0 + 128], in_=e.ob[:, :]).then_inc(st.h, 16)
        p.y(osto)
        st.inc(16)

        def bkv(t, e, c=c, NAC=NAC, nvo=vV.n, nsup=sS.n):
            t.wait_ge(vV.h, nvo)
            if c > 0:
                t.wait_ge(sS.h, nsup)
            t.matmul(e.pKV[:, :], e.kfT[:, :], e.vT[:, :], start=True, stop=True)
            t.matmul(e.pks[:, :], e.kfT[:, :], e.onesc[:, :],
                     start=True, stop=True).then_inc(tP.h, 1)
        p.t(bkv)
        nkvu = tP.inc()

        def bkve(v, e, nkvu=nkvu):
            v.wait_ge(tP.h, nkvu)
            v.tensor_add(e.KVf[:, :], e.KVf[:, :], e.pKV[:, :])
            v.tensor_add(e.ksf[:, :], e.ksf[:, :], e.pks[:, :])
            v.drain()
            v.tensor_copy(e.KV[:, :], e.KVf[:, :])
            v.tensor_copy(e.ks[:, :], e.ksf[:, :])
            v.engine_nop().then_inc(sS.h, 1)
        p.v(bkve)
        sS.inc()

    nB_v = vV.n
    nB_s = sS.n
    nB_t = tP.n
    nB_st = st.n

    for c in range(NCH):
        t0 = c * 512

        def cld(g, e, t0=t0, c=c, ntp=tP.n, nst=st.n, nB_t=nB_t, nB_st=nB_st):
            if c == 0:
                g.wait_ge(tP.h, nB_t)
                g.wait_ge(st.h, nB_st)
            if c > 0:
                g.wait_ge(tP.h, ntp)
                g.wait_ge(st.h, nst)
            g.dma_start(out=e.xnc[:, :].rearrange("p (k t) -> p k t", k=8),
                        in_=e.xn1f[:, t0:t0 + 512].rearrange("(k p) t -> p k t", p=128)
                        ).then_inc(ld.h, 16)
            g.dma_start(out=e.sc8[:, :], in_=e.scfm[:, t0:t0 + 512]).then_inc(ld.h, 16)
            g.dma_start(out=e.oc[:, :], in_=e.odram[:, t0:t0 + 512]).then_inc(ld.h, 16)
        p.g(cld)
        nld = ld.inc(48)

        def csc(v, e, nld=nld, nB_s=nB_s):
            v.wait_ge(ld.h, nld)
            v.tensor_scalar(e.sc0[:, :], e.sc8[:, :], e.c0_s[:, 0:1], None, ALU.mult)
            v.engine_nop().then_inc(vV.h, 1)
        p.v(csc)
        ncsc = vV.inc()

        for ei in range(8):
            def wbm(t, e, ei=ei, c=c, nld=nld, ncsc=ncsc, nB_v=nB_v, nhw=vV.n):
                if c == 0 and ei == 0:
                    t.wait_ge(vV.h, nB_v)
                t.wait_ge(vV.h, nhw)
                t.matmul(e.pb2[:, :], e.sel8_s[:, ei * 128:(ei + 1) * 128], e.sc8[:, :],
                         start=True, stop=True).then_inc(tP.h, 1)
            p.t(wbm)
            nwb = tP.inc()

            for m in range(2):
                def hmm(t, e, ei=ei, m=m, nld=nld, nse=sS.n):
                    t.wait_ge(ld.h, nld)
                    t.wait_ge(sS.h, nse)
                    for k in range(8):
                        mm = t.matmul(e.pa[:, :],
                                      e.w1_s[:, (k * 8 + ei) * 256 + m * 128: (k * 8 + ei) * 256 + (m + 1) * 128],
                                      e.xnc[:, k * 512:(k + 1) * 512], start=(k == 0), stop=(k == 7))
                    mm.then_inc(tP.h, 1)
                p.t(hmm)
                nhm = tP.inc()

                def hre(s, e, ei=ei, m=m, nhm=nhm, nhw=vV.n):
                    s.wait_ge(tP.h, nhm)
                    if not (c == 0 and ei == 0 and m == 0):
                        s.wait_ge(vV.h, nhw)
                    s.activation(e.hr[:, :], e.pa[:, :], AF.Relu,
                                 bias=e.b1_s[:, ei * 2 + m: ei * 2 + m + 1]).then_inc(sS.h, 1)
                p.s(hre)
                nhr = sS.inc()

                def hwv(v, e, ei=ei, m=m, nhr=nhr, nwb=nwb):
                    v.wait_ge(sS.h, nhr)
                    v.wait_ge(tP.h, nwb)
                    v.tensor_mul(e.hstore[:, (ei * 2 + m) * 512:(ei * 2 + m + 1) * 512],
                                 e.hr[:, :], e.pb2[:, :])
                    v.engine_nop().then_inc(vV.h, 1)
                p.v(hwv)
                vV.inc()

        for m in range(8):
            def omm(t, e, m=m, t0=t0, nse=sS.n, nhid=vV.n):
                t.wait_ge(vV.h, nhid)
                if m > 0:
                    t.wait_ge(sS.h, nse)
                t.matmul(e.pa[:, :], e.wo_s[:, m * 128:(m + 1) * 128],
                         e.oc[:, :], start=True, stop=True).then_inc(tP.h, 1)
            p.t(omm)
            nom = tP.inc()

            def oev(s, e, m=m, nom=nom, nst=st.n):
                s.wait_ge(tP.h, nom)
                if m >= 2:
                    s.wait_ge(st.h, nst)
                s.copy(e.po2[:, (m % 2) * 512:(m % 2 + 1) * 512], e.pa[:, :]).then_inc(sS.h, 1)
            p.s(oev)
            noev = sS.inc()

            def osto2(sy, e, m=m, t0=t0, noev=noev):
                sy.wait_ge(sS.h, noev)
                sy.dma_start(out=e.parto[m * 128:(m + 1) * 128, t0:t0 + 512],
                             in_=e.po2[:, (m % 2) * 512:(m % 2 + 1) * 512]).then_inc(st.h, 16)
            p.y(osto2)
            st.inc(16)

        for m in range(8):
            def fmm(t, e, m=m, ncsc=ncsc, nse=sS.n):
                t.wait_ge(vV.h, ncsc)
                t.wait_ge(sS.h, nse)
                for ei in range(8):
                    for k2 in range(2):
                        t.matmul(e.pa[:, :],
                                 e.w2_s[:, (ei * 2 + k2) * 1024 + m * 128: (ei * 2 + k2) * 1024 + (m + 1) * 128],
                                 e.hstore[:, (ei * 2 + k2) * 512:(ei * 2 + k2 + 1) * 512],
                                 start=(ei == 0 and k2 == 0), stop=False)
                mm = t.matmul(e.pa[:, :], e.b2_s[:, m * 128:(m + 1) * 128], e.sc0[:, :],
                              start=False, stop=True)
                mm.then_inc(tP.h, 1)
            p.t(fmm)
            nfm = tP.inc()

            def fev(s, e, m=m, nfm=nfm, nst=st.n):
                s.wait_ge(tP.h, nfm)
                if m >= 2:
                    s.wait_ge(st.h, nst)
                s.copy(e.pf2[:, (m % 2) * 512:(m % 2 + 1) * 512], e.pa[:, :]).then_inc(sS.h, 1)
            p.s(fev)
            nfev = sS.inc()

            def fsto2(sy, e, m=m, t0=t0, nfev=nfev):
                sy.wait_ge(sS.h, nfev)
                sy.dma_start(out=e.partf[m * 128:(m + 1) * 128, t0:t0 + 512],
                             in_=e.pf2[:, (m % 2) * 512:(m % 2 + 1) * 512]).then_inc(st.h, 16)
            p.y(fsto2)
            st.inc(16)


    def fin(sy, e, n=st.n):
        sy.wait_ge(st.h, n)
    p.y(fin)
    return p



def make_p5(TSL):
    p = P()
    NCH = TSL // 512

    p.din("x1fm", (D, TSL), F32)
    p.din("xn1fm", (D, TSL), F32)
    p.din("parto_a", (NC, D, TSL), F32)
    p.din("partf_a", (NC, D, TSL), F32)
    p.din("bo_c", (128, 8), F32)
    p.din("gam_c", (128, 8), F32)
    p.din("bet_c", (128, 8), F32)
    p.dout("out_o", (D, TSL), F32)

    p.sbuf("x1r", (128, 8 * 512), F32)
    p.sbuf("xnr", (128, 8 * 512), F32)
    p.sbuf("orr", (128, 8 * 512), F32)
    p.sbuf("frr", (128, 8 * 512), F32)
    p.sbuf("tr5", (128, 8 * 512), F32)
    p.sbuf("yr", (128, 8 * 512), F32)
    p.sbuf("a1r", (128, 8 * 512), F32)
    p.sbuf("oro", (128, 8 * 512), F32)
    p.sbuf("bo_s", (128, 8), F32)
    p.sbuf("gam_s", (128, 8), F32)
    p.sbuf("bet_s", (128, 8), F32)
    p.sbuf("w1024", (128, 1), F32)
    p.sbuf("ones1", (1, 128), F32)
    p.sbuf("ceps", (1, 1), F32)
    p.sbuf("sq", (128, 8 * 512), F32)
    p.sbuf("mean", (1, 512), F32)
    p.sbuf("m2", (1, 512), F32)
    p.sbuf("var", (1, 512), F32)
    p.sbuf("lnv", (1, 512), F32)
    p.sbuf("rstd", (1, 512), F32)
    p.sbuf("tmp", (128, 512), F32)

    p.psum("psA", (1, 512))
    p.psum("psB", (1, 512))
    p.psum("pm", (128, 512))
    p.psum("pr", (128, 512))

    ld = p.sem("ld")
    ini = p.sem("ini")
    ac = p.sem("ac")
    tP = p.sem("tP")
    sS = p.sem("sS")
    vV = p.sem("vV")
    st = p.sem("st")

    @p.g
    def _(g, e):
        g.dma_start(out=e.bo_s[:, :], in_=e.bo_c).then_inc(ld.h, 16)
        g.dma_start(out=e.gam_s[:, :], in_=e.gam_c).then_inc(ld.h, 16)
        g.dma_start(out=e.bet_s[:, :], in_=e.bet_c).then_inc(ld.h, 16)
        g.memset(e.w1024[:, :], 1.0 / 1024.0)
        g.memset(e.ones1[:, :], 1.0)
        g.memset(e.ceps[:, :], 1e-5)
        g.engine_nop().then_inc(ini.h, 1)
    nwld = ld.inc(48)
    nini = ini.inc()

    for c in range(NCH):
        t0 = c * 512

        def cld(g, e, t0=t0, c=c, nwld=nwld, nvv=vV.n, nst=st.n, nsum=vV.n):
            if c == 0:
                g.wait_ge(ld.h, nwld)
            if c > 0:
                g.wait_ge(vV.h, nvv)
                g.wait_ge(st.h, nst)
            g.dma_start(out=e.x1r[:, :].rearrange("p (k t) -> p k t", k=8),
                        in_=e.x1fm[:, t0:t0 + 512].rearrange("(k p) t -> p k t", p=128)).then_inc(ld.h, 16)
            g.dma_start(out=e.xnr[:, :].rearrange("p (k t) -> p k t", k=8),
                        in_=e.xn1fm[:, t0:t0 + 512].rearrange("(k p) t -> p k t", p=128)).then_inc(ld.h, 16)
            g.dma_start(out=e.orr[:, :].rearrange("p (k t) -> p k t", k=8),
                        in_=e.parto_a[0][:, t0:t0 + 512].rearrange("(k p) t -> p k t", p=128)).then_inc(ld.h, 16)
            g.dma_start(out=e.frr[:, :].rearrange("p (k t) -> p k t", k=8),
                        in_=e.partf_a[0][:, t0:t0 + 512].rearrange("(k p) t -> p k t", p=128)).then_inc(ld.h, 16)
            g.dma_start(out=e.tr5[:, :].rearrange("p (k t) -> p k t", k=8),
                        in_=e.parto_a[1][:, t0:t0 + 512].rearrange("(k p) t -> p k t", p=128)).then_inc(ld.h, 16)
            for rr in range(1, 2 * NC - 2):
                r = rr + 1
                g.wait_ge(vV.h, nsum + rr)
                if r < NC:
                    g.dma_start(out=e.tr5[:, :].rearrange("p (k t) -> p k t", k=8),
                                in_=e.parto_a[r][:, t0:t0 + 512].rearrange("(k p) t -> p k t", p=128)).then_inc(ld.h, 16)
                else:
                    g.dma_start(out=e.tr5[:, :].rearrange("p (k t) -> p k t", k=8),
                                in_=e.partf_a[r - NC + 1][:, t0:t0 + 512].rearrange("(k p) t -> p k t", p=128)).then_inc(ld.h, 16)
        p.g(cld)
        nld0 = ld.inc(80)
        nld = nld0
        nlds = [ld.inc(16) for _ in range(2 * NC - 3)]

        for rr in range(1, 2 * NC - 1):
            def psum_add(v, e, rr=rr, nl=(nld0 if rr == 1 else nlds[rr - 2])):
                v.wait_ge(ld.h, nl)
                dst = e.orr if rr < NC else e.frr
                v.tensor_add(dst[:, :], dst[:, :], e.tr5[:, :])
                v.drain()
                v.engine_nop().then_inc(vV.h, 1)
            p.v(psum_add)
            vV.inc()

        def yb(v, e, nld=nld, nini=nini):
            v.wait_ge(ini.h, nini)
            for k in range(8):
                v.tensor_scalar(e.tmp[:, :], e.orr[:, k * 512:(k + 1) * 512],
                                e.bo_s[:, k:k + 1], None, ALU.add)
                v.drain()
                v.tensor_add(e.yr[:, k * 512:(k + 1) * 512], e.xnr[:, k * 512:(k + 1) * 512],
                             e.tmp[:, :])
                v.drain()
            v.engine_nop().then_inc(vV.h, 1)
        p.v(yb)
        nyb = vV.inc()

        def sqf(s, e, nyb=nyb):
            s.wait_ge(vV.h, nyb)
            for k in range(8):
                a = s.activation(e.sq[:, k * 512:(k + 1) * 512], e.yr[:, k * 512:(k + 1) * 512],
                                 AF.Square)
            a.then_inc(sS.h, 1)
        p.s(sqf)
        nsq = sS.inc()

        def mmf(t, e, c=c, nsq=nsq, nvs=vV.n):
            t.wait_ge(sS.h, nsq)
            if c > 0:
                t.wait_ge(vV.h, nvs)
            for k in range(8):
                t.matmul(e.psA[:, :], e.w1024[:, :], e.yr[:, k * 512:(k + 1) * 512],
                         start=(k == 0), stop=(k == 7))
            for k in range(8):
                mm = t.matmul(e.psB[:, :], e.w1024[:, :], e.sq[:, k * 512:(k + 1) * 512],
                              start=(k == 0), stop=(k == 7))
            mm.then_inc(tP.h, 1)
        p.t(mmf)
        nmm = tP.inc()

        def stv(v, e, nmm=nmm):
            v.wait_ge(tP.h, nmm)
            v.tensor_copy(e.mean[:, :], e.psA[:, :])
            v.drain()
            v.tensor_mul(e.m2[:, :], e.mean[:, :], e.mean[:, :])
            v.drain()
            v.tensor_sub(e.var[:, :], e.psB[:, :], e.m2[:, :])
            v.engine_nop().then_inc(vV.h, 1)
        p.v(stv)
        nvs = vV.inc()

        def sts(s, e, nvs=nvs):
            s.wait_ge(vV.h, nvs)
            s.activation(e.lnv[:, :], e.var[:, :], AF.Ln, bias=e.ceps[:, :])
            s.drain()
            s.activation(e.rstd[:, :], e.lnv[:, :], AF.Exp, scale=-0.5).then_inc(sS.h, 1)
        p.s(sts)
        nss = sS.inc()

        def bcm(t, e, c=c, nss=nss, nxd=vV.n):
            t.wait_ge(sS.h, nss)
            if c > 0:
                t.wait_ge(vV.h, nxd)
            t.matmul(e.pm[:, :], e.ones1[:, :], e.mean[:, :], start=True, stop=True)
            t.matmul(e.pr[:, :], e.ones1[:, :], e.rstd[:, :], start=True, stop=True).then_inc(tP.h, 1)
        p.t(bcm)
        nbc = tP.inc()

        def fino(v, e, c=c, nbc=nbc):
            v.wait_ge(tP.h, nbc)
            for k in range(8):
                v.tensor_sub(e.tmp[:, :], e.yr[:, k * 512:(k + 1) * 512], e.pm[:, :])
                v.drain()
                v.tensor_mul(e.tmp[:, :], e.tmp[:, :], e.pr[:, :])
                v.drain()
                v.tensor_scalar(e.a1r[:, k * 512:(k + 1) * 512], e.tmp[:, :],
                                e.gam_s[:, k:k + 1], e.bet_s[:, k:k + 1], ALU.mult, ALU.add)
                v.drain()
            for k in range(8):
                v.tensor_add(e.tmp[:, :], e.x1r[:, k * 512:(k + 1) * 512],
                             e.a1r[:, k * 512:(k + 1) * 512])
                v.drain()
                v.tensor_add(e.oro[:, k * 512:(k + 1) * 512], e.tmp[:, :],
                             e.frr[:, k * 512:(k + 1) * 512])
                v.drain()
            v.engine_nop().then_inc(vV.h, 1)
        p.v(fino)
        nfo = vV.inc()

        def sto(sy, e, t0=t0, nfo=nfo):
            sy.wait_ge(vV.h, nfo)
            sy.dma_start(out=e.out_o[:, t0:t0 + 512].rearrange("(k p) t -> p k t", p=128),
                         in_=e.oro[:, :].rearrange("p (k t) -> p k t", k=8)).then_inc(st.h, 16)
        p.y(sto)
        st.inc(16)

    def fin(sy, e, n=st.n):
        sy.wait_ge(st.h, n)
    p.y(fin)
    return p


_TOL_T = 16384


def _runs(p, in_maps):
    nc = build_bass(p)
    return run_bass_kernel_spmd(nc, in_maps, core_ids=list(range(NC))).results


def kernel(**w):
    T = T_FULL
    TSL = T // NC
    F = np.float32
    C = np.ascontiguousarray

    x = np.asarray(w["L0_attn_x"])[0]
    xfm = C(x.T)

    gwT = C(np.asarray(w["L0_ffn_gate_w"]).T)
    p0 = make_p0(TSL)
    r0 = _runs(p0, [{"xfm": C(xfm[:, i * TSL:(i + 1) * TSL]), "gwT": gwT}
                    for i in range(NC)])
    xn0f = C(np.concatenate([r["xn0b"] for r in r0], axis=1))
    cnt_all = np.concatenate([r["cnto"] for r in r0], axis=0)

    tril = C(np.tril(np.ones((128, 128), F)).T)
    cap = max(int(np.ceil(T / 8 * 1.25)), 4)
    p0b = make_p0b(TSL, cap)
    r0b = _runs(p0b, [{"lgi": r0[i]["lgo"], "rwi": r0[i]["rwo"], "maski": r0[i]["masko"],
                       "cntall": cnt_all, "pmask": (np.arange(NC) < i).astype(F).reshape(NC, 1),
                       "tril": tril} for i in range(NC)])
    gw_fm = C(np.concatenate([r["gwo"] for r in r0b], axis=0).T)

    ipw = np.asarray(w["L0_attn_in_proj_w"])
    cw = np.asarray(w["L0_attn_conv_w"])[:, 0, :]
    cb = np.asarray(w["L0_attn_conv_b"])
    xpw = np.asarray(w["L0_attn_x_proj_w"])
    p1 = make_p1(T)
    r1 = _runs(p1, [{
        "xn0f": xn0f,
        "wxs": C(ipw[:, 256 * i:256 * (i + 1)]),
        "wz": C(ipw[:, DI + 256 * i:DI + 256 * (i + 1)]),
        "cwc": C(cw[256 * i:256 * (i + 1)].reshape(2, 128, 4).transpose(1, 0, 2).reshape(128, 8)),
        "cbc": C(cb[256 * i:256 * (i + 1)].reshape(2, 128).T),
        "xpw": C(xpw[256 * i:256 * (i + 1)]),
    } for i in range(NC)])
    xpp8 = np.stack([r["xpp_o"] for r in r1])

    A_log = np.asarray(w["L0_attn_A_log"])
    dt_w = np.asarray(w["L0_attn_dt_w"])[0]
    dt_b = np.asarray(w["L0_attn_dt_b"])
    Dv = np.asarray(w["L0_attn_D"])
    selgB, selR, sel64q = p2a_consts()
    p2a = make_p2a(T)
    r2a = _runs(p2a, [{
        "u_i": r1[i]["u_o"], "zs_i": r1[i]["zs_o"], "xpp8": xpp8,
        "A_lg": p2a_A_layout(A_log[256 * i:256 * (i + 1)]),
        "dtw_c": C(dt_w[256 * i:256 * (i + 1)].reshape(2, 128).T),
        "dtb_c": C(dt_b[256 * i:256 * (i + 1)].reshape(2, 128).T),
        "D_c": C(Dv[256 * i:256 * (i + 1)].reshape(2, 128).T),
        "selgB": selgB, "selR": selR, "sel64q": sel64q,
    } for i in range(NC)])

    p2b = make_p2b(T)
    in2b = []
    for i in range(NC):
        wop, wgu, wdn, wsgu, wsdn = p2b_weights(
            np.asarray(w["L0_attn_out_proj_w"]), np.asarray(w["L0_ffn_dyn_gate_w"]),
            np.asarray(w["L0_ffn_dyn_up_w"]), np.asarray(w["L0_ffn_dyn_down_w"]),
            np.asarray(w["L0_ffn_sh_gate_w"]), np.asarray(w["L0_ffn_sh_up_w"]),
            np.asarray(w["L0_ffn_sh_down_w"]), i)
        in2b.append({"ym_i": r2a[i]["ym_o"], "xn0f": xn0f, "gw_fm": gw_fm,
                     "wop": wop, "wgu": wgu, "wdn": wdn, "wsgu": wsgu, "wsdn": wsdn})
    r2b = _runs(p2b, in2b)

    part0 = np.stack([r["part0"] for r in r2b])
    Wk = np.asarray(w["L1_attn_Wk"]); bk = np.asarray(w["L1_attn_bk"])
    Wv = np.asarray(w["L1_attn_Wv"]); bv = np.asarray(w["L1_attn_bv"])
    Wc = np.asarray(w["L1_attn_Wc"])
    Wg1 = np.asarray(w["L1_ffn_Wg"])
    eye = np.eye(128, dtype=F)
    p3 = make_p3(TSL)
    r3 = _runs(p3, [{
        "x0fm": C(xfm[:, i * TSL:(i + 1) * TSL]),
        "part0a": C(part0[:, :, i * TSL:(i + 1) * TSL]),
        "wk": tile_kpm(Wk), "bkc": C(bk.reshape(2, 128).T),
        "wv": tile_kpm(Wv), "bvc": C(bv.reshape(2, 128).T),
        "wc2": tile_kpm(Wc), "wg1": tile_kpm(Wg1), "eye": eye,
    } for i in range(NC)])
    x1f = C(np.concatenate([r["x1_o"] for r in r3], axis=1))
    xn1b = C(np.concatenate([r["xn1b_o"] for r in r3], axis=1))
    xn1f = C(np.concatenate([r["xn1f_o"] for r in r3], axis=1))
    kcf = C(np.concatenate([r["kc_o"] for r in r3], axis=1))
    vcf = C(np.concatenate([r["vc_o"] for r in r3], axis=1))
    scfm = C(np.concatenate([r["scfm_o"] for r in r3], axis=1))

    Wq = np.asarray(w["L1_attn_Wq"]); bq = np.asarray(w["L1_attn_bq"])
    We = np.asarray(w["L1_attn_We"]); Wo = np.asarray(w["L1_attn_Wo"])
    W1 = np.asarray(w["L1_ffn_W1"]); b1 = np.asarray(w["L1_ffn_b1"])
    W2 = np.asarray(w["L1_ffn_W2"]); b2 = np.asarray(w["L1_ffn_b2"])
    sel8 = np.zeros((8, 8 * 128), F)
    for e_ in range(8):
        sel8[e_, e_ * 128:(e_ + 1) * 128] = 1.0
    utri = np.triu(np.ones((128, 128), F))
    p4 = make_p4(T)
    in4 = []
    for i in range(NC):
        h = i // 2
        hsl = slice(256 * i, 256 * (i + 1))
        w1t = np.stack([tile_kpm(W1[e_][:, hsl]) for e_ in range(8)])
        w2t = np.stack([tile_kpm(W2[e_][hsl, :]) for e_ in range(8)])
        in4.append({
            "xn1f": xn1b, "kcf": kcf, "vcf": vcf, "scfm": scfm,
            "wq": tile_kpm(Wq[:, 128 * i:128 * (i + 1)]),
            "bq_c": C(bq[128 * i:128 * (i + 1)].reshape(128, 1)),
            "we_h": C(We[:, 64 * h:64 * (h + 1)]),
            "wo_sl": C(Wo[128 * i:128 * (i + 1), :]),
            "w1_t": C(w1t.reshape(8, 128, 8, 256).transpose(1, 2, 0, 3).reshape(128, -1)),
            "b1_c": C(b1[:, hsl].reshape(8, 2, 128).transpose(2, 0, 1).reshape(128, 16)),
            "w2_t": C(w2t.transpose(1, 0, 2).reshape(128, -1)),
            "b2_f": C(b2),
            "c0v": np.full((8, 1), 1.0 if i == 0 else 0.0, F),
            "sel8": sel8, "utri": utri, "eye": eye,
        })
    r4 = _runs(p4, in4)

    parto = np.stack([r["parto"] for r in r4])
    partf = np.stack([r["partf"] for r in r4])
    bo = np.asarray(w["L1_attn_bo"])
    gam = np.asarray(w["L1_attn_gamma"]); bet = np.asarray(w["L1_attn_beta"])
    p5 = make_p5(TSL)
    r5 = _runs(p5, [{
        "x1fm": C(x1f[:, i * TSL:(i + 1) * TSL]),
        "xn1fm": C(xn1f[:, i * TSL:(i + 1) * TSL]),
        "parto_a": C(parto[:, :, i * TSL:(i + 1) * TSL]),
        "partf_a": C(partf[:, :, i * TSL:(i + 1) * TSL]),
        "bo_c": C(bo.reshape(8, 128).T),
        "gam_c": C(gam.reshape(8, 128).T),
        "bet_c": C(bet.reshape(8, 128).T),
    } for i in range(NC)])
    out = np.concatenate([r["out_o"] for r in r5], axis=1)
    return np.ascontiguousarray(out.T).reshape(1, T, D).astype(np.float32)


# version 09
# speedup vs baseline: 1.0905x; 1.0905x over previous
import numpy as np
import concourse.bass as bass
import concourse.mybir as mybir
from concourse.bass_utils import run_bass_kernel_spmd

F32 = mybir.dt.float32
BF16 = mybir.dt.bfloat16
AF = mybir.ActivationFunctionType
ALU = mybir.AluOpType
AX = mybir.AxisListType

NC = 8
T_FULL = 16384
D = 1024
DI = 2048
DS = 64
NEG = -1e38



class Sem:

    def __init__(self, name):
        self.name = name
        self.n = 0
        self.h = None

    def inc(self, k=1):
        self.n += k
        return self.n


class P:

    def __init__(self):
        self.ops = []
        self.sems = []
        self.sem_by_name = {}
        self.sbufs = []
        self.psums = []
        self.drams_in = []
        self.drams_out = []
        self.drams_int = []
        self.ns = ""
        self.dram_override = {}

    def sem(self, name, shared=False):
        key = name if shared else self.ns + name
        if key in self.sem_by_name:
            return self.sem_by_name[key]
        s = Sem(key)
        self.sems.append(s)
        self.sem_by_name[key] = s
        return s

    def sbuf(self, name, shape, dtype):
        self.sbufs.append((self.ns + name, shape, dtype, self.ns))
        return name

    def psum(self, name, shape, dtype=F32):
        self.psums.append((self.ns + name, shape, dtype, self.ns))
        return name

    def _dram(self, lst, kind, name, shape, dtype):
        ov = self.dram_override.get(self.ns + name)
        if ov is not None:
            return name
        lst.append((self.ns + name, shape, dtype))
        return name

    def din(self, name, shape, dtype=F32):
        return self._dram(self.drams_in, "in", name, shape, dtype)

    def dout(self, name, shape, dtype=F32):
        return self._dram(self.drams_out, "out", name, shape, dtype)

    def dint(self, name, shape, dtype=F32):
        return self._dram(self.drams_int, "int", name, shape, dtype)

    def add(self, tag, fn):
        self.ops.append((tag, fn, self.ns))

    def g(self, fn): self.add("g", fn)
    def v(self, fn): self.add("v", fn)
    def s(self, fn): self.add("s", fn)
    def t(self, fn): self.add("t", fn)
    def y(self, fn): self.add("y", fn)


class Env(dict):
    __getattr__ = dict.__getitem__


class NsEnv:

    def __init__(self, e, ns):
        self._e = e
        self._ns = ns

    def __getitem__(self, k):
        ns = self._ns + k
        if ns in self._e:
            return self._e[ns]
        return self._e[k]

    __getattr__ = __getitem__


def build_bass(p: P, phase_spans=None, nc=None):
    import contextlib
    if nc is None:
        nc = bass.Bass("TRN2", target_bir_lowering=False, debug=False)
    e = Env()
    for name, shape, dt_ in p.drams_in:
        e[name] = nc.declare_dram_parameter(name, list(shape), dt_, isOutput=False).ap()
    for name, shape, dt_ in p.drams_out:
        e[name] = nc.declare_dram_parameter(name, list(shape), dt_, isOutput=True).ap()
    for name, shape, dt_ in p.drams_int:
        e[name] = nc.dram_tensor(name, list(shape), dt_).ap()
    for ov_name, target in p.dram_override.items():
        e[ov_name] = e[target]
    with contextlib.ExitStack() as stack:
        if phase_spans is None:
            for name, shape, dt_, ns in p.sbufs:
                e[name] = stack.enter_context(nc.sbuf_tensor(name, list(shape), dt_))
            for name, shape, dt_, ns in p.psums:
                e[name] = stack.enter_context(nc.psum_tensor(name, list(shape), dt_))
        else:
            by_ns_s = {}
            by_ns_p = {}
            for name, shape, dt_, ns in p.sbufs:
                by_ns_s.setdefault(ns, []).append((name, shape, dt_))
            for name, shape, dt_, ns in p.psums:
                by_ns_p.setdefault(ns, []).append((name, shape, dt_))
            for ns in phase_spans:
                with contextlib.ExitStack() as ph:
                    for name, shape, dt_ in by_ns_s.get(ns, []):
                        e[name] = ph.enter_context(nc.sbuf_tensor(name, list(shape), dt_))
                    for name, shape, dt_ in by_ns_p.get(ns, []):
                        e[name] = ph.enter_context(nc.psum_tensor(name, list(shape), dt_))
        for s in p.sems:
            s.h = stack.enter_context(nc.semaphore(s.name))
        block = stack.enter_context(nc.Block())

        ops = p.ops

        def replay(eng, tag):
            for tg, fn, ns in ops:
                if tg == tag:
                    fn(eng, NsEnv(e, ns) if ns else e)

        @block.gpsimd
        def _(g):
            replay(g, "g")

        @block.vector
        def _(v):
            replay(v, "v")

        @block.scalar
        def _(s):
            replay(s, "s")

        @block.tensor
        def _(t):
            replay(t, "t")

        @block.sync
        def _(sy):
            replay(sy, "y")

    return nc



def make_p0(TSL, p=None, fused=False):
    if p is None:
        p = P()
    NCH = TSL // 512
    NW = TSL // 128

    p.din("xfm", (D, TSL), F32)
    p.din("gwT", (D, 9), F32)
    p.dout("xn0b", (D, TSL), BF16)
    p.dout("lgo", (TSL, 9), F32)
    p.dout("rwo", (TSL, 8), F32)
    p.dout("masko", (TSL, 8), F32)
    p.dout("cnto", (1, 8), F32)

    p.sbuf("xr", (128, 8 * TSL), F32)
    p.sbuf("xn", (128, 8 * TSL), BF16)
    p.sbuf("gw9", (128, 8 * 9), BF16)
    p.sbuf("w1024", (128, 1), F32)
    p.sbuf("onesc", (128, 1), F32)
    p.sbuf("ones1", (1, 128), F32)
    p.sbuf("sq", (128, 8 * 512), F32)
    p.sbuf("mean", (1, 512), F32)
    p.sbuf("m2", (1, 512), F32)
    p.sbuf("var", (1, 512), F32)
    p.sbuf("lnv", (1, 512), F32)
    p.sbuf("rstd", (1, 512), F32)
    p.sbuf("ceps", (1, 1), F32)
    p.sbuf("tmp", (128, 512), F32)
    p.sbuf("lgsb", (128, NW * 9), F32)
    p.sbuf("rwsb", (128, NW * 8), F32)
    p.sbuf("msksb", (128, NW * 8), F32)
    p.sbuf("cntsb", (1, 8), F32)
    for nm, w in [("srt", 8), ("esr", 8), ("es2", 8), ("psl", 8), ("cums", 4),
                  ("asrt", 8),
                  ("vld", 4), ("adyn", 8), ("gtm", 8), ("fac", 8), ("nf", 8),
                  ("dmt", 8), ("jm", 8), ("ex", 8), ("kp", 8), ("ee", 8),
                  ("ek", 8), ("rww", 8), ("oh", 8), ("ctr", 8)]:
        p.sbuf(nm, (128, w), F32)
    for nm in ["esum", "rec", "athr", "esm2", "mr", "mrv", "rs", "rr"]:
        p.sbuf(nm, (128, 1), F32)

    p.psum("psA", (1, 512))
    p.psum("psB", (1, 512))
    p.psum("pm", (128, 512))
    p.psum("pr", (128, 512))
    p.psum("lgps", (128, 9))
    p.psum("cntps", (1, 8))

    ld = p.sem("ld")
    ini = p.sem("ini")
    sqs = p.sem("sqs")
    mmst = p.sem("mmst")
    vstat = p.sem("vstat")
    sstat = p.sem("sstat")
    bcmm = p.sem("bcmm")
    xnd = p.sem("xnd")
    rtmm = p.sem("rtmm")
    rtev = p.sem("rtev")
    vsub = p.sem("vsub")
    ssub = p.sem("ssub")
    vrt = p.sem("vrt")
    tcnt = p.sem("tcnt")
    st = p.sem("st")

    nld = ld.inc(32)

    @p.g
    def _(g, e):
        g.dma_start(out=e.xr[:, :].rearrange("p (k t) -> p k t", k=8),
                    in_=e.xfm.rearrange("(k p) t -> p k t", p=128)).then_inc(ld.h, 16)
        g.dma_start(out=e.gw9[:, :].rearrange("p (k e) -> p k e", k=8),
                    in_=e.gwT.rearrange("(k p) e -> p k e", p=128)).then_inc(ld.h, 16)
        g.memset(e.w1024[:, :], 1.0 / 1024.0)
        g.memset(e.onesc[:, :], 1.0)
        g.memset(e.ones1[:, :], 1.0)
        g.memset(e.ceps[:, :], 1e-5)
        g.engine_nop().then_inc(ini.h, 1)
    nini = ini.inc(1)

    for c in range(NCH):
        cs = slice(c * 512, (c + 1) * 512)

        n_sq = [None] * 8
        for k in range(8):
            def sqf(s, e, k=k, c=c, nw=nld, nini=nini, nmm=mmst.n):
                if k == 0:
                    s.wait_ge(ld.h, nw)
                    s.wait_ge(ini.h, nini)
                    if c > 0:
                        s.wait_ge(mmst.h, nmm)
                s.activation(e.sq[:, k * 512:(k + 1) * 512],
                             e.xr[:, k * TSL + c * 512: k * TSL + (c + 1) * 512],
                             AF.Square).then_inc(sqs.h, 1)
            p.s(sqf)
            n_sq[k] = sqs.inc()

        def mmf(t, e, c=c, n_sq=tuple(n_sq), nw=nld, nini=nini, nvs=vstat.n):
            t.wait_ge(ld.h, nw)
            t.wait_ge(ini.h, nini)
            if c > 0:
                t.wait_ge(vstat.h, nvs)
            for k in range(8):
                t.matmul(e.psA[:, :], e.w1024[:, :],
                         e.xr[:, k * TSL + c * 512: k * TSL + (c + 1) * 512],
                         start=(k == 0), stop=(k == 7))
            for k in range(8):
                t.wait_ge(sqs.h, n_sq[k])
                mm = t.matmul(e.psB[:, :], e.w1024[:, :],
                              e.sq[:, k * 512:(k + 1) * 512],
                              start=(k == 0), stop=(k == 7))
            mm.then_inc(mmst.h, 1)
        p.t(mmf)
        nmm = mmst.inc()

        def stv(v, e, nmm=nmm):
            v.wait_ge(mmst.h, nmm)
            v.tensor_copy(e.mean[:, :], e.psA[:, :])
            v.drain()
            v.tensor_mul(e.m2[:, :], e.mean[:, :], e.mean[:, :])
            v.drain()
            v.tensor_sub(e.var[:, :], e.psB[:, :], e.m2[:, :]).then_inc(vstat.h, 1)
        p.v(stv)
        nvs = vstat.inc()

        def sts(s, e, nvs=nvs):
            s.wait_ge(vstat.h, nvs)
            s.activation(e.lnv[:, :], e.var[:, :], AF.Ln, bias=e.ceps[:, :])
            s.drain()
            s.activation(e.rstd[:, :], e.lnv[:, :], AF.Exp, scale=-0.5).then_inc(sstat.h, 1)
        p.s(sts)
        nss = sstat.inc()

        def bcm(t, e, nss=nss, nxd=xnd.n):
            t.wait_ge(sstat.h, nss)
            if c > 0:
                t.wait_ge(xnd.h, nxd)
            t.matmul(e.pm[:, :], e.ones1[:, :], e.mean[:, :], start=True, stop=True)
            t.matmul(e.pr[:, :], e.ones1[:, :], e.rstd[:, :], start=True, stop=True).then_inc(bcmm.h, 1)
        p.t(bcm)
        nbc = bcmm.inc()

        def xnf(v, e, c=c, nbc=nbc):
            v.wait_ge(bcmm.h, nbc)
            for k in range(8):
                v.tensor_sub(e.tmp[:, :], e.xr[:, k * TSL + c * 512: k * TSL + (c + 1) * 512], e.pm[:, :])
                v.drain()
                v.tensor_mul(e.xn[:, k * TSL + c * 512: k * TSL + (c + 1) * 512], e.tmp[:, :], e.pr[:, :])
                v.drain()
            v.engine_nop().then_inc(xnd.h, 1)
        p.v(xnf)
        xnd.inc()

    for w in range(NW):
        c = (w * 128) // 512
        ws = slice(w * 128, (w + 1) * 128)

        def rmm(t, e, w=w, nxd=c + 1, nrtev=rtev.n):
            t.wait_ge(xnd.h, nxd)
            if w > 0:
                t.wait_ge(rtev.h, nrtev)
            for k in range(8):
                mm = t.matmul(e.lgps[:, :], e.xn[:, k * TSL + w * 128: k * TSL + (w + 1) * 128],
                              e.gw9[:, k * 9:(k + 1) * 9], start=(k == 0), stop=(k == 7))
            mm.then_inc(rtmm.h, 1)
        p.t(rmm)
        nrm = rtmm.inc()

        def rev(s, e, w=w, nrm=nrm, nvr=vrt.n):
            s.wait_ge(rtmm.h, nrm)
            if w > 0:
                s.wait_ge(vrt.h, nvr)
            s.copy(e.lgsb[:, w * 9:(w + 1) * 9], e.lgps[:, :]).then_inc(rtev.h, 1)
        p.s(rev)
        nre = rtev.inc()

        def rv1(v, e, w=w, nre=nre):
            dyn = e.lgsb[:, w * 9: w * 9 + 8]
            v.wait_ge(rtev.h, nre)
            v.max(e.srt[:, 0:8], dyn)
            v.drain()
            v.tensor_scalar(e.esr[:, :], e.srt[:, :], e.srt[:, 0:1], None, ALU.subtract).then_inc(vsub.h, 1)
        p.v(rv1)
        nv1 = vsub.inc()

        def rs1(s, e, w=w, nv1=nv1):
            s.wait_ge(vsub.h, nv1)
            s.activation(e.asrt[:, :], e.srt[:, :], AF.Abs)
            s.activation(e.adyn[:, :], e.lgsb[:, w * 9: w * 9 + 8], AF.Abs)
            s.drain()
            s.activation(e.ee[:, :], e.esr[:, :], AF.Exp).then_inc(ssub.h, 1)
        p.s(rs1)
        ns1 = ssub.inc()

        def rv2(v, e, w=w, ns1=ns1):
            dyn = e.lgsb[:, w * 9: w * 9 + 8]
            v.wait_ge(ssub.h, ns1)
            v.reduce_sum(e.esum[:, :], e.ee[:, :], AX.X)
            v.drain()
            v.reciprocal(e.rec[:, :], e.esum[:, :])
            v.drain()
            v.tensor_scalar(e.psl[:, :], e.ee[:, :], e.rec[:, 0:1], None, ALU.mult)
            v.drain()
            v.tensor_add(e.cums[:, 0:1], e.psl[:, 0:1], e.psl[:, 1:2])
            v.drain()
            v.tensor_add(e.cums[:, 1:2], e.cums[:, 0:1], e.psl[:, 2:3])
            v.drain()
            v.tensor_single_scalar(e.vld[:, 1:2], e.psl[:, 0:1], 0.75, ALU.is_lt)
            v.tensor_single_scalar(e.vld[:, 2:3], e.cums[:, 0:1], 0.75, ALU.is_lt)
            v.tensor_single_scalar(e.vld[:, 3:4], e.cums[:, 1:2], 0.75, ALU.is_lt)
            v.memset(e.rww[:, :], 0.0)
            v.engine_nop().then_inc(vsub.h, 1)
        p.v(rv2)
        nv2 = vsub.inc()

        for r in range(4):
            def rva(v, e, w=w, r=r, nprev=(nv2 if r == 0 else ssub.n)):
                dyn = e.lgsb[:, w * 9: w * 9 + 8]
                thr = e.srt[:, r:r + 1]
                if r == 0:
                    v.wait_ge(vsub.h, nprev)
                v.tensor_scalar(e.gtm[:, :], dyn, thr, None, ALU.is_gt)
                v.tensor_scalar(e.fac[:, :], e.adyn[:, :], e.asrt[:, r:r + 1], None, ALU.max)
                v.drain()
                v.tensor_scalar(e.nf[:, :], e.fac[:, :], -0.2, None, ALU.mult)
                v.tensor_scalar(e.dmt[:, :], dyn, thr, None, ALU.subtract)
                v.drain()
                v.tensor_tensor(e.jm[:, :], e.dmt[:, :], e.nf[:, :], ALU.is_lt)
                v.drain()
                v.tensor_tensor(e.ex[:, :], e.gtm[:, :], e.jm[:, :], ALU.max)
                v.drain()
                v.tensor_scalar(e.kp[:, :], e.ex[:, :], -1.0, 1.0, ALU.mult, ALU.add)
                v.engine_nop().then_inc(vsub.h, 1)
            p.v(rva)
            nva = vsub.inc()

            def rsa(s, e, nva=nva):
                s.wait_ge(vsub.h, nva)
                s.activation(e.ee[:, :], e.dmt[:, :], AF.Exp).then_inc(ssub.h, 1)
            p.s(rsa)
            nsa = ssub.inc()

            def rvb(v, e, w=w, r=r, nsa=nsa):
                dyn = e.lgsb[:, w * 9: w * 9 + 8]
                thr = e.srt[:, r:r + 1]
                v.wait_ge(ssub.h, nsa)
                v.tensor_tensor(e.ek[:, :], e.ee[:, :], e.kp[:, :], ALU.mult)
                v.drain()
                v.reduce_sum(e.esm2[:, :], e.ek[:, :], AX.X)
                v.drain()
                v.reciprocal(e.mr[:, :], e.esm2[:, :])
                v.drain()
                if r > 0:
                    v.tensor_mul(e.mrv[:, :], e.mr[:, :], e.vld[:, r:r + 1])
                else:
                    v.tensor_copy(e.mrv[:, :], e.mr[:, :])
                v.tensor_scalar(e.oh[:, :], dyn, thr, None, ALU.is_equal)
                v.drain()
                v.tensor_scalar(e.ctr[:, :], e.oh[:, :], e.mrv[:, 0:1], None, ALU.mult)
                v.drain()
                v.tensor_add(e.rww[:, :], e.rww[:, :], e.ctr[:, :])
                v.drain()
                v.engine_nop().then_inc(vsub.h, 1)
            p.v(rvb)
            vsub.inc()

        def rvf(v, e, w=w):
            v.reduce_sum(e.rs[:, :], e.rww[:, :], AX.X)
            v.drain()
            v.reciprocal(e.rr[:, :], e.rs[:, :])
            v.drain()
            v.tensor_scalar(e.rwsb[:, w * 8:(w + 1) * 8], e.rww[:, :], e.rr[:, 0:1], None, ALU.mult)
            v.drain()
            v.tensor_single_scalar(e.msksb[:, w * 8:(w + 1) * 8],
                                   e.rwsb[:, w * 8:(w + 1) * 8], 0.0, ALU.is_gt)
            v.engine_nop().then_inc(vrt.h, 1)
        p.v(rvf)
        nvrt = vrt.inc()

        def cmm(t, e, w=w, nvrt=nvrt, NW=NW):
            t.wait_ge(vrt.h, nvrt)
            mm = t.matmul(e.cntps[:, :], e.onesc[:, :], e.msksb[:, w * 8:(w + 1) * 8],
                          start=(w == 0), stop=(w == NW - 1))
            if w == NW - 1:
                mm.then_inc(tcnt.h, 1)
        p.t(cmm)
    tcnt.inc()

    def cev(s, e, n=tcnt.n):
        s.wait_ge(tcnt.h, n)
        s.copy(e.cntsb[:, :], e.cntps[:, :]).then_inc(sstat.h, 1)
    p.s(cev)
    ncev = sstat.inc()

    def stf(sy, e, NW=NW, NCH=NCH, ncev=ncev, nvrt=vrt.n, nxd=xnd.n):
        sy.wait_ge(xnd.h, nxd)
        sy.dma_start(out=e.xn0b.rearrange("(k p) t -> p k t", p=128),
                     in_=e.xn[:, :].rearrange("p (k t) -> p k t", k=8)).then_inc(st.h, 16)
        sy.wait_ge(vrt.h, nvrt)
        sy.dma_start(out=e.lgo.rearrange("(w p) e -> p w e", p=128),
                     in_=e.lgsb[:, :].rearrange("p (w e) -> p w e", e=9)).then_inc(st.h, 16)
        sy.dma_start(out=e.rwo.rearrange("(w p) e -> p w e", p=128),
                     in_=e.rwsb[:, :].rearrange("p (w e) -> p w e", e=8)).then_inc(st.h, 16)
        sy.dma_start(out=e.masko.rearrange("(w p) e -> p w e", p=128),
                     in_=e.msksb[:, :].rearrange("p (w e) -> p w e", e=8)).then_inc(st.h, 16)
        sy.wait_ge(sstat.h, ncev)
        sy.dma_start(out=e.cnto, in_=e.cntsb[:, :]).then_inc(st.h, 16)
        sy.wait_ge(st.h, nfin)
    nfin = st.inc(80)
    p.y(stf)

    return p



def make_p0b(TSL, CAP, p=None, fused=False):
    if p is None:
        p = P()
    NW = TSL // 128
    p.din("lgi", (TSL, 9), F32)
    p.din("rwi", (TSL, 8), F32)
    p.din("maski", (TSL, 8), F32)
    p.din("cntall", (NC, 8), F32)
    p.din("pmask", (NC, 1), F32)
    p.din("tril", (128, 128), F32)
    p.din("eyeb", (128, 128), F32)
    p.dout("gwfm_o", (9, TSL), BF16)

    p.sbuf("lgs", (128, NW * 9), F32)
    p.sbuf("rws", (128, NW * 8), F32)
    p.sbuf("msks", (128, NW * 8), F32)
    p.sbuf("cnts", (NC, 8), F32)
    p.sbuf("pms", (NC, 1), F32)
    p.sbuf("trils", (128, 128), F32)
    p.sbuf("ones1", (1, 128), F32)
    p.sbuf("offs", (1, 8), F32)
    p.sbuf("wcnt", (1, 8), F32)
    p.sbuf("capm", (128, 8), F32)
    p.sbuf("pen", (128, 8), F32)
    p.sbuf("lgm", (128, 9), F32)
    p.sbuf("mx", (128, 1), F32)
    p.sbuf("lsh", (128, 9), F32)
    p.sbuf("e9", (128, 9), F32)
    p.sbuf("s9", (128, 1), F32)
    p.sbuf("r9", (128, 1), F32)
    p.sbuf("gw_", (128, 9), F32)
    p.sbuf("sdyn", (128, 1), F32)
    p.sbuf("gd", (128, 8), F32)
    p.sbuf("gws", (128, NW * 9), F32)
    p.sbuf("eyeb_s", (128, 128), BF16)
    p.sbuf("gwb", (128, 9), BF16)
    p.sbuf("gfm", (9, NW * 128), BF16)

    p.psum("ptr9", (9, 128), BF16)
    p.psum("offps", (1, 8))
    p.psum("cumps", (128, 8))
    p.psum("wcps", (1, 8))

    ld = p.sem("ld")
    ini = p.sem("ini")
    mm1 = p.sem("mm1")
    vv = p.sem("vv")
    ss = p.sem("ss")
    tt = p.sem("tt")
    st = p.sem("st")

    @p.g
    def _(g, e):
        g.dma_start(out=e.lgs[:, :].rearrange("p (w e) -> p w e", e=9),
                    in_=e.lgi.rearrange("(w p) e -> p w e", p=128)).then_inc(ld.h, 16)
        g.dma_start(out=e.rws[:, :].rearrange("p (w e) -> p w e", e=8),
                    in_=e.rwi.rearrange("(w p) e -> p w e", p=128)).then_inc(ld.h, 16)
        g.dma_start(out=e.msks[:, :].rearrange("p (w e) -> p w e", e=8),
                    in_=e.maski.rearrange("(w p) e -> p w e", p=128)).then_inc(ld.h, 16)
        g.dma_start(out=e.cnts[:, :], in_=e.cntall).then_inc(ld.h, 16)
        g.dma_start(out=e.pms[:, :], in_=e.pmask).then_inc(ld.h, 16)
        g.dma_start(out=e.trils[:, :], in_=e.tril).then_inc(ld.h, 16)
        g.dma_start(out=e.eyeb_s[:, :], in_=e.eyeb).then_inc(ld.h, 16)
        g.memset(e.ones1[:, :], 1.0)
        g.engine_nop().then_inc(ini.h, 1)
    nld = ld.inc(112)
    nini = ini.inc(1)

    def omm(t, e, nld=nld, nini=nini):
        t.wait_ge(ld.h, nld)
        t.wait_ge(ini.h, nini)
        t.matmul(e.offps[:, :], e.pms[:, :], e.cnts[:, :], start=True, stop=True).then_inc(mm1.h, 1)
    p.t(omm)
    nomm = mm1.inc()

    def oev(v, e, nomm=nomm):
        v.wait_ge(mm1.h, nomm)
        v.tensor_copy(e.offs[:, :], e.offps[:, :]).then_inc(vv.h, 1)
    p.v(oev)
    vv.inc()

    for w in range(NW):
        def wmm(t, e, w=w, nvv=vv.n):
            t.wait_ge(vv.h, nvv)
            t.matmul(e.cumps[:, :], e.trils[:, :], e.msks[:, w * 8:(w + 1) * 8],
                     start=True, stop=False)
            t.matmul(e.cumps[:, :], e.ones1[:, :], e.offs[:, :], start=False, stop=True)
            t.matmul(e.wcps[:, :], e.trils[:, 127:128],
                     e.msks[:, w * 8:(w + 1) * 8], start=True, stop=True).then_inc(tt.h, 1)
        p.t(wmm)
        ntt = tt.inc()

        def wv1(v, e, w=w, ntt=ntt, CAP=CAP):
            v.wait_ge(tt.h, ntt)
            v.tensor_single_scalar(e.capm[:, :], e.cumps[:, :], float(CAP), ALU.is_le)
            v.tensor_copy(e.wcnt[:, :], e.wcps[:, :])
            v.drain()
            v.tensor_mul(e.capm[:, :], e.capm[:, :], e.msks[:, w * 8:(w + 1) * 8])
            v.tensor_add(e.offs[:, :], e.offs[:, :], e.wcnt[:, :])
            v.drain()
            v.tensor_scalar(e.pen[:, :], e.capm[:, :], 1e38, -1e38, ALU.mult, ALU.add)
            v.tensor_copy(e.lgm[:, :], e.lgs[:, w * 9:(w + 1) * 9])
            v.drain()
            v.tensor_add(e.lgm[:, 0:8], e.lgm[:, 0:8], e.pen[:, :])
            v.drain()
            v.reduce_max(e.mx[:, :], e.lgm[:, :], AX.X)
            v.drain()
            v.tensor_scalar(e.lsh[:, :], e.lgm[:, :], e.mx[:, 0:1], None, ALU.subtract)
            v.engine_nop().then_inc(vv.h, 1)
        p.v(wv1)
        nv1 = vv.inc()

        def ws1(s, e, nv1=nv1):
            s.wait_ge(vv.h, nv1)
            s.activation(e.e9[:, :], e.lsh[:, :], AF.Exp).then_inc(ss.h, 1)
        p.s(ws1)
        ns1 = ss.inc()

        def wv2(v, e, w=w, ns1=ns1):
            v.wait_ge(ss.h, ns1)
            v.reduce_sum(e.s9[:, :], e.e9[:, :], AX.X)
            v.drain()
            v.reciprocal(e.r9[:, :], e.s9[:, :])
            v.drain()
            v.tensor_scalar(e.gw_[:, :], e.e9[:, :], e.r9[:, 0:1], None, ALU.mult)
            v.drain()
            v.reduce_sum(e.sdyn[:, :], e.gw_[:, 0:8], AX.X)
            v.drain()
            v.tensor_scalar(e.gd[:, :], e.rws[:, w * 8:(w + 1) * 8], e.sdyn[:, 0:1], None, ALU.mult)
            v.drain()
            v.tensor_mul(e.gws[:, w * 9: w * 9 + 8], e.gd[:, :], e.capm[:, :])
            v.tensor_copy(e.gws[:, w * 9 + 8: w * 9 + 9], e.gw_[:, 8:9])
            v.drain()
            v.tensor_copy(e.gwb[:, :], e.gws[:, w * 9:(w + 1) * 9])
            v.engine_nop().then_inc(vv.h, 1)
        p.v(wv2)
        nwv2 = vv.inc()

        def wtr(t, e, w=w, nwv2=nwv2, nprev=ss.n):
            t.wait_ge(vv.h, nwv2)
            if w > 0:
                t.wait_ge(ss.h, nprev)
            t.transpose(e.ptr9[:, :], e.gwb[:, :], e.eyeb_s[:, :]).then_inc(tt.h, 1)
        p.t(wtr)
        nwtr = tt.inc()

        def wtre(s, e, w=w, nwtr=nwtr):
            s.wait_ge(tt.h, nwtr)
            s.copy(e.gfm[:, w * 128:(w + 1) * 128], e.ptr9[:, :]).then_inc(ss.h, 1)
        p.s(wtre)
        ss.inc()

    def stf(sy, e, nvv=vv.n, nss=ss.n):
        sy.wait_ge(ss.h, nss)
        sy.dma_start(out=e.gwfm_o.rearrange("a (w t) -> a w t", t=128),
                     in_=e.gfm[:, :].rearrange("a (w t) -> a w t", t=128)).then_inc(st.h, 16)
        sy.wait_ge(st.h, nfin)
    nfin = st.inc(16)
    p.y(stf)

    return p




def make_p1(T, p=None, fused=False):
    if p is None:
        p = P()
    NCH = T // 512
    TSL = T // NC if T >= NC * 512 else T
    NSL = T // TSL

    p.din("xn0f", (NSL, D, TSL), BF16)
    p.din("wxs", (D, 256), F32)
    p.din("wz", (D, 256), F32)
    p.din("cwc", (128, 2 * 4), F32)
    p.din("cbc", (128, 2), F32)
    p.din("xpw", (256, 129), F32)
    p.dout("u_o", (256, T), BF16)
    p.dout("zs_o", (256, T), BF16)
    p.dout("xpp_o", (129, T), F32)

    p.sbuf("wxs_s", (128, 8 * 256), BF16)
    p.sbuf("cw_s", (128, 8), F32)
    p.sbuf("cb_s", (128, 2), F32)
    p.sbuf("wz_s", (128, 8 * 256), BF16)
    p.sbuf("xpj_s", (128, 2 * 129), BF16)
    p.sbuf("xnc", (128, 8 * 512), BF16)
    p.sbuf("xs0", (128, 515), F32)
    p.sbuf("xs1", (128, 515), F32)
    p.sbuf("xc", (128, 512), F32)
    p.sbuf("u0", (128, 512), BF16)
    p.sbuf("u1", (128, 512), BF16)
    p.sbuf("z0", (128, 512), BF16)
    p.sbuf("z1", (128, 512), BF16)
    p.sbuf("xpa", (128, 512), F32)
    p.sbuf("xpb", (1, 512), F32)

    p.psum("pxs0", (128, 512))
    p.psum("pxs1", (128, 512))
    p.psum("pz0", (128, 512))
    p.psum("pz1", (128, 512))
    p.psum("pxp0", (128, 512))
    p.psum("pxp1", (1, 512))

    ld = p.sem("ld")
    ini = p.sem("ini")
    mmxz = p.sem("mmxz")
    sev = p.sem("sev")
    cvv = p.sem("cvv")
    mmxp = p.sem("mmxp")
    xpe = p.sem("xpe")
    st = p.sem("st")

    @p.g
    def _(g, e):
        g.dma_start(out=e.wxs_s[:, :].rearrange("p (k m) -> p k m", k=8),
                    in_=e.wxs.rearrange("(k p) m -> p k m", p=128)).then_inc(ld.h, 16)
        g.dma_start(out=e.wz_s[:, :].rearrange("p (k m) -> p k m", k=8),
                    in_=e.wz.rearrange("(k p) m -> p k m", p=128)).then_inc(ld.h, 16)
        g.dma_start(out=e.xpj_s[:, :].rearrange("p (k j) -> p k j", k=2),
                    in_=e.xpw.rearrange("(k p) j -> p k j", p=128)).then_inc(ld.h, 16)
        g.dma_start(out=e.cw_s[:, :], in_=e.cwc).then_inc(ld.h, 16)
        g.dma_start(out=e.cb_s[:, :], in_=e.cbc).then_inc(ld.h, 16)
        g.memset(e.xs0[:, 0:3], 0.0)
        g.memset(e.xs1[:, 0:3], 0.0)
        g.engine_nop().then_inc(ini.h, 1)
    nwld = ld.inc(80)
    nini = ini.inc()

    for c in range(NCH):
        t0 = c * 512

        def gld(g, e, c=c, t0=t0, nmm=mmxz.n, TSL=TSL):
            if c > 0:
                g.wait_ge(mmxz.h, nmm)
            g.dma_start(out=e.xnc[:, :].rearrange("p (k t) -> p k t", k=8),
                        in_=e.xn0f.rearrange("s d t -> (s d) t")[
                            (t0 // TSL) * D:(t0 // TSL) * D + D, t0 % TSL:t0 % TSL + 512]
                        .rearrange("(k p) t -> p k t", p=128)).then_inc(ld.h, 16)
        p.g(gld)
        nld = ld.inc(16)

        def mf(t, e, c=c, nld=nld, nwld=nwld, nsev=sev.n, nini=nini):
            t.wait_ge(ld.h, nld)
            if c == 0:
                t.wait_ge(ini.h, nini)
            if c > 0:
                t.wait_ge(sev.h, nsev)
            for m in range(2):
                ps = e.pxs0 if m == 0 else e.pxs1
                for k in range(8):
                    mm = t.matmul(ps[:, :], e.wxs_s[:, k * 256 + m * 128: k * 256 + (m + 1) * 128],
                                  e.xnc[:, k * 512:(k + 1) * 512], start=(k == 0), stop=(k == 7))
            for m in range(2):
                ps = e.pz0 if m == 0 else e.pz1
                for k in range(8):
                    mm = t.matmul(ps[:, :], e.wz_s[:, k * 256 + m * 128: k * 256 + (m + 1) * 128],
                                  e.xnc[:, k * 512:(k + 1) * 512], start=(k == 0), stop=(k == 7))
            mm.then_inc(mmxz.h, 1)
        p.t(mf)
        nmm = mmxz.inc()

        def sevac(s, e, c=c, nmm=nmm, ncv=cvv.n, nxp=mmxp.n, nst=st.n):
            s.wait_ge(mmxz.h, nmm)
            if c > 0:
                s.wait_ge(cvv.h, ncv)
                s.wait_ge(mmxp.h, nxp)
                s.wait_ge(st.h, nst)
            s.copy(e.xs0[:, 3:515], e.pxs0[:, :])
            s.copy(e.xs1[:, 3:515], e.pxs1[:, :])
            s.copy(e.z0[:, :], e.pz0[:, :])
            s.drain()
            s.activation(e.z0[:, :], e.pz0[:, :], AF.Silu)
            s.activation(e.z1[:, :], e.pz1[:, :], AF.Silu).then_inc(sev.h, 1)
        p.s(sevac)
        nse = sev.inc()

        for j in range(2):
            def cvf(v, e, j=j, nse=nse, nse2=sev.n):
                xs = e.xs0 if j == 0 else e.xs1
                if j == 0:
                    v.wait_ge(sev.h, nse)
                else:
                    v.wait_ge(sev.h, nse2)
                v.tensor_scalar(e.xc[:, :], xs[:, 0:512], e.cw_s[:, j * 4: j * 4 + 1], None, ALU.mult)
                v.drain()
                for k in range(1, 4):
                    v.scalar_tensor_tensor(e.xc[:, :], xs[:, k:k + 512],
                                           e.cw_s[:, j * 4 + k: j * 4 + k + 1], e.xc[:, :],
                                           ALU.mult, ALU.add)
                    v.drain()
                v.engine_nop().then_inc(cvv.h, 1)
            p.v(cvf)
            ncv = cvv.inc()

            def suf(s, e, j=j, ncv=ncv):
                s.wait_ge(cvv.h, ncv)
                u = e.u0 if j == 0 else e.u1
                s.activation(u[:, :], e.xc[:, :], AF.Silu, bias=e.cb_s[:, j:j + 1]).then_inc(sev.h, 1)
            p.s(suf)
            sev.inc()

            def chal(v, e, j=j):
                xs = e.xs0 if j == 0 else e.xs1
                v.tensor_copy(xs[:, 0:3], xs[:, 512:515])
                v.engine_nop().then_inc(cvv.h, 1)
            p.v(chal)
            cvv.inc()

        def xpf(t, e, c=c, nse=sev.n, nxpe=xpe.n):
            t.wait_ge(sev.h, nse)
            if c > 0:
                t.wait_ge(xpe.h, nxpe)
            for kt in range(2):
                u = e.u0 if kt == 0 else e.u1
                t.matmul(e.pxp0[:, :], e.xpj_s[:, kt * 129: kt * 129 + 128], u[:, :],
                         start=(kt == 0), stop=(kt == 1))
            for kt in range(2):
                u = e.u0 if kt == 0 else e.u1
                mm = t.matmul(e.pxp1[:, :], e.xpj_s[:, kt * 129 + 128: kt * 129 + 129], u[:, :],
                              start=(kt == 0), stop=(kt == 1))
            mm.then_inc(mmxp.h, 1)
        p.t(xpf)
        nxp = mmxp.inc()

        def xpev(v, e, nxp=nxp, nst=st.n):
            v.wait_ge(mmxp.h, nxp)
            if nst > 0:
                v.wait_ge(st.h, nst)
            v.tensor_copy(e.xpa[:, :], e.pxp0[:, :])
            v.tensor_copy(e.xpb[:, :], e.pxp1[:, :])
            v.engine_nop().then_inc(xpe.h, 1)
        p.v(xpev)
        nxpe = xpe.inc()

        def stf(sy, e, c=c, t0=t0, nxpe=nxpe, nse=sev.n, nst0=st.n):
            sy.wait_ge(sev.h, nse)
            sy.dma_start(out=e.u_o[0:128, t0:t0 + 512], in_=e.u0[:, :]).then_inc(st.h, 16)
            sy.dma_start(out=e.u_o[128:256, t0:t0 + 512], in_=e.u1[:, :]).then_inc(st.h, 16)
            sy.dma_start(out=e.zs_o[0:128, t0:t0 + 512], in_=e.z0[:, :]).then_inc(st.h, 16)
            sy.dma_start(out=e.zs_o[128:256, t0:t0 + 512], in_=e.z1[:, :]).then_inc(st.h, 16)
            sy.wait_ge(xpe.h, nxpe)
            sy.dma_start(out=e.xpp_o[0:128, t0:t0 + 512], in_=e.xpa[:, :]).then_inc(st.h, 16)
            sy.dma_start(out=e.xpp_o[128:129, t0:t0 + 512], in_=e.xpb[:, :]).then_inc(st.h, 16)
        p.y(stf)
        st.inc(96)

    def fin(sy, e, n=st.n):
        sy.wait_ge(st.h, n)
    p.y(fin)
    return p



def make_p2a(T, p=None, fused=False, moe=False):
    if p is None:
        p = P()
    TSL = T // NC
    CH = min(1024, T)
    NK = T // CH
    NW5 = CH // 512
    H = CH // 2

    p.din("u_i", (256, T), BF16)
    p.din("zs_i", (256, T), BF16)
    if fused:
        p.din("xpr", (129, T), F32)
    else:
        p.din("xpp8", (NC, 129, T), F32)
    p.din("A_lg", (128, 128), F32)
    p.din("dtw_c", (128, 2), F32)
    p.din("dtb_c", (128, 2), F32)
    p.din("D_c", (128, 2), F32)
    p.din("selgB", (128, 16 * 128), F32)
    p.din("selR", (128, 16 * 128), F32)
    p.din("sel64q", (64, 4 * 128), F32)
    p.dout("ym_o", (256, T), BF16)

    p.sbuf("selgB_s", (128, 16 * 128), BF16)
    p.sbuf("selR_s", (128, 16 * 128), BF16)
    p.sbuf("sel64q_s", (64, 4 * 128), BF16)
    p.sbuf("A_s", (128, 128), F32)
    p.sbuf("Aex", (128, 128), F32)
    p.sbuf("dtw_s", (128, 2), F32)
    p.sbuf("dtb_s", (128, 2), F32)
    p.sbuf("D_s", (128, 2), F32)
    p.sbuf("ones1", (1, 128), F32)
    p.sbuf("cone", (128, 1), F32)
    p.sbuf("carr", (128, 128), F32)
    p.sbuf("Bc", (64, CH), BF16)
    p.sbuf("Cc", (64, CH), BF16)
    if not fused:
        p.sbuf("xq", (128, CH), F32)
        p.sbuf("xq2", (128, CH), F32)
    p.sbuf("dr2", (8, CH), F32)
    p.sbuf("ones8", (8, 128), F32)
    for q in range(4):
        p.sbuf("Bb%d" % q, (128, CH), BF16)
        p.sbuf("Cb%d" % q, (128, CH), BF16)
    p.sbuf("pre", (128, CH), F32)
    p.sbuf("sln", (128, CH), F32)
    p.sbuf("srl", (128, CH), F32)
    for j in range(2):
        p.sbuf("dt%d" % j, (128, CH), BF16)
        p.sbuf("dtu%d" % j, (128, CH), BF16)
        p.sbuf("uc%d" % j, (128, CH), BF16)
        p.sbuf("zc%d" % j, (128, CH), BF16)
    p.sbuf("dub", (128, CH), BF16)
    for b in range(2):
        p.sbuf("dA%d" % b, (128, CH), F32)
        p.sbuf("bq%d" % b, (128, CH), BF16)
        p.sbuf("state%d" % b, (128, CH), F32)
        p.sbuf("cs%d" % b, (128, CH), BF16)
    p.sbuf("du", (128, CH), F32)
    p.sbuf("yf", (128, CH), F32)
    p.sbuf("ym", (128, CH), BF16)

    p.psum("pp1", (128, CH))
    p.psum("pp2", (128, CH))

    if moe:
        NCH = T // 512
        p.din("ym_i", (256, T), BF16)
        p.din("xn0f", (NC, D, TSL), BF16)
        p.din("gw_fm", (NC, 9, TSL), BF16)
        p.din("wop", (128, 2 * D), F32)
        p.din("wgu", (128, 8 * 8 * 128), F32)
        p.din("wdn", (64, 8 * D), F32)
        p.din("wsgu", (128, 8 * 512), F32)
        p.din("wsdn", (128, 2 * D), F32)
        p.dout("part0", (NC, D, TSL), F32)
        p.sbuf("wop_s", (128, 2 * D), BF16)
        p.sbuf("wgu_s", (128, 8 * 8 * 128), BF16)
        p.sbuf("wdn_s", (64, 8 * D), BF16)
        p.sbuf("wsgu_s", (128, 8 * 512), BF16)
        p.sbuf("wsdn_s", (128, 2 * D), BF16)
        p.sbuf("xnc", (128, 8 * 512), BF16)
        p.sbuf("ymc", (128, 2 * 512), BF16)
        p.sbuf("gA", (128, 512), BF16)
        p.sbuf("gB", (128, 512), BF16)
        p.sbuf("gC", (128, 512), BF16)
        p.sbuf("gs64", (64, 512), F32)
        p.sbuf("hwu", (64, 512), F32)
        p.sbuf("gs128", (128, 512), F32)
        p.sbuf("hwu2", (128, 512), F32)
        p.sbuf("hsb", (64, 8 * 512), BF16)
        p.sbuf("hsh", (128, 2 * 512), BF16)
        p.sbuf("po", (128, 4 * 512), F32)
        p.sbuf("onesw", (128, 128), BF16)
        p.psum("ph", (128, 512))
        p.psum("ph2", (128, 512))
        p.psum("pwb", (128, 512))
        p.psum("pout", (128, 512))
        ld2 = p.sem("ld2")
        tP2 = p.sem("tP2")
        sS2 = p.sem("sS2")
        vV2 = p.sem("vV2")
        st2 = p.sem("st2")

    ld = p.sem("ld")
    ini = p.sem("ini")
    ac = p.sem("ac")
    tP = p.sem("tP")
    sS = p.sem("sS")
    vV = p.sem("vV")
    vS = p.sem("vS")
    vC = p.sem("vC")
    st = p.sem("st")
    dAb = ["dA0", "dA1"]
    bqb = ["bq0", "bq1"]
    stb = ["state0", "state1"]
    csb = ["cs0", "cs1"]
    nst_prev = None

    @p.g
    def _(g, e):
        g.dma_start(out=e.selgB_s[:, :], in_=e.selgB).then_inc(ld.h, 16)
        g.dma_start(out=e.selR_s[:, :], in_=e.selR).then_inc(ld.h, 16)
        g.dma_start(out=e.sel64q_s[:, :], in_=e.sel64q).then_inc(ld.h, 16)
        g.dma_start(out=e.A_s[:, :], in_=e.A_lg).then_inc(ld.h, 16)
        g.dma_start(out=e.dtw_s[:, :], in_=e.dtw_c).then_inc(ld.h, 16)
        g.dma_start(out=e.dtb_s[:, :], in_=e.dtb_c).then_inc(ld.h, 16)
        g.dma_start(out=e.D_s[:, :], in_=e.D_c).then_inc(ld.h, 16)
        g.memset(e.ones1[:, :], 1.0)
        g.memset(e.ones8[:, :], 1.0)
        g.memset(e.cone[:, :], 1.0)
        g.memset(e.carr[:, :], 0.0)
        g.engine_nop().then_inc(ini.h, 1)
    nwld = ld.inc(112)
    nini = ini.inc()

    def aexp(s, e, nwld=nwld):
        s.wait_ge(ld.h, nwld)
        s.activation(e.Aex[:, :], e.A_s[:, :], AF.Exp).then_inc(sS.h, 1)
    p.s(aexp)
    naex = sS.inc()

    def aneg(v, e, naex=naex):
        v.wait_ge(sS.h, naex)
        v.tensor_scalar(e.A_s[:, :], e.Aex[:, :], -1.0, None, ALU.mult)
        v.engine_nop().then_inc(vV.h, 1)
    p.v(aneg)
    nang = vV.inc()

    if moe:
        @p.g
        def _(g, e):
            g.dma_start(out=e.wop_s[:, :], in_=e.wop).then_inc(ld2.h, 16)
            g.dma_start(out=e.wgu_s[:, :], in_=e.wgu).then_inc(ld2.h, 16)
            g.dma_start(out=e.wdn_s[:, :], in_=e.wdn).then_inc(ld2.h, 16)
            g.dma_start(out=e.wsgu_s[:, :], in_=e.wsgu).then_inc(ld2.h, 16)
            g.dma_start(out=e.wsdn_s[:, :], in_=e.wsdn).then_inc(ld2.h, 16)
            g.memset(e.onesw[:, :], 1.0)
        nwld2 = ld2.inc(80)

        def gen_moe_chunk(c, nymst):
            t0b = c * 512
            sl, off = t0b // TSL, t0b % TSL

            def cld2(g, e, c=c, nymst=nymst, ntp=tP2.n, nvv=vV2.n, sl=sl, off=off,
                     t0b=t0b):
                if c == 0:
                    g.wait_ge(ld2.h, nwld2)
                if c > 0:
                    g.wait_ge(tP2.h, ntp)
                    g.wait_ge(vV2.h, nvv)
                g.wait_ge(st.h, nymst)
                g.dma_start(out=e.xnc[:, :].rearrange("p (k t) -> p k t", k=8),
                            in_=e.xn0f.rearrange("s d t -> (s d) t")
                            [sl * D:sl * D + D, off:off + 512]
                            .rearrange("(k p) t -> p k t", p=128)).then_inc(ld2.h, 16)
                g.dma_start(out=e.ymc[:, :].rearrange("p (k t) -> p k t", k=2),
                            in_=e.ym_i[:, t0b:t0b + 512]
                            .rearrange("(k p) t -> p k t", p=128)).then_inc(ld2.h, 16)
                gwf = e.gw_fm.rearrange("s a t -> (s a) t")
                g.dma_start(out=e.gA[0:96:32, :],
                            in_=gwf[sl * 9 + 0:sl * 9 + 3, off:off + 512]).then_inc(ld2.h, 16)
                g.dma_start(out=e.gB[0:96:32, :],
                            in_=gwf[sl * 9 + 3:sl * 9 + 6, off:off + 512]).then_inc(ld2.h, 16)
                g.dma_start(out=e.gC[0:96:32, :],
                            in_=gwf[sl * 9 + 6:sl * 9 + 9, off:off + 512]).then_inc(ld2.h, 16)
            p.g(cld2)
            nld = ld2.inc(80)
            yield

            for ei in range(8):
                phx = "ph" if ei % 2 == 0 else "ph2"

                def hmm(t, e, ei=ei, phx=phx, nld=nld, nve=vV2.n):
                    if ei == 0:
                        t.wait_ge(ld2.h, nld)
                    if ei >= 2:
                        t.wait_ge(vV2.h, nve)
                    for k in range(8):
                        mm = t.matmul(e[phx][:, :],
                                      e.wgu_s[:, (k * 8 + ei) * 128:(k * 8 + ei) * 128 + 128],
                                      e.xnc[:, k * 512:(k + 1) * 512],
                                      start=(k == 0), stop=(k == 7))
                    mm.then_inc(tP2.h, 1)
                p.t(hmm)
                nhm = tP2.inc()

                def wbm2(t, e, ei=ei, nvv=vV2.n):
                    ga = [e.gA, e.gB, e.gC][ei // 3]
                    r = (ei % 3) * 32
                    t.wait_ge(vV2.h, nvv)
                    t.matmul(e.pwb[:, :], e.onesw[r:r + 1, :], ga[r:r + 1, :],
                             start=True, stop=True).then_inc(tP2.h, 1)
                p.t(wbm2)
                nwb = tP2.inc()

                def hep_s(s, e, ei=ei, phx=phx, nhm=nhm, nve=vV2.n):
                    s.wait_ge(tP2.h, nhm)
                    if ei > 0:
                        s.wait_ge(vV2.h, nve)
                    s.activation(e.gs64[:, :], e[phx][0:64, :], AF.Silu).then_inc(sS2.h, 1)
                p.s(hep_s)
                nsi = sS2.inc()

                def hep_v(v, e, ei=ei, phx=phx, nsi=nsi, nwb=nwb):
                    v.wait_ge(sS2.h, nsi)
                    v.wait_ge(tP2.h, nwb)
                    v.tensor_mul(e.hwu[:, :], e.gs64[:, :], e[phx][64:128, :])
                    v.drain()
                    v.tensor_mul(e.hsb[:, ei * 512:(ei + 1) * 512],
                                 e.hwu[:, :], e.pwb[0:64, :])
                    v.engine_nop().then_inc(vV2.h, 1)
                p.v(hep_v)
                vV2.inc()
                yield

            def wbsh(t, e, nvv=vV2.n):
                t.wait_ge(vV2.h, nvv)
                t.matmul(e.pwb[:, :], e.onesw[64:65, :], e.gC[64:65, :],
                         start=True, stop=True).then_inc(tP2.h, 1)
            p.t(wbsh)
            nwbs = tP2.inc()
            yield

            for m in range(2):
                def shm(t, e, m=m, nve=vV2.n):
                    t.wait_ge(vV2.h, nve)
                    for k in range(8):
                        t.matmul(e.ph[:, :],
                                 e.wsgu_s[:, k * 512 + m * 128: k * 512 + (m + 1) * 128],
                                 e.xnc[:, k * 512:(k + 1) * 512], start=(k == 0), stop=(k == 7))
                    for k in range(8):
                        mm = t.matmul(e.ph2[:, :],
                                      e.wsgu_s[:, k * 512 + 256 + m * 128: k * 512 + 256 + (m + 1) * 128],
                                      e.xnc[:, k * 512:(k + 1) * 512], start=(k == 0), stop=(k == 7))
                    mm.then_inc(tP2.h, 1)
                p.t(shm)
                nshm = tP2.inc()

                def shs(s, e, m=m, nshm=nshm, nve=vV2.n):
                    s.wait_ge(tP2.h, nshm)
                    if m > 0:
                        s.wait_ge(vV2.h, nve)
                    s.activation(e.gs128[:, :], e.ph[:, :], AF.Silu).then_inc(sS2.h, 1)
                p.s(shs)
                nshs = sS2.inc()

                def shv(v, e, m=m, nshs=nshs, nwbs=nwbs):
                    v.wait_ge(sS2.h, nshs)
                    v.wait_ge(tP2.h, nwbs)
                    v.tensor_mul(e.hwu2[:, :], e.gs128[:, :], e.ph2[:, :])
                    v.drain()
                    v.tensor_mul(e.hsh[:, m * 512:(m + 1) * 512], e.hwu2[:, :], e.pwb[:, :])
                    v.engine_nop().then_inc(vV2.h, 1)
                p.v(shv)
                vV2.inc()
                yield

            for m in range(8):
                def omm(t, e, m=m, nhid=vV2.n, nsev=sS2.n):
                    t.wait_ge(vV2.h, nhid)
                    if m > 0:
                        t.wait_ge(sS2.h, nsev)
                    for k in range(2):
                        t.matmul(e.pout[:, :],
                                 e.wop_s[:, k * D + m * 128: k * D + (m + 1) * 128],
                                 e.ymc[:, k * 512:(k + 1) * 512], start=(k == 0), stop=False)
                    for ei in range(8):
                        t.matmul(e.pout[:, :],
                                 e.wdn_s[:, ei * D + m * 128: ei * D + (m + 1) * 128],
                                 e.hsb[:, ei * 512:(ei + 1) * 512], start=False, stop=False)
                    for k in range(2):
                        mm = t.matmul(e.pout[:, :],
                                      e.wsdn_s[:, k * D + m * 128: k * D + (m + 1) * 128],
                                      e.hsh[:, k * 512:(k + 1) * 512], start=False, stop=(k == 1))
                    mm.then_inc(tP2.h, 1)
                p.t(omm)
                nom = tP2.inc()

                def oev(s, e, m=m, nom=nom, nsth=st2.n):
                    s.wait_ge(tP2.h, nom)
                    if (m == 0 or m == 4) and nsth > 0:
                        s.wait_ge(st2.h, nsth)
                    s.copy(e.po[:, (m % 4) * 512:(m % 4 + 1) * 512], e.pout[:, :]).then_inc(sS2.h, 1)
                p.s(oev)
                sS2.inc()

                if m == 3 or m == 7:
                    def sto2(sy, e, m=m, nse=sS2.n, sl=sl, off=off):
                        sy.wait_ge(sS2.h, nse)
                        base = sl * D + (0 if m == 3 else 4) * 128
                        sy.dma_start(out=e.part0.rearrange("s d t -> (s d) t")
                                     [base:base + 512, off:off + 512]
                                     .rearrange("(m p) t -> p m t", p=128),
                                     in_=e.po[:, :].rearrange("p (m t) -> p m t", m=4)
                                     ).then_inc(st2.h, 16)
                    p.y(sto2)
                    st2.inc(16)
                if m % 2 == 1:
                    yield

        pending = []

    for ck in range(NK):
        t0 = ck * CH

        def cld(g, e, t0=t0, ck=ck, nwld=nwld, nep=vV.n, nsum=vV.n):
            if ck == 0:
                g.wait_ge(ld.h, nwld)
            if ck > 0:
                g.wait_ge(vV.h, nep)
            if fused:
                g.dma_start(out=e.Bc[:, :], in_=e.xpr[0:64, t0:t0 + CH]).then_inc(ld.h, 16)
                g.dma_start(out=e.Cc[:, :], in_=e.xpr[64:128, t0:t0 + CH]).then_inc(ld.h, 16)
                g.dma_start(out=e.dr2[0:1, :], in_=e.xpr[128:129, t0:t0 + CH]).then_inc(ld.h, 16)
            else:
                g.dma_start(out=e.xq[:, :], in_=e.xpp8[0, 0:128, t0:t0 + CH]).then_inc(ld.h, 16)
                g.dma_start(out=e.dr2[:, :], in_=e.xpp8[:, 128, t0:t0 + CH]).then_inc(ld.h, 16)
            for j in range(2):
                g.dma_start(out=e["uc%d" % j][:, :],
                            in_=e.u_i[j * 128:(j + 1) * 128, t0:t0 + CH]).then_inc(ld.h, 16)
                g.dma_start(out=e["zc%d" % j][:, :],
                            in_=e.zs_i[j * 128:(j + 1) * 128, t0:t0 + CH]).then_inc(ld.h, 16)
            if not fused:
                g.dma_start(out=e.xq2[:, :], in_=e.xpp8[1, 0:128, t0:t0 + CH]).then_inc(ld.h, 16)
                for r in range(2, NC):
                    g.wait_ge(vV.h, nsum + (r - 1))
                    g.dma_start(out=e.xq2[:, :], in_=e.xpp8[r, 0:128, t0:t0 + CH]).then_inc(ld.h, 16)
        p.g(cld)
        if fused:
            nld0 = ld.inc(112)
            nld = nld0
        else:
            nld0 = ld.inc(112)
            nld = nld0
            nlds = [ld.inc(16) for _ in range(NC - 2)]

        if not fused:
            for r in range(1, NC):
                def xsum(v, e, r=r, nl=(nld0 if r == 1 else nlds[r - 2])):
                    v.wait_ge(ld.h, nl)
                    v.tensor_add(e.xq[:, :], e.xq[:, :], e.xq2[:, :])
                    v.drain()
                    v.engine_nop().then_inc(vV.h, 1)
                p.v(xsum)
                vV.inc()

            def xfin(v, e):
                v.tensor_copy(e.Bc[:, :], e.xq[0:64, :])
                v.tensor_copy(e.Cc[:, :], e.xq[64:128, :])
                v.engine_nop().then_inc(vV.h, 1)
            p.v(xfin)
            vV.inc()

        nprev_ep = vV.n
        nprev_ex = sS.n
        for r8 in range(8):
            q, isC = r8 % 4, r8 >= 4
            src, dst = ("Cc", "Cb%d" % q) if isC else ("Bc", "Bb%d" % q)

            def bmm(t, e, q=q, src=src, r8=r8, ck=ck, nld=nld, nini=nini,
                    nprev_ep=nprev_ep, nse=sS.n):
                if r8 == 0:
                    t.wait_ge(ld.h, nld)
                    t.wait_ge(ini.h, nini)
                    t.wait_ge(vV.h, nprev_ep)
                else:
                    t.wait_ge(sS.h, nse)
                for w in range(NW5):
                    mm = t.matmul(e.pp1[:, w * 512:(w + 1) * 512],
                                  e.sel64q_s[:, q * 128:(q + 1) * 128],
                                  e[src][:, w * 512:(w + 1) * 512], start=True, stop=True)
                mm.then_inc(tP.h, 1)
            p.t(bmm)
            nbm = tP.inc()

            def bev(s, e, dst=dst, nbm=nbm):
                s.wait_ge(tP.h, nbm)
                s.copy(e[dst][:, :], e.pp1[:, :]).then_inc(sS.h, 1)
            p.s(bev)
            sS.inc()

        def dmm(t, e, ck=ck, nld=nld, nprev_ex=nprev_ex, nxf=nprev_ep):
            if fused:
                t.wait_ge(ld.h, nld)
            else:
                t.wait_ge(vV.h, nxf)
            if ck > 0:
                t.wait_ge(sS.h, nprev_ex)
            for w in range(NW5):
                if fused:
                    mm = t.matmul(e.pp2[:, w * 512:(w + 1) * 512], e.ones1[:, :],
                                  e.dr2[0:1, w * 512:(w + 1) * 512], start=True, stop=True)
                else:
                    mm = t.matmul(e.pp2[:, w * 512:(w + 1) * 512], e.ones8[:, :],
                                  e.dr2[:, w * 512:(w + 1) * 512], start=True, stop=True)
            mm.then_inc(tP.h, 1)
        p.t(dmm)
        ndmm = tP.inc()

        for j in range(2):
            def dtp(v, e, j=j, ndmm=ndmm, nsdt=sS.n):
                v.wait_ge(tP.h, ndmm)
                if j == 1:
                    v.wait_ge(sS.h, nsdt)
                v.tensor_scalar(e.pre[:, :], e.pp2[:, :], e.dtw_s[:, j:j + 1],
                                e.dtb_s[:, j:j + 1], ALU.mult, ALU.add)
                v.engine_nop().then_inc(vV.h, 1)
            p.v(dtp)
            ndtp = vV.inc()

            def dts(s, e, j=j, ndtp=ndtp):
                s.wait_ge(vV.h, ndtp)
                s.activation(e.srl[:, :], e.pre[:, :], AF.Relu)
                s.activation(e.du[:, :], e.pre[:, :], AF.Abs)
                s.drain()
                s.activation(e.yf[:, :], e.du[:, :], AF.Exp, scale=-1.0)
                s.drain()
                s.activation(e.sln[:, :], e.yf[:, :], AF.Ln, bias=e.cone[:, 0:1]).then_inc(sS.h, 1)
            p.s(dts)
            ndts = sS.inc()

            def dtv(v, e, j=j, ndts=ndts, nld=nld):
                v.wait_ge(sS.h, ndts)
                v.wait_ge(ld.h, nld)
                v.tensor_add(e["dt%d" % j][:, :], e.srl[:, :], e.sln[:, :])
                v.drain()
                v.tensor_mul(e["dtu%d" % j][:, :], e["dt%d" % j][:, :], e["uc%d" % j][:, :])
                v.engine_nop().then_inc(vV.h, 1)
            p.v(dtv)
            vV.inc()

        nbq_h, nex_h, nsc_h, ncs_h, nqy_h = {}, {}, {}, {}, {}
        for j in range(2):
            dtu_name = "dtu%d" % j
            dt_name = "dt%d" % j
            for k in range(66):
                if k < 64:
                    gi, q = k // 4, k % 4
                    sb = gi * 128
                    I = j * 64 + k
                    if q == 0 and moe:
                        g_abs = j * 16 + gi
                        budget = 1 if g_abs == 0 else (0 if g_abs < 4 else 2)
                        while budget > 0 and pending:
                            try:
                                next(pending[0])
                                budget -= 1
                            except StopIteration:
                                pending.pop(0)
                    if q == 0:
                        def gmm1(t, e, sb=sb, dtu_name=dtu_name, nfree=sS.n, nvv=vV.n):
                            t.wait_ge(sS.h, nfree)
                            t.wait_ge(vV.h, nvv)
                            for w in range(H // 512):
                                mm = t.matmul(e.pp2[:, w * 512:(w + 1) * 512],
                                              e.selgB_s[:, sb:sb + 128],
                                              e[dtu_name][:, w * 512:(w + 1) * 512],
                                              start=True, stop=True)
                            mm.then_inc(tP.h, 1)
                        p.t(gmm1)
                        ng1 = tP.inc()

                        def gev1(s, e, ng1=ng1):
                            s.wait_ge(tP.h, ng1)
                            s.copy(e.dub[:, 0:H], e.pp2[:, 0:H]).then_inc(sS.h, 1)
                        p.s(gev1)
                        sS.inc()

                        def gmm2(t, e, sb=sb, dtu_name=dtu_name):
                            for w in range(H // 512, NW5):
                                mm = t.matmul(e.pp2[:, w * 512:(w + 1) * 512],
                                              e.selgB_s[:, sb:sb + 128],
                                              e[dtu_name][:, w * 512:(w + 1) * 512],
                                              start=True, stop=True)
                            mm.then_inc(tP.h, 1)
                        p.t(gmm2)
                        ng2 = tP.inc()

                        def gev2(s, e, ng2=ng2):
                            s.wait_ge(tP.h, ng2)
                            s.copy(e.dub[:, H:CH], e.pp2[:, H:CH]).then_inc(sS.h, 1)
                        p.s(gev2)
                        ne2 = sS.inc()

                        def gmm3(t, e, sb=sb, dt_name=dt_name, ne2=ne2):
                            t.wait_ge(sS.h, ne2)
                            for w in range(NW5):
                                mm = t.matmul(e.pp2[:, w * 512:(w + 1) * 512],
                                              e.selgB_s[:, sb:sb + 128],
                                              e[dt_name][:, w * 512:(w + 1) * 512],
                                              start=True, stop=True)
                            mm.then_inc(tP.h, 1)
                        p.t(gmm3)
                        ng3 = tP.inc()

                    def bqf(v, e, q=q, par=I % 2, ne2w=(ne2 if q == 0 else None),
                            nwar=(nsc_h[I - 2] if I >= 2 else None)):
                        if ne2w is not None:
                            v.wait_ge(sS.h, ne2w)
                        if nwar is not None:
                            v.wait_ge(vS.h, nwar)
                        v.tensor_mul(e[bqb[par]][:, :], e.dub[:, :], e["Bb%d" % q][:, :])
                        v.engine_nop().then_inc(vV.h, 1)
                    p.v(bqf)
                    nbq_h[I] = vV.inc()

                    def qexf(s, e, col=I, par=I % 2, ng3w=(ng3 if q == 0 else None),
                             nfr=(nsc_h[I - 2] if I >= 2 else None)):
                        if ng3w is not None:
                            s.wait_ge(tP.h, ng3w)
                        if nfr is not None:
                            s.wait_ge(vS.h, nfr)
                        s.activation(e[dAb[par]][:, :], e.pp2[:, :], AF.Exp,
                                     scale=e.A_s[:, col:col + 1]).then_inc(sS.h, 1)
                    p.s(qexf)
                    nex_h[I] = sS.inc()

                if 1 <= k and k - 1 < 64:
                    I1 = j * 64 + (k - 1)
                    def scf(v, e, col=I1, par=I1 % 2, nb=nbq_h[I1], nx=nex_h[I1],
                            nwar=(ncs_h[I1 - 2] if I1 >= 2 else None)):
                        v.wait_ge(vV.h, nb)
                        v.wait_ge(sS.h, nx)
                        if nwar is not None:
                            v.wait_ge(vC.h, nwar)
                        v.tensor_tensor_scan(e[stb[par]][:, :], e[dAb[par]][:, :],
                                             e[bqb[par]][:, :], e.carr[:, col:col + 1],
                                             ALU.mult, ALU.add)
                        v.engine_nop().then_inc(vS.h, 1)
                    p.v(scf)
                    nsc_h[I1] = vS.inc()

                if 2 <= k:
                    k2 = k - 2
                    I2 = j * 64 + k2
                    def ccf(v, e, col=I2, q2=k2 % 4, par=I2 % 2, ns=nsc_h[I2],
                            nyy=(nqy_h[I2 - 2] if I2 >= 2 else None)):
                        v.wait_ge(vS.h, ns)
                        if nyy is not None:
                            v.wait_ge(tP.h, nyy)
                        v.tensor_copy(e.carr[:, col:col + 1], e[stb[par]][:, CH - 1:CH])
                        v.tensor_mul(e[csb[par]][:, :], e[stb[par]][:, :], e["Cb%d" % q2][:, :])
                        v.engine_nop().then_inc(vC.h, 1)
                    p.v(ccf)
                    ncs_h[I2] = vC.inc()

                    def qyf(t, e, sb=(k2 // 4) * 128, par=I2 % 2, ncc=ncs_h[I2],
                            first=(k2 == 0), last=(k2 == 63)):
                        t.wait_ge(vC.h, ncc)
                        for w in range(NW5):
                            mm = t.matmul(e.pp1[:, w * 512:(w + 1) * 512],
                                          e.selR_s[:, sb:sb + 128],
                                          e[csb[par]][:, w * 512:(w + 1) * 512],
                                          start=first, stop=last)
                        mm.then_inc(tP.h, 1)
                    p.t(qyf)
                    nqy_h[I2] = tP.inc()

            def epi(v, e, j=j, nylast=tP.n, nstp=nst_prev, t0=t0):
                v.wait_ge(tP.h, nylast)
                if nstp is not None:
                    v.wait_ge(st.h, nstp)
                v.tensor_scalar(e.du[:, :], e["uc%d" % j][:, :], e.D_s[:, j:j + 1], None, ALU.mult)
                v.drain()
                v.tensor_add(e.yf[:, :], e.pp1[:, :], e.du[:, :])
                v.drain()
                v.tensor_mul(e.ym[:, :], e.yf[:, :], e["zc%d" % j][:, :])
                v.engine_nop().then_inc(vV.h, 1)
            p.v(epi)
            nepi = vV.inc()

            def sto(sy, e, j=j, t0=t0, nepi=nepi):
                sy.wait_ge(vV.h, nepi)
                sy.dma_start(out=e.ym_o[j * 128:(j + 1) * 128, t0:t0 + CH],
                             in_=e.ym[:, :]).then_inc(st.h, 16)
            p.y(sto)
            nst_prev = st.inc(16)

        if moe:
            for c2 in range(ck * (CH // 512), (ck + 1) * (CH // 512)):
                if c2 < NCH:
                    pending.append(gen_moe_chunk(c2, nst_prev))

    if moe:
        while pending:
            try:
                next(pending[0])
            except StopIteration:
                pending.pop(0)

        def fin2(sy, e, n=st2.n):
            sy.wait_ge(st2.h, n)
        p.y(fin2)

    def fin(sy, e, n=st.n):
        sy.wait_ge(st.h, n)
    p.y(fin)
    return p




def make_p2b(T, p=None, fused=False):
    if p is None:
        p = P()
    NCH = T // 512
    TSL = T // NC if T >= NC * 512 else T
    NSL = T // TSL

    p.din("ym_i", (256, T), BF16)
    p.din("xn0f", (NSL, D, TSL), BF16)
    p.din("gw_fm", (NSL, 9, TSL), BF16)
    p.din("wop", (128, 2 * D), F32)
    p.din("wgu", (128, 8 * 8 * 128), F32)
    p.din("wdn", (64, 8 * D), F32)
    p.din("wsgu", (128, 8 * 512), F32)
    p.din("wsdn", (128, 2 * D), F32)
    p.dout("part0", (NSL, D, TSL), F32)

    p.sbuf("wop_s", (128, 2 * D), BF16)
    p.sbuf("wgu_s", (128, 8 * 8 * 128), BF16)
    p.sbuf("wdn_s", (64, 8 * D), BF16)
    p.sbuf("wsgu_s", (128, 8 * 512), BF16)
    p.sbuf("wsdn_s", (128, 2 * D), BF16)
    p.sbuf("xnc", (128, 8 * 512), BF16)
    p.sbuf("ymc", (128, 2 * 512), BF16)
    p.sbuf("gA", (128, 512), BF16)
    p.sbuf("gB", (128, 512), BF16)
    p.sbuf("gC", (128, 512), BF16)
    p.sbuf("gs64", (64, 512), F32)
    p.sbuf("hwu", (64, 512), F32)
    p.sbuf("gs128", (128, 512), F32)
    p.sbuf("hwu2", (128, 512), F32)
    p.sbuf("hsb", (64, 8 * 512), BF16)
    p.sbuf("hsh", (128, 2 * 512), BF16)
    p.sbuf("po", (128, 8 * 512), F32)
    p.sbuf("onesw", (128, 128), BF16)

    p.psum("ph", (128, 512))
    p.psum("ph2", (128, 512))
    p.psum("pwb", (128, 512))
    p.psum("pout", (128, 512))

    ld = p.sem("ld")
    tP = p.sem("tP")
    sS = p.sem("sS")
    vV = p.sem("vV")
    st = p.sem("st")

    @p.g
    def _(g, e):
        g.dma_start(out=e.wop_s[:, :], in_=e.wop).then_inc(ld.h, 16)
        g.dma_start(out=e.wgu_s[:, :], in_=e.wgu).then_inc(ld.h, 16)
        g.dma_start(out=e.wdn_s[:, :], in_=e.wdn).then_inc(ld.h, 16)
        g.dma_start(out=e.wsgu_s[:, :], in_=e.wsgu).then_inc(ld.h, 16)
        g.dma_start(out=e.wsdn_s[:, :], in_=e.wsdn).then_inc(ld.h, 16)
        g.memset(e.onesw[:, :], 1.0)
    nwld = ld.inc(80)

    for c in range(NCH):
        t0 = c * 512

        def cld(g, e, t0=t0, c=c, nwld=nwld, ntp=tP.n, nvv=vV.n, TSL=TSL):
            if c == 0:
                g.wait_ge(ld.h, nwld)
            if c > 0:
                g.wait_ge(tP.h, ntp)
                g.wait_ge(vV.h, nvv)
            sl, off = t0 // TSL, t0 % TSL
            g.dma_start(out=e.xnc[:, :].rearrange("p (k t) -> p k t", k=8),
                        in_=e.xn0f.rearrange("s d t -> (s d) t")[sl * D:sl * D + D, off:off + 512]
                        .rearrange("(k p) t -> p k t", p=128)).then_inc(ld.h, 16)
            g.dma_start(out=e.ymc[:, :].rearrange("p (k t) -> p k t", k=2),
                        in_=e.ym_i[:, t0:t0 + 512].rearrange("(k p) t -> p k t", p=128)
                        ).then_inc(ld.h, 16)
            gwf = e.gw_fm.rearrange("s a t -> (s a) t")
            g.dma_start(out=e.gA[0:96:32, :], in_=gwf[sl * 9 + 0:sl * 9 + 3, off:off + 512]).then_inc(ld.h, 16)
            g.dma_start(out=e.gB[0:96:32, :], in_=gwf[sl * 9 + 3:sl * 9 + 6, off:off + 512]).then_inc(ld.h, 16)
            g.dma_start(out=e.gC[0:96:32, :], in_=gwf[sl * 9 + 6:sl * 9 + 9, off:off + 512]).then_inc(ld.h, 16)
        p.g(cld)
        nld = ld.inc(80)

        for ei in range(8):
            phx = "ph" if ei % 2 == 0 else "ph2"

            def hmm(t, e, ei=ei, phx=phx, nld=nld, nve=vV.n):
                if ei == 0:
                    t.wait_ge(ld.h, nld)
                if ei >= 2:
                    t.wait_ge(vV.h, nve)
                for k in range(8):
                    mm = t.matmul(e[phx][:, :],
                                  e.wgu_s[:, (k * 8 + ei) * 128:(k * 8 + ei) * 128 + 128],
                                  e.xnc[:, k * 512:(k + 1) * 512], start=(k == 0), stop=(k == 7))
                mm.then_inc(tP.h, 1)
            p.t(hmm)
            nhm = tP.inc()

            def wbm2(t, e, ei=ei, nvv=vV.n):
                ga = [e.gA, e.gB, e.gC][ei // 3]
                r = (ei % 3) * 32
                t.wait_ge(vV.h, nvv)
                t.matmul(e.pwb[:, :], e.onesw[r:r + 1, :], ga[r:r + 1, :],
                         start=True, stop=True).then_inc(tP.h, 1)
            p.t(wbm2)
            nwb = tP.inc()

            def hep_s(s, e, ei=ei, phx=phx, nhm=nhm, nve=vV.n):
                s.wait_ge(tP.h, nhm)
                if ei > 0:
                    s.wait_ge(vV.h, nve)
                s.activation(e.gs64[:, :], e[phx][0:64, :], AF.Silu).then_inc(sS.h, 1)
            p.s(hep_s)
            nsi = sS.inc()

            def hep_v(v, e, ei=ei, phx=phx, nsi=nsi, nwb=nwb):
                v.wait_ge(sS.h, nsi)
                v.wait_ge(tP.h, nwb)
                v.tensor_mul(e.hwu[:, :], e.gs64[:, :], e[phx][64:128, :])
                v.drain()
                v.tensor_mul(e.hsb[:, ei * 512:(ei + 1) * 512],
                             e.hwu[:, :], e.pwb[0:64, :])
                v.engine_nop().then_inc(vV.h, 1)
            p.v(hep_v)
            vV.inc()

        def wbsh(t, e, nvv=vV.n):
            t.wait_ge(vV.h, nvv)
            t.matmul(e.pwb[:, :], e.onesw[64:65, :], e.gC[64:65, :], start=True, stop=True).then_inc(tP.h, 1)
        p.t(wbsh)
        nwbs = tP.inc()

        for m in range(2):
            def shm(t, e, m=m, nld=nld, nve=vV.n):
                t.wait_ge(vV.h, nve)
                for k in range(8):
                    t.matmul(e.ph[:, :], e.wsgu_s[:, k * 512 + m * 128: k * 512 + (m + 1) * 128],
                             e.xnc[:, k * 512:(k + 1) * 512], start=(k == 0), stop=(k == 7))
                for k in range(8):
                    mm = t.matmul(e.ph2[:, :], e.wsgu_s[:, k * 512 + 256 + m * 128: k * 512 + 256 + (m + 1) * 128],
                                  e.xnc[:, k * 512:(k + 1) * 512], start=(k == 0), stop=(k == 7))
                mm.then_inc(tP.h, 1)
            p.t(shm)
            nshm = tP.inc()

            def shs(s, e, m=m, nshm=nshm, nve=vV.n):
                s.wait_ge(tP.h, nshm)
                if m > 0:
                    s.wait_ge(vV.h, nve)
                s.activation(e.gs128[:, :], e.ph[:, :], AF.Silu).then_inc(sS.h, 1)
            p.s(shs)
            nshs = sS.inc()

            def shv(v, e, m=m, nshs=nshs, nwbs=nwbs):
                v.wait_ge(sS.h, nshs)
                v.wait_ge(tP.h, nwbs)
                v.tensor_mul(e.hwu2[:, :], e.gs128[:, :], e.ph2[:, :])
                v.drain()
                v.tensor_mul(e.hsh[:, m * 512:(m + 1) * 512], e.hwu2[:, :], e.pwb[:, :])
                v.engine_nop().then_inc(vV.h, 1)
            p.v(shv)
            vV.inc()

        for m in range(8):
            def omm(t, e, m=m, nhid=vV.n, nsev=sS.n):
                t.wait_ge(vV.h, nhid)
                if m > 0:
                    t.wait_ge(sS.h, nsev)
                for k in range(2):
                    t.matmul(e.pout[:, :], e.wop_s[:, k * D + m * 128: k * D + (m + 1) * 128],
                             e.ymc[:, k * 512:(k + 1) * 512], start=(k == 0), stop=False)
                for ei in range(8):
                    t.matmul(e.pout[:, :], e.wdn_s[:, ei * D + m * 128: ei * D + (m + 1) * 128],
                             e.hsb[:, ei * 512:(ei + 1) * 512],
                             start=False, stop=False)
                for k in range(2):
                    mm = t.matmul(e.pout[:, :], e.wsdn_s[:, k * D + m * 128: k * D + (m + 1) * 128],
                                  e.hsh[:, k * 512:(k + 1) * 512], start=False, stop=(k == 1))
                mm.then_inc(tP.h, 1)
            p.t(omm)
            nom = tP.inc()

            def oev(s, e, m=m, nom=nom, nst0=st.n):
                s.wait_ge(tP.h, nom)
                if m == 0 and nst0 > 0:
                    s.wait_ge(st.h, nst0)
                s.copy(e.po[:, m * 512:(m + 1) * 512], e.pout[:, :]).then_inc(sS.h, 1)
            p.s(oev)
            sS.inc()

        def sto(sy, e, t0=t0, nse=sS.n, nst=st.n, TSL=TSL):
            sy.wait_ge(sS.h, nse)
            sl, off = t0 // TSL, t0 % TSL
            sy.dma_start(out=e.part0.rearrange("s d t -> (s d) t")[sl * D:sl * D + D, off:off + 512]
                         .rearrange("(m p) t -> p m t", p=128),
                         in_=e.po[:, :].rearrange("p (m t) -> p m t", m=8)).then_inc(st.h, 16)
        p.y(sto)
        st.inc(16)

    def fin(sy, e, n=st.n):
        sy.wait_ge(st.h, n)
    p.y(fin)
    return p


def tile_kpm(w):
    Kd, M = w.shape
    return np.ascontiguousarray(w.reshape(Kd // 128, 128, M).transpose(1, 0, 2).reshape(128, -1))


def p2b_weights(out_proj_w, dyn_gate_w, dyn_up_w, dyn_down_w, sh_gate_w, sh_up_w, sh_down_w, core):
    cs = slice(core * 256, (core + 1) * 256)
    isl = slice(core * 64, (core + 1) * 64)
    ish = slice(core * 256, (core + 1) * 256)
    wop = tile_kpm(out_proj_w[cs, :])
    wgu = np.stack([np.concatenate([dyn_gate_w[e][isl].T, dyn_up_w[e][isl].T], axis=1)
                    for e in range(8)])
    wgu = np.ascontiguousarray(wgu.reshape(8, 8, 128, 128).transpose(2, 1, 0, 3).reshape(128, -1))
    wdn = np.ascontiguousarray(np.stack([dyn_down_w[e].T[isl] for e in range(8)])
                               .transpose(1, 0, 2).reshape(64, -1))
    wsgu = np.concatenate([sh_gate_w[0][ish].T, sh_up_w[0][ish].T], axis=1)
    wsgu = tile_kpm(wsgu)
    wsdn = tile_kpm(sh_down_w[0].T[ish])
    return wop, wgu, wdn, wsgu, wsdn


def p2a_consts():
    selgB = np.zeros((128, 16 * 128), np.float32)
    selR = np.zeros((128, 16 * 128), np.float32)
    sel64q = np.zeros((64, 4 * 128), np.float32)
    for gi in range(16):
        for m in range(128):
            ch = gi * 8 + (m >> 5) * 2 + ((m >> 4) & 1)
            selgB[ch, gi * 128 + m] = 1.0
    for gi in range(16):
        for k in range(128):
            m = gi * 8 + (k >> 5) * 2 + ((k >> 4) & 1)
            selR[k, gi * 128 + m] = 1.0
    for q in range(4):
        for m in range(128):
            sel64q[q * 16 + (m & 15), q * 128 + m] = 1.0
    return selgB, selR, sel64q


def p2a_part_maps():
    p_idx = np.arange(128)
    choff = (p_idx >> 5) * 2 + ((p_idx >> 4) & 1)
    nq = p_idx & 15
    return choff, nq


def p2a_A_layout(A_slice):
    choff, nq = p2a_part_maps()
    A_sc = np.zeros((128, 128), np.float32)
    for g in range(32):
        base_ch = (g // 16) * 128 + (g % 16) * 8
        for q in range(4):
            A_sc[:, 4 * g + q] = A_slice[base_ch + choff, q * 16 + nq]
    return A_sc



def make_p3(TSL, p=None, fused=False):
    if p is None:
        p = P()
    NCH = TSL // 512
    NW = TSL // 128

    p.din("x0fm", (D, TSL), F32)
    if fused:
        p.din("part0a", (D, TSL), F32)
    else:
        p.din("part0a", (NC, D, TSL), F32)
    p.din("wk", (128, 8 * 256), F32)
    p.din("bkc", (128, 2), F32)
    p.din("wv", (128, 8 * 256), F32)
    p.din("bvc", (128, 2), F32)
    p.din("wc2", (128, 2 * 64), F32)
    p.din("wg1", (128, 8 * 8), F32)
    p.din("eye", (128, 128), F32)
    p.dout("x1_o", (D, TSL), F32)
    p.dout("xn1b_o", (D, TSL), BF16)
    p.dout("xn1f_o", (D, TSL), F32)
    p.dout("kc_o", (64, TSL), BF16)
    p.dout("vc_o", (64, TSL), BF16)
    p.dout("sc_o", (TSL, 8), F32)
    p.dout("scfm_o", (8, TSL), BF16)

    p.sbuf("x0r", (128, 8 * 512), F32)
    p.sbuf("s0r", (128, 8 * 512), F32)
    if not fused:
        p.sbuf("trr", (128, 8 * 512), F32)
    p.sbuf("x1r", (128, 8 * 512), F32)
    p.sbuf("xn1", (128, 8 * TSL), BF16)
    p.sbuf("xnf", (128, 8 * 512), F32)
    p.sbuf("wk_s", (128, 8 * 256), BF16)
    p.sbuf("wv_s", (128, 8 * 256), BF16)
    p.sbuf("wc_s", (128, 2 * 64), BF16)
    p.sbuf("wg_s", (128, 8 * 8), BF16)
    p.sbuf("eye_s", (128, 128), BF16)
    p.sbuf("bk_s", (128, 2), F32)
    p.sbuf("bv_s", (128, 2), F32)
    p.sbuf("w1024", (128, 1), F32)
    p.sbuf("ones1", (1, 128), F32)
    p.sbuf("ceps", (1, 1), F32)
    p.sbuf("sq", (128, 8 * 512), F32)
    p.sbuf("mean", (1, 512), F32)
    p.sbuf("m2", (1, 512), F32)
    p.sbuf("var", (1, 512), F32)
    p.sbuf("lnv", (1, 512), F32)
    p.sbuf("rstd", (1, 512), F32)
    p.sbuf("tmp", (128, 512), F32)
    p.sbuf("k256", (128, 2 * 512), BF16)
    p.sbuf("v256", (128, 2 * 512), BF16)
    p.sbuf("kcs", (64, 512), BF16)
    p.sbuf("vcs", (64, 512), BF16)
    p.sbuf("lgs", (128, 8), F32)
    p.sbuf("srt", (128, 8), F32)
    p.sbuf("dd", (128, 1), F32)
    p.sbuf("dp1", (128, 1), F32)
    p.sbuf("rec", (128, 1), F32)
    p.sbuf("n1", (128, 1), F32)
    p.sbuf("oh0", (128, 8), F32)
    p.sbuf("oh1", (128, 8), F32)
    p.sbuf("sc0", (128, 8), F32)
    p.sbuf("sc1", (128, 8), F32)
    p.sbuf("scw", (128, NW * 8), F32)
    p.sbuf("scb", (128, 8), BF16)
    p.sbuf("scf", (8, NW * 128), BF16)

    p.psum("psA", (1, 512))
    p.psum("psB", (1, 512))
    p.psum("pm", (128, 512))
    p.psum("pr", (128, 512))
    p.psum("pk0", (128, 512))
    p.psum("pkc", (64, 512))
    p.psum("plg", (128, 8))
    p.psum("ptr", (8, 128), BF16)

    ld = p.sem("ld")
    ini = p.sem("ini")
    ac = p.sem("ac")
    tP = p.sem("tP")
    sS = p.sem("sS")
    vV = p.sem("vV")
    st = p.sem("st")

    @p.g
    def _(g, e):
        g.dma_start(out=e.wk_s[:, :], in_=e.wk).then_inc(ld.h, 16)
        g.dma_start(out=e.wv_s[:, :], in_=e.wv).then_inc(ld.h, 16)
        g.dma_start(out=e.wc_s[:, :], in_=e.wc2).then_inc(ld.h, 16)
        g.dma_start(out=e.wg_s[:, :], in_=e.wg1).then_inc(ld.h, 16)
        g.dma_start(out=e.eye_s[:, :], in_=e.eye).then_inc(ld.h, 16)
        g.dma_start(out=e.bk_s[:, :], in_=e.bkc).then_inc(ld.h, 16)
        g.dma_start(out=e.bv_s[:, :], in_=e.bvc).then_inc(ld.h, 16)
        g.memset(e.w1024[:, :], 1.0 / 1024.0)
        g.memset(e.ones1[:, :], 1.0)
        g.memset(e.ceps[:, :], 1e-5)
        g.engine_nop().then_inc(ini.h, 1)
    nwld = ld.inc(112)
    nini = ini.inc()

    for c in range(NCH):
        t0 = c * 512

        def cld(g, e, t0=t0, c=c, nwld=nwld, nvv=vV.n, nss=sS.n, nst=st.n, nsum=vV.n):
            if c == 0:
                g.wait_ge(ld.h, nwld)
            if c > 0:
                g.wait_ge(vV.h, nvv)
                g.wait_ge(sS.h, nss)
                g.wait_ge(st.h, nst)
            g.dma_start(out=e.x0r[:, :].rearrange("p (k t) -> p k t", k=8),
                        in_=e.x0fm[:, t0:t0 + 512].rearrange("(k p) t -> p k t", p=128)
                        ).then_inc(ld.h, 16)
            if fused:
                g.dma_start(out=e.s0r[:, :].rearrange("p (k t) -> p k t", k=8),
                            in_=e.part0a[:, t0:t0 + 512].rearrange("(k p) t -> p k t", p=128)
                            ).then_inc(ld.h, 16)
            else:
                g.dma_start(out=e.s0r[:, :].rearrange("p (k t) -> p k t", k=8),
                            in_=e.part0a[0][:, t0:t0 + 512].rearrange("(k p) t -> p k t", p=128)
                            ).then_inc(ld.h, 16)
                g.dma_start(out=e.trr[:, :].rearrange("p (k t) -> p k t", k=8),
                            in_=e.part0a[1][:, t0:t0 + 512].rearrange("(k p) t -> p k t", p=128)
                            ).then_inc(ld.h, 16)
                for r in range(2, NC):
                    g.wait_ge(vV.h, nsum + (r - 1))
                    g.dma_start(out=e.trr[:, :].rearrange("p (k t) -> p k t", k=8),
                                in_=e.part0a[r][:, t0:t0 + 512].rearrange("(k p) t -> p k t", p=128)
                                ).then_inc(ld.h, 16)
        p.g(cld)
        if fused:
            nld0 = ld.inc(32)
            nld = nld0
        else:
            nld0 = ld.inc(48)
            nld = nld0
            nlds = [ld.inc(16) for _ in range(NC - 2)]

        if not fused:
            for r in range(1, NC):
                def psum_add(v, e, r=r, nl=(nld0 if r == 1 else nlds[r - 2])):
                    v.wait_ge(ld.h, nl)
                    v.tensor_add(e.s0r[:, :], e.s0r[:, :], e.trr[:, :])
                    v.drain()
                    v.engine_nop().then_inc(vV.h, 1)
                p.v(psum_add)
                vV.inc()

        def addf(v, e, nl0=nld0):
            if fused:
                v.wait_ge(ld.h, nl0)
            for k in range(8):
                v.tensor_add(e.x1r[:, k * 512:(k + 1) * 512], e.x0r[:, k * 512:(k + 1) * 512],
                             e.s0r[:, k * 512:(k + 1) * 512])
            v.engine_nop().then_inc(vV.h, 1)
        p.v(addf)
        nx1 = vV.inc()

        def sqf(s, e, nx1=nx1, nini=nini):
            s.wait_ge(vV.h, nx1)
            s.wait_ge(ini.h, nini)
            for k in range(8):
                a = s.activation(e.sq[:, k * 512:(k + 1) * 512], e.x1r[:, k * 512:(k + 1) * 512],
                                 AF.Square)
            a.then_inc(sS.h, 1)
        p.s(sqf)
        nsq = sS.inc()

        def mmf(t, e, c=c, nsq=nsq, nvs=vV.n):
            t.wait_ge(sS.h, nsq)
            if c > 0:
                t.wait_ge(vV.h, nvs)
            for k in range(8):
                t.matmul(e.psA[:, :], e.w1024[:, :], e.x1r[:, k * 512:(k + 1) * 512],
                         start=(k == 0), stop=(k == 7))
            for k in range(8):
                mm = t.matmul(e.psB[:, :], e.w1024[:, :], e.sq[:, k * 512:(k + 1) * 512],
                              start=(k == 0), stop=(k == 7))
            mm.then_inc(tP.h, 1)
        p.t(mmf)
        nmm = tP.inc()

        def stv(v, e, nmm=nmm):
            v.wait_ge(tP.h, nmm)
            v.tensor_copy(e.mean[:, :], e.psA[:, :])
            v.drain()
            v.tensor_mul(e.m2[:, :], e.mean[:, :], e.mean[:, :])
            v.drain()
            v.tensor_sub(e.var[:, :], e.psB[:, :], e.m2[:, :])
            v.engine_nop().then_inc(vV.h, 1)
        p.v(stv)
        nvs = vV.inc()

        def sts(s, e, nvs=nvs):
            s.wait_ge(vV.h, nvs)
            s.activation(e.lnv[:, :], e.var[:, :], AF.Ln, bias=e.ceps[:, :])
            s.drain()
            s.activation(e.rstd[:, :], e.lnv[:, :], AF.Exp, scale=-0.5).then_inc(sS.h, 1)
        p.s(sts)
        nss = sS.inc()

        def bcm(t, e, c=c, nss=nss, nxd=vV.n):
            t.wait_ge(sS.h, nss)
            if c > 0:
                t.wait_ge(vV.h, nxd)
            t.matmul(e.pm[:, :], e.ones1[:, :], e.mean[:, :], start=True, stop=True)
            t.matmul(e.pr[:, :], e.ones1[:, :], e.rstd[:, :], start=True, stop=True).then_inc(tP.h, 1)
        p.t(bcm)
        nbc = tP.inc()

        def xnf(v, e, c=c, t0=t0, nbc=nbc):
            v.wait_ge(tP.h, nbc)
            for k in range(8):
                v.tensor_sub(e.tmp[:, :], e.x1r[:, k * 512:(k + 1) * 512], e.pm[:, :])
                v.drain()
                v.tensor_mul(e.xn1[:, k * TSL + t0: k * TSL + t0 + 512], e.tmp[:, :], e.pr[:, :])
                v.drain()
                v.tensor_mul(e.xnf[:, k * 512:(k + 1) * 512], e.tmp[:, :], e.pr[:, :])
                v.drain()
            v.engine_nop().then_inc(vV.h, 1)
        p.v(xnf)
        nxn = vV.inc()

        def xsto(sy, e, t0=t0, nxn=nxn):
            sy.wait_ge(vV.h, nxn)
            sy.dma_start(out=e.x1_o[:, t0:t0 + 512].rearrange("(k p) t -> p k t", p=128),
                         in_=e.x1r[:, :].rearrange("p (k t) -> p k t", k=8)).then_inc(st.h, 16)
            sy.dma_start(out=e.xn1f_o[:, t0:t0 + 512].rearrange("(k p) t -> p k t", p=128),
                         in_=e.xnf[:, :].rearrange("p (k t) -> p k t", k=8)).then_inc(st.h, 16)
        p.y(xsto)
        st.inc(32)

        for which in ("k", "v"):
            wnm = "wk_s" if which == "k" else "wv_s"
            bnm = "bk_s" if which == "k" else "bv_s"
            s1nm = "k256" if which == "k" else "v256"
            onm = "kcs" if which == "k" else "vcs"
            dnm = "kc_o" if which == "k" else "vc_o"

            nke = None
            for m in range(2):
                def kmm(t, e, wnm=wnm, m=m, t0=t0, nxn=nxn, nse=sS.n):
                    t.wait_ge(vV.h, nxn)
                    t.wait_ge(sS.h, nse)
                    for k in range(8):
                        mm = t.matmul(e.pk0[:, :], e[wnm][:, k * 256 + m * 128: k * 256 + (m + 1) * 128],
                                      e.xn1[:, k * TSL + t0: k * TSL + t0 + 512],
                                      start=(k == 0), stop=(k == 7))
                    mm.then_inc(tP.h, 1)
                p.t(kmm)
                nkm = tP.inc()

                def kev(v, e, bnm=bnm, s1nm=s1nm, m=m, nkm=nkm):
                    v.wait_ge(tP.h, nkm)
                    v.tensor_scalar(e[s1nm][:, m * 512:(m + 1) * 512], e.pk0[:, :],
                                    e[bnm][:, m:m + 1], None, ALU.add)
                    v.engine_nop().then_inc(sS.h, 1)
                p.v(kev)
                nke = sS.inc()

            def cmm(t, e, s1nm=s1nm, nke=nke, nse2=sS.n):
                t.wait_ge(sS.h, nke)
                for k in range(2):
                    mm = t.matmul(e.pkc[:, :], e.wc_s[:, k * 64:(k + 1) * 64],
                                  e[s1nm][:, k * 512:(k + 1) * 512], start=(k == 0), stop=(k == 1))
                mm.then_inc(tP.h, 1)
            p.t(cmm)
            ncm = tP.inc()

            def cev(s, e, onm=onm, ncm=ncm, nst=st.n):
                s.wait_ge(tP.h, ncm)
                if nst > 0:
                    s.wait_ge(st.h, nst)
                s.copy(e[onm][:, :], e.pkc[:, :]).then_inc(sS.h, 1)
            p.s(cev)
            nce = sS.inc()

            def ksto(sy, e, onm=onm, dnm=dnm, t0=t0, nce=nce):
                sy.wait_ge(sS.h, nce)
                sy.dma_start(out=e[dnm][:, t0:t0 + 512], in_=e[onm][:, :]).then_inc(st.h, 16)
            p.y(ksto)
            st.inc(16)

        for wi in range(4):
            w = c * 4 + wi

            def smm(t, e, w=w, t0=t0, wi=wi, nxn=nxn, nsev=sS.n):
                t.wait_ge(vV.h, nxn)
                if w > 0:
                    t.wait_ge(sS.h, nsev)
                for k in range(8):
                    mm = t.matmul(e.plg[:, :], e.xn1[:, k * TSL + t0 + wi * 128: k * TSL + t0 + (wi + 1) * 128],
                                  e.wg_s[:, k * 8:(k + 1) * 8], start=(k == 0), stop=(k == 7))
                mm.then_inc(tP.h, 1)
            p.t(smm)
            nsm = tP.inc()

            def sev(s, e, nsm=nsm, nvr=vV.n):
                s.wait_ge(tP.h, nsm)
                if nvr > 0:
                    s.wait_ge(vV.h, nvr)
                s.copy(e.lgs[:, :], e.plg[:, :]).then_inc(sS.h, 1)
            p.s(sev)
            nsev = sS.inc()

            def rt1(v, e, nsev=nsev):
                v.wait_ge(sS.h, nsev)
                v.max(e.srt[:, :], e.lgs[:, :])
                v.drain()
                v.tensor_scalar(e.dd[:, :], e.srt[:, 1:2], e.srt[:, 0:1], None, ALU.subtract)
                v.engine_nop().then_inc(vV.h, 1)
            p.v(rt1)
            nr1 = vV.inc()

            def rt2(s, e, nr1=nr1):
                s.wait_ge(vV.h, nr1)
                s.activation(e.dd[:, :], e.dd[:, :], AF.Exp).then_inc(sS.h, 1)
            p.s(rt2)
            nr2 = sS.inc()

            def rt3(v, e, w=w, nr2=nr2, ntp=tP.n):
                v.wait_ge(sS.h, nr2)
                if w > 0:
                    v.wait_ge(tP.h, ntp)
                v.tensor_scalar(e.dp1[:, :], e.dd[:, :], 1.0, None, ALU.add)
                v.drain()
                v.reciprocal(e.rec[:, :], e.dp1[:, :])
                v.drain()
                v.tensor_mul(e.n1[:, :], e.dd[:, :], e.rec[:, :])
                v.tensor_scalar(e.oh0[:, :], e.lgs[:, :], e.srt[:, 0:1], None, ALU.is_equal)
                v.tensor_scalar(e.oh1[:, :], e.lgs[:, :], e.srt[:, 1:2], None, ALU.is_equal)
                v.drain()
                v.tensor_scalar(e.sc0[:, :], e.oh0[:, :], e.rec[:, 0:1], None, ALU.mult)
                v.tensor_scalar(e.sc1[:, :], e.oh1[:, :], e.n1[:, 0:1], None, ALU.mult)
                v.drain()
                v.tensor_add(e.scw[:, w * 8:(w + 1) * 8], e.sc0[:, :], e.sc1[:, :])
                v.drain()
                v.tensor_copy(e.scb[:, :], e.scw[:, w * 8:(w + 1) * 8])
                v.engine_nop().then_inc(vV.h, 1)
            p.v(rt3)
            nr3 = vV.inc()

            def trx(t, e, w=w, nr3=nr3, nprev=sS.n):
                t.wait_ge(vV.h, nr3)
                if w > 0:
                    t.wait_ge(sS.h, nprev)
                t.transpose(e.ptr[:, :], e.scb[:, :], e.eye_s[:, :]).then_inc(tP.h, 1)
            p.t(trx)
            ntr = tP.inc()

            def trev(s, e, w=w, ntr=ntr):
                s.wait_ge(tP.h, ntr)
                s.copy(e.scf[:, w * 128:(w + 1) * 128], e.ptr[:, :]).then_inc(sS.h, 1)
            p.s(trev)
            sS.inc()

    def stf(sy, e, nvv=vV.n, nss=sS.n):
        sy.wait_ge(vV.h, nvv)
        sy.wait_ge(sS.h, nss)
        sy.dma_start(out=e.xn1b_o.rearrange("(k p) t -> p k t", p=128),
                     in_=e.xn1[:, :].rearrange("p (k t) -> p k t", k=8)).then_inc(st.h, 16)
        sy.dma_start(out=e.sc_o.rearrange("(w p) e -> p w e", p=128),
                     in_=e.scw[:, :].rearrange("p (w e) -> p w e", e=8)).then_inc(st.h, 16)
        sy.dma_start(out=e.scfm_o.rearrange("a (w t) -> a w t", t=128),
                     in_=e.scf[:, :].rearrange("a (w t) -> a w t", t=128)).then_inc(st.h, 16)
        sy.wait_ge(st.h, nfin)
    nfin = st.inc(48)
    p.y(stf)
    return p



def make_p4(T, p=None, fused=False):
    if p is None:
        p = P()
    NCH = T // 512
    NAC = T // 128
    TSL = T // NC if T >= NC * 512 else T
    NSL = T // TSL
    KH = T // 2 if T >= 1024 else T

    p.din("xn1f", (NSL, D, TSL), BF16)
    p.din("kcf", (NSL, 64, TSL), BF16)
    p.din("vcf", (NSL, 64, TSL), BF16)
    p.din("scfm", (NSL, 8, TSL), BF16)
    p.din("wq", (128, 8 * 128), F32)
    p.din("bq_c", (128, 1), F32)
    p.din("we_h", (64, 64), F32)
    p.din("wo_sl", (128, 8 * 128), F32)
    p.din("w1_t", (128, 8 * 8 * 256), F32)
    p.din("b1_c", (128, 2 * 8), F32)
    p.din("w2_t", (128, 8 * 2 * 1024), F32)
    p.din("b2_f", (8, D), F32)
    p.din("c0v", (8, 1), F32)
    p.din("sel8", (8, 8 * 128), F32)
    p.din("utri", (128, 128), F32)
    p.din("eye", (128, 128), F32)
    p.dout("parto", (NSL, D, TSL), F32)
    p.dout("partf", (NSL, D, TSL), F32)
    p.dint("odram", (128, T), BF16)

    p.sbuf("wq_s", (128, 8 * 128), BF16)
    p.sbuf("we_s", (64, 64), BF16)
    p.sbuf("wo_s", (128, 8 * 128), BF16)
    p.sbuf("w1_s", (128, 8 * 8 * 256), BF16)
    p.sbuf("w2_s", (128, 8 * 2 * 1024), BF16)
    p.sbuf("b1_s", (128, 2 * 8), F32)
    p.sbuf("b2_s", (8, D), BF16)
    p.sbuf("c0_s", (8, 1), F32)
    p.sbuf("sel8_s", (8, 8 * 128), BF16)
    p.sbuf("bq_s", (128, 1), F32)
    p.sbuf("bqn", (128, 1), F32)
    p.sbuf("utri_s", (128, 128), F32)
    p.sbuf("eye_s", (128, 64), BF16)
    p.sbuf("onesc", (128, 1), BF16)
    p.sbuf("onesf", (1, 64), F32)
    p.sbuf("qf", (128, T), BF16)
    p.sbuf("kf", (128, KH), BF16)
    p.sbuf("ve", (128, KH), BF16)
    p.sbuf("ob", (128, 128), BF16)
    p.sbuf("oc", (128, 512), BF16)
    p.sbuf("xnc", (128, 8 * 512), BF16)
    p.sbuf("kcc", (64, 512), BF16)
    p.sbuf("vcc", (64, 512), BF16)
    p.sbuf("a1", (128, 512), F32)
    p.sbuf("r2", (128, 512), F32)
    p.sbuf("e2", (128, 512), F32)
    p.sbuf("qh", (64, 128), BF16)
    p.sbuf("kfc0", (64, 128), BF16)
    p.sbuf("vec0", (64, 128), BF16)
    p.sbuf("kfT", (128, 64), BF16)
    p.sbuf("vT", (128, 64), BF16)
    p.sbuf("Sm", (128, 128), BF16)
    p.sbuf("dnv", (1, 128), F32)
    p.sbuf("rcv", (1, 128), F32)
    p.sbuf("rb", (64, 128), F32)
    p.sbuf("KV", (64, 64), BF16)
    p.sbuf("ks", (64, 1), BF16)
    p.sbuf("KVf", (64, 64), F32)
    p.sbuf("ksf", (64, 1), F32)
    p.sbuf("hr", (128, 512), F32)
    p.sbuf("hstore", (128, 16 * 512), BF16)
    p.sbuf("sc8", (8, 512), BF16)
    p.sbuf("sc0", (8, 512), BF16)
    p.sbuf("po2", (128, 2 * 512), F32)
    p.sbuf("pf2", (128, 2 * 512), F32)

    p.psum("pa", (128, 512))
    p.psum("pb2", (128, 512))
    p.psum("pc64", (64, 512))
    p.psum("ptA", (128, 64), BF16)
    p.psum("ptB", (128, 64), BF16)
    p.psum("pD", (1, 128))
    p.psum("pKV", (64, 64))
    p.psum("pks", (64, 1))

    ld = p.sem("ld")
    ini = p.sem("ini")
    tP = p.sem("tP")
    sS = p.sem("sS")
    vV = p.sem("vV")
    st = p.sem("st")

    @p.g
    def _(g, e):
        g.dma_start(out=e.wq_s[:, :], in_=e.wq).then_inc(ld.h, 16)
        g.dma_start(out=e.we_s[:, :], in_=e.we_h).then_inc(ld.h, 16)
        g.dma_start(out=e.wo_s[:, :], in_=e.wo_sl).then_inc(ld.h, 16)
        g.dma_start(out=e.w1_s[:, :], in_=e.w1_t).then_inc(ld.h, 16)
        g.dma_start(out=e.w2_s[:, :], in_=e.w2_t).then_inc(ld.h, 16)
        g.dma_start(out=e.b1_s[:, :], in_=e.b1_c).then_inc(ld.h, 16)
        g.dma_start(out=e.b2_s[:, :], in_=e.b2_f).then_inc(ld.h, 16)
        g.dma_start(out=e.c0_s[:, :], in_=e.c0v).then_inc(ld.h, 16)
        g.dma_start(out=e.sel8_s[:, :], in_=e.sel8).then_inc(ld.h, 16)
        g.dma_start(out=e.bq_s[:, :], in_=e.bq_c).then_inc(ld.h, 16)
        g.dma_start(out=e.utri_s[:, :], in_=e.utri).then_inc(ld.h, 16)
        g.dma_start(out=e.eye_s[0:64, :], in_=e.eye[0:64, 0:64]).then_inc(ld.h, 16)
        g.dma_start(out=e.eye_s[64:128, :], in_=e.eye[0:64, 0:64]).then_inc(ld.h, 16)
        g.memset(e.onesc[:, :], 1.0)
        g.memset(e.onesf[:, :], 1.0)
        g.memset(e.KV[:, :], 0.0)
        g.memset(e.ks[:, :], 0.0)
        g.memset(e.KVf[:, :], 0.0)
        g.memset(e.ksf[:, :], 0.0)
        g.engine_nop().then_inc(ini.h, 1)
    nwld = ld.inc(208)
    nini = ini.inc()

    def bqneg(v, e, nwld=nwld, nini=nini):
        v.wait_ge(ld.h, nwld)
        v.wait_ge(ini.h, nini)
        v.tensor_scalar(e.bqn[:, :], e.bq_s[:, :], -1.0, None, ALU.mult)
        v.engine_nop().then_inc(vV.h, 1)
    p.v(bqneg)
    vV.inc()

    for c in range(NCH):
        t0 = c * 512

        def ald(g, e, t0=t0, c=c, ntp=tP.n, nwld=nwld, TSL=TSL):
            if c == 0:
                g.wait_ge(ld.h, nwld)
            if c > 0:
                g.wait_ge(tP.h, ntp)
            sl, off = t0 // TSL, t0 % TSL
            g.dma_start(out=e.xnc[:, :].rearrange("p (k t) -> p k t", k=8),
                        in_=e.xn1f.rearrange("s d t -> (s d) t")[sl * D:sl * D + D, off:off + 512]
                        .rearrange("(k p) t -> p k t", p=128)).then_inc(ld.h, 16)
            g.dma_start(out=e.kcc[:, :],
                        in_=e.kcf.rearrange("s a t -> (s a) t")[sl * 64:sl * 64 + 64, off:off + 512]).then_inc(ld.h, 16)
            g.dma_start(out=e.vcc[:, :],
                        in_=e.vcf.rearrange("s a t -> (s a) t")[sl * 64:sl * 64 + 64, off:off + 512]).then_inc(ld.h, 16)
        p.g(ald)
        nld = ld.inc(48)

        def qmm(t, e, nld=nld, nse=sS.n, c=c):
            t.wait_ge(ld.h, nld)
            if c > 0:
                t.wait_ge(sS.h, nse)
            for k in range(8):
                mm = t.matmul(e.pa[:, :], e.wq_s[:, k * 128:(k + 1) * 128],
                              e.xnc[:, k * 512:(k + 1) * 512], start=(k == 0), stop=(k == 7))
            mm.then_inc(tP.h, 1)
        p.t(qmm)
        nqm = tP.inc()

        def qep_s(s, e, nqm=nqm, nvp=vV.n):
            s.wait_ge(tP.h, nqm)
            s.wait_ge(vV.h, nvp)
            s.activation(e.a1[:, :], e.pa[:, :], AF.Relu, bias=e.bq_s[:, 0:1])
            s.activation(e.r2[:, :], e.pa[:, :], AF.Relu, scale=-1.0, bias=e.bqn[:, 0:1])
            s.drain()
            s.activation(e.e2[:, :], e.r2[:, :], AF.Exp, scale=-1.0).then_inc(sS.h, 1)
        p.s(qep_s)
        nqe = sS.inc()

        def qep_v(v, e, t0=t0, nqe=nqe):
            v.wait_ge(sS.h, nqe)
            v.tensor_add(e.qf[:, t0:t0 + 512], e.a1[:, :], e.e2[:, :])
            v.engine_nop().then_inc(vV.h, 1)
        p.v(qep_v)
        vV.inc()

        def kvmm(t, e, nld=nld, nse=sS.n, c=c):
            if c > 0:
                t.wait_ge(sS.h, nse)
            t.matmul(e.pc64[:, :], e.we_s[:, :], e.kcc[:, :], start=True, stop=True).then_inc(tP.h, 1)
        p.t(kvmm)
        nkm = tP.inc()

        def kep_s(s, e, nkm=nkm, nvp=vV.n):
            s.wait_ge(tP.h, nkm)
            s.wait_ge(vV.h, nvp)
            s.activation(e.a1[0:64, :], e.pc64[:, :], AF.Relu)
            s.activation(e.r2[0:64, :], e.pc64[:, :], AF.Relu, scale=-1.0)
            s.drain()
            s.activation(e.e2[0:64, :], e.r2[0:64, :], AF.Exp, scale=-1.0).then_inc(sS.h, 1)
        p.s(kep_s)
        nke = sS.inc()

        def kep_v(v, e, t0=t0, nke=nke, KH=KH):
            v.wait_ge(sS.h, nke)
            hb = 64 * (t0 // KH)
            tt = t0 % KH
            v.tensor_add(e.kf[hb:hb + 64, tt:tt + 512], e.a1[0:64, :], e.e2[0:64, :])
            v.engine_nop().then_inc(vV.h, 1)
        p.v(kep_v)
        nkv = vV.inc()

        def vmm(t, e, nkv=nkv):
            t.wait_ge(vV.h, nkv)
            t.matmul(e.pc64[:, :], e.we_s[:, :], e.vcc[:, :], start=True, stop=True).then_inc(tP.h, 1)
        p.t(vmm)
        nvm = tP.inc()

        def vev(s, e, t0=t0, nvm=nvm, KH=KH):
            s.wait_ge(tP.h, nvm)
            hb = 64 * (t0 // KH)
            tt = t0 % KH
            s.copy(e.ve[hb:hb + 64, tt:tt + 512], e.pc64[:, :]).then_inc(sS.h, 1)
        p.s(vev)
        sS.inc()

    nA_v = vV.n
    nA_s = sS.n

    for c in range(NAC):
        u0 = c * 128

        def bcp(v, e, c=c, u0=u0, nA_v=nA_v, nA_s=nA_s, KH=KH, ntpp=tP.n):
            if c == 0:
                v.wait_ge(vV.h, nA_v)
                v.wait_ge(sS.h, nA_s)
            else:
                v.wait_ge(tP.h, ntpp)
            hb = 64 * (u0 // KH)
            uu = u0 % KH
            v.tensor_copy(e.kfc0[:, :], e.kf[hb:hb + 64, uu:uu + 128])
            v.tensor_copy(e.vec0[:, :], e.ve[hb:hb + 64, uu:uu + 128])
            v.engine_nop().then_inc(vV.h, 1)
        p.v(bcp)
        nbcp = vV.inc()

        def btr(t, e, c=c, nbcp=nbcp, nsev=sS.n):
            t.wait_ge(vV.h, nbcp)
            if c > 0:
                t.wait_ge(sS.h, nsev)
            t.transpose(e.ptA[:, :], e.kfc0[:, :], e.eye_s[0:64, :])
            t.transpose(e.ptB[:, :], e.vec0[:, :], e.eye_s[0:64, :]).then_inc(tP.h, 1)
        p.t(btr)
        ntr = tP.inc()

        def bev(s, e, ntr=ntr, nvv=vV.n):
            s.wait_ge(tP.h, ntr)
            if nvv > 0:
                s.wait_ge(vV.h, nvv)
            s.copy(e.kfT[:, :], e.ptA[:, :])
            s.copy(e.vT[:, :], e.ptB[:, :]).then_inc(sS.h, 1)
        p.s(bev)
        nbe = sS.inc()

        for hh in range(2):
            def qcp(v, e, hh=hh, u0=u0, nbe=nbe, ntp=tP.n):
                v.wait_ge(sS.h, nbe)
                if not (c == 0 and hh == 0):
                    v.wait_ge(tP.h, ntp)
                v.tensor_copy(e.qh[:, :], e.qf[hh * 64:(hh + 1) * 64, u0:u0 + 128])
                v.engine_nop().then_inc(vV.h, 1)
            p.v(qcp)
            nqc = vV.inc()

            def bs(t, e, hh=hh, nqc=nqc):
                t.wait_ge(vV.h, nqc)
                t.matmul(e.pa[:, 0:128], e.kfc0[:, :], e.qh[:, :],
                         start=True, stop=True).then_inc(tP.h, 1)
            p.t(bs)
            nbs = tP.inc()

            def bsm(v, e, nbs=nbs):
                v.wait_ge(tP.h, nbs)
                v.tensor_mul(e.Sm[:, :], e.pa[:, 0:128], e.utri_s[:, :])
                v.engine_nop().then_inc(vV.h, 1)
            p.v(bsm)
            nsm = vV.inc()

            def bnd(t, e, hh=hh, c=c, nsm=nsm):
                t.wait_ge(vV.h, nsm)
                t.matmul(e.pc64[:, 0:128], e.vT[:, :], e.Sm[:, :], start=True, stop=False)
                t.matmul(e.pc64[:, 0:128], e.KV[:, :], e.qh[:, :], start=False, stop=True)
                t.matmul(e.pD[:, :], e.onesc[:, :], e.Sm[:, :], start=True, stop=False)
                t.matmul(e.pD[:, :], e.ks[:, :], e.qh[:, :], start=False, stop=True).then_inc(tP.h, 1)
            p.t(bnd)
            nnd = tP.inc()

            def brc(v, e, nnd=nnd):
                v.wait_ge(tP.h, nnd)
                v.tensor_scalar(e.dnv[:, :], e.pD[:, :], 1e-6, None, ALU.add)
                v.drain()
                v.reciprocal(e.rcv[:, :], e.dnv[:, :])
                v.engine_nop().then_inc(vV.h, 1)
            p.v(brc)
            nrc = vV.inc()

            def brb(t, e, nrc=nrc, nsrb=sS.n):
                t.wait_ge(vV.h, nrc)
                if nsrb > 0:
                    t.wait_ge(sS.h, nsrb)
                t.matmul(e.pb2[0:64, 0:128], e.onesf[:, :], e.rcv[:, :],
                         start=True, stop=True).then_inc(tP.h, 1)
            p.t(brb)
            nrb = tP.inc()

            def brbe(s, e, nrb=nrb):
                s.wait_ge(tP.h, nrb)
                s.copy(e.rb[:, :], e.pb2[0:64, 0:128]).then_inc(sS.h, 1)
            p.s(brbe)
            nrbe = sS.inc()

            def bo(v, e, hh=hh, u0=u0, nrbe=nrbe, nstb=st.n):
                v.wait_ge(sS.h, nrbe)
                if hh == 0 and nstb > 0:
                    v.wait_ge(st.h, nstb)
                v.tensor_mul(e.ob[hh * 64:(hh + 1) * 64, :], e.pc64[:, 0:128],
                             e.rb[:, :])
                v.engine_nop().then_inc(vV.h, 1)
            p.v(bo)
            vV.inc()

        def osto(sy, e, c=c, u0=u0, nvo=vV.n):
            sy.wait_ge(vV.h, nvo)
            sy.dma_start(out=e.odram[:, u0:u0 + 128], in_=e.ob[:, :]).then_inc(st.h, 16)
        p.y(osto)
        st.inc(16)

        def bkv(t, e, c=c, NAC=NAC, nvo=vV.n, nsup=sS.n):
            t.wait_ge(vV.h, nvo)
            if c > 0:
                t.wait_ge(sS.h, nsup)
            t.matmul(e.pKV[:, :], e.kfT[:, :], e.vT[:, :], start=True, stop=True)
            t.matmul(e.pks[:, :], e.kfT[:, :], e.onesc[:, :],
                     start=True, stop=True).then_inc(tP.h, 1)
        p.t(bkv)
        nkvu = tP.inc()

        def bkve(v, e, nkvu=nkvu):
            v.wait_ge(tP.h, nkvu)
            v.tensor_add(e.KVf[:, :], e.KVf[:, :], e.pKV[:, :])
            v.tensor_add(e.ksf[:, :], e.ksf[:, :], e.pks[:, :])
            v.drain()
            v.tensor_copy(e.KV[:, :], e.KVf[:, :])
            v.tensor_copy(e.ks[:, :], e.ksf[:, :])
            v.engine_nop().then_inc(sS.h, 1)
        p.v(bkve)
        sS.inc()

    nB_v = vV.n
    nB_s = sS.n
    nB_t = tP.n
    nB_st = st.n

    for c in range(NCH):
        t0 = c * 512

        def cld(g, e, t0=t0, c=c, ntp=tP.n, nst=st.n, nB_t=nB_t, nB_st=nB_st, TSL=TSL):
            if c == 0:
                g.wait_ge(tP.h, nB_t)
                g.wait_ge(st.h, nB_st)
            if c > 0:
                g.wait_ge(tP.h, ntp)
                g.wait_ge(st.h, nst)
            sl, off = t0 // TSL, t0 % TSL
            g.dma_start(out=e.xnc[:, :].rearrange("p (k t) -> p k t", k=8),
                        in_=e.xn1f.rearrange("s d t -> (s d) t")[sl * D:sl * D + D, off:off + 512]
                        .rearrange("(k p) t -> p k t", p=128)).then_inc(ld.h, 16)
            g.dma_start(out=e.sc8[:, :],
                        in_=e.scfm.rearrange("s a t -> (s a) t")[sl * 8:sl * 8 + 8, off:off + 512]).then_inc(ld.h, 16)
            g.dma_start(out=e.oc[:, :], in_=e.odram[:, t0:t0 + 512]).then_inc(ld.h, 16)
        p.g(cld)
        nld = ld.inc(48)

        def csc(v, e, nld=nld, nB_s=nB_s):
            v.wait_ge(ld.h, nld)
            v.tensor_scalar(e.sc0[:, :], e.sc8[:, :], e.c0_s[:, 0:1], None, ALU.mult)
            v.engine_nop().then_inc(vV.h, 1)
        p.v(csc)
        ncsc = vV.inc()

        for ei in range(8):
            def wbm(t, e, ei=ei, c=c, nld=nld, ncsc=ncsc, nB_v=nB_v, nhw=vV.n):
                if c == 0 and ei == 0:
                    t.wait_ge(vV.h, nB_v)
                t.wait_ge(vV.h, nhw)
                t.matmul(e.pb2[:, :], e.sel8_s[:, ei * 128:(ei + 1) * 128], e.sc8[:, :],
                         start=True, stop=True).then_inc(tP.h, 1)
            p.t(wbm)
            nwb = tP.inc()

            for m in range(2):
                def hmm(t, e, ei=ei, m=m, nld=nld, nse=sS.n):
                    t.wait_ge(ld.h, nld)
                    t.wait_ge(sS.h, nse)
                    for k in range(8):
                        mm = t.matmul(e.pa[:, :],
                                      e.w1_s[:, (k * 8 + ei) * 256 + m * 128: (k * 8 + ei) * 256 + (m + 1) * 128],
                                      e.xnc[:, k * 512:(k + 1) * 512], start=(k == 0), stop=(k == 7))
                    mm.then_inc(tP.h, 1)
                p.t(hmm)
                nhm = tP.inc()

                def hre(s, e, ei=ei, m=m, nhm=nhm, nhw=vV.n):
                    s.wait_ge(tP.h, nhm)
                    if not (c == 0 and ei == 0 and m == 0):
                        s.wait_ge(vV.h, nhw)
                    s.activation(e.hr[:, :], e.pa[:, :], AF.Relu,
                                 bias=e.b1_s[:, ei * 2 + m: ei * 2 + m + 1]).then_inc(sS.h, 1)
                p.s(hre)
                nhr = sS.inc()

                def hwv(v, e, ei=ei, m=m, nhr=nhr, nwb=nwb):
                    v.wait_ge(sS.h, nhr)
                    v.wait_ge(tP.h, nwb)
                    v.tensor_mul(e.hstore[:, (ei * 2 + m) * 512:(ei * 2 + m + 1) * 512],
                                 e.hr[:, :], e.pb2[:, :])
                    v.engine_nop().then_inc(vV.h, 1)
                p.v(hwv)
                vV.inc()

        for m in range(8):
            def omm(t, e, m=m, t0=t0, nse=sS.n, nhid=vV.n):
                t.wait_ge(vV.h, nhid)
                if m > 0:
                    t.wait_ge(sS.h, nse)
                t.matmul(e.pa[:, :], e.wo_s[:, m * 128:(m + 1) * 128],
                         e.oc[:, :], start=True, stop=True).then_inc(tP.h, 1)
            p.t(omm)
            nom = tP.inc()

            def oev(s, e, m=m, nom=nom, nst=st.n):
                s.wait_ge(tP.h, nom)
                if m >= 2:
                    s.wait_ge(st.h, nst)
                s.copy(e.po2[:, (m % 2) * 512:(m % 2 + 1) * 512], e.pa[:, :]).then_inc(sS.h, 1)
            p.s(oev)
            noev = sS.inc()

            def osto2(sy, e, m=m, t0=t0, noev=noev, TSL=TSL):
                sy.wait_ge(sS.h, noev)
                sl, off = t0 // TSL, t0 % TSL
                sy.dma_start(out=e.parto.rearrange("s d t -> (s d) t")[
                                 sl * D + m * 128:sl * D + (m + 1) * 128, off:off + 512],
                             in_=e.po2[:, (m % 2) * 512:(m % 2 + 1) * 512]).then_inc(st.h, 16)
            p.y(osto2)
            st.inc(16)

        for m in range(8):
            def fmm(t, e, m=m, ncsc=ncsc, nse=sS.n):
                t.wait_ge(vV.h, ncsc)
                t.wait_ge(sS.h, nse)
                for ei in range(8):
                    for k2 in range(2):
                        t.matmul(e.pa[:, :],
                                 e.w2_s[:, (ei * 2 + k2) * 1024 + m * 128: (ei * 2 + k2) * 1024 + (m + 1) * 128],
                                 e.hstore[:, (ei * 2 + k2) * 512:(ei * 2 + k2 + 1) * 512],
                                 start=(ei == 0 and k2 == 0), stop=False)
                mm = t.matmul(e.pa[:, :], e.b2_s[:, m * 128:(m + 1) * 128], e.sc0[:, :],
                              start=False, stop=True)
                mm.then_inc(tP.h, 1)
            p.t(fmm)
            nfm = tP.inc()

            def fev(s, e, m=m, nfm=nfm, nst=st.n):
                s.wait_ge(tP.h, nfm)
                if m >= 2:
                    s.wait_ge(st.h, nst)
                s.copy(e.pf2[:, (m % 2) * 512:(m % 2 + 1) * 512], e.pa[:, :]).then_inc(sS.h, 1)
            p.s(fev)
            nfev = sS.inc()

            def fsto2(sy, e, m=m, t0=t0, nfev=nfev, TSL=TSL):
                sy.wait_ge(sS.h, nfev)
                sl, off = t0 // TSL, t0 % TSL
                sy.dma_start(out=e.partf.rearrange("s d t -> (s d) t")[
                                 sl * D + m * 128:sl * D + (m + 1) * 128, off:off + 512],
                             in_=e.pf2[:, (m % 2) * 512:(m % 2 + 1) * 512]).then_inc(st.h, 16)
            p.y(fsto2)
            st.inc(16)


    def fin(sy, e, n=st.n):
        sy.wait_ge(st.h, n)
    p.y(fin)
    return p



def make_p5(TSL, p=None, fused=False):
    if p is None:
        p = P()
    NCH = TSL // 512

    p.din("x1fm", (D, TSL), F32)
    p.din("xn1fm", (D, TSL), F32)
    if fused:
        p.din("parto_a", (D, TSL), F32)
        p.din("partf_a", (D, TSL), F32)
    else:
        p.din("parto_a", (NC, D, TSL), F32)
        p.din("partf_a", (NC, D, TSL), F32)
    p.din("bo_c", (128, 8), F32)
    p.din("gam_c", (128, 8), F32)
    p.din("bet_c", (128, 8), F32)
    p.dout("out_o", (D, TSL), F32)

    p.sbuf("x1r", (128, 8 * 512), F32)
    p.sbuf("xnr", (128, 8 * 512), F32)
    p.sbuf("orr", (128, 8 * 512), F32)
    p.sbuf("frr", (128, 8 * 512), F32)
    if not fused:
        p.sbuf("tr5", (128, 8 * 512), F32)
    p.sbuf("yr", (128, 8 * 512), F32)
    p.sbuf("a1r", (128, 8 * 512), F32)
    p.sbuf("oro", (128, 8 * 512), F32)
    p.sbuf("bo_s", (128, 8), F32)
    p.sbuf("gam_s", (128, 8), F32)
    p.sbuf("bet_s", (128, 8), F32)
    p.sbuf("w1024", (128, 1), F32)
    p.sbuf("ones1", (1, 128), F32)
    p.sbuf("ceps", (1, 1), F32)
    p.sbuf("sq", (128, 8 * 512), F32)
    p.sbuf("mean", (1, 512), F32)
    p.sbuf("m2", (1, 512), F32)
    p.sbuf("var", (1, 512), F32)
    p.sbuf("lnv", (1, 512), F32)
    p.sbuf("rstd", (1, 512), F32)
    p.sbuf("tmp", (128, 512), F32)

    p.psum("psA", (1, 512))
    p.psum("psB", (1, 512))
    p.psum("pm", (128, 512))
    p.psum("pr", (128, 512))

    ld = p.sem("ld")
    ini = p.sem("ini")
    ac = p.sem("ac")
    tP = p.sem("tP")
    sS = p.sem("sS")
    vV = p.sem("vV")
    st = p.sem("st")

    @p.g
    def _(g, e):
        g.dma_start(out=e.bo_s[:, :], in_=e.bo_c).then_inc(ld.h, 16)
        g.dma_start(out=e.gam_s[:, :], in_=e.gam_c).then_inc(ld.h, 16)
        g.dma_start(out=e.bet_s[:, :], in_=e.bet_c).then_inc(ld.h, 16)
        g.memset(e.w1024[:, :], 1.0 / 1024.0)
        g.memset(e.ones1[:, :], 1.0)
        g.memset(e.ceps[:, :], 1e-5)
        g.engine_nop().then_inc(ini.h, 1)
    nwld = ld.inc(48)
    nini = ini.inc()

    for c in range(NCH):
        t0 = c * 512

        def cld(g, e, t0=t0, c=c, nwld=nwld, nvv=vV.n, nst=st.n, nsum=vV.n):
            if c == 0:
                g.wait_ge(ld.h, nwld)
            if c > 0:
                g.wait_ge(vV.h, nvv)
                g.wait_ge(st.h, nst)
            g.dma_start(out=e.x1r[:, :].rearrange("p (k t) -> p k t", k=8),
                        in_=e.x1fm[:, t0:t0 + 512].rearrange("(k p) t -> p k t", p=128)).then_inc(ld.h, 16)
            g.dma_start(out=e.xnr[:, :].rearrange("p (k t) -> p k t", k=8),
                        in_=e.xn1fm[:, t0:t0 + 512].rearrange("(k p) t -> p k t", p=128)).then_inc(ld.h, 16)
            if fused:
                g.dma_start(out=e.orr[:, :].rearrange("p (k t) -> p k t", k=8),
                            in_=e.parto_a[:, t0:t0 + 512].rearrange("(k p) t -> p k t", p=128)).then_inc(ld.h, 16)
                g.dma_start(out=e.frr[:, :].rearrange("p (k t) -> p k t", k=8),
                            in_=e.partf_a[:, t0:t0 + 512].rearrange("(k p) t -> p k t", p=128)).then_inc(ld.h, 16)
            else:
                g.dma_start(out=e.orr[:, :].rearrange("p (k t) -> p k t", k=8),
                            in_=e.parto_a[0][:, t0:t0 + 512].rearrange("(k p) t -> p k t", p=128)).then_inc(ld.h, 16)
                g.dma_start(out=e.frr[:, :].rearrange("p (k t) -> p k t", k=8),
                            in_=e.partf_a[0][:, t0:t0 + 512].rearrange("(k p) t -> p k t", p=128)).then_inc(ld.h, 16)
                g.dma_start(out=e.tr5[:, :].rearrange("p (k t) -> p k t", k=8),
                            in_=e.parto_a[1][:, t0:t0 + 512].rearrange("(k p) t -> p k t", p=128)).then_inc(ld.h, 16)
                for rr in range(1, 2 * NC - 2):
                    r = rr + 1
                    g.wait_ge(vV.h, nsum + rr)
                    if r < NC:
                        g.dma_start(out=e.tr5[:, :].rearrange("p (k t) -> p k t", k=8),
                                    in_=e.parto_a[r][:, t0:t0 + 512].rearrange("(k p) t -> p k t", p=128)).then_inc(ld.h, 16)
                    else:
                        g.dma_start(out=e.tr5[:, :].rearrange("p (k t) -> p k t", k=8),
                                    in_=e.partf_a[r - NC + 1][:, t0:t0 + 512].rearrange("(k p) t -> p k t", p=128)).then_inc(ld.h, 16)
        p.g(cld)
        if fused:
            nld0 = ld.inc(64)
            nld = nld0
        else:
            nld0 = ld.inc(80)
            nld = nld0
            nlds = [ld.inc(16) for _ in range(2 * NC - 3)]

            for rr in range(1, 2 * NC - 1):
                def psum_add(v, e, rr=rr, nl=(nld0 if rr == 1 else nlds[rr - 2])):
                    v.wait_ge(ld.h, nl)
                    dst = e.orr if rr < NC else e.frr
                    v.tensor_add(dst[:, :], dst[:, :], e.tr5[:, :])
                    v.drain()
                    v.engine_nop().then_inc(vV.h, 1)
                p.v(psum_add)
                vV.inc()

        def yb(v, e, nld=nld, nini=nini):
            v.wait_ge(ld.h, nld)
            v.wait_ge(ini.h, nini)
            for k in range(8):
                v.tensor_scalar(e.tmp[:, :], e.orr[:, k * 512:(k + 1) * 512],
                                e.bo_s[:, k:k + 1], None, ALU.add)
                v.drain()
                v.tensor_add(e.yr[:, k * 512:(k + 1) * 512], e.xnr[:, k * 512:(k + 1) * 512],
                             e.tmp[:, :])
                v.drain()
            v.engine_nop().then_inc(vV.h, 1)
        p.v(yb)
        nyb = vV.inc()

        def sqf(s, e, nyb=nyb):
            s.wait_ge(vV.h, nyb)
            for k in range(8):
                a = s.activation(e.sq[:, k * 512:(k + 1) * 512], e.yr[:, k * 512:(k + 1) * 512],
                                 AF.Square)
            a.then_inc(sS.h, 1)
        p.s(sqf)
        nsq = sS.inc()

        def mmf(t, e, c=c, nsq=nsq, nvs=vV.n):
            t.wait_ge(sS.h, nsq)
            if c > 0:
                t.wait_ge(vV.h, nvs)
            for k in range(8):
                t.matmul(e.psA[:, :], e.w1024[:, :], e.yr[:, k * 512:(k + 1) * 512],
                         start=(k == 0), stop=(k == 7))
            for k in range(8):
                mm = t.matmul(e.psB[:, :], e.w1024[:, :], e.sq[:, k * 512:(k + 1) * 512],
                              start=(k == 0), stop=(k == 7))
            mm.then_inc(tP.h, 1)
        p.t(mmf)
        nmm = tP.inc()

        def stv(v, e, nmm=nmm):
            v.wait_ge(tP.h, nmm)
            v.tensor_copy(e.mean[:, :], e.psA[:, :])
            v.drain()
            v.tensor_mul(e.m2[:, :], e.mean[:, :], e.mean[:, :])
            v.drain()
            v.tensor_sub(e.var[:, :], e.psB[:, :], e.m2[:, :])
            v.engine_nop().then_inc(vV.h, 1)
        p.v(stv)
        nvs = vV.inc()

        def sts(s, e, nvs=nvs):
            s.wait_ge(vV.h, nvs)
            s.activation(e.lnv[:, :], e.var[:, :], AF.Ln, bias=e.ceps[:, :])
            s.drain()
            s.activation(e.rstd[:, :], e.lnv[:, :], AF.Exp, scale=-0.5).then_inc(sS.h, 1)
        p.s(sts)
        nss = sS.inc()

        def bcm(t, e, c=c, nss=nss, nxd=vV.n):
            t.wait_ge(sS.h, nss)
            if c > 0:
                t.wait_ge(vV.h, nxd)
            t.matmul(e.pm[:, :], e.ones1[:, :], e.mean[:, :], start=True, stop=True)
            t.matmul(e.pr[:, :], e.ones1[:, :], e.rstd[:, :], start=True, stop=True).then_inc(tP.h, 1)
        p.t(bcm)
        nbc = tP.inc()

        def fino(v, e, c=c, nbc=nbc):
            v.wait_ge(tP.h, nbc)
            for k in range(8):
                v.tensor_sub(e.tmp[:, :], e.yr[:, k * 512:(k + 1) * 512], e.pm[:, :])
                v.drain()
                v.tensor_mul(e.tmp[:, :], e.tmp[:, :], e.pr[:, :])
                v.drain()
                v.tensor_scalar(e.a1r[:, k * 512:(k + 1) * 512], e.tmp[:, :],
                                e.gam_s[:, k:k + 1], e.bet_s[:, k:k + 1], ALU.mult, ALU.add)
                v.drain()
            for k in range(8):
                v.tensor_add(e.tmp[:, :], e.x1r[:, k * 512:(k + 1) * 512],
                             e.a1r[:, k * 512:(k + 1) * 512])
                v.drain()
                v.tensor_add(e.oro[:, k * 512:(k + 1) * 512], e.tmp[:, :],
                             e.frr[:, k * 512:(k + 1) * 512])
                v.drain()
            v.engine_nop().then_inc(vV.h, 1)
        p.v(fino)
        nfo = vV.inc()

        def sto(sy, e, t0=t0, nfo=nfo):
            sy.wait_ge(vV.h, nfo)
            sy.dma_start(out=e.out_o[:, t0:t0 + 512].rearrange("(k p) t -> p k t", p=128),
                         in_=e.oro[:, :].rearrange("p (k t) -> p k t", k=8)).then_inc(st.h, 16)
        p.y(sto)
        st.inc(16)

    def fin(sy, e, n=st.n):
        sy.wait_ge(st.h, n)
    p.y(fin)
    return p



def make_fused(T):
    TSL = T // NC
    cap = max(int(np.ceil(T / 8 * 1.25)), 4)
    p = P()
    RG = [list(range(NC))]

    for nm, sh, dt_ in [
        ("L_xn0s", (D, TSL), BF16), ("G_xn0", (NC, D, TSL), BF16),
        ("L_cnt", (1, 8), F32), ("G_cnt", (NC, 8), F32),
        ("L_lg", (TSL, 9), F32), ("L_rw", (TSL, 8), F32), ("L_mask", (TSL, 8), F32),
        ("L_gws", (9, TSL), BF16), ("G_gw", (NC, 9, TSL), BF16),
        ("L_u", (256, T), BF16), ("L_zs", (256, T), BF16),
        ("L_xpp", (129, T), F32), ("G_xpr", (129, T), F32),
        ("L_ym", (256, T), BF16),
        ("L_p0", (NC, D, TSL), F32), ("R_p0", (D, TSL), F32),
        ("L_x1", (D, TSL), F32), ("L_xn1f", (D, TSL), F32),
        ("L_xn1s", (D, TSL), BF16), ("G_xn1", (NC, D, TSL), BF16),
        ("L_kc", (64, TSL), BF16), ("G_kc", (NC, 64, TSL), BF16),
        ("L_vc", (64, TSL), BF16), ("G_vc", (NC, 64, TSL), BF16),
        ("L_sc", (8, TSL), BF16), ("G_sc", (NC, 8, TSL), BF16),
        ("L_scdump", (TSL, 8), F32),
        ("L_po", (NC, D, TSL), F32), ("R_po", (D, TSL), F32),
        ("L_pf", (NC, D, TSL), F32), ("R_pf", (D, TSL), F32),
    ]:
        p.dint(nm, sh, dt_)

    p.dram_override = {
        "a_xn0b": "L_xn0s", "a_lgo": "L_lg", "a_rwo": "L_rw", "a_masko": "L_mask",
        "a_cnto": "L_cnt",
        "b_lgi": "L_lg", "b_rwi": "L_rw", "b_maski": "L_mask", "b_cntall": "G_cnt",
        "b_gwfm_o": "L_gws",
        "c_xn0f": "G_xn0", "c_u_o": "L_u", "c_zs_o": "L_zs", "c_xpp_o": "L_xpp",
        "d_u_i": "L_u", "d_zs_i": "L_zs", "d_xpr": "G_xpr", "d_ym_o": "L_ym",
        "d_ym_i": "L_ym", "d_xn0f": "G_xn0", "d_gw_fm": "G_gw", "d_part0": "L_p0",
        "f_part0a": "R_p0", "f_x1_o": "L_x1", "f_xn1b_o": "L_xn1s", "f_xn1f_o": "L_xn1f",
        "f_kc_o": "L_kc", "f_vc_o": "L_vc", "f_scfm_o": "L_sc", "f_sc_o": "L_scdump",
        "g_xn1f": "G_xn1", "g_kcf": "G_kc", "g_vcf": "G_vc", "g_scfm": "G_sc",
        "g_parto": "L_po", "g_partf": "L_pf",
        "h_parto_a": "R_po", "h_partf_a": "R_pf", "h_x1fm": "L_x1", "h_xn1fm": "L_xn1f",
    }

    cc = p.sem("cc", shared=True)

    def barrier(prev_ns, colls, stname="st"):
        stp = p.sem_by_name.get(prev_ns + stname)
        nst = stp.n if stp else 0
        ncc0 = cc.n
        cc.inc(len(colls))
        ncc1 = cc.n
        saved_ns = p.ns
        p.ns = ""

        def bar(g, e, stp=stp, nst=nst, colls=tuple(colls), ncc1=ncc1):
            if stp is not None:
                g.wait_ge(stp.h, nst)
            for kind, op, i, o in colls:
                iap = e[i].opt()
                oap = e[o].opt()
                if kind == "ReduceScatter":
                    iap = e[i].rearrange("s d t -> (s d) t").opt()
                g.collective_compute(kind, op, [list(range(NC))],
                                     [iap], [oap]).then_inc(cc.h, 1)
            if colls:
                g.wait_ge(cc.h, ncc1)
        p.g(bar)
        p.ns = saved_ns

    BP = ALU.bypass
    AD = ALU.add
    p.ns = "a_"
    make_p0(TSL, p=p)
    barrier("a_", [("AllGather", BP, "L_xn0s", "G_xn0"),
                   ("AllGather", BP, "L_cnt", "G_cnt")])
    p.ns = "b_"
    make_p0b(TSL, cap, p=p)
    barrier("b_", [("AllGather", BP, "L_gws", "G_gw")])
    p.ns = "c_"
    make_p1(T, p=p)
    barrier("c_", [("AllReduce", AD, "L_xpp", "G_xpr")])
    p.ns = "d_"
    make_p2a(T, p=p, fused=True, moe=True)
    barrier("d_", [("ReduceScatter", AD, "L_p0", "R_p0")], stname="st2")
    p.ns = "f_"
    make_p3(TSL, p=p, fused=True)
    barrier("f_", [("AllGather", BP, "L_xn1s", "G_xn1"),
                   ("AllGather", BP, "L_kc", "G_kc"),
                   ("AllGather", BP, "L_vc", "G_vc"),
                   ("AllGather", BP, "L_sc", "G_sc")])
    p.ns = "g_"
    make_p4(T, p=p)
    barrier("g_", [("ReduceScatter", AD, "L_po", "R_po"),
                   ("ReduceScatter", AD, "L_pf", "R_pf")])
    p.ns = "h_"
    make_p5(TSL, p=p, fused=True)
    p.ns = ""
    return p


FUSED_PHASES = ["a_", "", "b_", "c_", "d_", "f_", "g_", "h_"]


_TOL_T = 16384


def _runs(p, in_maps):
    nc = build_bass(p)
    return run_bass_kernel_spmd(nc, in_maps, core_ids=list(range(NC))).results


def kernel_fused(**w):
    T = T_FULL
    TSL = T // NC
    F = np.float32
    C = np.ascontiguousarray

    x = np.asarray(w["L0_attn_x"])[0]
    xfm = C(x.T)
    gwT = C(np.asarray(w["L0_ffn_gate_w"]).T)
    tril = C(np.tril(np.ones((128, 128), F)).T)
    eye = np.eye(128, dtype=F)
    utri = np.triu(np.ones((128, 128), F))
    sel8 = np.zeros((8, 8 * 128), F)
    for e_ in range(8):
        sel8[e_, e_ * 128:(e_ + 1) * 128] = 1.0
    selgB, selR, sel64q = p2a_consts()
    ipw = np.asarray(w["L0_attn_in_proj_w"])
    cw = np.asarray(w["L0_attn_conv_w"])[:, 0, :]
    cb = np.asarray(w["L0_attn_conv_b"])
    xpw = np.asarray(w["L0_attn_x_proj_w"])
    A_log = np.asarray(w["L0_attn_A_log"])
    dt_w = np.asarray(w["L0_attn_dt_w"])[0]
    dt_b = np.asarray(w["L0_attn_dt_b"])
    Dv = np.asarray(w["L0_attn_D"])
    Wk = np.asarray(w["L1_attn_Wk"]); bk = np.asarray(w["L1_attn_bk"])
    Wv = np.asarray(w["L1_attn_Wv"]); bv = np.asarray(w["L1_attn_bv"])
    Wc = np.asarray(w["L1_attn_Wc"]); Wg1 = np.asarray(w["L1_ffn_Wg"])
    Wq = np.asarray(w["L1_attn_Wq"]); bq = np.asarray(w["L1_attn_bq"])
    We = np.asarray(w["L1_attn_We"]); Wo = np.asarray(w["L1_attn_Wo"])
    W1 = np.asarray(w["L1_ffn_W1"]); b1 = np.asarray(w["L1_ffn_b1"])
    W2 = np.asarray(w["L1_ffn_W2"]); b2 = np.asarray(w["L1_ffn_b2"])
    bo = np.asarray(w["L1_attn_bo"])
    gam = np.asarray(w["L1_attn_gamma"]); bet = np.asarray(w["L1_attn_beta"])

    in_maps = []
    for i in range(NC):
        h = i // 2
        hsl = slice(256 * i, 256 * (i + 1))
        wop, wgu, wdn, wsgu, wsdn = p2b_weights(
            np.asarray(w["L0_attn_out_proj_w"]), np.asarray(w["L0_ffn_dyn_gate_w"]),
            np.asarray(w["L0_ffn_dyn_up_w"]), np.asarray(w["L0_ffn_dyn_down_w"]),
            np.asarray(w["L0_ffn_sh_gate_w"]), np.asarray(w["L0_ffn_sh_up_w"]),
            np.asarray(w["L0_ffn_sh_down_w"]), i)
        w1t = np.stack([tile_kpm(W1[e_][:, hsl]) for e_ in range(8)])
        w2t = np.stack([tile_kpm(W2[e_][hsl, :]) for e_ in range(8)])
        in_maps.append({
            "a_xfm": C(xfm[:, i * TSL:(i + 1) * TSL]), "a_gwT": gwT,
            "b_pmask": (np.arange(NC) < i).astype(F).reshape(NC, 1),
            "b_tril": tril, "b_eyeb": eye,
            "c_wxs": C(ipw[:, 256 * i:256 * (i + 1)]),
            "c_wz": C(ipw[:, DI + 256 * i:DI + 256 * (i + 1)]),
            "c_cwc": C(cw[256 * i:256 * (i + 1)].reshape(2, 128, 4).transpose(1, 0, 2).reshape(128, 8)),
            "c_cbc": C(cb[256 * i:256 * (i + 1)].reshape(2, 128).T),
            "c_xpw": C(xpw[256 * i:256 * (i + 1)]),
            "d_A_lg": p2a_A_layout(A_log[256 * i:256 * (i + 1)]),
            "d_dtw_c": C(dt_w[256 * i:256 * (i + 1)].reshape(2, 128).T),
            "d_dtb_c": C(dt_b[256 * i:256 * (i + 1)].reshape(2, 128).T),
            "d_D_c": C(Dv[256 * i:256 * (i + 1)].reshape(2, 128).T),
            "d_selgB": selgB, "d_selR": selR, "d_sel64q": sel64q,
            "d_wop": wop, "d_wgu": wgu, "d_wdn": wdn, "d_wsgu": wsgu, "d_wsdn": wsdn,
            "f_x0fm": C(xfm[:, i * TSL:(i + 1) * TSL]),
            "f_wk": tile_kpm(Wk), "f_bkc": C(bk.reshape(2, 128).T),
            "f_wv": tile_kpm(Wv), "f_bvc": C(bv.reshape(2, 128).T),
            "f_wc2": tile_kpm(Wc), "f_wg1": tile_kpm(Wg1), "f_eye": eye,
            "g_wq": tile_kpm(Wq[:, 128 * i:128 * (i + 1)]),
            "g_bq_c": C(bq[128 * i:128 * (i + 1)].reshape(128, 1)),
            "g_we_h": C(We[:, 64 * h:64 * (h + 1)]),
            "g_wo_sl": C(Wo[128 * i:128 * (i + 1), :]),
            "g_w1_t": C(w1t.reshape(8, 128, 8, 256).transpose(1, 2, 0, 3).reshape(128, -1)),
            "g_b1_c": C(b1[:, hsl].reshape(8, 2, 128).transpose(2, 0, 1).reshape(128, 16)),
            "g_w2_t": C(w2t.transpose(1, 0, 2).reshape(128, -1)),
            "g_b2_f": C(b2),
            "g_c0v": np.full((8, 1), 1.0 if i == 0 else 0.0, F),
            "g_sel8": sel8, "g_utri": utri, "g_eye": eye,
            "h_bo_c": C(bo.reshape(8, 128).T),
            "h_gam_c": C(gam.reshape(8, 128).T),
            "h_bet_c": C(bet.reshape(8, 128).T),
        })

    pf = make_fused(T)
    nc = build_bass(pf, phase_spans=FUSED_PHASES)
    res = run_bass_kernel_spmd(nc, in_maps, core_ids=list(range(NC))).results
    out = np.concatenate([r["h_out_o"] for r in res], axis=1)
    return np.ascontiguousarray(out.T).reshape(1, T, D).astype(np.float32)


def kernel(**w):
    return kernel_fused(**w)


def kernel_unfused(**w):
    T = T_FULL
    TSL = T // NC
    F = np.float32
    C = np.ascontiguousarray

    x = np.asarray(w["L0_attn_x"])[0]
    xfm = C(x.T)

    gwT = C(np.asarray(w["L0_ffn_gate_w"]).T)
    p0 = make_p0(TSL)
    r0 = _runs(p0, [{"xfm": C(xfm[:, i * TSL:(i + 1) * TSL]), "gwT": gwT}
                    for i in range(NC)])
    xn0f8 = np.stack([r["xn0b"] for r in r0])
    cnt_all = np.concatenate([r["cnto"] for r in r0], axis=0)

    tril = C(np.tril(np.ones((128, 128), F)).T)
    cap = max(int(np.ceil(T / 8 * 1.25)), 4)
    p0b = make_p0b(TSL, cap)
    eye = np.eye(128, dtype=F)
    r0b = _runs(p0b, [{"lgi": r0[i]["lgo"], "rwi": r0[i]["rwo"], "maski": r0[i]["masko"],
                       "cntall": cnt_all, "pmask": (np.arange(NC) < i).astype(F).reshape(NC, 1),
                       "tril": tril, "eyeb": eye} for i in range(NC)])
    gw_fm8 = np.stack([r["gwfm_o"] for r in r0b])

    ipw = np.asarray(w["L0_attn_in_proj_w"])
    cw = np.asarray(w["L0_attn_conv_w"])[:, 0, :]
    cb = np.asarray(w["L0_attn_conv_b"])
    xpw = np.asarray(w["L0_attn_x_proj_w"])
    p1 = make_p1(T)
    r1 = _runs(p1, [{
        "xn0f": xn0f8,
        "wxs": C(ipw[:, 256 * i:256 * (i + 1)]),
        "wz": C(ipw[:, DI + 256 * i:DI + 256 * (i + 1)]),
        "cwc": C(cw[256 * i:256 * (i + 1)].reshape(2, 128, 4).transpose(1, 0, 2).reshape(128, 8)),
        "cbc": C(cb[256 * i:256 * (i + 1)].reshape(2, 128).T),
        "xpw": C(xpw[256 * i:256 * (i + 1)]),
    } for i in range(NC)])
    xpp8 = np.stack([r["xpp_o"] for r in r1])

    A_log = np.asarray(w["L0_attn_A_log"])
    dt_w = np.asarray(w["L0_attn_dt_w"])[0]
    dt_b = np.asarray(w["L0_attn_dt_b"])
    Dv = np.asarray(w["L0_attn_D"])
    selgB, selR, sel64q = p2a_consts()
    p2a = make_p2a(T)
    r2a = _runs(p2a, [{
        "u_i": r1[i]["u_o"], "zs_i": r1[i]["zs_o"], "xpp8": xpp8,
        "A_lg": p2a_A_layout(A_log[256 * i:256 * (i + 1)]),
        "dtw_c": C(dt_w[256 * i:256 * (i + 1)].reshape(2, 128).T),
        "dtb_c": C(dt_b[256 * i:256 * (i + 1)].reshape(2, 128).T),
        "D_c": C(Dv[256 * i:256 * (i + 1)].reshape(2, 128).T),
        "selgB": selgB, "selR": selR, "sel64q": sel64q,
    } for i in range(NC)])

    p2b = make_p2b(T)
    in2b = []
    for i in range(NC):
        wop, wgu, wdn, wsgu, wsdn = p2b_weights(
            np.asarray(w["L0_attn_out_proj_w"]), np.asarray(w["L0_ffn_dyn_gate_w"]),
            np.asarray(w["L0_ffn_dyn_up_w"]), np.asarray(w["L0_ffn_dyn_down_w"]),
            np.asarray(w["L0_ffn_sh_gate_w"]), np.asarray(w["L0_ffn_sh_up_w"]),
            np.asarray(w["L0_ffn_sh_down_w"]), i)
        in2b.append({"ym_i": r2a[i]["ym_o"], "xn0f": xn0f8, "gw_fm": gw_fm8,
                     "wop": wop, "wgu": wgu, "wdn": wdn, "wsgu": wsgu, "wsdn": wsdn})
    r2b = _runs(p2b, in2b)

    part0 = np.stack([r["part0"] for r in r2b])
    Wk = np.asarray(w["L1_attn_Wk"]); bk = np.asarray(w["L1_attn_bk"])
    Wv = np.asarray(w["L1_attn_Wv"]); bv = np.asarray(w["L1_attn_bv"])
    Wc = np.asarray(w["L1_attn_Wc"])
    Wg1 = np.asarray(w["L1_ffn_Wg"])
    p3 = make_p3(TSL)
    r3 = _runs(p3, [{
        "x0fm": C(xfm[:, i * TSL:(i + 1) * TSL]),
        "part0a": C(part0[:, i]),
        "wk": tile_kpm(Wk), "bkc": C(bk.reshape(2, 128).T),
        "wv": tile_kpm(Wv), "bvc": C(bv.reshape(2, 128).T),
        "wc2": tile_kpm(Wc), "wg1": tile_kpm(Wg1), "eye": eye,
    } for i in range(NC)])
    x1f = C(np.concatenate([r["x1_o"] for r in r3], axis=1))
    xn1b8 = np.stack([r["xn1b_o"] for r in r3])
    xn1f = C(np.concatenate([r["xn1f_o"] for r in r3], axis=1))
    kcf8 = np.stack([r["kc_o"] for r in r3])
    vcf8 = np.stack([r["vc_o"] for r in r3])
    scfm8 = np.stack([r["scfm_o"] for r in r3])

    Wq = np.asarray(w["L1_attn_Wq"]); bq = np.asarray(w["L1_attn_bq"])
    We = np.asarray(w["L1_attn_We"]); Wo = np.asarray(w["L1_attn_Wo"])
    W1 = np.asarray(w["L1_ffn_W1"]); b1 = np.asarray(w["L1_ffn_b1"])
    W2 = np.asarray(w["L1_ffn_W2"]); b2 = np.asarray(w["L1_ffn_b2"])
    sel8 = np.zeros((8, 8 * 128), F)
    for e_ in range(8):
        sel8[e_, e_ * 128:(e_ + 1) * 128] = 1.0
    utri = np.triu(np.ones((128, 128), F))
    p4 = make_p4(T)
    in4 = []
    for i in range(NC):
        h = i // 2
        hsl = slice(256 * i, 256 * (i + 1))
        w1t = np.stack([tile_kpm(W1[e_][:, hsl]) for e_ in range(8)])
        w2t = np.stack([tile_kpm(W2[e_][hsl, :]) for e_ in range(8)])
        in4.append({
            "xn1f": xn1b8, "kcf": kcf8, "vcf": vcf8, "scfm": scfm8,
            "wq": tile_kpm(Wq[:, 128 * i:128 * (i + 1)]),
            "bq_c": C(bq[128 * i:128 * (i + 1)].reshape(128, 1)),
            "we_h": C(We[:, 64 * h:64 * (h + 1)]),
            "wo_sl": C(Wo[128 * i:128 * (i + 1), :]),
            "w1_t": C(w1t.reshape(8, 128, 8, 256).transpose(1, 2, 0, 3).reshape(128, -1)),
            "b1_c": C(b1[:, hsl].reshape(8, 2, 128).transpose(2, 0, 1).reshape(128, 16)),
            "w2_t": C(w2t.transpose(1, 0, 2).reshape(128, -1)),
            "b2_f": C(b2),
            "c0v": np.full((8, 1), 1.0 if i == 0 else 0.0, F),
            "sel8": sel8, "utri": utri, "eye": eye,
        })
    r4 = _runs(p4, in4)

    parto = np.stack([r["parto"] for r in r4])
    partf = np.stack([r["partf"] for r in r4])
    bo = np.asarray(w["L1_attn_bo"])
    gam = np.asarray(w["L1_attn_gamma"]); bet = np.asarray(w["L1_attn_beta"])
    p5 = make_p5(TSL)
    r5 = _runs(p5, [{
        "x1fm": C(x1f[:, i * TSL:(i + 1) * TSL]),
        "xn1fm": C(xn1f[:, i * TSL:(i + 1) * TSL]),
        "parto_a": C(parto[:, i]),
        "partf_a": C(partf[:, i]),
        "bo_c": C(bo.reshape(8, 128).T),
        "gam_c": C(gam.reshape(8, 128).T),
        "bet_c": C(bet.reshape(8, 128).T),
    } for i in range(NC)])
    out = np.concatenate([r["out_o"] for r in r5], axis=1)
    return np.ascontiguousarray(out.T).reshape(1, T, D).astype(np.float32)


# version 10
# speedup vs baseline: 1.1042x; 1.0126x over previous
import numpy as np
import concourse.bass as bass
import concourse.mybir as mybir
from concourse.bass_utils import run_bass_kernel_spmd

F32 = mybir.dt.float32
BF16 = mybir.dt.bfloat16
AF = mybir.ActivationFunctionType
ALU = mybir.AluOpType
AX = mybir.AxisListType

NC = 8
T_FULL = 16384
D = 1024
DI = 2048
DS = 64
NEG = -1e38



class Sem:

    def __init__(self, name):
        self.name = name
        self.n = 0
        self.h = None

    def inc(self, k=1):
        self.n += k
        return self.n


class P:

    def __init__(self):
        self.ops = []
        self.sems = []
        self.sem_by_name = {}
        self.sbufs = []
        self.psums = []
        self.drams_in = []
        self.drams_out = []
        self.drams_int = []
        self.ns = ""
        self.dram_override = {}

    def sem(self, name, shared=False):
        key = name if shared else self.ns + name
        if key in self.sem_by_name:
            return self.sem_by_name[key]
        s = Sem(key)
        self.sems.append(s)
        self.sem_by_name[key] = s
        return s

    def sbuf(self, name, shape, dtype):
        self.sbufs.append((self.ns + name, shape, dtype, self.ns))
        return name

    def psum(self, name, shape, dtype=F32):
        self.psums.append((self.ns + name, shape, dtype, self.ns))
        return name

    def _dram(self, lst, kind, name, shape, dtype):
        ov = self.dram_override.get(self.ns + name)
        if ov is not None:
            return name
        lst.append((self.ns + name, shape, dtype))
        return name

    def din(self, name, shape, dtype=F32):
        return self._dram(self.drams_in, "in", name, shape, dtype)

    def dout(self, name, shape, dtype=F32):
        return self._dram(self.drams_out, "out", name, shape, dtype)

    def dint(self, name, shape, dtype=F32):
        return self._dram(self.drams_int, "int", name, shape, dtype)

    def add(self, tag, fn):
        self.ops.append((tag, fn, self.ns))

    def g(self, fn): self.add("g", fn)
    def v(self, fn): self.add("v", fn)
    def s(self, fn): self.add("s", fn)
    def t(self, fn): self.add("t", fn)
    def y(self, fn): self.add("y", fn)


class Env(dict):
    __getattr__ = dict.__getitem__


class NsEnv:

    def __init__(self, e, ns):
        self._e = e
        self._ns = ns

    def __getitem__(self, k):
        ns = self._ns + k
        if ns in self._e:
            return self._e[ns]
        return self._e[k]

    __getattr__ = __getitem__


def build_bass(p: P, phase_spans=None, nc=None):
    import contextlib
    if nc is None:
        nc = bass.Bass("TRN2", target_bir_lowering=False, debug=False)
    e = Env()
    for name, shape, dt_ in p.drams_in:
        e[name] = nc.declare_dram_parameter(name, list(shape), dt_, isOutput=False).ap()
    for name, shape, dt_ in p.drams_out:
        e[name] = nc.declare_dram_parameter(name, list(shape), dt_, isOutput=True).ap()
    for name, shape, dt_ in p.drams_int:
        e[name] = nc.dram_tensor(name, list(shape), dt_).ap()
    for ov_name, target in p.dram_override.items():
        e[ov_name] = e[target]
    with contextlib.ExitStack() as stack:
        if phase_spans is None:
            for name, shape, dt_, ns in p.sbufs:
                e[name] = stack.enter_context(nc.sbuf_tensor(name, list(shape), dt_))
            for name, shape, dt_, ns in p.psums:
                e[name] = stack.enter_context(nc.psum_tensor(name, list(shape), dt_))
        else:
            by_ns_s = {}
            by_ns_p = {}
            for name, shape, dt_, ns in p.sbufs:
                by_ns_s.setdefault(ns, []).append((name, shape, dt_))
            for name, shape, dt_, ns in p.psums:
                by_ns_p.setdefault(ns, []).append((name, shape, dt_))
            for ns in phase_spans:
                with contextlib.ExitStack() as ph:
                    for name, shape, dt_ in by_ns_s.get(ns, []):
                        e[name] = ph.enter_context(nc.sbuf_tensor(name, list(shape), dt_))
                    for name, shape, dt_ in by_ns_p.get(ns, []):
                        e[name] = ph.enter_context(nc.psum_tensor(name, list(shape), dt_))
        for s in p.sems:
            s.h = stack.enter_context(nc.semaphore(s.name))
        block = stack.enter_context(nc.Block())

        ops = p.ops

        def replay(eng, tag):
            for tg, fn, ns in ops:
                if tg == tag:
                    fn(eng, NsEnv(e, ns) if ns else e)

        @block.gpsimd
        def _(g):
            replay(g, "g")

        @block.vector
        def _(v):
            replay(v, "v")

        @block.scalar
        def _(s):
            replay(s, "s")

        @block.tensor
        def _(t):
            replay(t, "t")

        @block.sync
        def _(sy):
            replay(sy, "y")

    return nc



def make_p0(TSL, p=None, fused=False):
    if p is None:
        p = P()
    NCH = TSL // 512
    NW = TSL // 128

    p.din("xfm", (D, TSL), F32)
    p.din("gwT", (D, 9), F32)
    p.dout("xn0b", (D, TSL), BF16)
    p.dout("lgo", (TSL, 9), F32)
    p.dout("rwo", (TSL, 8), F32)
    p.dout("masko", (TSL, 8), F32)
    p.dout("cnto", (1, 8), F32)

    p.sbuf("xr", (128, 8 * TSL), F32)
    p.sbuf("xn", (128, 8 * TSL), BF16)
    p.sbuf("gw9", (128, 8 * 9), BF16)
    p.sbuf("w1024", (128, 1), F32)
    p.sbuf("onesc", (128, 1), F32)
    p.sbuf("ones1", (1, 128), F32)
    p.sbuf("sq", (128, 8 * 512), F32)
    p.sbuf("mean", (1, 512), F32)
    p.sbuf("m2", (1, 512), F32)
    p.sbuf("var", (1, 512), F32)
    p.sbuf("lnv", (1, 512), F32)
    p.sbuf("rstd", (1, 512), F32)
    p.sbuf("ceps", (1, 1), F32)
    p.sbuf("tmp", (128, 512), F32)
    p.sbuf("lgsb", (128, NW * 9), F32)
    p.sbuf("rwsb", (128, NW * 8), F32)
    p.sbuf("msksb", (128, NW * 8), F32)
    p.sbuf("cntsb", (1, 8), F32)
    for nm, w in [("srt", 8), ("esr", 8), ("es2", 8), ("psl", 8), ("cums", 4),
                  ("asrt", 8),
                  ("vld", 4), ("adyn", 8), ("gtm", 8), ("fac", 8), ("nf", 8),
                  ("dmt", 8), ("jm", 8), ("ex", 8), ("kp", 8), ("ee", 8),
                  ("ek", 8), ("rww", 8), ("oh", 8), ("ctr", 8)]:
        p.sbuf(nm, (128, w), F32)
    for nm in ["esum", "rec", "athr", "esm2", "mr", "mrv", "rs", "rr"]:
        p.sbuf(nm, (128, 1), F32)

    p.psum("psA", (1, 512))
    p.psum("psB", (1, 512))
    p.psum("pm", (128, 512))
    p.psum("pr", (128, 512))
    p.psum("lgps", (128, 9))
    p.psum("cntps", (1, 8))

    ld = p.sem("ld")
    ini = p.sem("ini")
    sqs = p.sem("sqs")
    mmst = p.sem("mmst")
    vstat = p.sem("vstat")
    sstat = p.sem("sstat")
    bcmm = p.sem("bcmm")
    xnd = p.sem("xnd")
    rtmm = p.sem("rtmm")
    rtev = p.sem("rtev")
    vsub = p.sem("vsub")
    ssub = p.sem("ssub")
    vrt = p.sem("vrt")
    tcnt = p.sem("tcnt")
    st = p.sem("st")

    nld = ld.inc(32)

    @p.g
    def _(g, e):
        g.dma_start(out=e.xr[:, :].rearrange("p (k t) -> p k t", k=8),
                    in_=e.xfm.rearrange("(k p) t -> p k t", p=128)).then_inc(ld.h, 16)
        g.dma_start(out=e.gw9[:, :].rearrange("p (k e) -> p k e", k=8),
                    in_=e.gwT.rearrange("(k p) e -> p k e", p=128)).then_inc(ld.h, 16)
        g.memset(e.w1024[:, :], 1.0 / 1024.0)
        g.memset(e.onesc[:, :], 1.0)
        g.memset(e.ones1[:, :], 1.0)
        g.memset(e.ceps[:, :], 1e-5)
        g.engine_nop().then_inc(ini.h, 1)
    nini = ini.inc(1)

    for c in range(NCH):
        cs = slice(c * 512, (c + 1) * 512)

        n_sq = [None] * 8
        for k in range(8):
            def sqf(s, e, k=k, c=c, nw=nld, nini=nini, nmm=mmst.n):
                if k == 0:
                    s.wait_ge(ld.h, nw)
                    s.wait_ge(ini.h, nini)
                    if c > 0:
                        s.wait_ge(mmst.h, nmm)
                s.activation(e.sq[:, k * 512:(k + 1) * 512],
                             e.xr[:, k * TSL + c * 512: k * TSL + (c + 1) * 512],
                             AF.Square).then_inc(sqs.h, 1)
            p.s(sqf)
            n_sq[k] = sqs.inc()

        def mmf(t, e, c=c, n_sq=tuple(n_sq), nw=nld, nini=nini, nvs=vstat.n):
            t.wait_ge(ld.h, nw)
            t.wait_ge(ini.h, nini)
            if c > 0:
                t.wait_ge(vstat.h, nvs)
            for k in range(8):
                t.matmul(e.psA[:, :], e.w1024[:, :],
                         e.xr[:, k * TSL + c * 512: k * TSL + (c + 1) * 512],
                         start=(k == 0), stop=(k == 7))
            for k in range(8):
                t.wait_ge(sqs.h, n_sq[k])
                mm = t.matmul(e.psB[:, :], e.w1024[:, :],
                              e.sq[:, k * 512:(k + 1) * 512],
                              start=(k == 0), stop=(k == 7))
            mm.then_inc(mmst.h, 1)
        p.t(mmf)
        nmm = mmst.inc()

        def stv(v, e, nmm=nmm):
            v.wait_ge(mmst.h, nmm)
            v.tensor_copy(e.mean[:, :], e.psA[:, :])
            v.drain()
            v.tensor_mul(e.m2[:, :], e.mean[:, :], e.mean[:, :])
            v.drain()
            v.tensor_sub(e.var[:, :], e.psB[:, :], e.m2[:, :]).then_inc(vstat.h, 1)
        p.v(stv)
        nvs = vstat.inc()

        def sts(s, e, nvs=nvs):
            s.wait_ge(vstat.h, nvs)
            s.activation(e.lnv[:, :], e.var[:, :], AF.Ln, bias=e.ceps[:, :])
            s.drain()
            s.activation(e.rstd[:, :], e.lnv[:, :], AF.Exp, scale=-0.5).then_inc(sstat.h, 1)
        p.s(sts)
        nss = sstat.inc()

        def bcm(t, e, nss=nss, nxd=xnd.n):
            t.wait_ge(sstat.h, nss)
            if c > 0:
                t.wait_ge(xnd.h, nxd)
            t.matmul(e.pm[:, :], e.ones1[:, :], e.mean[:, :], start=True, stop=True)
            t.matmul(e.pr[:, :], e.ones1[:, :], e.rstd[:, :], start=True, stop=True).then_inc(bcmm.h, 1)
        p.t(bcm)
        nbc = bcmm.inc()

        def xnf(v, e, c=c, nbc=nbc):
            v.wait_ge(bcmm.h, nbc)
            for k in range(8):
                v.tensor_sub(e.tmp[:, :], e.xr[:, k * TSL + c * 512: k * TSL + (c + 1) * 512], e.pm[:, :])
                v.drain()
                v.tensor_mul(e.xn[:, k * TSL + c * 512: k * TSL + (c + 1) * 512], e.tmp[:, :], e.pr[:, :])
                v.drain()
            v.engine_nop().then_inc(xnd.h, 1)
        p.v(xnf)
        xnd.inc()

    for w in range(NW):
        c = (w * 128) // 512
        ws = slice(w * 128, (w + 1) * 128)

        def rmm(t, e, w=w, nxd=c + 1, nrtev=rtev.n):
            t.wait_ge(xnd.h, nxd)
            if w > 0:
                t.wait_ge(rtev.h, nrtev)
            for k in range(8):
                mm = t.matmul(e.lgps[:, :], e.xn[:, k * TSL + w * 128: k * TSL + (w + 1) * 128],
                              e.gw9[:, k * 9:(k + 1) * 9], start=(k == 0), stop=(k == 7))
            mm.then_inc(rtmm.h, 1)
        p.t(rmm)
        nrm = rtmm.inc()

        def rev(s, e, w=w, nrm=nrm, nvr=vrt.n):
            s.wait_ge(rtmm.h, nrm)
            if w > 0:
                s.wait_ge(vrt.h, nvr)
            s.copy(e.lgsb[:, w * 9:(w + 1) * 9], e.lgps[:, :]).then_inc(rtev.h, 1)
        p.s(rev)
        nre = rtev.inc()

        def rv1(v, e, w=w, nre=nre):
            dyn = e.lgsb[:, w * 9: w * 9 + 8]
            v.wait_ge(rtev.h, nre)
            v.max(e.srt[:, 0:8], dyn)
            v.drain()
            v.tensor_scalar(e.esr[:, :], e.srt[:, :], e.srt[:, 0:1], None, ALU.subtract).then_inc(vsub.h, 1)
        p.v(rv1)
        nv1 = vsub.inc()

        def rs1(s, e, w=w, nv1=nv1):
            s.wait_ge(vsub.h, nv1)
            s.activation(e.asrt[:, :], e.srt[:, :], AF.Abs)
            s.activation(e.adyn[:, :], e.lgsb[:, w * 9: w * 9 + 8], AF.Abs)
            s.drain()
            s.activation(e.ee[:, :], e.esr[:, :], AF.Exp).then_inc(ssub.h, 1)
        p.s(rs1)
        ns1 = ssub.inc()

        def rv2(v, e, w=w, ns1=ns1):
            dyn = e.lgsb[:, w * 9: w * 9 + 8]
            v.wait_ge(ssub.h, ns1)
            v.reduce_sum(e.esum[:, :], e.ee[:, :], AX.X)
            v.drain()
            v.reciprocal(e.rec[:, :], e.esum[:, :])
            v.drain()
            v.tensor_scalar(e.psl[:, :], e.ee[:, :], e.rec[:, 0:1], None, ALU.mult)
            v.drain()
            v.tensor_add(e.cums[:, 0:1], e.psl[:, 0:1], e.psl[:, 1:2])
            v.drain()
            v.tensor_add(e.cums[:, 1:2], e.cums[:, 0:1], e.psl[:, 2:3])
            v.drain()
            v.tensor_single_scalar(e.vld[:, 1:2], e.psl[:, 0:1], 0.75, ALU.is_lt)
            v.tensor_single_scalar(e.vld[:, 2:3], e.cums[:, 0:1], 0.75, ALU.is_lt)
            v.tensor_single_scalar(e.vld[:, 3:4], e.cums[:, 1:2], 0.75, ALU.is_lt)
            v.memset(e.rww[:, :], 0.0)
            v.engine_nop().then_inc(vsub.h, 1)
        p.v(rv2)
        nv2 = vsub.inc()

        for r in range(4):
            def rva(v, e, w=w, r=r, nprev=(nv2 if r == 0 else ssub.n)):
                dyn = e.lgsb[:, w * 9: w * 9 + 8]
                thr = e.srt[:, r:r + 1]
                if r == 0:
                    v.wait_ge(vsub.h, nprev)
                v.tensor_scalar(e.gtm[:, :], dyn, thr, None, ALU.is_gt)
                v.tensor_scalar(e.fac[:, :], e.adyn[:, :], e.asrt[:, r:r + 1], None, ALU.max)
                v.drain()
                v.tensor_scalar(e.nf[:, :], e.fac[:, :], -0.2, None, ALU.mult)
                v.tensor_scalar(e.dmt[:, :], dyn, thr, None, ALU.subtract)
                v.drain()
                v.tensor_tensor(e.jm[:, :], e.dmt[:, :], e.nf[:, :], ALU.is_lt)
                v.drain()
                v.tensor_tensor(e.ex[:, :], e.gtm[:, :], e.jm[:, :], ALU.max)
                v.drain()
                v.tensor_scalar(e.kp[:, :], e.ex[:, :], -1.0, 1.0, ALU.mult, ALU.add)
                v.engine_nop().then_inc(vsub.h, 1)
            p.v(rva)
            nva = vsub.inc()

            def rsa(s, e, nva=nva):
                s.wait_ge(vsub.h, nva)
                s.activation(e.ee[:, :], e.dmt[:, :], AF.Exp).then_inc(ssub.h, 1)
            p.s(rsa)
            nsa = ssub.inc()

            def rvb(v, e, w=w, r=r, nsa=nsa):
                dyn = e.lgsb[:, w * 9: w * 9 + 8]
                thr = e.srt[:, r:r + 1]
                v.wait_ge(ssub.h, nsa)
                v.tensor_tensor(e.ek[:, :], e.ee[:, :], e.kp[:, :], ALU.mult)
                v.drain()
                v.reduce_sum(e.esm2[:, :], e.ek[:, :], AX.X)
                v.drain()
                v.reciprocal(e.mr[:, :], e.esm2[:, :])
                v.drain()
                if r > 0:
                    v.tensor_mul(e.mrv[:, :], e.mr[:, :], e.vld[:, r:r + 1])
                else:
                    v.tensor_copy(e.mrv[:, :], e.mr[:, :])
                v.tensor_scalar(e.oh[:, :], dyn, thr, None, ALU.is_equal)
                v.drain()
                v.tensor_scalar(e.ctr[:, :], e.oh[:, :], e.mrv[:, 0:1], None, ALU.mult)
                v.drain()
                v.tensor_add(e.rww[:, :], e.rww[:, :], e.ctr[:, :])
                v.drain()
                v.engine_nop().then_inc(vsub.h, 1)
            p.v(rvb)
            vsub.inc()

        def rvf(v, e, w=w):
            v.reduce_sum(e.rs[:, :], e.rww[:, :], AX.X)
            v.drain()
            v.reciprocal(e.rr[:, :], e.rs[:, :])
            v.drain()
            v.tensor_scalar(e.rwsb[:, w * 8:(w + 1) * 8], e.rww[:, :], e.rr[:, 0:1], None, ALU.mult)
            v.drain()
            v.tensor_single_scalar(e.msksb[:, w * 8:(w + 1) * 8],
                                   e.rwsb[:, w * 8:(w + 1) * 8], 0.0, ALU.is_gt)
            v.engine_nop().then_inc(vrt.h, 1)
        p.v(rvf)
        nvrt = vrt.inc()

        def cmm(t, e, w=w, nvrt=nvrt, NW=NW):
            t.wait_ge(vrt.h, nvrt)
            mm = t.matmul(e.cntps[:, :], e.onesc[:, :], e.msksb[:, w * 8:(w + 1) * 8],
                          start=(w == 0), stop=(w == NW - 1))
            if w == NW - 1:
                mm.then_inc(tcnt.h, 1)
        p.t(cmm)
    tcnt.inc()

    def cev(s, e, n=tcnt.n):
        s.wait_ge(tcnt.h, n)
        s.copy(e.cntsb[:, :], e.cntps[:, :]).then_inc(sstat.h, 1)
    p.s(cev)
    ncev = sstat.inc()

    def stf(sy, e, NW=NW, NCH=NCH, ncev=ncev, nvrt=vrt.n, nxd=xnd.n):
        sy.wait_ge(xnd.h, nxd)
        sy.dma_start(out=e.xn0b.rearrange("(k p) t -> p k t", p=128),
                     in_=e.xn[:, :].rearrange("p (k t) -> p k t", k=8)).then_inc(st.h, 16)
        sy.wait_ge(vrt.h, nvrt)
        sy.dma_start(out=e.lgo.rearrange("(w p) e -> p w e", p=128),
                     in_=e.lgsb[:, :].rearrange("p (w e) -> p w e", e=9)).then_inc(st.h, 16)
        sy.dma_start(out=e.rwo.rearrange("(w p) e -> p w e", p=128),
                     in_=e.rwsb[:, :].rearrange("p (w e) -> p w e", e=8)).then_inc(st.h, 16)
        sy.dma_start(out=e.masko.rearrange("(w p) e -> p w e", p=128),
                     in_=e.msksb[:, :].rearrange("p (w e) -> p w e", e=8)).then_inc(st.h, 16)
        sy.wait_ge(sstat.h, ncev)
        sy.dma_start(out=e.cnto, in_=e.cntsb[:, :]).then_inc(st.h, 16)
        sy.wait_ge(st.h, nfin)
    nfin = st.inc(80)
    p.y(stf)

    return p



def make_p0b(TSL, CAP, p=None, fused=False):
    if p is None:
        p = P()
    NW = TSL // 128
    p.din("lgi", (TSL, 9), F32)
    p.din("rwi", (TSL, 8), F32)
    p.din("maski", (TSL, 8), F32)
    p.din("cntall", (NC, 8), F32)
    p.din("pmask", (NC, 1), F32)
    p.din("tril", (128, 128), F32)
    p.din("eyeb", (128, 128), F32)
    p.dout("gwfm_o", (9, TSL), BF16)

    p.sbuf("lgs", (128, NW * 9), F32)
    p.sbuf("rws", (128, NW * 8), F32)
    p.sbuf("msks", (128, NW * 8), F32)
    p.sbuf("cnts", (NC, 8), F32)
    p.sbuf("pms", (NC, 1), F32)
    p.sbuf("trils", (128, 128), F32)
    p.sbuf("ones1", (1, 128), F32)
    p.sbuf("offs", (1, 8), F32)
    p.sbuf("wcnt", (1, 8), F32)
    p.sbuf("capm", (128, 8), F32)
    p.sbuf("pen", (128, 8), F32)
    p.sbuf("lgm", (128, 9), F32)
    p.sbuf("mx", (128, 1), F32)
    p.sbuf("lsh", (128, 9), F32)
    p.sbuf("e9", (128, 9), F32)
    p.sbuf("s9", (128, 1), F32)
    p.sbuf("r9", (128, 1), F32)
    p.sbuf("gw_", (128, 9), F32)
    p.sbuf("sdyn", (128, 1), F32)
    p.sbuf("gd", (128, 8), F32)
    p.sbuf("gws", (128, NW * 9), F32)
    p.sbuf("eyeb_s", (128, 128), BF16)
    p.sbuf("gwb", (128, 9), BF16)
    p.sbuf("gfm", (9, NW * 128), BF16)

    p.psum("ptr9", (9, 128), BF16)
    p.psum("offps", (1, 8))
    p.psum("cumps", (128, 8))
    p.psum("wcps", (1, 8))

    ld = p.sem("ld")
    ini = p.sem("ini")
    mm1 = p.sem("mm1")
    vv = p.sem("vv")
    ss = p.sem("ss")
    tt = p.sem("tt")
    st = p.sem("st")

    @p.g
    def _(g, e):
        g.dma_start(out=e.lgs[:, :].rearrange("p (w e) -> p w e", e=9),
                    in_=e.lgi.rearrange("(w p) e -> p w e", p=128)).then_inc(ld.h, 16)
        g.dma_start(out=e.rws[:, :].rearrange("p (w e) -> p w e", e=8),
                    in_=e.rwi.rearrange("(w p) e -> p w e", p=128)).then_inc(ld.h, 16)
        g.dma_start(out=e.msks[:, :].rearrange("p (w e) -> p w e", e=8),
                    in_=e.maski.rearrange("(w p) e -> p w e", p=128)).then_inc(ld.h, 16)
        g.dma_start(out=e.cnts[:, :], in_=e.cntall).then_inc(ld.h, 16)
        g.dma_start(out=e.pms[:, :], in_=e.pmask).then_inc(ld.h, 16)
        g.dma_start(out=e.trils[:, :], in_=e.tril).then_inc(ld.h, 16)
        g.dma_start(out=e.eyeb_s[:, :], in_=e.eyeb).then_inc(ld.h, 16)
        g.memset(e.ones1[:, :], 1.0)
        g.engine_nop().then_inc(ini.h, 1)
    nld = ld.inc(112)
    nini = ini.inc(1)

    def omm(t, e, nld=nld, nini=nini):
        t.wait_ge(ld.h, nld)
        t.wait_ge(ini.h, nini)
        t.matmul(e.offps[:, :], e.pms[:, :], e.cnts[:, :], start=True, stop=True).then_inc(mm1.h, 1)
    p.t(omm)
    nomm = mm1.inc()

    def oev(v, e, nomm=nomm):
        v.wait_ge(mm1.h, nomm)
        v.tensor_copy(e.offs[:, :], e.offps[:, :]).then_inc(vv.h, 1)
    p.v(oev)
    vv.inc()

    for w in range(NW):
        def wmm(t, e, w=w, nvv=vv.n):
            t.wait_ge(vv.h, nvv)
            t.matmul(e.cumps[:, :], e.trils[:, :], e.msks[:, w * 8:(w + 1) * 8],
                     start=True, stop=False)
            t.matmul(e.cumps[:, :], e.ones1[:, :], e.offs[:, :], start=False, stop=True)
            t.matmul(e.wcps[:, :], e.trils[:, 127:128],
                     e.msks[:, w * 8:(w + 1) * 8], start=True, stop=True).then_inc(tt.h, 1)
        p.t(wmm)
        ntt = tt.inc()

        def wv1(v, e, w=w, ntt=ntt, CAP=CAP):
            v.wait_ge(tt.h, ntt)
            v.tensor_single_scalar(e.capm[:, :], e.cumps[:, :], float(CAP), ALU.is_le)
            v.tensor_copy(e.wcnt[:, :], e.wcps[:, :])
            v.drain()
            v.tensor_mul(e.capm[:, :], e.capm[:, :], e.msks[:, w * 8:(w + 1) * 8])
            v.tensor_add(e.offs[:, :], e.offs[:, :], e.wcnt[:, :])
            v.drain()
            v.tensor_scalar(e.pen[:, :], e.capm[:, :], 1e38, -1e38, ALU.mult, ALU.add)
            v.tensor_copy(e.lgm[:, :], e.lgs[:, w * 9:(w + 1) * 9])
            v.drain()
            v.tensor_add(e.lgm[:, 0:8], e.lgm[:, 0:8], e.pen[:, :])
            v.drain()
            v.reduce_max(e.mx[:, :], e.lgm[:, :], AX.X)
            v.drain()
            v.tensor_scalar(e.lsh[:, :], e.lgm[:, :], e.mx[:, 0:1], None, ALU.subtract)
            v.engine_nop().then_inc(vv.h, 1)
        p.v(wv1)
        nv1 = vv.inc()

        def ws1(s, e, nv1=nv1):
            s.wait_ge(vv.h, nv1)
            s.activation(e.e9[:, :], e.lsh[:, :], AF.Exp).then_inc(ss.h, 1)
        p.s(ws1)
        ns1 = ss.inc()

        def wv2(v, e, w=w, ns1=ns1):
            v.wait_ge(ss.h, ns1)
            v.reduce_sum(e.s9[:, :], e.e9[:, :], AX.X)
            v.drain()
            v.reciprocal(e.r9[:, :], e.s9[:, :])
            v.drain()
            v.tensor_scalar(e.gw_[:, :], e.e9[:, :], e.r9[:, 0:1], None, ALU.mult)
            v.drain()
            v.reduce_sum(e.sdyn[:, :], e.gw_[:, 0:8], AX.X)
            v.drain()
            v.tensor_scalar(e.gd[:, :], e.rws[:, w * 8:(w + 1) * 8], e.sdyn[:, 0:1], None, ALU.mult)
            v.drain()
            v.tensor_mul(e.gws[:, w * 9: w * 9 + 8], e.gd[:, :], e.capm[:, :])
            v.tensor_copy(e.gws[:, w * 9 + 8: w * 9 + 9], e.gw_[:, 8:9])
            v.drain()
            v.tensor_copy(e.gwb[:, :], e.gws[:, w * 9:(w + 1) * 9])
            v.engine_nop().then_inc(vv.h, 1)
        p.v(wv2)
        nwv2 = vv.inc()

        def wtr(t, e, w=w, nwv2=nwv2, nprev=ss.n):
            t.wait_ge(vv.h, nwv2)
            if w > 0:
                t.wait_ge(ss.h, nprev)
            t.transpose(e.ptr9[:, :], e.gwb[:, :], e.eyeb_s[:, :]).then_inc(tt.h, 1)
        p.t(wtr)
        nwtr = tt.inc()

        def wtre(s, e, w=w, nwtr=nwtr):
            s.wait_ge(tt.h, nwtr)
            s.copy(e.gfm[:, w * 128:(w + 1) * 128], e.ptr9[:, :]).then_inc(ss.h, 1)
        p.s(wtre)
        ss.inc()

    def stf(sy, e, nvv=vv.n, nss=ss.n):
        sy.wait_ge(ss.h, nss)
        sy.dma_start(out=e.gwfm_o.rearrange("a (w t) -> a w t", t=128),
                     in_=e.gfm[:, :].rearrange("a (w t) -> a w t", t=128)).then_inc(st.h, 16)
        sy.wait_ge(st.h, nfin)
    nfin = st.inc(16)
    p.y(stf)

    return p




def make_p1(T, p=None, fused=False):
    if p is None:
        p = P()
    NCH = T // 512
    TSL = T // NC if T >= NC * 512 else T
    NSL = T // TSL

    p.din("xn0f", (NSL, D, TSL), BF16)
    p.din("wxs", (D, 256), F32)
    p.din("wz", (D, 256), F32)
    p.din("cwc", (128, 2 * 4), F32)
    p.din("cbc", (128, 2), F32)
    p.din("xpw", (256, 129), F32)
    p.dout("u_o", (256, T), BF16)
    p.dout("zs_o", (256, T), BF16)
    p.dout("xpp_o", (129, T), F32)

    p.sbuf("wxs_s", (128, 8 * 256), BF16)
    p.sbuf("cw_s", (128, 8), F32)
    p.sbuf("cb_s", (128, 2), F32)
    p.sbuf("wz_s", (128, 8 * 256), BF16)
    p.sbuf("xpj_s", (128, 2 * 129), BF16)
    p.sbuf("xnc", (128, 8 * 512), BF16)
    p.sbuf("xs0", (128, 515), F32)
    p.sbuf("xs1", (128, 515), F32)
    p.sbuf("xc", (128, 512), F32)
    p.sbuf("u0", (128, 512), BF16)
    p.sbuf("u1", (128, 512), BF16)
    p.sbuf("z0", (128, 512), BF16)
    p.sbuf("z1", (128, 512), BF16)
    p.sbuf("xpa", (128, 512), F32)
    p.sbuf("xpb", (1, 512), F32)

    p.psum("pxs0", (128, 512))
    p.psum("pxs1", (128, 512))
    p.psum("pz0", (128, 512))
    p.psum("pz1", (128, 512))
    p.psum("pxp0", (128, 512))
    p.psum("pxp1", (1, 512))

    ld = p.sem("ld")
    ini = p.sem("ini")
    mmxz = p.sem("mmxz")
    sev = p.sem("sev")
    cvv = p.sem("cvv")
    mmxp = p.sem("mmxp")
    xpe = p.sem("xpe")
    st = p.sem("st")

    @p.g
    def _(g, e):
        g.dma_start(out=e.wxs_s[:, :].rearrange("p (k m) -> p k m", k=8),
                    in_=e.wxs.rearrange("(k p) m -> p k m", p=128)).then_inc(ld.h, 16)
        g.dma_start(out=e.wz_s[:, :].rearrange("p (k m) -> p k m", k=8),
                    in_=e.wz.rearrange("(k p) m -> p k m", p=128)).then_inc(ld.h, 16)
        g.dma_start(out=e.xpj_s[:, :].rearrange("p (k j) -> p k j", k=2),
                    in_=e.xpw.rearrange("(k p) j -> p k j", p=128)).then_inc(ld.h, 16)
        g.dma_start(out=e.cw_s[:, :], in_=e.cwc).then_inc(ld.h, 16)
        g.dma_start(out=e.cb_s[:, :], in_=e.cbc).then_inc(ld.h, 16)
        g.memset(e.xs0[:, 0:3], 0.0)
        g.memset(e.xs1[:, 0:3], 0.0)
        g.engine_nop().then_inc(ini.h, 1)
    nwld = ld.inc(80)
    nini = ini.inc()

    for c in range(NCH):
        t0 = c * 512

        def gld(g, e, c=c, t0=t0, nmm=mmxz.n, TSL=TSL):
            if c > 0:
                g.wait_ge(mmxz.h, nmm)
            g.dma_start(out=e.xnc[:, :].rearrange("p (k t) -> p k t", k=8),
                        in_=e.xn0f.rearrange("s d t -> (s d) t")[
                            (t0 // TSL) * D:(t0 // TSL) * D + D, t0 % TSL:t0 % TSL + 512]
                        .rearrange("(k p) t -> p k t", p=128)).then_inc(ld.h, 16)
        p.g(gld)
        nld = ld.inc(16)

        def mf(t, e, c=c, nld=nld, nwld=nwld, nsev=sev.n, nini=nini):
            t.wait_ge(ld.h, nld)
            if c == 0:
                t.wait_ge(ini.h, nini)
            if c > 0:
                t.wait_ge(sev.h, nsev)
            for m in range(2):
                ps = e.pxs0 if m == 0 else e.pxs1
                for k in range(8):
                    mm = t.matmul(ps[:, :], e.wxs_s[:, k * 256 + m * 128: k * 256 + (m + 1) * 128],
                                  e.xnc[:, k * 512:(k + 1) * 512], start=(k == 0), stop=(k == 7))
            for m in range(2):
                ps = e.pz0 if m == 0 else e.pz1
                for k in range(8):
                    mm = t.matmul(ps[:, :], e.wz_s[:, k * 256 + m * 128: k * 256 + (m + 1) * 128],
                                  e.xnc[:, k * 512:(k + 1) * 512], start=(k == 0), stop=(k == 7))
            mm.then_inc(mmxz.h, 1)
        p.t(mf)
        nmm = mmxz.inc()

        def sevac(s, e, c=c, nmm=nmm, ncv=cvv.n, nxp=mmxp.n, nst=st.n):
            s.wait_ge(mmxz.h, nmm)
            if c > 0:
                s.wait_ge(cvv.h, ncv)
                s.wait_ge(mmxp.h, nxp)
                s.wait_ge(st.h, nst)
            s.copy(e.xs0[:, 3:515], e.pxs0[:, :])
            s.copy(e.xs1[:, 3:515], e.pxs1[:, :])
            s.copy(e.z0[:, :], e.pz0[:, :])
            s.drain()
            s.activation(e.z0[:, :], e.pz0[:, :], AF.Silu)
            s.activation(e.z1[:, :], e.pz1[:, :], AF.Silu).then_inc(sev.h, 1)
        p.s(sevac)
        nse = sev.inc()

        for j in range(2):
            def cvf(v, e, j=j, nse=nse, nse2=sev.n):
                xs = e.xs0 if j == 0 else e.xs1
                if j == 0:
                    v.wait_ge(sev.h, nse)
                else:
                    v.wait_ge(sev.h, nse2)
                v.tensor_scalar(e.xc[:, :], xs[:, 0:512], e.cw_s[:, j * 4: j * 4 + 1], None, ALU.mult)
                v.drain()
                for k in range(1, 4):
                    v.scalar_tensor_tensor(e.xc[:, :], xs[:, k:k + 512],
                                           e.cw_s[:, j * 4 + k: j * 4 + k + 1], e.xc[:, :],
                                           ALU.mult, ALU.add)
                    v.drain()
                v.engine_nop().then_inc(cvv.h, 1)
            p.v(cvf)
            ncv = cvv.inc()

            def suf(s, e, j=j, ncv=ncv):
                s.wait_ge(cvv.h, ncv)
                u = e.u0 if j == 0 else e.u1
                s.activation(u[:, :], e.xc[:, :], AF.Silu, bias=e.cb_s[:, j:j + 1]).then_inc(sev.h, 1)
            p.s(suf)
            sev.inc()

            def chal(v, e, j=j):
                xs = e.xs0 if j == 0 else e.xs1
                v.tensor_copy(xs[:, 0:3], xs[:, 512:515])
                v.engine_nop().then_inc(cvv.h, 1)
            p.v(chal)
            cvv.inc()

        def xpf(t, e, c=c, nse=sev.n, nxpe=xpe.n):
            t.wait_ge(sev.h, nse)
            if c > 0:
                t.wait_ge(xpe.h, nxpe)
            for kt in range(2):
                u = e.u0 if kt == 0 else e.u1
                t.matmul(e.pxp0[:, :], e.xpj_s[:, kt * 129: kt * 129 + 128], u[:, :],
                         start=(kt == 0), stop=(kt == 1))
            for kt in range(2):
                u = e.u0 if kt == 0 else e.u1
                mm = t.matmul(e.pxp1[:, :], e.xpj_s[:, kt * 129 + 128: kt * 129 + 129], u[:, :],
                              start=(kt == 0), stop=(kt == 1))
            mm.then_inc(mmxp.h, 1)
        p.t(xpf)
        nxp = mmxp.inc()

        def xpev(v, e, nxp=nxp, nst=st.n):
            v.wait_ge(mmxp.h, nxp)
            if nst > 0:
                v.wait_ge(st.h, nst)
            v.tensor_copy(e.xpa[:, :], e.pxp0[:, :])
            v.tensor_copy(e.xpb[:, :], e.pxp1[:, :])
            v.engine_nop().then_inc(xpe.h, 1)
        p.v(xpev)
        nxpe = xpe.inc()

        def stf(sy, e, c=c, t0=t0, nxpe=nxpe, nse=sev.n, nst0=st.n):
            sy.wait_ge(sev.h, nse)
            sy.dma_start(out=e.u_o[0:128, t0:t0 + 512], in_=e.u0[:, :]).then_inc(st.h, 16)
            sy.dma_start(out=e.u_o[128:256, t0:t0 + 512], in_=e.u1[:, :]).then_inc(st.h, 16)
            sy.dma_start(out=e.zs_o[0:128, t0:t0 + 512], in_=e.z0[:, :]).then_inc(st.h, 16)
            sy.dma_start(out=e.zs_o[128:256, t0:t0 + 512], in_=e.z1[:, :]).then_inc(st.h, 16)
            sy.wait_ge(xpe.h, nxpe)
            sy.dma_start(out=e.xpp_o[0:128, t0:t0 + 512], in_=e.xpa[:, :]).then_inc(st.h, 16)
            sy.dma_start(out=e.xpp_o[128:129, t0:t0 + 512], in_=e.xpb[:, :]).then_inc(st.h, 16)
        p.y(stf)
        st.inc(96)

    def fin(sy, e, n=st.n):
        sy.wait_ge(st.h, n)
    p.y(fin)
    return p



def make_p2a(T, p=None, fused=False, moe=False):
    if p is None:
        p = P()
    TSL = T // NC
    CH = min(1024, T)
    NK = T // CH
    NW5 = CH // 512
    H = CH // 2

    p.din("u_i", (256, T), BF16)
    p.din("zs_i", (256, T), BF16)
    if fused:
        p.din("xpr", (129, T), F32)
    else:
        p.din("xpp8", (NC, 129, T), F32)
    p.din("A_lg", (128, 128), F32)
    p.din("dtw_c", (128, 2), F32)
    p.din("dtb_c", (128, 2), F32)
    p.din("D_c", (128, 2), F32)
    p.din("selgB", (128, 16 * 128), F32)
    p.din("selR", (128, 16 * 128), F32)
    p.din("sel64q", (64, 4 * 128), F32)
    p.dout("ym_o", (256, T), BF16)

    p.sbuf("selgB_s", (128, 16 * 128), BF16)
    p.sbuf("selR_s", (128, 16 * 128), BF16)
    p.sbuf("sel64q_s", (64, 4 * 128), BF16)
    p.sbuf("A_s", (128, 128), F32)
    p.sbuf("Aex", (128, 128), F32)
    p.sbuf("dtw_s", (128, 2), F32)
    p.sbuf("dtb_s", (128, 2), F32)
    p.sbuf("D_s", (128, 2), F32)
    p.sbuf("ones1", (1, 128), F32)
    p.sbuf("cone", (128, 1), F32)
    p.sbuf("carr", (128, 128), F32)
    p.sbuf("Bc", (64, CH), BF16)
    p.sbuf("Cc", (64, CH), BF16)
    if not fused:
        p.sbuf("xq", (128, CH), F32)
        p.sbuf("xq2", (128, CH), F32)
    p.sbuf("dr2", (8, CH), F32)
    p.sbuf("ones8", (8, 128), F32)
    for q in range(4):
        p.sbuf("Bb%d" % q, (128, CH), BF16)
        p.sbuf("Cb%d" % q, (128, CH), BF16)
    p.sbuf("pre", (128, CH), F32)
    p.sbuf("sln", (128, CH), F32)
    p.sbuf("srl", (128, CH), F32)
    for j in range(2):
        p.sbuf("dt%d" % j, (128, CH), BF16)
        p.sbuf("dtu%d" % j, (128, CH), BF16)
        p.sbuf("uc%d" % j, (128, CH), BF16)
        p.sbuf("zc%d" % j, (128, CH), BF16)
    p.sbuf("dub0", (128, CH), BF16)
    p.sbuf("dub1", (128, CH), BF16)
    for b in range(2):
        p.sbuf("dA%d" % b, (128, CH), F32)
        p.sbuf("bq%d" % b, (128, CH), BF16)
        p.sbuf("state%d" % b, (128, CH), F32)
        p.sbuf("cs%d" % b, (128, CH), BF16)
    p.sbuf("du", (128, CH), F32)
    p.sbuf("yf", (128, CH), F32)
    p.sbuf("ym", (128, CH), BF16)

    p.psum("pp1", (128, CH))
    p.psum("pp2", (128, CH))

    if moe:
        NCH = T // 512
        p.din("ym_i", (256, T), BF16)
        p.din("xn0f", (NC, D, TSL), BF16)
        p.din("gw_fm", (NC, 9, TSL), BF16)
        p.din("wop", (128, 2 * D), F32)
        p.din("wgu", (128, 8 * 8 * 128), F32)
        p.din("wdn", (64, 8 * D), F32)
        p.din("wsgu", (128, 8 * 512), F32)
        p.din("wsdn", (128, 2 * D), F32)
        p.dout("part0", (NC, D, TSL), F32)
        p.sbuf("wop_s", (128, 2 * D), BF16)
        p.sbuf("wgu_s", (128, 8 * 8 * 128), BF16)
        p.sbuf("wdn_s", (64, 8 * D), BF16)
        p.sbuf("wsgu_s", (128, 8 * 512), BF16)
        p.sbuf("wsdn_s", (128, 2 * D), BF16)
        p.sbuf("xnc", (128, 8 * 512), BF16)
        p.sbuf("ymc", (128, 2 * 512), BF16)
        p.sbuf("gA", (128, 512), BF16)
        p.sbuf("gB", (128, 512), BF16)
        p.sbuf("gC", (128, 512), BF16)
        p.sbuf("gs64", (64, 512), F32)
        p.sbuf("hwu", (64, 512), F32)
        p.sbuf("gs128", (128, 512), F32)
        p.sbuf("hwu2", (128, 512), F32)
        p.sbuf("hsb", (64, 8 * 512), BF16)
        p.sbuf("hsh", (128, 2 * 512), BF16)
        p.sbuf("po", (128, 4 * 512), F32)
        p.sbuf("onesw", (128, 128), BF16)
        p.psum("ph", (128, 512))
        p.psum("ph2", (128, 512))
        p.psum("pwb", (128, 512))
        p.psum("pout", (128, 512))
        ld2 = p.sem("ld2")
        tP2 = p.sem("tP2")
        sS2 = p.sem("sS2")
        vV2 = p.sem("vV2")
        st2 = p.sem("st2")

    ld = p.sem("ld")
    ini = p.sem("ini")
    ac = p.sem("ac")
    tP = p.sem("tP")
    sS = p.sem("sS")
    vV = p.sem("vV")
    vS = p.sem("vS")
    vC = p.sem("vC")
    st = p.sem("st")
    dAb = ["dA0", "dA1"]
    bqb = ["bq0", "bq1"]
    stb = ["state0", "state1"]
    csb = ["cs0", "cs1"]
    dubb = ["dub0", "dub1"]
    nst_prev = None

    @p.g
    def _(g, e):
        g.dma_start(out=e.selgB_s[:, :], in_=e.selgB).then_inc(ld.h, 16)
        g.dma_start(out=e.selR_s[:, :], in_=e.selR).then_inc(ld.h, 16)
        g.dma_start(out=e.sel64q_s[:, :], in_=e.sel64q).then_inc(ld.h, 16)
        g.dma_start(out=e.A_s[:, :], in_=e.A_lg).then_inc(ld.h, 16)
        g.dma_start(out=e.dtw_s[:, :], in_=e.dtw_c).then_inc(ld.h, 16)
        g.dma_start(out=e.dtb_s[:, :], in_=e.dtb_c).then_inc(ld.h, 16)
        g.dma_start(out=e.D_s[:, :], in_=e.D_c).then_inc(ld.h, 16)
        g.memset(e.ones1[:, :], 1.0)
        g.memset(e.ones8[:, :], 1.0)
        g.memset(e.cone[:, :], 1.0)
        g.memset(e.carr[:, :], 0.0)
        g.engine_nop().then_inc(ini.h, 1)
    nwld = ld.inc(112)
    nini = ini.inc()

    def aexp(s, e, nwld=nwld):
        s.wait_ge(ld.h, nwld)
        s.activation(e.Aex[:, :], e.A_s[:, :], AF.Exp).then_inc(sS.h, 1)
    p.s(aexp)
    naex = sS.inc()

    def aneg(v, e, naex=naex):
        v.wait_ge(sS.h, naex)
        v.tensor_scalar(e.A_s[:, :], e.Aex[:, :], -1.0, None, ALU.mult)
        v.engine_nop().then_inc(vV.h, 1)
    p.v(aneg)
    nang = vV.inc()

    if moe:
        @p.g
        def _(g, e):
            g.dma_start(out=e.wop_s[:, :], in_=e.wop).then_inc(ld2.h, 16)
            g.dma_start(out=e.wgu_s[:, :], in_=e.wgu).then_inc(ld2.h, 16)
            g.dma_start(out=e.wdn_s[:, :], in_=e.wdn).then_inc(ld2.h, 16)
            g.dma_start(out=e.wsgu_s[:, :], in_=e.wsgu).then_inc(ld2.h, 16)
            g.dma_start(out=e.wsdn_s[:, :], in_=e.wsdn).then_inc(ld2.h, 16)
            g.memset(e.onesw[:, :], 1.0)
        nwld2 = ld2.inc(80)

        def gen_moe_chunk(c, nymst):
            t0b = c * 512
            sl, off = t0b // TSL, t0b % TSL

            def cld2(g, e, c=c, nymst=nymst, ntp=tP2.n, nvv=vV2.n, sl=sl, off=off,
                     t0b=t0b):
                if c == 0:
                    g.wait_ge(ld2.h, nwld2)
                if c > 0:
                    g.wait_ge(tP2.h, ntp)
                    g.wait_ge(vV2.h, nvv)
                g.wait_ge(st.h, nymst)
                g.dma_start(out=e.xnc[:, :].rearrange("p (k t) -> p k t", k=8),
                            in_=e.xn0f.rearrange("s d t -> (s d) t")
                            [sl * D:sl * D + D, off:off + 512]
                            .rearrange("(k p) t -> p k t", p=128)).then_inc(ld2.h, 16)
                g.dma_start(out=e.ymc[:, :].rearrange("p (k t) -> p k t", k=2),
                            in_=e.ym_i[:, t0b:t0b + 512]
                            .rearrange("(k p) t -> p k t", p=128)).then_inc(ld2.h, 16)
                gwf = e.gw_fm.rearrange("s a t -> (s a) t")
                g.dma_start(out=e.gA[0:96:32, :],
                            in_=gwf[sl * 9 + 0:sl * 9 + 3, off:off + 512]).then_inc(ld2.h, 16)
                g.dma_start(out=e.gB[0:96:32, :],
                            in_=gwf[sl * 9 + 3:sl * 9 + 6, off:off + 512]).then_inc(ld2.h, 16)
                g.dma_start(out=e.gC[0:96:32, :],
                            in_=gwf[sl * 9 + 6:sl * 9 + 9, off:off + 512]).then_inc(ld2.h, 16)
            p.g(cld2)
            nld = ld2.inc(80)
            yield

            hv = {}
            hepq = []

            def mk_hep_v(ei, phx, nsi, nwb):
                def hep_v(v, e, ei=ei, phx=phx, nsi=nsi, nwb=nwb):
                    v.wait_ge(sS2.h, nsi)
                    v.wait_ge(tP2.h, nwb)
                    v.tensor_mul(e.hwu[:, :], e.gs64[:, :], e[phx][64:128, :])
                    v.drain()
                    v.tensor_mul(e.hsb[:, ei * 512:(ei + 1) * 512],
                                 e.hwu[:, :], e.pwb[0:64, :])
                    v.engine_nop().then_inc(vV2.h, 1)
                return hep_v

            for ei in range(8):
                phx = "ph" if ei % 2 == 0 else "ph2"

                def hmm(t, e, ei=ei, phx=phx, nld=nld,
                        nve=(hv.get(ei - 2) if ei >= 2 else None)):
                    if ei == 0:
                        t.wait_ge(ld2.h, nld)
                    if nve is not None:
                        t.wait_ge(vV2.h, nve)
                    for k in range(8):
                        mm = t.matmul(e[phx][:, :],
                                      e.wgu_s[:, (k * 8 + ei) * 128:(k * 8 + ei) * 128 + 128],
                                      e.xnc[:, k * 512:(k + 1) * 512],
                                      start=(k == 0), stop=(k == 7))
                    mm.then_inc(tP2.h, 1)
                p.t(hmm)
                nhm = tP2.inc()

                if hepq:
                    p.v(hepq.pop(0))
                    hv[ei - 1] = vV2.inc()

                def wbm2(t, e, ei=ei, nvv=(hv.get(ei - 1) if ei >= 1 else vV2.n)):
                    ga = [e.gA, e.gB, e.gC][ei // 3]
                    r = (ei % 3) * 32
                    t.wait_ge(vV2.h, nvv)
                    t.matmul(e.pwb[:, :], e.onesw[r:r + 1, :], ga[r:r + 1, :],
                             start=True, stop=True).then_inc(tP2.h, 1)
                p.t(wbm2)
                nwb = tP2.inc()

                def hep_s(s, e, ei=ei, phx=phx, nhm=nhm,
                          nve=(hv.get(ei - 1) if ei >= 1 else None)):
                    s.wait_ge(tP2.h, nhm)
                    if nve is not None:
                        s.wait_ge(vV2.h, nve)
                    s.activation(e.gs64[:, :], e[phx][0:64, :], AF.Silu).then_inc(sS2.h, 1)
                p.s(hep_s)
                nsi = sS2.inc()
                hepq.append(mk_hep_v(ei, phx, nsi, nwb))
                yield

            p.v(hepq.pop(0))
            hv[7] = vV2.inc()

            def wbsh(t, e, nvv=hv[7]):
                t.wait_ge(vV2.h, nvv)
                t.matmul(e.pwb[:, :], e.onesw[64:65, :], e.gC[64:65, :],
                         start=True, stop=True).then_inc(tP2.h, 1)
            p.t(wbsh)
            nwbs = tP2.inc()
            yield

            shq = []

            def mk_shv(m, nshs, nwbs):
                def shv(v, e, m=m, nshs=nshs, nwbs=nwbs):
                    v.wait_ge(sS2.h, nshs)
                    v.wait_ge(tP2.h, nwbs)
                    v.tensor_mul(e.hwu2[:, :], e.gs128[:, :], e.ph2[:, :])
                    v.drain()
                    v.tensor_mul(e.hsh[:, m * 512:(m + 1) * 512], e.hwu2[:, :], e.pwb[:, :])
                    v.engine_nop().then_inc(vV2.h, 1)
                return shv

            for m in range(2):
                if shq:
                    p.v(shq.pop(0))
                    hv[8 + m - 1] = vV2.inc()

                def shm(t, e, m=m, nve=(hv[7] if m == 0 else hv[8])):
                    t.wait_ge(vV2.h, nve)
                    for k in range(8):
                        t.matmul(e.ph[:, :],
                                 e.wsgu_s[:, k * 512 + m * 128: k * 512 + (m + 1) * 128],
                                 e.xnc[:, k * 512:(k + 1) * 512], start=(k == 0), stop=(k == 7))
                    for k in range(8):
                        mm = t.matmul(e.ph2[:, :],
                                      e.wsgu_s[:, k * 512 + 256 + m * 128: k * 512 + 256 + (m + 1) * 128],
                                      e.xnc[:, k * 512:(k + 1) * 512], start=(k == 0), stop=(k == 7))
                    mm.then_inc(tP2.h, 1)
                p.t(shm)
                nshm = tP2.inc()

                def shs(s, e, m=m, nshm=nshm, nve=(hv.get(8) if m > 0 else None)):
                    s.wait_ge(tP2.h, nshm)
                    if nve is not None:
                        s.wait_ge(vV2.h, nve)
                    s.activation(e.gs128[:, :], e.ph[:, :], AF.Silu).then_inc(sS2.h, 1)
                p.s(shs)
                nshs = sS2.inc()
                shq.append(mk_shv(m, nshs, nwbs))
                yield

            p.v(shq.pop(0))
            hv[9] = vV2.inc()

            for m in range(8):
                def omm(t, e, m=m, nhid=vV2.n, nsev=sS2.n):
                    t.wait_ge(vV2.h, nhid)
                    if m > 0:
                        t.wait_ge(sS2.h, nsev)
                    for k in range(2):
                        t.matmul(e.pout[:, :],
                                 e.wop_s[:, k * D + m * 128: k * D + (m + 1) * 128],
                                 e.ymc[:, k * 512:(k + 1) * 512], start=(k == 0), stop=False)
                    for ei in range(8):
                        t.matmul(e.pout[:, :],
                                 e.wdn_s[:, ei * D + m * 128: ei * D + (m + 1) * 128],
                                 e.hsb[:, ei * 512:(ei + 1) * 512], start=False, stop=False)
                    for k in range(2):
                        mm = t.matmul(e.pout[:, :],
                                      e.wsdn_s[:, k * D + m * 128: k * D + (m + 1) * 128],
                                      e.hsh[:, k * 512:(k + 1) * 512], start=False, stop=(k == 1))
                    mm.then_inc(tP2.h, 1)
                p.t(omm)
                nom = tP2.inc()

                def oev(s, e, m=m, nom=nom, nsth=st2.n):
                    s.wait_ge(tP2.h, nom)
                    if (m == 0 or m == 4) and nsth > 0:
                        s.wait_ge(st2.h, nsth)
                    s.copy(e.po[:, (m % 4) * 512:(m % 4 + 1) * 512], e.pout[:, :]).then_inc(sS2.h, 1)
                p.s(oev)
                sS2.inc()

                if m == 3 or m == 7:
                    def sto2(sy, e, m=m, nse=sS2.n, sl=sl, off=off):
                        sy.wait_ge(sS2.h, nse)
                        base = sl * D + (0 if m == 3 else 4) * 128
                        sy.dma_start(out=e.part0.rearrange("s d t -> (s d) t")
                                     [base:base + 512, off:off + 512]
                                     .rearrange("(m p) t -> p m t", p=128),
                                     in_=e.po[:, :].rearrange("p (m t) -> p m t", m=4)
                                     ).then_inc(st2.h, 16)
                    p.y(sto2)
                    st2.inc(16)
                if m % 2 == 1:
                    yield

        pending = []

    for ck in range(NK):
        t0 = ck * CH

        def cld(g, e, t0=t0, ck=ck, nwld=nwld, nep=vV.n, nsum=vV.n):
            if ck == 0:
                g.wait_ge(ld.h, nwld)
            if ck > 0:
                g.wait_ge(vV.h, nep)
            if fused:
                g.dma_start(out=e.Bc[:, :], in_=e.xpr[0:64, t0:t0 + CH]).then_inc(ld.h, 16)
                g.dma_start(out=e.Cc[:, :], in_=e.xpr[64:128, t0:t0 + CH]).then_inc(ld.h, 16)
                g.dma_start(out=e.dr2[0:1, :], in_=e.xpr[128:129, t0:t0 + CH]).then_inc(ld.h, 16)
            else:
                g.dma_start(out=e.xq[:, :], in_=e.xpp8[0, 0:128, t0:t0 + CH]).then_inc(ld.h, 16)
                g.dma_start(out=e.dr2[:, :], in_=e.xpp8[:, 128, t0:t0 + CH]).then_inc(ld.h, 16)
            for j in range(2):
                g.dma_start(out=e["uc%d" % j][:, :],
                            in_=e.u_i[j * 128:(j + 1) * 128, t0:t0 + CH]).then_inc(ld.h, 16)
                g.dma_start(out=e["zc%d" % j][:, :],
                            in_=e.zs_i[j * 128:(j + 1) * 128, t0:t0 + CH]).then_inc(ld.h, 16)
            if not fused:
                g.dma_start(out=e.xq2[:, :], in_=e.xpp8[1, 0:128, t0:t0 + CH]).then_inc(ld.h, 16)
                for r in range(2, NC):
                    g.wait_ge(vV.h, nsum + (r - 1))
                    g.dma_start(out=e.xq2[:, :], in_=e.xpp8[r, 0:128, t0:t0 + CH]).then_inc(ld.h, 16)
        p.g(cld)
        if fused:
            nld0 = ld.inc(112)
            nld = nld0
        else:
            nld0 = ld.inc(112)
            nld = nld0
            nlds = [ld.inc(16) for _ in range(NC - 2)]

        if not fused:
            for r in range(1, NC):
                def xsum(v, e, r=r, nl=(nld0 if r == 1 else nlds[r - 2])):
                    v.wait_ge(ld.h, nl)
                    v.tensor_add(e.xq[:, :], e.xq[:, :], e.xq2[:, :])
                    v.drain()
                    v.engine_nop().then_inc(vV.h, 1)
                p.v(xsum)
                vV.inc()

            def xfin(v, e):
                v.tensor_copy(e.Bc[:, :], e.xq[0:64, :])
                v.tensor_copy(e.Cc[:, :], e.xq[64:128, :])
                v.engine_nop().then_inc(vV.h, 1)
            p.v(xfin)
            vV.inc()

        nprev_ep = vV.n
        nprev_ex = sS.n
        for r8 in range(8):
            q, isC = r8 % 4, r8 >= 4
            src, dst = ("Cc", "Cb%d" % q) if isC else ("Bc", "Bb%d" % q)

            def bmm(t, e, q=q, src=src, r8=r8, ck=ck, nld=nld, nini=nini,
                    nprev_ep=nprev_ep, nse=sS.n):
                if r8 == 0:
                    t.wait_ge(ld.h, nld)
                    t.wait_ge(ini.h, nini)
                    t.wait_ge(vV.h, nprev_ep)
                else:
                    t.wait_ge(sS.h, nse)
                for w in range(NW5):
                    mm = t.matmul(e.pp1[:, w * 512:(w + 1) * 512],
                                  e.sel64q_s[:, q * 128:(q + 1) * 128],
                                  e[src][:, w * 512:(w + 1) * 512], start=True, stop=True)
                mm.then_inc(tP.h, 1)
            p.t(bmm)
            nbm = tP.inc()

            def bev(s, e, dst=dst, nbm=nbm):
                s.wait_ge(tP.h, nbm)
                s.copy(e[dst][:, :], e.pp1[:, :]).then_inc(sS.h, 1)
            p.s(bev)
            sS.inc()

        def dmm(t, e, ck=ck, nld=nld, nprev_ex=nprev_ex, nxf=nprev_ep):
            if fused:
                t.wait_ge(ld.h, nld)
            else:
                t.wait_ge(vV.h, nxf)
            if ck > 0:
                t.wait_ge(sS.h, nprev_ex)
            for w in range(NW5):
                if fused:
                    mm = t.matmul(e.pp2[:, w * 512:(w + 1) * 512], e.ones1[:, :],
                                  e.dr2[0:1, w * 512:(w + 1) * 512], start=True, stop=True)
                else:
                    mm = t.matmul(e.pp2[:, w * 512:(w + 1) * 512], e.ones8[:, :],
                                  e.dr2[:, w * 512:(w + 1) * 512], start=True, stop=True)
            mm.then_inc(tP.h, 1)
        p.t(dmm)
        ndmm = tP.inc()

        for j in range(2):
            def dtp(v, e, j=j, ndmm=ndmm, nsdt=sS.n):
                v.wait_ge(tP.h, ndmm)
                if j == 1:
                    v.wait_ge(sS.h, nsdt)
                v.tensor_scalar(e.pre[:, :], e.pp2[:, :], e.dtw_s[:, j:j + 1],
                                e.dtb_s[:, j:j + 1], ALU.mult, ALU.add)
                v.engine_nop().then_inc(vV.h, 1)
            p.v(dtp)
            ndtp = vV.inc()

            def dts(s, e, j=j, ndtp=ndtp):
                s.wait_ge(vV.h, ndtp)
                s.activation(e.srl[:, :], e.pre[:, :], AF.Relu)
                s.activation(e.du[:, :], e.pre[:, :], AF.Abs)
                s.drain()
                s.activation(e.yf[:, :], e.du[:, :], AF.Exp, scale=-1.0)
                s.drain()
                s.activation(e.sln[:, :], e.yf[:, :], AF.Ln, bias=e.cone[:, 0:1]).then_inc(sS.h, 1)
            p.s(dts)
            ndts = sS.inc()

            def dtv(v, e, j=j, ndts=ndts, nld=nld):
                v.wait_ge(sS.h, ndts)
                v.wait_ge(ld.h, nld)
                v.tensor_add(e["dt%d" % j][:, :], e.srl[:, :], e.sln[:, :])
                v.drain()
                v.tensor_mul(e["dtu%d" % j][:, :], e["dt%d" % j][:, :], e["uc%d" % j][:, :])
                v.engine_nop().then_inc(vV.h, 1)
            p.v(dtv)
            vV.inc()

        nbq_h, nex_h, nsc_h, ncs_h, nqy_h = {}, {}, {}, {}, {}
        for j in range(2):
            dtu_name = "dtu%d" % j
            dt_name = "dt%d" % j
            for k in range(66):
                if k < 64:
                    gi, q = k // 4, k % 4
                    sb = gi * 128
                    I = j * 64 + k
                    if q == 0 and moe:
                        g_abs = j * 16 + gi
                        if g_abs == 0:
                            budget = 1
                        elif g_abs < 4:
                            budget = 0
                        else:
                            budget = 2 if g_abs % 2 == 0 else 1
                        while budget > 0 and pending:
                            try:
                                next(pending[0])
                                budget -= 1
                            except StopIteration:
                                pending.pop(0)
                    if q == 0:
                        gpar = (I // 4) % 2
                        def gmm1(t, e, sb=sb, dtu_name=dtu_name, nfree=sS.n,
                                 nvv=nbq_h.get(I - 5, vV.n)):
                            t.wait_ge(sS.h, nfree)
                            t.wait_ge(vV.h, nvv)
                            for w in range(H // 512):
                                mm = t.matmul(e.pp2[:, w * 512:(w + 1) * 512],
                                              e.selgB_s[:, sb:sb + 128],
                                              e[dtu_name][:, w * 512:(w + 1) * 512],
                                              start=True, stop=True)
                            mm.then_inc(tP.h, 1)
                        p.t(gmm1)
                        ng1 = tP.inc()

                        def gev1(s, e, ng1=ng1, gpar=gpar):
                            s.wait_ge(tP.h, ng1)
                            s.copy(e[dubb[gpar]][:, 0:H], e.pp2[:, 0:H]).then_inc(sS.h, 1)
                        p.s(gev1)
                        sS.inc()

                        def gmm2(t, e, sb=sb, dtu_name=dtu_name):
                            for w in range(H // 512, NW5):
                                mm = t.matmul(e.pp2[:, w * 512:(w + 1) * 512],
                                              e.selgB_s[:, sb:sb + 128],
                                              e[dtu_name][:, w * 512:(w + 1) * 512],
                                              start=True, stop=True)
                            mm.then_inc(tP.h, 1)
                        p.t(gmm2)
                        ng2 = tP.inc()

                        def gev2(s, e, ng2=ng2, gpar=gpar):
                            s.wait_ge(tP.h, ng2)
                            s.copy(e[dubb[gpar]][:, H:CH], e.pp2[:, H:CH]).then_inc(sS.h, 1)
                        p.s(gev2)
                        ne2 = sS.inc()

                        def gmm3(t, e, sb=sb, dt_name=dt_name, ne2=ne2):
                            t.wait_ge(sS.h, ne2)
                            for w in range(NW5):
                                mm = t.matmul(e.pp2[:, w * 512:(w + 1) * 512],
                                              e.selgB_s[:, sb:sb + 128],
                                              e[dt_name][:, w * 512:(w + 1) * 512],
                                              start=True, stop=True)
                            mm.then_inc(tP.h, 1)
                        p.t(gmm3)
                        ng3 = tP.inc()

                    def bqf(v, e, q=q, par=I % 2, gpar=(I // 4) % 2,
                            ne2w=(ne2 if q == 0 else None),
                            nwar=(nsc_h[I - 2] if I >= 2 else None)):
                        if ne2w is not None:
                            v.wait_ge(sS.h, ne2w)
                        if nwar is not None:
                            v.wait_ge(vS.h, nwar)
                        v.tensor_mul(e[bqb[par]][:, :], e[dubb[gpar]][:, :], e["Bb%d" % q][:, :])
                        v.engine_nop().then_inc(vV.h, 1)
                    p.v(bqf)
                    nbq_h[I] = vV.inc()

                    def qexf(s, e, col=I, par=I % 2, ng3w=(ng3 if q == 0 else None),
                             nfr=(nsc_h[I - 2] if I >= 2 else None)):
                        if ng3w is not None:
                            s.wait_ge(tP.h, ng3w)
                        if nfr is not None:
                            s.wait_ge(vS.h, nfr)
                        s.activation(e[dAb[par]][:, :], e.pp2[:, :], AF.Exp,
                                     scale=e.A_s[:, col:col + 1]).then_inc(sS.h, 1)
                    p.s(qexf)
                    nex_h[I] = sS.inc()

                if 1 <= k and k - 1 < 64:
                    I1 = j * 64 + (k - 1)
                    def scf(v, e, col=I1, par=I1 % 2, nb=nbq_h[I1], nx=nex_h[I1],
                            nwar=(ncs_h[I1 - 2] if I1 >= 2 else None)):
                        v.wait_ge(vV.h, nb)
                        v.wait_ge(sS.h, nx)
                        if nwar is not None:
                            v.wait_ge(vC.h, nwar)
                        v.tensor_tensor_scan(e[stb[par]][:, :], e[dAb[par]][:, :],
                                             e[bqb[par]][:, :], e.carr[:, col:col + 1],
                                             ALU.mult, ALU.add)
                        v.engine_nop().then_inc(vS.h, 1)
                    p.v(scf)
                    nsc_h[I1] = vS.inc()

                if 2 <= k:
                    k2 = k - 2
                    I2 = j * 64 + k2
                    def ccf(v, e, col=I2, q2=k2 % 4, par=I2 % 2, ns=nsc_h[I2],
                            nyy=(nqy_h[I2 - 2] if I2 >= 2 else None)):
                        v.wait_ge(vS.h, ns)
                        if nyy is not None:
                            v.wait_ge(tP.h, nyy)
                        v.tensor_copy(e.carr[:, col:col + 1], e[stb[par]][:, CH - 1:CH])
                        v.tensor_mul(e[csb[par]][:, :], e[stb[par]][:, :], e["Cb%d" % q2][:, :])
                        v.engine_nop().then_inc(vC.h, 1)
                    p.v(ccf)
                    ncs_h[I2] = vC.inc()

                    def qyf(t, e, sb=(k2 // 4) * 128, par=I2 % 2, ncc=ncs_h[I2],
                            first=(k2 == 0), last=(k2 == 63)):
                        t.wait_ge(vC.h, ncc)
                        for w in range(NW5):
                            mm = t.matmul(e.pp1[:, w * 512:(w + 1) * 512],
                                          e.selR_s[:, sb:sb + 128],
                                          e[csb[par]][:, w * 512:(w + 1) * 512],
                                          start=first, stop=last)
                        mm.then_inc(tP.h, 1)
                    p.t(qyf)
                    nqy_h[I2] = tP.inc()

            def epi(v, e, j=j, nylast=tP.n, nstp=nst_prev, t0=t0):
                v.wait_ge(tP.h, nylast)
                if nstp is not None:
                    v.wait_ge(st.h, nstp)
                v.tensor_scalar(e.du[:, :], e["uc%d" % j][:, :], e.D_s[:, j:j + 1], None, ALU.mult)
                v.drain()
                v.tensor_add(e.yf[:, :], e.pp1[:, :], e.du[:, :])
                v.drain()
                v.tensor_mul(e.ym[:, :], e.yf[:, :], e["zc%d" % j][:, :])
                v.engine_nop().then_inc(vV.h, 1)
            p.v(epi)
            nepi = vV.inc()

            def sto(sy, e, j=j, t0=t0, nepi=nepi):
                sy.wait_ge(vV.h, nepi)
                sy.dma_start(out=e.ym_o[j * 128:(j + 1) * 128, t0:t0 + CH],
                             in_=e.ym[:, :]).then_inc(st.h, 16)
            p.y(sto)
            nst_prev = st.inc(16)

        if moe:
            for c2 in range(ck * (CH // 512), (ck + 1) * (CH // 512)):
                if c2 < NCH:
                    pending.append(gen_moe_chunk(c2, nst_prev))

    if moe:
        while pending:
            try:
                next(pending[0])
            except StopIteration:
                pending.pop(0)

        def fin2(sy, e, n=st2.n):
            sy.wait_ge(st2.h, n)
        p.y(fin2)

    def fin(sy, e, n=st.n):
        sy.wait_ge(st.h, n)
    p.y(fin)
    return p




def make_p2b(T, p=None, fused=False):
    if p is None:
        p = P()
    NCH = T // 512
    TSL = T // NC if T >= NC * 512 else T
    NSL = T // TSL

    p.din("ym_i", (256, T), BF16)
    p.din("xn0f", (NSL, D, TSL), BF16)
    p.din("gw_fm", (NSL, 9, TSL), BF16)
    p.din("wop", (128, 2 * D), F32)
    p.din("wgu", (128, 8 * 8 * 128), F32)
    p.din("wdn", (64, 8 * D), F32)
    p.din("wsgu", (128, 8 * 512), F32)
    p.din("wsdn", (128, 2 * D), F32)
    p.dout("part0", (NSL, D, TSL), F32)

    p.sbuf("wop_s", (128, 2 * D), BF16)
    p.sbuf("wgu_s", (128, 8 * 8 * 128), BF16)
    p.sbuf("wdn_s", (64, 8 * D), BF16)
    p.sbuf("wsgu_s", (128, 8 * 512), BF16)
    p.sbuf("wsdn_s", (128, 2 * D), BF16)
    p.sbuf("xnc", (128, 8 * 512), BF16)
    p.sbuf("ymc", (128, 2 * 512), BF16)
    p.sbuf("gA", (128, 512), BF16)
    p.sbuf("gB", (128, 512), BF16)
    p.sbuf("gC", (128, 512), BF16)
    p.sbuf("gs64", (64, 512), F32)
    p.sbuf("hwu", (64, 512), F32)
    p.sbuf("gs128", (128, 512), F32)
    p.sbuf("hwu2", (128, 512), F32)
    p.sbuf("hsb", (64, 8 * 512), BF16)
    p.sbuf("hsh", (128, 2 * 512), BF16)
    p.sbuf("po", (128, 8 * 512), F32)
    p.sbuf("onesw", (128, 128), BF16)

    p.psum("ph", (128, 512))
    p.psum("ph2", (128, 512))
    p.psum("pwb", (128, 512))
    p.psum("pout", (128, 512))

    ld = p.sem("ld")
    tP = p.sem("tP")
    sS = p.sem("sS")
    vV = p.sem("vV")
    st = p.sem("st")

    @p.g
    def _(g, e):
        g.dma_start(out=e.wop_s[:, :], in_=e.wop).then_inc(ld.h, 16)
        g.dma_start(out=e.wgu_s[:, :], in_=e.wgu).then_inc(ld.h, 16)
        g.dma_start(out=e.wdn_s[:, :], in_=e.wdn).then_inc(ld.h, 16)
        g.dma_start(out=e.wsgu_s[:, :], in_=e.wsgu).then_inc(ld.h, 16)
        g.dma_start(out=e.wsdn_s[:, :], in_=e.wsdn).then_inc(ld.h, 16)
        g.memset(e.onesw[:, :], 1.0)
    nwld = ld.inc(80)

    for c in range(NCH):
        t0 = c * 512

        def cld(g, e, t0=t0, c=c, nwld=nwld, ntp=tP.n, nvv=vV.n, TSL=TSL):
            if c == 0:
                g.wait_ge(ld.h, nwld)
            if c > 0:
                g.wait_ge(tP.h, ntp)
                g.wait_ge(vV.h, nvv)
            sl, off = t0 // TSL, t0 % TSL
            g.dma_start(out=e.xnc[:, :].rearrange("p (k t) -> p k t", k=8),
                        in_=e.xn0f.rearrange("s d t -> (s d) t")[sl * D:sl * D + D, off:off + 512]
                        .rearrange("(k p) t -> p k t", p=128)).then_inc(ld.h, 16)
            g.dma_start(out=e.ymc[:, :].rearrange("p (k t) -> p k t", k=2),
                        in_=e.ym_i[:, t0:t0 + 512].rearrange("(k p) t -> p k t", p=128)
                        ).then_inc(ld.h, 16)
            gwf = e.gw_fm.rearrange("s a t -> (s a) t")
            g.dma_start(out=e.gA[0:96:32, :], in_=gwf[sl * 9 + 0:sl * 9 + 3, off:off + 512]).then_inc(ld.h, 16)
            g.dma_start(out=e.gB[0:96:32, :], in_=gwf[sl * 9 + 3:sl * 9 + 6, off:off + 512]).then_inc(ld.h, 16)
            g.dma_start(out=e.gC[0:96:32, :], in_=gwf[sl * 9 + 6:sl * 9 + 9, off:off + 512]).then_inc(ld.h, 16)
        p.g(cld)
        nld = ld.inc(80)

        for ei in range(8):
            phx = "ph" if ei % 2 == 0 else "ph2"

            def hmm(t, e, ei=ei, phx=phx, nld=nld, nve=vV.n):
                if ei == 0:
                    t.wait_ge(ld.h, nld)
                if ei >= 2:
                    t.wait_ge(vV.h, nve)
                for k in range(8):
                    mm = t.matmul(e[phx][:, :],
                                  e.wgu_s[:, (k * 8 + ei) * 128:(k * 8 + ei) * 128 + 128],
                                  e.xnc[:, k * 512:(k + 1) * 512], start=(k == 0), stop=(k == 7))
                mm.then_inc(tP.h, 1)
            p.t(hmm)
            nhm = tP.inc()

            def wbm2(t, e, ei=ei, nvv=vV.n):
                ga = [e.gA, e.gB, e.gC][ei // 3]
                r = (ei % 3) * 32
                t.wait_ge(vV.h, nvv)
                t.matmul(e.pwb[:, :], e.onesw[r:r + 1, :], ga[r:r + 1, :],
                         start=True, stop=True).then_inc(tP.h, 1)
            p.t(wbm2)
            nwb = tP.inc()

            def hep_s(s, e, ei=ei, phx=phx, nhm=nhm, nve=vV.n):
                s.wait_ge(tP.h, nhm)
                if ei > 0:
                    s.wait_ge(vV.h, nve)
                s.activation(e.gs64[:, :], e[phx][0:64, :], AF.Silu).then_inc(sS.h, 1)
            p.s(hep_s)
            nsi = sS.inc()

            def hep_v(v, e, ei=ei, phx=phx, nsi=nsi, nwb=nwb):
                v.wait_ge(sS.h, nsi)
                v.wait_ge(tP.h, nwb)
                v.tensor_mul(e.hwu[:, :], e.gs64[:, :], e[phx][64:128, :])
                v.drain()
                v.tensor_mul(e.hsb[:, ei * 512:(ei + 1) * 512],
                             e.hwu[:, :], e.pwb[0:64, :])
                v.engine_nop().then_inc(vV.h, 1)
            p.v(hep_v)
            vV.inc()

        def wbsh(t, e, nvv=vV.n):
            t.wait_ge(vV.h, nvv)
            t.matmul(e.pwb[:, :], e.onesw[64:65, :], e.gC[64:65, :], start=True, stop=True).then_inc(tP.h, 1)
        p.t(wbsh)
        nwbs = tP.inc()

        for m in range(2):
            def shm(t, e, m=m, nld=nld, nve=vV.n):
                t.wait_ge(vV.h, nve)
                for k in range(8):
                    t.matmul(e.ph[:, :], e.wsgu_s[:, k * 512 + m * 128: k * 512 + (m + 1) * 128],
                             e.xnc[:, k * 512:(k + 1) * 512], start=(k == 0), stop=(k == 7))
                for k in range(8):
                    mm = t.matmul(e.ph2[:, :], e.wsgu_s[:, k * 512 + 256 + m * 128: k * 512 + 256 + (m + 1) * 128],
                                  e.xnc[:, k * 512:(k + 1) * 512], start=(k == 0), stop=(k == 7))
                mm.then_inc(tP.h, 1)
            p.t(shm)
            nshm = tP.inc()

            def shs(s, e, m=m, nshm=nshm, nve=vV.n):
                s.wait_ge(tP.h, nshm)
                if m > 0:
                    s.wait_ge(vV.h, nve)
                s.activation(e.gs128[:, :], e.ph[:, :], AF.Silu).then_inc(sS.h, 1)
            p.s(shs)
            nshs = sS.inc()

            def shv(v, e, m=m, nshs=nshs, nwbs=nwbs):
                v.wait_ge(sS.h, nshs)
                v.wait_ge(tP.h, nwbs)
                v.tensor_mul(e.hwu2[:, :], e.gs128[:, :], e.ph2[:, :])
                v.drain()
                v.tensor_mul(e.hsh[:, m * 512:(m + 1) * 512], e.hwu2[:, :], e.pwb[:, :])
                v.engine_nop().then_inc(vV.h, 1)
            p.v(shv)
            vV.inc()

        for m in range(8):
            def omm(t, e, m=m, nhid=vV.n, nsev=sS.n):
                t.wait_ge(vV.h, nhid)
                if m > 0:
                    t.wait_ge(sS.h, nsev)
                for k in range(2):
                    t.matmul(e.pout[:, :], e.wop_s[:, k * D + m * 128: k * D + (m + 1) * 128],
                             e.ymc[:, k * 512:(k + 1) * 512], start=(k == 0), stop=False)
                for ei in range(8):
                    t.matmul(e.pout[:, :], e.wdn_s[:, ei * D + m * 128: ei * D + (m + 1) * 128],
                             e.hsb[:, ei * 512:(ei + 1) * 512],
                             start=False, stop=False)
                for k in range(2):
                    mm = t.matmul(e.pout[:, :], e.wsdn_s[:, k * D + m * 128: k * D + (m + 1) * 128],
                                  e.hsh[:, k * 512:(k + 1) * 512], start=False, stop=(k == 1))
                mm.then_inc(tP.h, 1)
            p.t(omm)
            nom = tP.inc()

            def oev(s, e, m=m, nom=nom, nst0=st.n):
                s.wait_ge(tP.h, nom)
                if m == 0 and nst0 > 0:
                    s.wait_ge(st.h, nst0)
                s.copy(e.po[:, m * 512:(m + 1) * 512], e.pout[:, :]).then_inc(sS.h, 1)
            p.s(oev)
            sS.inc()

        def sto(sy, e, t0=t0, nse=sS.n, nst=st.n, TSL=TSL):
            sy.wait_ge(sS.h, nse)
            sl, off = t0 // TSL, t0 % TSL
            sy.dma_start(out=e.part0.rearrange("s d t -> (s d) t")[sl * D:sl * D + D, off:off + 512]
                         .rearrange("(m p) t -> p m t", p=128),
                         in_=e.po[:, :].rearrange("p (m t) -> p m t", m=8)).then_inc(st.h, 16)
        p.y(sto)
        st.inc(16)

    def fin(sy, e, n=st.n):
        sy.wait_ge(st.h, n)
    p.y(fin)
    return p


def tile_kpm(w):
    Kd, M = w.shape
    return np.ascontiguousarray(w.reshape(Kd // 128, 128, M).transpose(1, 0, 2).reshape(128, -1))


def p2b_weights(out_proj_w, dyn_gate_w, dyn_up_w, dyn_down_w, sh_gate_w, sh_up_w, sh_down_w, core):
    cs = slice(core * 256, (core + 1) * 256)
    isl = slice(core * 64, (core + 1) * 64)
    ish = slice(core * 256, (core + 1) * 256)
    wop = tile_kpm(out_proj_w[cs, :])
    wgu = np.stack([np.concatenate([dyn_gate_w[e][isl].T, dyn_up_w[e][isl].T], axis=1)
                    for e in range(8)])
    wgu = np.ascontiguousarray(wgu.reshape(8, 8, 128, 128).transpose(2, 1, 0, 3).reshape(128, -1))
    wdn = np.ascontiguousarray(np.stack([dyn_down_w[e].T[isl] for e in range(8)])
                               .transpose(1, 0, 2).reshape(64, -1))
    wsgu = np.concatenate([sh_gate_w[0][ish].T, sh_up_w[0][ish].T], axis=1)
    wsgu = tile_kpm(wsgu)
    wsdn = tile_kpm(sh_down_w[0].T[ish])
    return wop, wgu, wdn, wsgu, wsdn


def p2a_consts():
    selgB = np.zeros((128, 16 * 128), np.float32)
    selR = np.zeros((128, 16 * 128), np.float32)
    sel64q = np.zeros((64, 4 * 128), np.float32)
    for gi in range(16):
        for m in range(128):
            ch = gi * 8 + (m >> 5) * 2 + ((m >> 4) & 1)
            selgB[ch, gi * 128 + m] = 1.0
    for gi in range(16):
        for k in range(128):
            m = gi * 8 + (k >> 5) * 2 + ((k >> 4) & 1)
            selR[k, gi * 128 + m] = 1.0
    for q in range(4):
        for m in range(128):
            sel64q[q * 16 + (m & 15), q * 128 + m] = 1.0
    return selgB, selR, sel64q


def p2a_part_maps():
    p_idx = np.arange(128)
    choff = (p_idx >> 5) * 2 + ((p_idx >> 4) & 1)
    nq = p_idx & 15
    return choff, nq


def p2a_A_layout(A_slice):
    choff, nq = p2a_part_maps()
    A_sc = np.zeros((128, 128), np.float32)
    for g in range(32):
        base_ch = (g // 16) * 128 + (g % 16) * 8
        for q in range(4):
            A_sc[:, 4 * g + q] = A_slice[base_ch + choff, q * 16 + nq]
    return A_sc



def make_p3(TSL, p=None, fused=False):
    if p is None:
        p = P()
    NCH = TSL // 512
    NW = TSL // 128

    p.din("x0fm", (D, TSL), F32)
    if fused:
        p.din("part0a", (D, TSL), F32)
    else:
        p.din("part0a", (NC, D, TSL), F32)
    p.din("wk", (128, 8 * 256), F32)
    p.din("bkc", (128, 2), F32)
    p.din("wv", (128, 8 * 256), F32)
    p.din("bvc", (128, 2), F32)
    p.din("wc2", (128, 2 * 64), F32)
    p.din("wg1", (128, 8 * 8), F32)
    p.din("eye", (128, 128), F32)
    p.dout("x1_o", (D, TSL), F32)
    p.dout("xn1b_o", (D, TSL), BF16)
    p.dout("xn1f_o", (D, TSL), F32)
    p.dout("kc_o", (64, TSL), BF16)
    p.dout("vc_o", (64, TSL), BF16)
    p.dout("sc_o", (TSL, 8), F32)
    p.dout("scfm_o", (8, TSL), BF16)

    p.sbuf("x0r", (128, 8 * 512), F32)
    p.sbuf("s0r", (128, 8 * 512), F32)
    if not fused:
        p.sbuf("trr", (128, 8 * 512), F32)
    p.sbuf("x1r", (128, 8 * 512), F32)
    p.sbuf("xn1", (128, 8 * TSL), BF16)
    p.sbuf("xnf", (128, 8 * 512), F32)
    p.sbuf("wk_s", (128, 8 * 256), BF16)
    p.sbuf("wv_s", (128, 8 * 256), BF16)
    p.sbuf("wc_s", (128, 2 * 64), BF16)
    p.sbuf("wg_s", (128, 8 * 8), BF16)
    p.sbuf("eye_s", (128, 128), BF16)
    p.sbuf("bk_s", (128, 2), F32)
    p.sbuf("bv_s", (128, 2), F32)
    p.sbuf("w1024", (128, 1), F32)
    p.sbuf("ones1", (1, 128), F32)
    p.sbuf("ceps", (1, 1), F32)
    p.sbuf("sq", (128, 8 * 512), F32)
    p.sbuf("mean", (1, 512), F32)
    p.sbuf("m2", (1, 512), F32)
    p.sbuf("var", (1, 512), F32)
    p.sbuf("lnv", (1, 512), F32)
    p.sbuf("rstd", (1, 512), F32)
    p.sbuf("tmp", (128, 512), F32)
    p.sbuf("k256", (128, 2 * 512), BF16)
    p.sbuf("v256", (128, 2 * 512), BF16)
    p.sbuf("kcs", (64, 512), BF16)
    p.sbuf("vcs", (64, 512), BF16)
    p.sbuf("lgs", (128, 8), F32)
    p.sbuf("srt", (128, 8), F32)
    p.sbuf("dd", (128, 1), F32)
    p.sbuf("dp1", (128, 1), F32)
    p.sbuf("rec", (128, 1), F32)
    p.sbuf("n1", (128, 1), F32)
    p.sbuf("oh0", (128, 8), F32)
    p.sbuf("oh1", (128, 8), F32)
    p.sbuf("sc0", (128, 8), F32)
    p.sbuf("sc1", (128, 8), F32)
    p.sbuf("scw", (128, NW * 8), F32)
    p.sbuf("scb", (128, 8), BF16)
    p.sbuf("scf", (8, NW * 128), BF16)

    p.psum("psA", (1, 512))
    p.psum("psB", (1, 512))
    p.psum("pm", (128, 512))
    p.psum("pr", (128, 512))
    p.psum("pk0", (128, 512))
    p.psum("pkc", (64, 512))
    p.psum("plg", (128, 8))
    p.psum("ptr", (8, 128), BF16)

    ld = p.sem("ld")
    ini = p.sem("ini")
    ac = p.sem("ac")
    tP = p.sem("tP")
    sS = p.sem("sS")
    vV = p.sem("vV")
    st = p.sem("st")

    @p.g
    def _(g, e):
        g.dma_start(out=e.wk_s[:, :], in_=e.wk).then_inc(ld.h, 16)
        g.dma_start(out=e.wv_s[:, :], in_=e.wv).then_inc(ld.h, 16)
        g.dma_start(out=e.wc_s[:, :], in_=e.wc2).then_inc(ld.h, 16)
        g.dma_start(out=e.wg_s[:, :], in_=e.wg1).then_inc(ld.h, 16)
        g.dma_start(out=e.eye_s[:, :], in_=e.eye).then_inc(ld.h, 16)
        g.dma_start(out=e.bk_s[:, :], in_=e.bkc).then_inc(ld.h, 16)
        g.dma_start(out=e.bv_s[:, :], in_=e.bvc).then_inc(ld.h, 16)
        g.memset(e.w1024[:, :], 1.0 / 1024.0)
        g.memset(e.ones1[:, :], 1.0)
        g.memset(e.ceps[:, :], 1e-5)
        g.engine_nop().then_inc(ini.h, 1)
    nwld = ld.inc(112)
    nini = ini.inc()

    for c in range(NCH):
        t0 = c * 512

        def cld(g, e, t0=t0, c=c, nwld=nwld, nvv=vV.n, nss=sS.n, nst=st.n, nsum=vV.n):
            if c == 0:
                g.wait_ge(ld.h, nwld)
            if c > 0:
                g.wait_ge(vV.h, nvv)
                g.wait_ge(sS.h, nss)
                g.wait_ge(st.h, nst)
            g.dma_start(out=e.x0r[:, :].rearrange("p (k t) -> p k t", k=8),
                        in_=e.x0fm[:, t0:t0 + 512].rearrange("(k p) t -> p k t", p=128)
                        ).then_inc(ld.h, 16)
            if fused:
                g.dma_start(out=e.s0r[:, :].rearrange("p (k t) -> p k t", k=8),
                            in_=e.part0a[:, t0:t0 + 512].rearrange("(k p) t -> p k t", p=128)
                            ).then_inc(ld.h, 16)
            else:
                g.dma_start(out=e.s0r[:, :].rearrange("p (k t) -> p k t", k=8),
                            in_=e.part0a[0][:, t0:t0 + 512].rearrange("(k p) t -> p k t", p=128)
                            ).then_inc(ld.h, 16)
                g.dma_start(out=e.trr[:, :].rearrange("p (k t) -> p k t", k=8),
                            in_=e.part0a[1][:, t0:t0 + 512].rearrange("(k p) t -> p k t", p=128)
                            ).then_inc(ld.h, 16)
                for r in range(2, NC):
                    g.wait_ge(vV.h, nsum + (r - 1))
                    g.dma_start(out=e.trr[:, :].rearrange("p (k t) -> p k t", k=8),
                                in_=e.part0a[r][:, t0:t0 + 512].rearrange("(k p) t -> p k t", p=128)
                                ).then_inc(ld.h, 16)
        p.g(cld)
        if fused:
            nld0 = ld.inc(32)
            nld = nld0
        else:
            nld0 = ld.inc(48)
            nld = nld0
            nlds = [ld.inc(16) for _ in range(NC - 2)]

        if not fused:
            for r in range(1, NC):
                def psum_add(v, e, r=r, nl=(nld0 if r == 1 else nlds[r - 2])):
                    v.wait_ge(ld.h, nl)
                    v.tensor_add(e.s0r[:, :], e.s0r[:, :], e.trr[:, :])
                    v.drain()
                    v.engine_nop().then_inc(vV.h, 1)
                p.v(psum_add)
                vV.inc()

        def addf(v, e, nl0=nld0):
            if fused:
                v.wait_ge(ld.h, nl0)
            for k in range(8):
                v.tensor_add(e.x1r[:, k * 512:(k + 1) * 512], e.x0r[:, k * 512:(k + 1) * 512],
                             e.s0r[:, k * 512:(k + 1) * 512])
            v.engine_nop().then_inc(vV.h, 1)
        p.v(addf)
        nx1 = vV.inc()

        def sqf(s, e, nx1=nx1, nini=nini):
            s.wait_ge(vV.h, nx1)
            s.wait_ge(ini.h, nini)
            for k in range(8):
                a = s.activation(e.sq[:, k * 512:(k + 1) * 512], e.x1r[:, k * 512:(k + 1) * 512],
                                 AF.Square)
            a.then_inc(sS.h, 1)
        p.s(sqf)
        nsq = sS.inc()

        def mmf(t, e, c=c, nsq=nsq, nvs=vV.n):
            t.wait_ge(sS.h, nsq)
            if c > 0:
                t.wait_ge(vV.h, nvs)
            for k in range(8):
                t.matmul(e.psA[:, :], e.w1024[:, :], e.x1r[:, k * 512:(k + 1) * 512],
                         start=(k == 0), stop=(k == 7))
            for k in range(8):
                mm = t.matmul(e.psB[:, :], e.w1024[:, :], e.sq[:, k * 512:(k + 1) * 512],
                              start=(k == 0), stop=(k == 7))
            mm.then_inc(tP.h, 1)
        p.t(mmf)
        nmm = tP.inc()

        def stv(v, e, nmm=nmm):
            v.wait_ge(tP.h, nmm)
            v.tensor_copy(e.mean[:, :], e.psA[:, :])
            v.drain()
            v.tensor_mul(e.m2[:, :], e.mean[:, :], e.mean[:, :])
            v.drain()
            v.tensor_sub(e.var[:, :], e.psB[:, :], e.m2[:, :])
            v.engine_nop().then_inc(vV.h, 1)
        p.v(stv)
        nvs = vV.inc()

        def sts(s, e, nvs=nvs):
            s.wait_ge(vV.h, nvs)
            s.activation(e.lnv[:, :], e.var[:, :], AF.Ln, bias=e.ceps[:, :])
            s.drain()
            s.activation(e.rstd[:, :], e.lnv[:, :], AF.Exp, scale=-0.5).then_inc(sS.h, 1)
        p.s(sts)
        nss = sS.inc()

        def bcm(t, e, c=c, nss=nss, nxd=vV.n):
            t.wait_ge(sS.h, nss)
            if c > 0:
                t.wait_ge(vV.h, nxd)
            t.matmul(e.pm[:, :], e.ones1[:, :], e.mean[:, :], start=True, stop=True)
            t.matmul(e.pr[:, :], e.ones1[:, :], e.rstd[:, :], start=True, stop=True).then_inc(tP.h, 1)
        p.t(bcm)
        nbc = tP.inc()

        def xnf(v, e, c=c, t0=t0, nbc=nbc):
            v.wait_ge(tP.h, nbc)
            for k in range(8):
                v.tensor_sub(e.tmp[:, :], e.x1r[:, k * 512:(k + 1) * 512], e.pm[:, :])
                v.drain()
                v.tensor_mul(e.xn1[:, k * TSL + t0: k * TSL + t0 + 512], e.tmp[:, :], e.pr[:, :])
                v.drain()
                v.tensor_mul(e.xnf[:, k * 512:(k + 1) * 512], e.tmp[:, :], e.pr[:, :])
                v.drain()
            v.engine_nop().then_inc(vV.h, 1)
        p.v(xnf)
        nxn = vV.inc()

        def xsto(sy, e, t0=t0, nxn=nxn):
            sy.wait_ge(vV.h, nxn)
            sy.dma_start(out=e.x1_o[:, t0:t0 + 512].rearrange("(k p) t -> p k t", p=128),
                         in_=e.x1r[:, :].rearrange("p (k t) -> p k t", k=8)).then_inc(st.h, 16)
            sy.dma_start(out=e.xn1f_o[:, t0:t0 + 512].rearrange("(k p) t -> p k t", p=128),
                         in_=e.xnf[:, :].rearrange("p (k t) -> p k t", k=8)).then_inc(st.h, 16)
        p.y(xsto)
        st.inc(32)

        for which in ("k", "v"):
            wnm = "wk_s" if which == "k" else "wv_s"
            bnm = "bk_s" if which == "k" else "bv_s"
            s1nm = "k256" if which == "k" else "v256"
            onm = "kcs" if which == "k" else "vcs"
            dnm = "kc_o" if which == "k" else "vc_o"

            nke = None
            for m in range(2):
                def kmm(t, e, wnm=wnm, m=m, t0=t0, nxn=nxn, nse=sS.n):
                    t.wait_ge(vV.h, nxn)
                    t.wait_ge(sS.h, nse)
                    for k in range(8):
                        mm = t.matmul(e.pk0[:, :], e[wnm][:, k * 256 + m * 128: k * 256 + (m + 1) * 128],
                                      e.xn1[:, k * TSL + t0: k * TSL + t0 + 512],
                                      start=(k == 0), stop=(k == 7))
                    mm.then_inc(tP.h, 1)
                p.t(kmm)
                nkm = tP.inc()

                def kev(v, e, bnm=bnm, s1nm=s1nm, m=m, nkm=nkm):
                    v.wait_ge(tP.h, nkm)
                    v.tensor_scalar(e[s1nm][:, m * 512:(m + 1) * 512], e.pk0[:, :],
                                    e[bnm][:, m:m + 1], None, ALU.add)
                    v.engine_nop().then_inc(sS.h, 1)
                p.v(kev)
                nke = sS.inc()

            def cmm(t, e, s1nm=s1nm, nke=nke, nse2=sS.n):
                t.wait_ge(sS.h, nke)
                for k in range(2):
                    mm = t.matmul(e.pkc[:, :], e.wc_s[:, k * 64:(k + 1) * 64],
                                  e[s1nm][:, k * 512:(k + 1) * 512], start=(k == 0), stop=(k == 1))
                mm.then_inc(tP.h, 1)
            p.t(cmm)
            ncm = tP.inc()

            def cev(s, e, onm=onm, ncm=ncm, nst=st.n):
                s.wait_ge(tP.h, ncm)
                if nst > 0:
                    s.wait_ge(st.h, nst)
                s.copy(e[onm][:, :], e.pkc[:, :]).then_inc(sS.h, 1)
            p.s(cev)
            nce = sS.inc()

            def ksto(sy, e, onm=onm, dnm=dnm, t0=t0, nce=nce):
                sy.wait_ge(sS.h, nce)
                sy.dma_start(out=e[dnm][:, t0:t0 + 512], in_=e[onm][:, :]).then_inc(st.h, 16)
            p.y(ksto)
            st.inc(16)

        for wi in range(4):
            w = c * 4 + wi

            def smm(t, e, w=w, t0=t0, wi=wi, nxn=nxn, nsev=sS.n):
                t.wait_ge(vV.h, nxn)
                if w > 0:
                    t.wait_ge(sS.h, nsev)
                for k in range(8):
                    mm = t.matmul(e.plg[:, :], e.xn1[:, k * TSL + t0 + wi * 128: k * TSL + t0 + (wi + 1) * 128],
                                  e.wg_s[:, k * 8:(k + 1) * 8], start=(k == 0), stop=(k == 7))
                mm.then_inc(tP.h, 1)
            p.t(smm)
            nsm = tP.inc()

            def sev(s, e, nsm=nsm, nvr=vV.n):
                s.wait_ge(tP.h, nsm)
                if nvr > 0:
                    s.wait_ge(vV.h, nvr)
                s.copy(e.lgs[:, :], e.plg[:, :]).then_inc(sS.h, 1)
            p.s(sev)
            nsev = sS.inc()

            def rt1(v, e, nsev=nsev):
                v.wait_ge(sS.h, nsev)
                v.max(e.srt[:, :], e.lgs[:, :])
                v.drain()
                v.tensor_scalar(e.dd[:, :], e.srt[:, 1:2], e.srt[:, 0:1], None, ALU.subtract)
                v.engine_nop().then_inc(vV.h, 1)
            p.v(rt1)
            nr1 = vV.inc()

            def rt2(s, e, nr1=nr1):
                s.wait_ge(vV.h, nr1)
                s.activation(e.dd[:, :], e.dd[:, :], AF.Exp).then_inc(sS.h, 1)
            p.s(rt2)
            nr2 = sS.inc()

            def rt3(v, e, w=w, nr2=nr2, ntp=tP.n):
                v.wait_ge(sS.h, nr2)
                if w > 0:
                    v.wait_ge(tP.h, ntp)
                v.tensor_scalar(e.dp1[:, :], e.dd[:, :], 1.0, None, ALU.add)
                v.drain()
                v.reciprocal(e.rec[:, :], e.dp1[:, :])
                v.drain()
                v.tensor_mul(e.n1[:, :], e.dd[:, :], e.rec[:, :])
                v.tensor_scalar(e.oh0[:, :], e.lgs[:, :], e.srt[:, 0:1], None, ALU.is_equal)
                v.tensor_scalar(e.oh1[:, :], e.lgs[:, :], e.srt[:, 1:2], None, ALU.is_equal)
                v.drain()
                v.tensor_scalar(e.sc0[:, :], e.oh0[:, :], e.rec[:, 0:1], None, ALU.mult)
                v.tensor_scalar(e.sc1[:, :], e.oh1[:, :], e.n1[:, 0:1], None, ALU.mult)
                v.drain()
                v.tensor_add(e.scw[:, w * 8:(w + 1) * 8], e.sc0[:, :], e.sc1[:, :])
                v.drain()
                v.tensor_copy(e.scb[:, :], e.scw[:, w * 8:(w + 1) * 8])
                v.engine_nop().then_inc(vV.h, 1)
            p.v(rt3)
            nr3 = vV.inc()

            def trx(t, e, w=w, nr3=nr3, nprev=sS.n):
                t.wait_ge(vV.h, nr3)
                if w > 0:
                    t.wait_ge(sS.h, nprev)
                t.transpose(e.ptr[:, :], e.scb[:, :], e.eye_s[:, :]).then_inc(tP.h, 1)
            p.t(trx)
            ntr = tP.inc()

            def trev(s, e, w=w, ntr=ntr):
                s.wait_ge(tP.h, ntr)
                s.copy(e.scf[:, w * 128:(w + 1) * 128], e.ptr[:, :]).then_inc(sS.h, 1)
            p.s(trev)
            sS.inc()

    def stf(sy, e, nvv=vV.n, nss=sS.n):
        sy.wait_ge(vV.h, nvv)
        sy.wait_ge(sS.h, nss)
        sy.dma_start(out=e.xn1b_o.rearrange("(k p) t -> p k t", p=128),
                     in_=e.xn1[:, :].rearrange("p (k t) -> p k t", k=8)).then_inc(st.h, 16)
        sy.dma_start(out=e.sc_o.rearrange("(w p) e -> p w e", p=128),
                     in_=e.scw[:, :].rearrange("p (w e) -> p w e", e=8)).then_inc(st.h, 16)
        sy.dma_start(out=e.scfm_o.rearrange("a (w t) -> a w t", t=128),
                     in_=e.scf[:, :].rearrange("a (w t) -> a w t", t=128)).then_inc(st.h, 16)
        sy.wait_ge(st.h, nfin)
    nfin = st.inc(48)
    p.y(stf)
    return p



def make_p4(T, p=None, fused=False):
    if p is None:
        p = P()
    NCH = T // 512
    NAC = T // 128
    TSL = T // NC if T >= NC * 512 else T
    NSL = T // TSL
    KH = T // 2 if T >= 1024 else T

    p.din("xn1f", (NSL, D, TSL), BF16)
    p.din("kcf", (NSL, 64, TSL), BF16)
    p.din("vcf", (NSL, 64, TSL), BF16)
    p.din("scfm", (NSL, 8, TSL), BF16)
    p.din("wq", (128, 8 * 128), F32)
    p.din("bq_c", (128, 1), F32)
    p.din("we_h", (64, 64), F32)
    p.din("wo_sl", (128, 8 * 128), F32)
    p.din("w1_t", (128, 8 * 8 * 256), F32)
    p.din("b1_c", (128, 2 * 8), F32)
    p.din("w2_t", (128, 8 * 2 * 1024), F32)
    p.din("b2_f", (8, D), F32)
    p.din("c0v", (8, 1), F32)
    p.din("sel8", (8, 8 * 128), F32)
    p.din("utri", (128, 128), F32)
    p.din("eye", (128, 128), F32)
    p.dout("parto", (NSL, D, TSL), F32)
    p.dout("partf", (NSL, D, TSL), F32)
    p.dint("odram", (128, T), BF16)

    p.sbuf("wq_s", (128, 8 * 128), BF16)
    p.sbuf("we_s", (64, 64), BF16)
    p.sbuf("wo_s", (128, 8 * 128), BF16)
    p.sbuf("w1_s", (128, 8 * 8 * 256), BF16)
    p.sbuf("w2_s", (128, 8 * 2 * 1024), BF16)
    p.sbuf("b1_s", (128, 2 * 8), F32)
    p.sbuf("b2_s", (8, D), BF16)
    p.sbuf("c0_s", (8, 1), F32)
    p.sbuf("sel8_s", (8, 8 * 128), BF16)
    p.sbuf("bq_s", (128, 1), F32)
    p.sbuf("bqn", (128, 1), F32)
    p.sbuf("utri_s", (128, 128), F32)
    p.sbuf("eye_s", (128, 64), BF16)
    p.sbuf("onesc", (128, 1), BF16)
    p.sbuf("onesf", (1, 64), F32)
    p.sbuf("qf", (128, T), BF16)
    p.sbuf("kf", (128, KH), BF16)
    p.sbuf("ve", (128, KH), BF16)
    p.sbuf("ob", (128, 128), BF16)
    p.sbuf("oc", (128, 512), BF16)
    p.sbuf("xnc", (128, 8 * 512), BF16)
    p.sbuf("kcc", (64, 512), BF16)
    p.sbuf("vcc", (64, 512), BF16)
    p.sbuf("a1", (128, 512), F32)
    p.sbuf("r2", (128, 512), F32)
    p.sbuf("e2", (128, 512), F32)
    p.sbuf("qh", (64, 128), BF16)
    p.sbuf("kfc0", (64, 128), BF16)
    p.sbuf("vec0", (64, 128), BF16)
    p.sbuf("kfT", (128, 64), BF16)
    p.sbuf("vT", (128, 64), BF16)
    p.sbuf("Sm", (128, 128), BF16)
    p.sbuf("dnv", (1, 128), F32)
    p.sbuf("rcv", (1, 128), F32)
    p.sbuf("rb", (64, 128), F32)
    p.sbuf("KV", (64, 64), BF16)
    p.sbuf("ks", (64, 1), BF16)
    p.sbuf("KVf", (64, 64), F32)
    p.sbuf("ksf", (64, 1), F32)
    p.sbuf("hr", (128, 512), F32)
    p.sbuf("hstore", (128, 16 * 512), BF16)
    p.sbuf("sc8", (8, 512), BF16)
    p.sbuf("sc0", (8, 512), BF16)
    p.sbuf("po2", (128, 2 * 512), F32)
    p.sbuf("pf2", (128, 2 * 512), F32)

    p.psum("pa", (128, 512))
    p.psum("pb2", (128, 512))
    p.psum("pc64", (64, 512))
    p.psum("ptA", (128, 64), BF16)
    p.psum("ptB", (128, 64), BF16)
    p.psum("pD", (1, 128))
    p.psum("pKV", (64, 64))
    p.psum("pks", (64, 1))

    ld = p.sem("ld")
    ini = p.sem("ini")
    tP = p.sem("tP")
    sS = p.sem("sS")
    vV = p.sem("vV")
    st = p.sem("st")

    @p.g
    def _(g, e):
        g.dma_start(out=e.wq_s[:, :], in_=e.wq).then_inc(ld.h, 16)
        g.dma_start(out=e.we_s[:, :], in_=e.we_h).then_inc(ld.h, 16)
        g.dma_start(out=e.wo_s[:, :], in_=e.wo_sl).then_inc(ld.h, 16)
        g.dma_start(out=e.w1_s[:, :], in_=e.w1_t).then_inc(ld.h, 16)
        g.dma_start(out=e.w2_s[:, :], in_=e.w2_t).then_inc(ld.h, 16)
        g.dma_start(out=e.b1_s[:, :], in_=e.b1_c).then_inc(ld.h, 16)
        g.dma_start(out=e.b2_s[:, :], in_=e.b2_f).then_inc(ld.h, 16)
        g.dma_start(out=e.c0_s[:, :], in_=e.c0v).then_inc(ld.h, 16)
        g.dma_start(out=e.sel8_s[:, :], in_=e.sel8).then_inc(ld.h, 16)
        g.dma_start(out=e.bq_s[:, :], in_=e.bq_c).then_inc(ld.h, 16)
        g.dma_start(out=e.utri_s[:, :], in_=e.utri).then_inc(ld.h, 16)
        g.dma_start(out=e.eye_s[0:64, :], in_=e.eye[0:64, 0:64]).then_inc(ld.h, 16)
        g.dma_start(out=e.eye_s[64:128, :], in_=e.eye[0:64, 0:64]).then_inc(ld.h, 16)
        g.memset(e.onesc[:, :], 1.0)
        g.memset(e.onesf[:, :], 1.0)
        g.memset(e.KV[:, :], 0.0)
        g.memset(e.ks[:, :], 0.0)
        g.memset(e.KVf[:, :], 0.0)
        g.memset(e.ksf[:, :], 0.0)
        g.engine_nop().then_inc(ini.h, 1)
    nwld = ld.inc(208)
    nini = ini.inc()

    def bqneg(v, e, nwld=nwld, nini=nini):
        v.wait_ge(ld.h, nwld)
        v.wait_ge(ini.h, nini)
        v.tensor_scalar(e.bqn[:, :], e.bq_s[:, :], -1.0, None, ALU.mult)
        v.engine_nop().then_inc(vV.h, 1)
    p.v(bqneg)
    vV.inc()

    for c in range(NCH):
        t0 = c * 512

        def ald(g, e, t0=t0, c=c, ntp=tP.n, nwld=nwld, TSL=TSL):
            if c == 0:
                g.wait_ge(ld.h, nwld)
            if c > 0:
                g.wait_ge(tP.h, ntp)
            sl, off = t0 // TSL, t0 % TSL
            g.dma_start(out=e.xnc[:, :].rearrange("p (k t) -> p k t", k=8),
                        in_=e.xn1f.rearrange("s d t -> (s d) t")[sl * D:sl * D + D, off:off + 512]
                        .rearrange("(k p) t -> p k t", p=128)).then_inc(ld.h, 16)
            g.dma_start(out=e.kcc[:, :],
                        in_=e.kcf.rearrange("s a t -> (s a) t")[sl * 64:sl * 64 + 64, off:off + 512]).then_inc(ld.h, 16)
            g.dma_start(out=e.vcc[:, :],
                        in_=e.vcf.rearrange("s a t -> (s a) t")[sl * 64:sl * 64 + 64, off:off + 512]).then_inc(ld.h, 16)
        p.g(ald)
        nld = ld.inc(48)

        def qmm(t, e, nld=nld, nse=sS.n, c=c):
            t.wait_ge(ld.h, nld)
            if c > 0:
                t.wait_ge(sS.h, nse)
            for k in range(8):
                mm = t.matmul(e.pa[:, :], e.wq_s[:, k * 128:(k + 1) * 128],
                              e.xnc[:, k * 512:(k + 1) * 512], start=(k == 0), stop=(k == 7))
            mm.then_inc(tP.h, 1)
        p.t(qmm)
        nqm = tP.inc()

        def qep_s(s, e, nqm=nqm, nvp=vV.n):
            s.wait_ge(tP.h, nqm)
            s.wait_ge(vV.h, nvp)
            s.activation(e.a1[:, :], e.pa[:, :], AF.Relu, bias=e.bq_s[:, 0:1])
            s.activation(e.r2[:, :], e.pa[:, :], AF.Relu, scale=-1.0, bias=e.bqn[:, 0:1])
            s.drain()
            s.activation(e.e2[:, :], e.r2[:, :], AF.Exp, scale=-1.0).then_inc(sS.h, 1)
        p.s(qep_s)
        nqe = sS.inc()

        def qep_v(v, e, t0=t0, nqe=nqe):
            v.wait_ge(sS.h, nqe)
            v.tensor_add(e.qf[:, t0:t0 + 512], e.a1[:, :], e.e2[:, :])
            v.engine_nop().then_inc(vV.h, 1)
        p.v(qep_v)
        vV.inc()

        def kvmm(t, e, nld=nld, nse=sS.n, c=c):
            if c > 0:
                t.wait_ge(sS.h, nse)
            t.matmul(e.pc64[:, :], e.we_s[:, :], e.kcc[:, :], start=True, stop=True).then_inc(tP.h, 1)
        p.t(kvmm)
        nkm = tP.inc()

        def kep_s(s, e, nkm=nkm, nvp=vV.n):
            s.wait_ge(tP.h, nkm)
            s.wait_ge(vV.h, nvp)
            s.activation(e.a1[0:64, :], e.pc64[:, :], AF.Relu)
            s.activation(e.r2[0:64, :], e.pc64[:, :], AF.Relu, scale=-1.0)
            s.drain()
            s.activation(e.e2[0:64, :], e.r2[0:64, :], AF.Exp, scale=-1.0).then_inc(sS.h, 1)
        p.s(kep_s)
        nke = sS.inc()

        def kep_v(v, e, t0=t0, nke=nke, KH=KH):
            v.wait_ge(sS.h, nke)
            hb = 64 * (t0 // KH)
            tt = t0 % KH
            v.tensor_add(e.kf[hb:hb + 64, tt:tt + 512], e.a1[0:64, :], e.e2[0:64, :])
            v.engine_nop().then_inc(vV.h, 1)
        p.v(kep_v)
        nkv = vV.inc()

        def vmm(t, e, nkv=nkv):
            t.wait_ge(vV.h, nkv)
            t.matmul(e.pc64[:, :], e.we_s[:, :], e.vcc[:, :], start=True, stop=True).then_inc(tP.h, 1)
        p.t(vmm)
        nvm = tP.inc()

        def vev(s, e, t0=t0, nvm=nvm, KH=KH):
            s.wait_ge(tP.h, nvm)
            hb = 64 * (t0 // KH)
            tt = t0 % KH
            s.copy(e.ve[hb:hb + 64, tt:tt + 512], e.pc64[:, :]).then_inc(sS.h, 1)
        p.s(vev)
        sS.inc()

    nA_v = vV.n
    nA_s = sS.n

    for c in range(NAC):
        u0 = c * 128

        def bcp(v, e, c=c, u0=u0, nA_v=nA_v, nA_s=nA_s, KH=KH, ntpp=tP.n):
            if c == 0:
                v.wait_ge(vV.h, nA_v)
                v.wait_ge(sS.h, nA_s)
            else:
                v.wait_ge(tP.h, ntpp)
            hb = 64 * (u0 // KH)
            uu = u0 % KH
            v.tensor_copy(e.kfc0[:, :], e.kf[hb:hb + 64, uu:uu + 128])
            v.tensor_copy(e.vec0[:, :], e.ve[hb:hb + 64, uu:uu + 128])
            v.engine_nop().then_inc(vV.h, 1)
        p.v(bcp)
        nbcp = vV.inc()

        def btr(t, e, c=c, nbcp=nbcp, nsev=sS.n):
            t.wait_ge(vV.h, nbcp)
            if c > 0:
                t.wait_ge(sS.h, nsev)
            t.transpose(e.ptA[:, :], e.kfc0[:, :], e.eye_s[0:64, :])
            t.transpose(e.ptB[:, :], e.vec0[:, :], e.eye_s[0:64, :]).then_inc(tP.h, 1)
        p.t(btr)
        ntr = tP.inc()

        def bev(s, e, ntr=ntr, nvv=vV.n):
            s.wait_ge(tP.h, ntr)
            if nvv > 0:
                s.wait_ge(vV.h, nvv)
            s.copy(e.kfT[:, :], e.ptA[:, :])
            s.copy(e.vT[:, :], e.ptB[:, :]).then_inc(sS.h, 1)
        p.s(bev)
        nbe = sS.inc()

        for hh in range(2):
            def qcp(v, e, hh=hh, u0=u0, nbe=nbe, ntp=tP.n):
                v.wait_ge(sS.h, nbe)
                if not (c == 0 and hh == 0):
                    v.wait_ge(tP.h, ntp)
                v.tensor_copy(e.qh[:, :], e.qf[hh * 64:(hh + 1) * 64, u0:u0 + 128])
                v.engine_nop().then_inc(vV.h, 1)
            p.v(qcp)
            nqc = vV.inc()

            def bs(t, e, hh=hh, nqc=nqc):
                t.wait_ge(vV.h, nqc)
                t.matmul(e.pa[:, 0:128], e.kfc0[:, :], e.qh[:, :],
                         start=True, stop=True).then_inc(tP.h, 1)
            p.t(bs)
            nbs = tP.inc()

            def bsm(v, e, nbs=nbs):
                v.wait_ge(tP.h, nbs)
                v.tensor_mul(e.Sm[:, :], e.pa[:, 0:128], e.utri_s[:, :])
                v.engine_nop().then_inc(vV.h, 1)
            p.v(bsm)
            nsm = vV.inc()

            def bnd(t, e, hh=hh, c=c, nsm=nsm):
                t.wait_ge(vV.h, nsm)
                t.matmul(e.pc64[:, 0:128], e.vT[:, :], e.Sm[:, :], start=True, stop=False)
                t.matmul(e.pc64[:, 0:128], e.KV[:, :], e.qh[:, :], start=False, stop=True)
                t.matmul(e.pD[:, :], e.onesc[:, :], e.Sm[:, :], start=True, stop=False)
                t.matmul(e.pD[:, :], e.ks[:, :], e.qh[:, :], start=False, stop=True).then_inc(tP.h, 1)
            p.t(bnd)
            nnd = tP.inc()

            def brc(v, e, nnd=nnd):
                v.wait_ge(tP.h, nnd)
                v.tensor_scalar(e.dnv[:, :], e.pD[:, :], 1e-6, None, ALU.add)
                v.drain()
                v.reciprocal(e.rcv[:, :], e.dnv[:, :])
                v.engine_nop().then_inc(vV.h, 1)
            p.v(brc)
            nrc = vV.inc()

            def brb(t, e, nrc=nrc, nsrb=sS.n):
                t.wait_ge(vV.h, nrc)
                if nsrb > 0:
                    t.wait_ge(sS.h, nsrb)
                t.matmul(e.pb2[0:64, 0:128], e.onesf[:, :], e.rcv[:, :],
                         start=True, stop=True).then_inc(tP.h, 1)
            p.t(brb)
            nrb = tP.inc()

            def brbe(s, e, nrb=nrb):
                s.wait_ge(tP.h, nrb)
                s.copy(e.rb[:, :], e.pb2[0:64, 0:128]).then_inc(sS.h, 1)
            p.s(brbe)
            nrbe = sS.inc()

            def bo(v, e, hh=hh, u0=u0, nrbe=nrbe, nstb=st.n):
                v.wait_ge(sS.h, nrbe)
                if hh == 0 and nstb > 0:
                    v.wait_ge(st.h, nstb)
                v.tensor_mul(e.ob[hh * 64:(hh + 1) * 64, :], e.pc64[:, 0:128],
                             e.rb[:, :])
                v.engine_nop().then_inc(vV.h, 1)
            p.v(bo)
            vV.inc()

        def osto(sy, e, c=c, u0=u0, nvo=vV.n):
            sy.wait_ge(vV.h, nvo)
            sy.dma_start(out=e.odram[:, u0:u0 + 128], in_=e.ob[:, :]).then_inc(st.h, 16)
        p.y(osto)
        st.inc(16)

        def bkv(t, e, c=c, NAC=NAC, nvo=vV.n, nsup=sS.n):
            t.wait_ge(vV.h, nvo)
            if c > 0:
                t.wait_ge(sS.h, nsup)
            t.matmul(e.pKV[:, :], e.kfT[:, :], e.vT[:, :], start=True, stop=True)
            t.matmul(e.pks[:, :], e.kfT[:, :], e.onesc[:, :],
                     start=True, stop=True).then_inc(tP.h, 1)
        p.t(bkv)
        nkvu = tP.inc()

        def bkve(v, e, nkvu=nkvu):
            v.wait_ge(tP.h, nkvu)
            v.tensor_add(e.KVf[:, :], e.KVf[:, :], e.pKV[:, :])
            v.tensor_add(e.ksf[:, :], e.ksf[:, :], e.pks[:, :])
            v.drain()
            v.tensor_copy(e.KV[:, :], e.KVf[:, :])
            v.tensor_copy(e.ks[:, :], e.ksf[:, :])
            v.engine_nop().then_inc(sS.h, 1)
        p.v(bkve)
        sS.inc()

    nB_v = vV.n
    nB_s = sS.n
    nB_t = tP.n
    nB_st = st.n

    for c in range(NCH):
        t0 = c * 512

        def cld(g, e, t0=t0, c=c, ntp=tP.n, nst=st.n, nB_t=nB_t, nB_st=nB_st, TSL=TSL):
            if c == 0:
                g.wait_ge(tP.h, nB_t)
                g.wait_ge(st.h, nB_st)
            if c > 0:
                g.wait_ge(tP.h, ntp)
                g.wait_ge(st.h, nst)
            sl, off = t0 // TSL, t0 % TSL
            g.dma_start(out=e.xnc[:, :].rearrange("p (k t) -> p k t", k=8),
                        in_=e.xn1f.rearrange("s d t -> (s d) t")[sl * D:sl * D + D, off:off + 512]
                        .rearrange("(k p) t -> p k t", p=128)).then_inc(ld.h, 16)
            g.dma_start(out=e.sc8[:, :],
                        in_=e.scfm.rearrange("s a t -> (s a) t")[sl * 8:sl * 8 + 8, off:off + 512]).then_inc(ld.h, 16)
            g.dma_start(out=e.oc[:, :], in_=e.odram[:, t0:t0 + 512]).then_inc(ld.h, 16)
        p.g(cld)
        nld = ld.inc(48)

        def csc(v, e, nld=nld, nB_s=nB_s):
            v.wait_ge(ld.h, nld)
            v.tensor_scalar(e.sc0[:, :], e.sc8[:, :], e.c0_s[:, 0:1], None, ALU.mult)
            v.engine_nop().then_inc(vV.h, 1)
        p.v(csc)
        ncsc = vV.inc()

        for ei in range(8):
            def wbm(t, e, ei=ei, c=c, nld=nld, ncsc=ncsc, nB_v=nB_v, nhw=vV.n):
                if c == 0 and ei == 0:
                    t.wait_ge(vV.h, nB_v)
                t.wait_ge(vV.h, nhw)
                t.matmul(e.pb2[:, :], e.sel8_s[:, ei * 128:(ei + 1) * 128], e.sc8[:, :],
                         start=True, stop=True).then_inc(tP.h, 1)
            p.t(wbm)
            nwb = tP.inc()

            for m in range(2):
                def hmm(t, e, ei=ei, m=m, nld=nld, nse=sS.n):
                    t.wait_ge(ld.h, nld)
                    t.wait_ge(sS.h, nse)
                    for k in range(8):
                        mm = t.matmul(e.pa[:, :],
                                      e.w1_s[:, (k * 8 + ei) * 256 + m * 128: (k * 8 + ei) * 256 + (m + 1) * 128],
                                      e.xnc[:, k * 512:(k + 1) * 512], start=(k == 0), stop=(k == 7))
                    mm.then_inc(tP.h, 1)
                p.t(hmm)
                nhm = tP.inc()

                def hre(s, e, ei=ei, m=m, nhm=nhm, nhw=vV.n):
                    s.wait_ge(tP.h, nhm)
                    if not (c == 0 and ei == 0 and m == 0):
                        s.wait_ge(vV.h, nhw)
                    s.activation(e.hr[:, :], e.pa[:, :], AF.Relu,
                                 bias=e.b1_s[:, ei * 2 + m: ei * 2 + m + 1]).then_inc(sS.h, 1)
                p.s(hre)
                nhr = sS.inc()

                def hwv(v, e, ei=ei, m=m, nhr=nhr, nwb=nwb):
                    v.wait_ge(sS.h, nhr)
                    v.wait_ge(tP.h, nwb)
                    v.tensor_mul(e.hstore[:, (ei * 2 + m) * 512:(ei * 2 + m + 1) * 512],
                                 e.hr[:, :], e.pb2[:, :])
                    v.engine_nop().then_inc(vV.h, 1)
                p.v(hwv)
                vV.inc()

        for m in range(8):
            def omm(t, e, m=m, t0=t0, nse=sS.n, nhid=vV.n):
                t.wait_ge(vV.h, nhid)
                if m > 0:
                    t.wait_ge(sS.h, nse)
                t.matmul(e.pa[:, :], e.wo_s[:, m * 128:(m + 1) * 128],
                         e.oc[:, :], start=True, stop=True).then_inc(tP.h, 1)
            p.t(omm)
            nom = tP.inc()

            def oev(s, e, m=m, nom=nom, nst=st.n):
                s.wait_ge(tP.h, nom)
                if m >= 2:
                    s.wait_ge(st.h, nst)
                s.copy(e.po2[:, (m % 2) * 512:(m % 2 + 1) * 512], e.pa[:, :]).then_inc(sS.h, 1)
            p.s(oev)
            noev = sS.inc()

            def osto2(sy, e, m=m, t0=t0, noev=noev, TSL=TSL):
                sy.wait_ge(sS.h, noev)
                sl, off = t0 // TSL, t0 % TSL
                sy.dma_start(out=e.parto.rearrange("s d t -> (s d) t")[
                                 sl * D + m * 128:sl * D + (m + 1) * 128, off:off + 512],
                             in_=e.po2[:, (m % 2) * 512:(m % 2 + 1) * 512]).then_inc(st.h, 16)
            p.y(osto2)
            st.inc(16)

        for m in range(8):
            def fmm(t, e, m=m, ncsc=ncsc, nse=sS.n):
                t.wait_ge(vV.h, ncsc)
                t.wait_ge(sS.h, nse)
                for ei in range(8):
                    for k2 in range(2):
                        t.matmul(e.pa[:, :],
                                 e.w2_s[:, (ei * 2 + k2) * 1024 + m * 128: (ei * 2 + k2) * 1024 + (m + 1) * 128],
                                 e.hstore[:, (ei * 2 + k2) * 512:(ei * 2 + k2 + 1) * 512],
                                 start=(ei == 0 and k2 == 0), stop=False)
                mm = t.matmul(e.pa[:, :], e.b2_s[:, m * 128:(m + 1) * 128], e.sc0[:, :],
                              start=False, stop=True)
                mm.then_inc(tP.h, 1)
            p.t(fmm)
            nfm = tP.inc()

            def fev(s, e, m=m, nfm=nfm, nst=st.n):
                s.wait_ge(tP.h, nfm)
                if m >= 2:
                    s.wait_ge(st.h, nst)
                s.copy(e.pf2[:, (m % 2) * 512:(m % 2 + 1) * 512], e.pa[:, :]).then_inc(sS.h, 1)
            p.s(fev)
            nfev = sS.inc()

            def fsto2(sy, e, m=m, t0=t0, nfev=nfev, TSL=TSL):
                sy.wait_ge(sS.h, nfev)
                sl, off = t0 // TSL, t0 % TSL
                sy.dma_start(out=e.partf.rearrange("s d t -> (s d) t")[
                                 sl * D + m * 128:sl * D + (m + 1) * 128, off:off + 512],
                             in_=e.pf2[:, (m % 2) * 512:(m % 2 + 1) * 512]).then_inc(st.h, 16)
            p.y(fsto2)
            st.inc(16)


    def fin(sy, e, n=st.n):
        sy.wait_ge(st.h, n)
    p.y(fin)
    return p



def make_p5(TSL, p=None, fused=False):
    if p is None:
        p = P()
    NCH = TSL // 512

    p.din("x1fm", (D, TSL), F32)
    p.din("xn1fm", (D, TSL), F32)
    if fused:
        p.din("parto_a", (D, TSL), F32)
        p.din("partf_a", (D, TSL), F32)
    else:
        p.din("parto_a", (NC, D, TSL), F32)
        p.din("partf_a", (NC, D, TSL), F32)
    p.din("bo_c", (128, 8), F32)
    p.din("gam_c", (128, 8), F32)
    p.din("bet_c", (128, 8), F32)
    p.dout("out_o", (D, TSL), F32)

    p.sbuf("x1r", (128, 8 * 512), F32)
    p.sbuf("xnr", (128, 8 * 512), F32)
    p.sbuf("orr", (128, 8 * 512), F32)
    p.sbuf("frr", (128, 8 * 512), F32)
    if not fused:
        p.sbuf("tr5", (128, 8 * 512), F32)
    p.sbuf("yr", (128, 8 * 512), F32)
    p.sbuf("a1r", (128, 8 * 512), F32)
    p.sbuf("oro", (128, 8 * 512), F32)
    p.sbuf("bo_s", (128, 8), F32)
    p.sbuf("gam_s", (128, 8), F32)
    p.sbuf("bet_s", (128, 8), F32)
    p.sbuf("w1024", (128, 1), F32)
    p.sbuf("ones1", (1, 128), F32)
    p.sbuf("ceps", (1, 1), F32)
    p.sbuf("sq", (128, 8 * 512), F32)
    p.sbuf("mean", (1, 512), F32)
    p.sbuf("m2", (1, 512), F32)
    p.sbuf("var", (1, 512), F32)
    p.sbuf("lnv", (1, 512), F32)
    p.sbuf("rstd", (1, 512), F32)
    p.sbuf("tmp", (128, 512), F32)

    p.psum("psA", (1, 512))
    p.psum("psB", (1, 512))
    p.psum("pm", (128, 512))
    p.psum("pr", (128, 512))

    ld = p.sem("ld")
    ini = p.sem("ini")
    ac = p.sem("ac")
    tP = p.sem("tP")
    sS = p.sem("sS")
    vV = p.sem("vV")
    st = p.sem("st")

    @p.g
    def _(g, e):
        g.dma_start(out=e.bo_s[:, :], in_=e.bo_c).then_inc(ld.h, 16)
        g.dma_start(out=e.gam_s[:, :], in_=e.gam_c).then_inc(ld.h, 16)
        g.dma_start(out=e.bet_s[:, :], in_=e.bet_c).then_inc(ld.h, 16)
        g.memset(e.w1024[:, :], 1.0 / 1024.0)
        g.memset(e.ones1[:, :], 1.0)
        g.memset(e.ceps[:, :], 1e-5)
        g.engine_nop().then_inc(ini.h, 1)
    nwld = ld.inc(48)
    nini = ini.inc()

    for c in range(NCH):
        t0 = c * 512

        def cld(g, e, t0=t0, c=c, nwld=nwld, nvv=vV.n, nst=st.n, nsum=vV.n):
            if c == 0:
                g.wait_ge(ld.h, nwld)
            if c > 0:
                g.wait_ge(vV.h, nvv)
                g.wait_ge(st.h, nst)
            g.dma_start(out=e.x1r[:, :].rearrange("p (k t) -> p k t", k=8),
                        in_=e.x1fm[:, t0:t0 + 512].rearrange("(k p) t -> p k t", p=128)).then_inc(ld.h, 16)
            g.dma_start(out=e.xnr[:, :].rearrange("p (k t) -> p k t", k=8),
                        in_=e.xn1fm[:, t0:t0 + 512].rearrange("(k p) t -> p k t", p=128)).then_inc(ld.h, 16)
            if fused:
                g.dma_start(out=e.orr[:, :].rearrange("p (k t) -> p k t", k=8),
                            in_=e.parto_a[:, t0:t0 + 512].rearrange("(k p) t -> p k t", p=128)).then_inc(ld.h, 16)
                g.dma_start(out=e.frr[:, :].rearrange("p (k t) -> p k t", k=8),
                            in_=e.partf_a[:, t0:t0 + 512].rearrange("(k p) t -> p k t", p=128)).then_inc(ld.h, 16)
            else:
                g.dma_start(out=e.orr[:, :].rearrange("p (k t) -> p k t", k=8),
                            in_=e.parto_a[0][:, t0:t0 + 512].rearrange("(k p) t -> p k t", p=128)).then_inc(ld.h, 16)
                g.dma_start(out=e.frr[:, :].rearrange("p (k t) -> p k t", k=8),
                            in_=e.partf_a[0][:, t0:t0 + 512].rearrange("(k p) t -> p k t", p=128)).then_inc(ld.h, 16)
                g.dma_start(out=e.tr5[:, :].rearrange("p (k t) -> p k t", k=8),
                            in_=e.parto_a[1][:, t0:t0 + 512].rearrange("(k p) t -> p k t", p=128)).then_inc(ld.h, 16)
                for rr in range(1, 2 * NC - 2):
                    r = rr + 1
                    g.wait_ge(vV.h, nsum + rr)
                    if r < NC:
                        g.dma_start(out=e.tr5[:, :].rearrange("p (k t) -> p k t", k=8),
                                    in_=e.parto_a[r][:, t0:t0 + 512].rearrange("(k p) t -> p k t", p=128)).then_inc(ld.h, 16)
                    else:
                        g.dma_start(out=e.tr5[:, :].rearrange("p (k t) -> p k t", k=8),
                                    in_=e.partf_a[r - NC + 1][:, t0:t0 + 512].rearrange("(k p) t -> p k t", p=128)).then_inc(ld.h, 16)
        p.g(cld)
        if fused:
            nld0 = ld.inc(64)
            nld = nld0
        else:
            nld0 = ld.inc(80)
            nld = nld0
            nlds = [ld.inc(16) for _ in range(2 * NC - 3)]

            for rr in range(1, 2 * NC - 1):
                def psum_add(v, e, rr=rr, nl=(nld0 if rr == 1 else nlds[rr - 2])):
                    v.wait_ge(ld.h, nl)
                    dst = e.orr if rr < NC else e.frr
                    v.tensor_add(dst[:, :], dst[:, :], e.tr5[:, :])
                    v.drain()
                    v.engine_nop().then_inc(vV.h, 1)
                p.v(psum_add)
                vV.inc()

        def yb(v, e, nld=nld, nini=nini):
            v.wait_ge(ld.h, nld)
            v.wait_ge(ini.h, nini)
            for k in range(8):
                v.tensor_scalar(e.tmp[:, :], e.orr[:, k * 512:(k + 1) * 512],
                                e.bo_s[:, k:k + 1], None, ALU.add)
                v.drain()
                v.tensor_add(e.yr[:, k * 512:(k + 1) * 512], e.xnr[:, k * 512:(k + 1) * 512],
                             e.tmp[:, :])
                v.drain()
            v.engine_nop().then_inc(vV.h, 1)
        p.v(yb)
        nyb = vV.inc()

        def sqf(s, e, nyb=nyb):
            s.wait_ge(vV.h, nyb)
            for k in range(8):
                a = s.activation(e.sq[:, k * 512:(k + 1) * 512], e.yr[:, k * 512:(k + 1) * 512],
                                 AF.Square)
            a.then_inc(sS.h, 1)
        p.s(sqf)
        nsq = sS.inc()

        def mmf(t, e, c=c, nsq=nsq, nvs=vV.n):
            t.wait_ge(sS.h, nsq)
            if c > 0:
                t.wait_ge(vV.h, nvs)
            for k in range(8):
                t.matmul(e.psA[:, :], e.w1024[:, :], e.yr[:, k * 512:(k + 1) * 512],
                         start=(k == 0), stop=(k == 7))
            for k in range(8):
                mm = t.matmul(e.psB[:, :], e.w1024[:, :], e.sq[:, k * 512:(k + 1) * 512],
                              start=(k == 0), stop=(k == 7))
            mm.then_inc(tP.h, 1)
        p.t(mmf)
        nmm = tP.inc()

        def stv(v, e, nmm=nmm):
            v.wait_ge(tP.h, nmm)
            v.tensor_copy(e.mean[:, :], e.psA[:, :])
            v.drain()
            v.tensor_mul(e.m2[:, :], e.mean[:, :], e.mean[:, :])
            v.drain()
            v.tensor_sub(e.var[:, :], e.psB[:, :], e.m2[:, :])
            v.engine_nop().then_inc(vV.h, 1)
        p.v(stv)
        nvs = vV.inc()

        def sts(s, e, nvs=nvs):
            s.wait_ge(vV.h, nvs)
            s.activation(e.lnv[:, :], e.var[:, :], AF.Ln, bias=e.ceps[:, :])
            s.drain()
            s.activation(e.rstd[:, :], e.lnv[:, :], AF.Exp, scale=-0.5).then_inc(sS.h, 1)
        p.s(sts)
        nss = sS.inc()

        def bcm(t, e, c=c, nss=nss, nxd=vV.n):
            t.wait_ge(sS.h, nss)
            if c > 0:
                t.wait_ge(vV.h, nxd)
            t.matmul(e.pm[:, :], e.ones1[:, :], e.mean[:, :], start=True, stop=True)
            t.matmul(e.pr[:, :], e.ones1[:, :], e.rstd[:, :], start=True, stop=True).then_inc(tP.h, 1)
        p.t(bcm)
        nbc = tP.inc()

        def fino(v, e, c=c, nbc=nbc):
            v.wait_ge(tP.h, nbc)
            for k in range(8):
                v.tensor_sub(e.tmp[:, :], e.yr[:, k * 512:(k + 1) * 512], e.pm[:, :])
                v.drain()
                v.tensor_mul(e.tmp[:, :], e.tmp[:, :], e.pr[:, :])
                v.drain()
                v.tensor_scalar(e.a1r[:, k * 512:(k + 1) * 512], e.tmp[:, :],
                                e.gam_s[:, k:k + 1], e.bet_s[:, k:k + 1], ALU.mult, ALU.add)
                v.drain()
            for k in range(8):
                v.tensor_add(e.tmp[:, :], e.x1r[:, k * 512:(k + 1) * 512],
                             e.a1r[:, k * 512:(k + 1) * 512])
                v.drain()
                v.tensor_add(e.oro[:, k * 512:(k + 1) * 512], e.tmp[:, :],
                             e.frr[:, k * 512:(k + 1) * 512])
                v.drain()
            v.engine_nop().then_inc(vV.h, 1)
        p.v(fino)
        nfo = vV.inc()

        def sto(sy, e, t0=t0, nfo=nfo):
            sy.wait_ge(vV.h, nfo)
            sy.dma_start(out=e.out_o[:, t0:t0 + 512].rearrange("(k p) t -> p k t", p=128),
                         in_=e.oro[:, :].rearrange("p (k t) -> p k t", k=8)).then_inc(st.h, 16)
        p.y(sto)
        st.inc(16)

    def fin(sy, e, n=st.n):
        sy.wait_ge(st.h, n)
    p.y(fin)
    return p



def make_fused(T):
    TSL = T // NC
    cap = max(int(np.ceil(T / 8 * 1.25)), 4)
    p = P()
    RG = [list(range(NC))]

    for nm, sh, dt_ in [
        ("L_xn0s", (D, TSL), BF16), ("G_xn0", (NC, D, TSL), BF16),
        ("L_cnt", (1, 8), F32), ("G_cnt", (NC, 8), F32),
        ("L_lg", (TSL, 9), F32), ("L_rw", (TSL, 8), F32), ("L_mask", (TSL, 8), F32),
        ("L_gws", (9, TSL), BF16), ("G_gw", (NC, 9, TSL), BF16),
        ("L_u", (256, T), BF16), ("L_zs", (256, T), BF16),
        ("L_xpp", (129, T), F32), ("G_xpr", (129, T), F32),
        ("L_ym", (256, T), BF16),
        ("L_p0", (NC, D, TSL), F32), ("R_p0", (D, TSL), F32),
        ("L_x1", (D, TSL), F32), ("L_xn1f", (D, TSL), F32),
        ("L_xn1s", (D, TSL), BF16), ("G_xn1", (NC, D, TSL), BF16),
        ("L_kc", (64, TSL), BF16), ("G_kc", (NC, 64, TSL), BF16),
        ("L_vc", (64, TSL), BF16), ("G_vc", (NC, 64, TSL), BF16),
        ("L_sc", (8, TSL), BF16), ("G_sc", (NC, 8, TSL), BF16),
        ("L_scdump", (TSL, 8), F32),
        ("L_po", (NC, D, TSL), F32), ("R_po", (D, TSL), F32),
        ("L_pf", (NC, D, TSL), F32), ("R_pf", (D, TSL), F32),
    ]:
        p.dint(nm, sh, dt_)

    p.dram_override = {
        "a_xn0b": "L_xn0s", "a_lgo": "L_lg", "a_rwo": "L_rw", "a_masko": "L_mask",
        "a_cnto": "L_cnt",
        "b_lgi": "L_lg", "b_rwi": "L_rw", "b_maski": "L_mask", "b_cntall": "G_cnt",
        "b_gwfm_o": "L_gws",
        "c_xn0f": "G_xn0", "c_u_o": "L_u", "c_zs_o": "L_zs", "c_xpp_o": "L_xpp",
        "d_u_i": "L_u", "d_zs_i": "L_zs", "d_xpr": "G_xpr", "d_ym_o": "L_ym",
        "d_ym_i": "L_ym", "d_xn0f": "G_xn0", "d_gw_fm": "G_gw", "d_part0": "L_p0",
        "f_part0a": "R_p0", "f_x1_o": "L_x1", "f_xn1b_o": "L_xn1s", "f_xn1f_o": "L_xn1f",
        "f_kc_o": "L_kc", "f_vc_o": "L_vc", "f_scfm_o": "L_sc", "f_sc_o": "L_scdump",
        "g_xn1f": "G_xn1", "g_kcf": "G_kc", "g_vcf": "G_vc", "g_scfm": "G_sc",
        "g_parto": "L_po", "g_partf": "L_pf",
        "h_parto_a": "R_po", "h_partf_a": "R_pf", "h_x1fm": "L_x1", "h_xn1fm": "L_xn1f",
    }

    cc = p.sem("cc", shared=True)

    def barrier(prev_ns, colls, stname="st"):
        stp = p.sem_by_name.get(prev_ns + stname)
        nst = stp.n if stp else 0
        ncc0 = cc.n
        cc.inc(len(colls))
        ncc1 = cc.n
        saved_ns = p.ns
        p.ns = ""

        def bar(g, e, stp=stp, nst=nst, colls=tuple(colls), ncc1=ncc1):
            if stp is not None:
                g.wait_ge(stp.h, nst)
            for kind, op, i, o in colls:
                iap = e[i].opt()
                oap = e[o].opt()
                if kind == "ReduceScatter":
                    iap = e[i].rearrange("s d t -> (s d) t").opt()
                g.collective_compute(kind, op, [list(range(NC))],
                                     [iap], [oap]).then_inc(cc.h, 1)
            if colls:
                g.wait_ge(cc.h, ncc1)
        p.g(bar)
        p.ns = saved_ns

    BP = ALU.bypass
    AD = ALU.add
    p.ns = "a_"
    make_p0(TSL, p=p)
    barrier("a_", [("AllGather", BP, "L_xn0s", "G_xn0"),
                   ("AllGather", BP, "L_cnt", "G_cnt")])
    p.ns = "b_"
    make_p0b(TSL, cap, p=p)
    barrier("b_", [("AllGather", BP, "L_gws", "G_gw")])
    p.ns = "c_"
    make_p1(T, p=p)
    barrier("c_", [("AllReduce", AD, "L_xpp", "G_xpr")])
    p.ns = "d_"
    make_p2a(T, p=p, fused=True, moe=True)
    barrier("d_", [("ReduceScatter", AD, "L_p0", "R_p0")], stname="st2")
    p.ns = "f_"
    make_p3(TSL, p=p, fused=True)
    barrier("f_", [("AllGather", BP, "L_xn1s", "G_xn1"),
                   ("AllGather", BP, "L_kc", "G_kc"),
                   ("AllGather", BP, "L_vc", "G_vc"),
                   ("AllGather", BP, "L_sc", "G_sc")])
    p.ns = "g_"
    make_p4(T, p=p)
    barrier("g_", [("ReduceScatter", AD, "L_po", "R_po"),
                   ("ReduceScatter", AD, "L_pf", "R_pf")])
    p.ns = "h_"
    make_p5(TSL, p=p, fused=True)
    p.ns = ""
    return p


FUSED_PHASES = ["a_", "", "b_", "c_", "d_", "f_", "g_", "h_"]


_TOL_T = 16384


def _runs(p, in_maps):
    nc = build_bass(p)
    return run_bass_kernel_spmd(nc, in_maps, core_ids=list(range(NC))).results


def kernel_fused(**w):
    T = T_FULL
    TSL = T // NC
    F = np.float32
    C = np.ascontiguousarray

    x = np.asarray(w["L0_attn_x"])[0]
    xfm = C(x.T)
    gwT = C(np.asarray(w["L0_ffn_gate_w"]).T)
    tril = C(np.tril(np.ones((128, 128), F)).T)
    eye = np.eye(128, dtype=F)
    utri = np.triu(np.ones((128, 128), F))
    sel8 = np.zeros((8, 8 * 128), F)
    for e_ in range(8):
        sel8[e_, e_ * 128:(e_ + 1) * 128] = 1.0
    selgB, selR, sel64q = p2a_consts()
    ipw = np.asarray(w["L0_attn_in_proj_w"])
    cw = np.asarray(w["L0_attn_conv_w"])[:, 0, :]
    cb = np.asarray(w["L0_attn_conv_b"])
    xpw = np.asarray(w["L0_attn_x_proj_w"])
    A_log = np.asarray(w["L0_attn_A_log"])
    dt_w = np.asarray(w["L0_attn_dt_w"])[0]
    dt_b = np.asarray(w["L0_attn_dt_b"])
    Dv = np.asarray(w["L0_attn_D"])
    Wk = np.asarray(w["L1_attn_Wk"]); bk = np.asarray(w["L1_attn_bk"])
    Wv = np.asarray(w["L1_attn_Wv"]); bv = np.asarray(w["L1_attn_bv"])
    Wc = np.asarray(w["L1_attn_Wc"]); Wg1 = np.asarray(w["L1_ffn_Wg"])
    Wq = np.asarray(w["L1_attn_Wq"]); bq = np.asarray(w["L1_attn_bq"])
    We = np.asarray(w["L1_attn_We"]); Wo = np.asarray(w["L1_attn_Wo"])
    W1 = np.asarray(w["L1_ffn_W1"]); b1 = np.asarray(w["L1_ffn_b1"])
    W2 = np.asarray(w["L1_ffn_W2"]); b2 = np.asarray(w["L1_ffn_b2"])
    bo = np.asarray(w["L1_attn_bo"])
    gam = np.asarray(w["L1_attn_gamma"]); bet = np.asarray(w["L1_attn_beta"])

    in_maps = []
    for i in range(NC):
        h = i // 2
        hsl = slice(256 * i, 256 * (i + 1))
        wop, wgu, wdn, wsgu, wsdn = p2b_weights(
            np.asarray(w["L0_attn_out_proj_w"]), np.asarray(w["L0_ffn_dyn_gate_w"]),
            np.asarray(w["L0_ffn_dyn_up_w"]), np.asarray(w["L0_ffn_dyn_down_w"]),
            np.asarray(w["L0_ffn_sh_gate_w"]), np.asarray(w["L0_ffn_sh_up_w"]),
            np.asarray(w["L0_ffn_sh_down_w"]), i)
        w1t = np.stack([tile_kpm(W1[e_][:, hsl]) for e_ in range(8)])
        w2t = np.stack([tile_kpm(W2[e_][hsl, :]) for e_ in range(8)])
        in_maps.append({
            "a_xfm": C(xfm[:, i * TSL:(i + 1) * TSL]), "a_gwT": gwT,
            "b_pmask": (np.arange(NC) < i).astype(F).reshape(NC, 1),
            "b_tril": tril, "b_eyeb": eye,
            "c_wxs": C(ipw[:, 256 * i:256 * (i + 1)]),
            "c_wz": C(ipw[:, DI + 256 * i:DI + 256 * (i + 1)]),
            "c_cwc": C(cw[256 * i:256 * (i + 1)].reshape(2, 128, 4).transpose(1, 0, 2).reshape(128, 8)),
            "c_cbc": C(cb[256 * i:256 * (i + 1)].reshape(2, 128).T),
            "c_xpw": C(xpw[256 * i:256 * (i + 1)]),
            "d_A_lg": p2a_A_layout(A_log[256 * i:256 * (i + 1)]),
            "d_dtw_c": C(dt_w[256 * i:256 * (i + 1)].reshape(2, 128).T),
            "d_dtb_c": C(dt_b[256 * i:256 * (i + 1)].reshape(2, 128).T),
            "d_D_c": C(Dv[256 * i:256 * (i + 1)].reshape(2, 128).T),
            "d_selgB": selgB, "d_selR": selR, "d_sel64q": sel64q,
            "d_wop": wop, "d_wgu": wgu, "d_wdn": wdn, "d_wsgu": wsgu, "d_wsdn": wsdn,
            "f_x0fm": C(xfm[:, i * TSL:(i + 1) * TSL]),
            "f_wk": tile_kpm(Wk), "f_bkc": C(bk.reshape(2, 128).T),
            "f_wv": tile_kpm(Wv), "f_bvc": C(bv.reshape(2, 128).T),
            "f_wc2": tile_kpm(Wc), "f_wg1": tile_kpm(Wg1), "f_eye": eye,
            "g_wq": tile_kpm(Wq[:, 128 * i:128 * (i + 1)]),
            "g_bq_c": C(bq[128 * i:128 * (i + 1)].reshape(128, 1)),
            "g_we_h": C(We[:, 64 * h:64 * (h + 1)]),
            "g_wo_sl": C(Wo[128 * i:128 * (i + 1), :]),
            "g_w1_t": C(w1t.reshape(8, 128, 8, 256).transpose(1, 2, 0, 3).reshape(128, -1)),
            "g_b1_c": C(b1[:, hsl].reshape(8, 2, 128).transpose(2, 0, 1).reshape(128, 16)),
            "g_w2_t": C(w2t.transpose(1, 0, 2).reshape(128, -1)),
            "g_b2_f": C(b2),
            "g_c0v": np.full((8, 1), 1.0 if i == 0 else 0.0, F),
            "g_sel8": sel8, "g_utri": utri, "g_eye": eye,
            "h_bo_c": C(bo.reshape(8, 128).T),
            "h_gam_c": C(gam.reshape(8, 128).T),
            "h_bet_c": C(bet.reshape(8, 128).T),
        })

    pf = make_fused(T)
    nc = build_bass(pf, phase_spans=FUSED_PHASES)
    res = run_bass_kernel_spmd(nc, in_maps, core_ids=list(range(NC))).results
    out = np.concatenate([r["h_out_o"] for r in res], axis=1)
    return np.ascontiguousarray(out.T).reshape(1, T, D).astype(np.float32)


def kernel(**w):
    return kernel_fused(**w)


def kernel_unfused(**w):
    T = T_FULL
    TSL = T // NC
    F = np.float32
    C = np.ascontiguousarray

    x = np.asarray(w["L0_attn_x"])[0]
    xfm = C(x.T)

    gwT = C(np.asarray(w["L0_ffn_gate_w"]).T)
    p0 = make_p0(TSL)
    r0 = _runs(p0, [{"xfm": C(xfm[:, i * TSL:(i + 1) * TSL]), "gwT": gwT}
                    for i in range(NC)])
    xn0f8 = np.stack([r["xn0b"] for r in r0])
    cnt_all = np.concatenate([r["cnto"] for r in r0], axis=0)

    tril = C(np.tril(np.ones((128, 128), F)).T)
    cap = max(int(np.ceil(T / 8 * 1.25)), 4)
    p0b = make_p0b(TSL, cap)
    eye = np.eye(128, dtype=F)
    r0b = _runs(p0b, [{"lgi": r0[i]["lgo"], "rwi": r0[i]["rwo"], "maski": r0[i]["masko"],
                       "cntall": cnt_all, "pmask": (np.arange(NC) < i).astype(F).reshape(NC, 1),
                       "tril": tril, "eyeb": eye} for i in range(NC)])
    gw_fm8 = np.stack([r["gwfm_o"] for r in r0b])

    ipw = np.asarray(w["L0_attn_in_proj_w"])
    cw = np.asarray(w["L0_attn_conv_w"])[:, 0, :]
    cb = np.asarray(w["L0_attn_conv_b"])
    xpw = np.asarray(w["L0_attn_x_proj_w"])
    p1 = make_p1(T)
    r1 = _runs(p1, [{
        "xn0f": xn0f8,
        "wxs": C(ipw[:, 256 * i:256 * (i + 1)]),
        "wz": C(ipw[:, DI + 256 * i:DI + 256 * (i + 1)]),
        "cwc": C(cw[256 * i:256 * (i + 1)].reshape(2, 128, 4).transpose(1, 0, 2).reshape(128, 8)),
        "cbc": C(cb[256 * i:256 * (i + 1)].reshape(2, 128).T),
        "xpw": C(xpw[256 * i:256 * (i + 1)]),
    } for i in range(NC)])
    xpp8 = np.stack([r["xpp_o"] for r in r1])

    A_log = np.asarray(w["L0_attn_A_log"])
    dt_w = np.asarray(w["L0_attn_dt_w"])[0]
    dt_b = np.asarray(w["L0_attn_dt_b"])
    Dv = np.asarray(w["L0_attn_D"])
    selgB, selR, sel64q = p2a_consts()
    p2a = make_p2a(T)
    r2a = _runs(p2a, [{
        "u_i": r1[i]["u_o"], "zs_i": r1[i]["zs_o"], "xpp8": xpp8,
        "A_lg": p2a_A_layout(A_log[256 * i:256 * (i + 1)]),
        "dtw_c": C(dt_w[256 * i:256 * (i + 1)].reshape(2, 128).T),
        "dtb_c": C(dt_b[256 * i:256 * (i + 1)].reshape(2, 128).T),
        "D_c": C(Dv[256 * i:256 * (i + 1)].reshape(2, 128).T),
        "selgB": selgB, "selR": selR, "sel64q": sel64q,
    } for i in range(NC)])

    p2b = make_p2b(T)
    in2b = []
    for i in range(NC):
        wop, wgu, wdn, wsgu, wsdn = p2b_weights(
            np.asarray(w["L0_attn_out_proj_w"]), np.asarray(w["L0_ffn_dyn_gate_w"]),
            np.asarray(w["L0_ffn_dyn_up_w"]), np.asarray(w["L0_ffn_dyn_down_w"]),
            np.asarray(w["L0_ffn_sh_gate_w"]), np.asarray(w["L0_ffn_sh_up_w"]),
            np.asarray(w["L0_ffn_sh_down_w"]), i)
        in2b.append({"ym_i": r2a[i]["ym_o"], "xn0f": xn0f8, "gw_fm": gw_fm8,
                     "wop": wop, "wgu": wgu, "wdn": wdn, "wsgu": wsgu, "wsdn": wsdn})
    r2b = _runs(p2b, in2b)

    part0 = np.stack([r["part0"] for r in r2b])
    Wk = np.asarray(w["L1_attn_Wk"]); bk = np.asarray(w["L1_attn_bk"])
    Wv = np.asarray(w["L1_attn_Wv"]); bv = np.asarray(w["L1_attn_bv"])
    Wc = np.asarray(w["L1_attn_Wc"])
    Wg1 = np.asarray(w["L1_ffn_Wg"])
    p3 = make_p3(TSL)
    r3 = _runs(p3, [{
        "x0fm": C(xfm[:, i * TSL:(i + 1) * TSL]),
        "part0a": C(part0[:, i]),
        "wk": tile_kpm(Wk), "bkc": C(bk.reshape(2, 128).T),
        "wv": tile_kpm(Wv), "bvc": C(bv.reshape(2, 128).T),
        "wc2": tile_kpm(Wc), "wg1": tile_kpm(Wg1), "eye": eye,
    } for i in range(NC)])
    x1f = C(np.concatenate([r["x1_o"] for r in r3], axis=1))
    xn1b8 = np.stack([r["xn1b_o"] for r in r3])
    xn1f = C(np.concatenate([r["xn1f_o"] for r in r3], axis=1))
    kcf8 = np.stack([r["kc_o"] for r in r3])
    vcf8 = np.stack([r["vc_o"] for r in r3])
    scfm8 = np.stack([r["scfm_o"] for r in r3])

    Wq = np.asarray(w["L1_attn_Wq"]); bq = np.asarray(w["L1_attn_bq"])
    We = np.asarray(w["L1_attn_We"]); Wo = np.asarray(w["L1_attn_Wo"])
    W1 = np.asarray(w["L1_ffn_W1"]); b1 = np.asarray(w["L1_ffn_b1"])
    W2 = np.asarray(w["L1_ffn_W2"]); b2 = np.asarray(w["L1_ffn_b2"])
    sel8 = np.zeros((8, 8 * 128), F)
    for e_ in range(8):
        sel8[e_, e_ * 128:(e_ + 1) * 128] = 1.0
    utri = np.triu(np.ones((128, 128), F))
    p4 = make_p4(T)
    in4 = []
    for i in range(NC):
        h = i // 2
        hsl = slice(256 * i, 256 * (i + 1))
        w1t = np.stack([tile_kpm(W1[e_][:, hsl]) for e_ in range(8)])
        w2t = np.stack([tile_kpm(W2[e_][hsl, :]) for e_ in range(8)])
        in4.append({
            "xn1f": xn1b8, "kcf": kcf8, "vcf": vcf8, "scfm": scfm8,
            "wq": tile_kpm(Wq[:, 128 * i:128 * (i + 1)]),
            "bq_c": C(bq[128 * i:128 * (i + 1)].reshape(128, 1)),
            "we_h": C(We[:, 64 * h:64 * (h + 1)]),
            "wo_sl": C(Wo[128 * i:128 * (i + 1), :]),
            "w1_t": C(w1t.reshape(8, 128, 8, 256).transpose(1, 2, 0, 3).reshape(128, -1)),
            "b1_c": C(b1[:, hsl].reshape(8, 2, 128).transpose(2, 0, 1).reshape(128, 16)),
            "w2_t": C(w2t.transpose(1, 0, 2).reshape(128, -1)),
            "b2_f": C(b2),
            "c0v": np.full((8, 1), 1.0 if i == 0 else 0.0, F),
            "sel8": sel8, "utri": utri, "eye": eye,
        })
    r4 = _runs(p4, in4)

    parto = np.stack([r["parto"] for r in r4])
    partf = np.stack([r["partf"] for r in r4])
    bo = np.asarray(w["L1_attn_bo"])
    gam = np.asarray(w["L1_attn_gamma"]); bet = np.asarray(w["L1_attn_beta"])
    p5 = make_p5(TSL)
    r5 = _runs(p5, [{
        "x1fm": C(x1f[:, i * TSL:(i + 1) * TSL]),
        "xn1fm": C(xn1f[:, i * TSL:(i + 1) * TSL]),
        "parto_a": C(parto[:, i]),
        "partf_a": C(partf[:, i]),
        "bo_c": C(bo.reshape(8, 128).T),
        "gam_c": C(gam.reshape(8, 128).T),
        "bet_c": C(bet.reshape(8, 128).T),
    } for i in range(NC)])
    out = np.concatenate([r["out_o"] for r in r5], axis=1)
    return np.ascontiguousarray(out.T).reshape(1, T, D).astype(np.float32)
